# version 03
# baseline (speedup 1.0000x reference)
import contextlib
import math
import numpy as np
import ml_dtypes
import concourse.bass as bass
import concourse.mybir as mybir
from concourse.bass_utils import run_bass_kernel_spmd

F32 = mybir.dt.float32
BF16 = mybir.dt.bfloat16
AF = mybir.ActivationFunctionType
ALU = mybir.AluOpType
AX = mybir.AxisListType

T = 4096
D = 1024
CT = D // 128
NTT = T // 512
N_CORES = 8

ENGINES = ("sy", "ac", "dv", "pe", "po")
SAME_ENGINE_SYNC = {"sy", "ac", "dv", "po"}


class Trk:
    __slots__ = ("w", "readers")

    def __init__(self):
        self.w = None
        self.readers = []


def trks(n):
    return [Trk() for _ in range(n)]


def regt(lst, lo, hi, width=512):
    return lst[lo // width:(hi + width - 1) // width]


N_DMA_SEMS = 64


class Prog:

    def __init__(self, nc):
        self.nc = nc
        self.es = contextlib.ExitStack()
        self.ops = {e: [] for e in ENGINES}
        self.gseq = 0

    def sbuf(self, name, shape, dtype=F32):
        return self.es.enter_context(self.nc.sbuf_tensor(name, shape, dtype))

    def psum(self, name, shape, dtype=F32):
        return self.es.enter_context(self.nc.psum_tensor(name, shape, dtype))

    def op(self, eng, fn, r=(), w=(), dma=False):
        i = len(self.ops[eng])
        deps = set()
        for t in r:
            if t.w is not None:
                deps.add(t.w)
            t.readers.append((eng, i))
        for t in w:
            if t.w is not None:
                deps.add(t.w)
            for rd in t.readers:
                deps.add(rd)
            t.w = (eng, i)
            t.readers = []
        deps.discard((eng, i))
        deps = {d for d in deps if d[0] != eng or eng in SAME_ENGINE_SYNC}
        self.ops[eng].append([fn, deps, dma, False, self.gseq])
        self.gseq += 1
        return i

    def emit(self):
        nc = self.nc
        for eng in ENGINES:
            for rec in self.ops[eng]:
                for (de, dj) in rec[1]:
                    self.ops[de][dj][3] = True
        compval = {}
        for eng in ENGINES:
            c = 0
            for j, rec in enumerate(self.ops[eng]):
                if not rec[2] and rec[3]:
                    c += 1
                    compval[(eng, j)] = c
        dmaslot = {}
        pool_of = {}
        npools = 0
        for eng in ENGINES:
            if any(rec[2] for rec in self.ops[eng]):
                pool_of[eng] = npools
                npools += 1
        per_pool = N_DMA_SEMS // max(npools, 1)
        for eng, pi in pool_of.items():
            dma_ops = [(rec[4], j) for j, rec in enumerate(self.ops[eng])
                       if rec[2] and rec[3]]
            dma_ops.sort()
            gens = [0] * per_pool
            for n, (_, j) in enumerate(dma_ops):
                si = pi * per_pool + n % per_pool
                gens[n % per_pool] += 1
                dmaslot[(eng, j)] = (si, gens[n % per_pool])
        csem = {e: self.es.enter_context(nc.semaphore(f"sem_{e}"))
                for e in ENGINES}
        dsem = [self.es.enter_context(nc.semaphore(f"dsem{i}"))
                for i in range(npools * per_pool)]
        junk = {eng: self.es.enter_context(nc.semaphore(f"sem_junk_{eng}"))
                for eng in pool_of}

        final_gen = {}
        for (eng, j), (si, gen) in dmaslot.items():
            final_gen[si] = max(final_gen.get(si, 0), gen)
        junk_cnt = {eng: sum(1 for rec in self.ops[eng] if rec[2] and not rec[3])
                    for eng in pool_of}

        def section(eng_name):
            def body(engobj):
                waited = {}

                def wait(key, v):
                    if v > waited.get(key, 0):
                        sem = dsem[key[1]] if key[0] == "d" else csem[key[1]]
                        engobj.wait_ge(sem, v)
                        waited[key] = v
                for j, rec in enumerate(self.ops[eng_name]):
                    fn, deps, dma, inc, _ = rec
                    wants = {}
                    for (de, dj) in deps:
                        drec = self.ops[de][dj]
                        if drec[2]:
                            si, gen = dmaslot[(de, dj)]
                            key = ("d", si)
                            v = 16 * gen
                        else:
                            key = ("c", de)
                            v = compval[(de, dj)]
                        if v > wants.get(key, 0):
                            wants[key] = v
                    for key in sorted(wants):
                        wait(key, wants[key])
                    if dma and inc:
                        si, gen = dmaslot[(eng_name, j)]
                        if gen > 1:
                            wait(("d", si), 16 * (gen - 1))
                    inst = fn(engobj)
                    if dma:
                        if inc:
                            si, gen = dmaslot[(eng_name, j)]
                            inst.then_inc(dsem[si], 16)
                        else:
                            inst.then_inc(junk[eng_name], 16)
                    elif inc:
                        inst.then_inc(csem[eng_name], 1)
                if eng_name in pool_of:
                    pi = pool_of[eng_name]
                    for si in range(pi * per_pool, (pi + 1) * per_pool):
                        if final_gen.get(si):
                            wait(("d", si), 16 * final_gen[si])
                    if junk_cnt[eng_name]:
                        engobj.wait_ge(junk[eng_name], 16 * junk_cnt[eng_name])
            return body

        with nc.Block() as block:
            block.sync(section("sy"))
            block.scalar(section("ac"))
            block.vector(section("dv"))
            block.tensor(section("pe"))
            block.gpsimd(section("po"))
        return csem



def tile_lhs(w_eff):
    din, dout = w_eff.shape
    kt, ct = din // 128, dout // 128
    return np.ascontiguousarray(
        w_eff.reshape(kt, 128, ct, 128).transpose(1, 0, 2, 3).reshape(128, -1)
    ).astype(np.float32)


def tile_rhs(w_eff):
    din, dout = w_eff.shape
    kt = din // 128
    return np.ascontiguousarray(
        w_eff.reshape(kt, 128, dout).transpose(1, 0, 2)
    ).astype(np.float32)


def col_tile(vec):
    return np.ascontiguousarray(np.asarray(vec).reshape(CT, 128).T).astype(np.float32)


def build_consts():
    c = {}
    c["ones_col"] = np.ones((128, 1), np.float32)
    c["ones_row"] = np.ones((1, 128), np.float32)
    bo2 = np.zeros((128, 2), np.float32)
    bo2[:64, 0] = 1.0
    bo2[64:, 1] = 1.0
    c["bo2"] = bo2
    c["bo2T"] = np.ascontiguousarray(bo2.T)
    c["ident"] = np.eye(128, dtype=np.float32)
    tk = np.arange(128)[:, None]
    tq = np.arange(128)[None, :]
    mt = ((tk // 16 < tq // 16) | ((tk // 16 == tq // 16) & (tk <= tq))).astype(np.float32)
    c["maskT2"] = np.ascontiguousarray(np.concatenate([mt, mt], axis=1))
    eps = np.zeros((128, 8), np.float32)
    eps[:, 0] = 1e-5
    eps[:, 1] = 1e-12
    eps[:, 2] = 1e-6
    eps[:, 3] = 0.125
    eps[:, 4] = 128.0
    c["eps"] = eps
    jj = np.arange(128)[:, None]
    ii = np.arange(128)[None, :]
    c["pmatT"] = (ii - 1.0 - jj).astype(np.float32)
    c["strictT"] = (jj < ii).astype(np.float32)
    c["iota_row"] = np.arange(128, dtype=np.float32).reshape(1, 128)
    c["r127mj_col"] = (127.0 - np.arange(128, dtype=np.float32)).reshape(128, 1)
    return c



class Ctx:
    pass


def make_ctx(p, nc, const_host):
    cx = Ctx()
    cx.p = p
    cx.nc = nc
    cx.pb = [p.psum(f"pb{i}", [128, 512]) for i in range(6)]
    cx.pbk = trks(6)
    cx.pbf = [p.psum("pbfA", [128, 1024], BF16), p.psum("pbfB", [128, 1024], BF16)]
    cx.pbfk = trks(2)
    cx.rhs = [p.sbuf("ws_rhsA", [128, 8, 512], BF16),
              p.sbuf("ws_rhsB", [128, 8, 512], BF16)]
    cx.rhsk = [trks(8), trks(8)]
    cx.wa = p.sbuf("ws_wa", [128, 8192], BF16)
    cx.wak = Trk()
    cx.wb = p.sbuf("ws_wb", [128, 8192], BF16)
    cx.wbk = Trk()
    cx.wc = p.sbuf("ws_wc", [128, 2048], BF16)
    cx.wck = trks(4)
    cx.f32a = p.sbuf("ws_f32a", [128, 4096], F32)
    cx.fa_t = trks(8)
    cx.f32b = p.sbuf("ws_f32b", [128, 4096], F32)
    cx.fb_t = trks(8)
    cx.acc = p.sbuf("ws_acc", [128, 8192], F32)
    cx.acc_t = trks(16)
    cx.big = p.sbuf("ws_big", [128, 25600], BF16)
    cx.big_t = trks(50)
    cx.sm = p.sbuf("ws_sm", [128, 1024], F32)
    cx.sm_t = trks(16)
    cx.r2a = p.sbuf("ws_r2a", [2, 512], F32)
    cx.r2ak = Trk()
    cx.r2b = p.sbuf("ws_r2b", [2, 512], F32)
    cx.r2bk = Trk()
    cx.r1 = [p.sbuf(f"ws_r1{i}", [1, 512], F32) for i in range(4)]
    cx.r1k = trks(4)
    cx.c = {}
    cx.ck = {}
    for name, arr in const_host.items():
        dh = nc.dram_tensor("c_" + name, list(arr.shape), F32, kind="ExternalInput")
        sb = p.sbuf("cs_" + name, list(arr.shape), F32)
        k = Trk()
        ap = dh.ap()
        p.op("sy", lambda e, sb=sb, ap=ap: e.dma_start(out=sb[:], in_=ap),
             w=[k], dma=True)
        cx.c[name] = sb
        cx.ck[name] = k
    cx.c_id_bf = p.sbuf("cs_id_bf", [128, 128], BF16)
    cx.ck_id_bf = Trk()
    p.op("dv", lambda e: e.tensor_copy(cx.c_id_bf[:], cx.c["ident"][:]),
         r=[cx.ck["ident"]], w=[cx.ck_id_bf])
    cx.zero_bf = p.sbuf("cs_zero_bf", [128, 512], BF16)
    cx.ck_zero = Trk()
    p.op("dv", lambda e: e.memset(cx.zero_bf[:], 0.0), w=[cx.ck_zero])
    return cx


def win(cx, layer, name, shape, host_fn):
    nm = f"L{layer}_{name}"
    dh = cx.nc.dram_tensor(nm, list(shape), F32, kind="ExternalInput")
    cx.host_inputs[nm] = host_fn
    return dh.ap(), Trk()


def dram(cx, name, shape, dtype=F32, kind="Internal", n_trk=1):
    h = cx.nc.dram_tensor(name, list(shape), dtype, kind=kind)
    return h.ap(), trks(n_trk)


def load_w(cx, sb_ap, dram_ap, rtrk, wtrk):
    cx.p.op("po", lambda e: e.dma_start(out=sb_ap, in_=dram_ap),
            r=[rtrk], w=[wtrk], dma=True)


def precast(cx, layer, name, shape, host_fn, n_parts=4):
    ap32, k32 = win(cx, layer, name, shape, host_fn)
    flat = int(np.prod(shape))
    h = cx.nc.dram_tensor(f"L{layer}_{name}_bf", list(shape), BF16, kind="Internal")
    bf_ap = h.ap()
    kbf = Trk()
    step = flat // n_parts
    f32_flat = ap32.rearrange(" ".join(f"a{i}" for i in range(len(shape))) + " -> (" +
                              " ".join(f"a{i}" for i in range(len(shape))) + ")")         if False else None
    for i in range(n_parts):
        cx.p.op("po", lambda e, i=i: e.dma_start(
            out=bf_ap.flatten()[i * step:(i + 1) * step],
            in_=ap32.flatten()[i * step:(i + 1) * step]),
            r=[k32], w=[kbf], dma=True)
    return bf_ap, kbf



def ln_pass(cx, src, srck, dst32, dst32k, dstbf, dstbfk, gamma_col=None,
            beta_col=None, gbk=()):
    p = cx.p
    eps = cx.c["eps"]
    for tt in range(NTT):
        t0 = tt * 512
        xa = cx.f32a[:].rearrange("p (c f) -> p c f", f=512)
        sq = cx.f32b[:].rearrange("p (c f) -> p c f", f=512)
        p.op("sy", lambda e, t0=t0, xa=xa: e.dma_start(
            out=xa[:, :, :], in_=src[:, :, t0:t0 + 512].rearrange("c p t -> p c t")),
            r=list(srck), w=cx.fa_t, dma=True)
        for c in range(CT):
            p.op("ac", lambda e, c=c: e.activation(sq[:, c, :], xa[:, c, :], AF.Square),
                 r=[cx.fa_t[c]], w=[cx.fb_t[c]])

        def mmsum(srcv, srck_, bank):
            def f(e, srcv=srcv, bank=bank):
                last = None
                for c in range(CT):
                    last = e.matmul(cx.pb[bank][0:1, :], cx.c["ones_col"][:],
                                    srcv[:, c, :], start=(c == 0), stop=(c == CT - 1))
                return last
            p.op("pe", f, r=list(srck_) + [cx.ck["ones_col"]], w=[cx.pbk[bank]])
        mmsum(xa, cx.fa_t, 0)
        mmsum(sq, cx.fb_t, 1)
        r1, r1k = cx.r1, cx.r1k
        p.op("dv", lambda e: e.tensor_scalar(r1[0][:], cx.pb[0][0:1, :], 1.0 / D, None, ALU.mult),
             r=[cx.pbk[0]], w=[r1k[0]])
        p.op("dv", lambda e: e.tensor_scalar(r1[1][:], cx.pb[1][0:1, :], 1.0 / D, None, ALU.mult),
             r=[cx.pbk[1]], w=[r1k[1]])
        p.op("dv", lambda e: e.tensor_tensor(r1[2][:], r1[0][:], r1[0][:], ALU.mult),
             r=[r1k[0]], w=[r1k[2]])
        p.op("dv", lambda e: e.tensor_tensor(r1[2][:], r1[1][:], r1[2][:], ALU.subtract),
             r=[r1k[1], r1k[2]], w=[r1k[2]])
        p.op("ac", lambda e: e.activation(r1[2][:], r1[2][:], AF.Sqrt, bias=eps[0:1, 0:1]),
             r=[r1k[2], cx.ck["eps"]], w=[r1k[2]])
        p.op("dv", lambda e: e.reciprocal(r1[3][:], r1[2][:]),
             r=[r1k[2]], w=[r1k[3]])
        p.op("dv", lambda e: e.tensor_tensor(r1[1][:], r1[0][:], r1[3][:], ALU.mult),
             r=[r1k[0], r1k[3]], w=[r1k[1]])
        p.op("dv", lambda e: e.tensor_scalar(r1[1][:], r1[1][:], -1.0, None, ALU.mult),
             r=[r1k[1]], w=[r1k[1]])
        p.op("pe", lambda e: e.matmul(cx.pb[2][:], cx.c["ones_row"][:], r1[3][:],
                                      start=True, stop=True),
             r=[r1k[3], cx.ck["ones_row"]], w=[cx.pbk[2]])
        p.op("pe", lambda e: e.matmul(cx.pb[3][:], cx.c["ones_row"][:], r1[1][:],
                                      start=True, stop=True),
             r=[r1k[1], cx.ck["ones_row"]], w=[cx.pbk[3]])
        for c in range(CT):
            p.op("dv", lambda e, c=c: e.tensor_tensor(sq[:, c, :], xa[:, c, :], cx.pb[2][:], ALU.mult),
                 r=[cx.fa_t[c], cx.pbk[2], cx.fb_t[c]], w=[cx.fb_t[c]])
            p.op("dv", lambda e, c=c: e.tensor_tensor(sq[:, c, :], sq[:, c, :], cx.pb[3][:], ALU.add),
                 r=[cx.fb_t[c], cx.pbk[3]], w=[cx.fb_t[c]])
            if gamma_col is not None:
                p.op("dv", lambda e, c=c: e.tensor_scalar(
                    sq[:, c, :], sq[:, c, :], gamma_col[:, c:c + 1], beta_col[:, c:c + 1],
                    ALU.mult, ALU.add), r=[cx.fb_t[c]] + list(gbk), w=[cx.fb_t[c]])
        p.op("sy", lambda e, t0=t0, sq=sq: e.dma_start(
            out=dst32[:, :, t0:t0 + 512].rearrange("c p t -> p c t"), in_=sq[:, :, :]),
            r=cx.fb_t, w=[dst32k[tt]], dma=True)
        p.op("po", lambda e, t0=t0, sq=sq: e.dma_start(
            out=dstbf[:, :, t0:t0 + 512].rearrange("c p t -> p c t"), in_=sq[:, :, :]),
            r=cx.fb_t, w=[dstbfk[tt]], dma=True)


def mm_cm(cx, w_view, wk, src, srck_tt, epil, n_out=CT, kt=CT, banks=(0, 1)):
    p = cx.p
    nb = len(banks)
    for tt in range(NTT):
        t0 = tt * 512
        rhs = cx.rhs[tt % 2]
        rk = cx.rhsk[tt % 2]
        p.op("sy", lambda e, t0=t0, rhs=rhs: e.dma_start(
            out=rhs[:, 0:kt, :], in_=src[0:kt, :, t0:t0 + 512].rearrange("k p t -> p k t")),
            r=[srck_tt[min(tt, len(srck_tt) - 1)]], w=rk[:kt], dma=True)
        for c in range(n_out):
            b = banks[(tt * n_out + c) % nb]

            def f(e, c=c, rhs=rhs, b=b):
                last = None
                for k in range(kt):
                    last = e.matmul(cx.pb[b][:], w_view[:, k, c, :], rhs[:, k, :],
                                    start=(k == 0), stop=(k == kt - 1))
                return last
            p.op("pe", f, r=rk[:kt] + [wk], w=[cx.pbk[b]])
            epil(c, tt, b)


def conv_silu_pass(cx, src, srck_c, conv_sb, convk, dstbf, dstbfk_c, l2norm,
                   pad_heads=False):
    p = cx.p
    eps = cx.c["eps"]
    for c in range(CT):
        p.op("sy", lambda e, c=c: e.dma_start(out=cx.f32a[:], in_=src[c]),
             r=[srck_c[c]], w=cx.fa_t, dma=True)
        p.op("dv", lambda e, c=c: e.tensor_scalar(cx.sm[:, 960:961], conv_sb[:, c, 3:4],
                                                  1.0, None, ALU.add),
             r=[convk], w=[cx.sm_t[15]])
        p.op("dv", lambda e: e.tensor_scalar(cx.acc[:, 0:4096], cx.f32a[:], cx.sm[:, 960:961],
                                             None, ALU.mult),
             r=cx.fa_t + [cx.sm_t[15]], w=cx.acc_t[0:8])
        for j, sh in ((2, 1), (1, 2), (0, 3)):
            p.op("dv", lambda e, j=j, sh=sh, c=c: e.scalar_tensor_tensor(
                cx.acc[:, sh:4096], cx.f32a[:, 0:4096 - sh], conv_sb[:, c, j:j + 1],
                cx.acc[:, sh:4096], ALU.mult, ALU.add),
                r=cx.fa_t + [convk] + cx.acc_t[0:8], w=cx.acc_t[0:8])
        p.op("ac", lambda e: e.activation(cx.f32b[:], cx.acc[:, 0:4096], AF.Silu),
             r=cx.acc_t[0:8], w=cx.fb_t)
        if not l2norm:
            p.op("po", lambda e, c=c: e.dma_start(out=dstbf[c], in_=cx.f32b[:]),
                 r=cx.fb_t, w=[dstbfk_c[c]], dma=True)
            continue
        p.op("ac", lambda e: e.activation(cx.acc[:, 0:4096], cx.f32b[:], AF.Square),
             r=cx.fb_t, w=cx.acc_t[0:8])
        for ts_ in range(NTT):
            o = ts_ * 512
            p.op("pe", lambda e, o=o: e.matmul(cx.pb[4][0:2, :], cx.c["bo2"][:],
                                               cx.acc[:, o:o + 512], start=True, stop=True),
                 r=[cx.acc_t[ts_], cx.ck["bo2"]], w=[cx.pbk[4]])
            p.op("ac", lambda e: e.activation(cx.r2a[:], cx.pb[4][0:2, :], AF.Sqrt,
                                              bias=eps[0:2, 1:2]),
                 r=[cx.pbk[4], cx.ck["eps"]], w=[cx.r2ak])
            p.op("dv", lambda e: e.reciprocal(cx.r2b[:], cx.r2a[:]),
                 r=[cx.r2ak], w=[cx.r2bk])
            p.op("pe", lambda e: e.matmul(cx.pb[5][:], cx.c["bo2T"][:], cx.r2b[:],
                                          start=True, stop=True),
                 r=[cx.r2bk, cx.ck["bo2T"]], w=[cx.pbk[5]])
            reg = ts_ % 2
            p.op("dv", lambda e, o=o, reg=reg: e.tensor_tensor(
                cx.wc[:, reg * 512:reg * 512 + 512], cx.f32b[:, o:o + 512],
                cx.pb[5][:], ALU.mult),
                r=[cx.fb_t[ts_], cx.pbk[5]], w=[cx.wck[reg]])
            if not pad_heads:
                p.op("po", lambda e, c=c, o=o, reg=reg: e.dma_start(
                    out=dstbf[c, :, o:o + 512], in_=cx.wc[:, reg * 512:reg * 512 + 512]),
                    r=[cx.wck[reg]], w=[dstbfk_c[c]], dma=True)
            else:
                p.op("po", lambda e, c=c, o=o, reg=reg: e.dma_start(
                    out=dstbf[c, 0, 0:64, o:o + 512],
                    in_=cx.wc[0:64, reg * 512:reg * 512 + 512]),
                    r=[cx.wck[reg]], w=[dstbfk_c[c]], dma=True)
                p.op("po", lambda e, c=c, o=o, reg=reg: e.dma_start(
                    out=dstbf[c, 1, 64:128, o:o + 512],
                    in_=cx.wc[64:128, reg * 512:reg * 512 + 512]),
                    r=[cx.wck[reg]], w=[dstbfk_c[c]], dma=True)
                p.op("po", lambda e, c=c, o=o: e.dma_start(
                    out=dstbf[c, 0, 64:128, o:o + 512], in_=cx.zero_bf[64:128, :]),
                    r=[cx.ck_zero], w=[dstbfk_c[c]], dma=True)
                p.op("po", lambda e, c=c, o=o: e.dma_start(
                    out=dstbf[c, 1, 0:64, o:o + 512], in_=cx.zero_bf[0:64, :]),
                    r=[cx.ck_zero], w=[dstbfk_c[c]], dma=True)


def l0_attn_core(cx, qpad, qpadk, kbf, kbfk, vbf, vbfk, xnbf, xnbfk,
                 winit_ap, winitk, lng_sb, lngk, lnb_sb, lnbk,
                 wg_sb, wgk, wo_view, wok, sT, sTk):
    import os
    kph = int(os.environ.get("KPH", "99"))
    p = cx.p
    big = cx.big

    def bview(off, w3):
        v = big[:, off:off + w3[1] * w3[2]]
        return v.rearrange("p (a b) -> p a b", b=w3[2])
    xnc = cx.rhs[0][:]
    xnck = cx.rhsk[0]
    qpA = cx.rhs[1][:]
    qpAk = cx.rhsk[1]
    kc = bview(0, (128, 8, 512))
    vc = bview(4096, (128, 8, 512))
    ogbf = bview(4096, (128, 4, 1024))
    ktok = bview(8192, (128, 8, 512))
    vtok = bview(12288, (128, 8, 512))
    ogT = bview(16384, (128, 8, 512))
    qpB = bview(20480, (128, 8, 512))
    abf = [big[:, 24576:24832], big[:, 25088:25344]]
    kB = lambda lo, hi: regt(cx.big_t, lo, hi)
    kck = [kB(c * 512, c * 512 + 512)[0] for c in range(8)]
    vck = [kB(4096 + c * 512, 4096 + c * 512 + 512)[0] for c in range(8)]
    ogbfk = [kB(4096 + b * 1024, 4096 + b * 1024 + 1024) for b in range(4)]
    ktokk = [kB(8192 + c * 512, 8192 + c * 512 + 512)[0] for c in range(8)]
    vtokk = [kB(12288 + c * 512, 12288 + c * 512 + 512)[0] for c in range(8)]
    ogTk = kB(16384, 20480)
    qpBk = [kB(20480 + c * 512, 20480 + c * 512 + 512)[0] for c in range(8)]
    abfk = [cx.big_t[48], cx.big_t[49]]
    wacc = cx.sm[:, 0:512]
    wacck = cx.sm_t[0:8]
    wbf = cx.wc[:, 1024:1536]
    wbfk = cx.wck[2]
    gats = cx.acc[:, 4096:5120]
    gatsk = cx.acc_t[8:10]
    p.op("sy", lambda e: e.dma_start(out=wacc, in_=winit_ap),
         r=[winitk], w=wacck, dma=True)
    eps = cx.c["eps"]
    sums_o = cx.sm[:, 512:528]
    sums_q = cx.sm[:, 576:592]
    mean = cx.sm[:, 640:656]
    var = cx.sm[:, 704:720]
    rstd = cx.sm[:, 768:784]
    m2 = cx.sm[:, 832:848]
    sok, sqk_, mk, vk, rk_, m2k = (cx.sm_t[8], cx.sm_t[9], cx.sm_t[10],
                                   cx.sm_t[11], cx.sm_t[12], cx.sm_t[13])

    for ch in range(NTT):
        t0 = ch * 512
        p.op("sy", lambda e, t0=t0: e.dma_start(
            out=qpA[:, :, :], in_=qpad[:, 0, :, t0:t0 + 512].rearrange("c p t -> p c t")),
            r=list(qpadk), w=qpAk, dma=True)
        p.op("sy", lambda e, t0=t0: e.dma_start(
            out=qpB[:, :, :], in_=qpad[:, 1, :, t0:t0 + 512].rearrange("c p t -> p c t")),
            r=list(qpadk), w=qpBk, dma=True)
        p.op("sy", lambda e, t0=t0: e.dma_start(
            out=kc[:, :, :], in_=kbf[:, :, t0:t0 + 512].rearrange("c p t -> p c t")),
            r=list(kbfk), w=kck, dma=True)
        p.op("sy", lambda e, t0=t0: e.dma_start(
            out=vc[:, :, :], in_=vbf[:, :, t0:t0 + 512].rearrange("c p t -> p c t")),
            r=list(vbfk), w=vck, dma=True)
        p.op("sy", lambda e, t0=t0: e.dma_start(
            out=xnc[:, :, :], in_=xnbf[:, :, t0:t0 + 512].rearrange("c p t -> p c t")),
            r=[xnbfk[min(ch, len(xnbfk) - 1)]], w=xnck, dma=True)
        for blk in range(4):
            bo = blk * 128
            for c in range(CT):
                kps = cx.pbf[0][:, c * 128:c * 128 + 128]
                vps = cx.pbf[1][:, c * 128:c * 128 + 128]
                p.op("pe", lambda e, c=c, bo=bo, kps=kps: e.transpose(
                    kps, kc[:, c, bo:bo + 128], cx.c_id_bf[:]),
                    r=[kck[c], cx.ck_id_bf], w=[cx.pbfk[0]])
                p.op("pe", lambda e, c=c, bo=bo, vps=vps: e.transpose(
                    vps, vc[:, c, bo:bo + 128], cx.c_id_bf[:]),
                    r=[vck[c], cx.ck_id_bf], w=[cx.pbfk[1]])
                p.op("dv", lambda e, c=c, bo=bo, kps=kps: e.tensor_copy(
                    ktok[:, c, bo:bo + 128], kps), r=[cx.pbfk[0]], w=[ktokk[c]])
                p.op("dv", lambda e, c=c, bo=bo, vps=vps: e.tensor_copy(
                    vtok[:, c, bo:bo + 128], vps), r=[cx.pbfk[1]], w=[vtokk[c]])
        for blk in range(4):
            bo = blk * 128
            if kph < 2:
                continue
            p.op("dv", lambda e: e.tensor_copy(wbf, wacc), r=list(wacck), w=[wbfk])
            for half in range(2):
                def gmm(e, half=half, bo=bo):
                    last = None
                    for k in range(CT):
                        last = e.matmul(cx.pb[5][:], xnc[:, k, bo:bo + 128],
                                        wg_sb[:, k, half * 512:half * 512 + 512],
                                        start=(k == 0), stop=(k == CT - 1))
                    return last
                p.op("pe", gmm, r=xnck + [wgk], w=[cx.pbk[5]])
                p.op("dv", lambda e, half=half: e.tensor_copy(
                    gats[:, half * 512:half * 512 + 512], cx.pb[5][:]),
                    r=[cx.pbk[5]], w=[gatsk[half]])
            for c in range(CT):
                if kph < 3:
                    continue
                qp = [qpA, qpB]
                qpk = [qpAk, qpBk]

                def amm(e, c=c, bo=bo):
                    last = None
                    for j in range(2):
                        last = e.matmul(cx.pb[0][:, j * 128:j * 128 + 128],
                                        kc[:, c, bo:bo + 128],
                                        qp[j][:, c, bo:bo + 128],
                                        start=True, stop=True)
                    return last
                p.op("pe", amm, r=[kck[c], qpAk[c], qpBk[c]], w=[cx.pbk[0]])
                ab = c % 2
                p.op("dv", lambda e, c=c, ab=ab: e.tensor_tensor(
                    abf[ab], cx.pb[0][:, 0:256], cx.c["maskT2"][:], ALU.mult),
                    r=[cx.pbk[0], cx.ck["maskT2"]], w=[abfk[ab]])
                if kph < 4:
                    continue
                obank = 1 + c // 4
                ocol = (c % 4) * 128

                def oim(e, c=c, bo=bo, ab=ab, obank=obank, ocol=ocol):
                    last = None
                    for hl in range(2):
                        h = 2 * c + hl
                        last = e.matmul(cx.pb[obank][:, ocol + hl * 64:ocol + hl * 64 + 64],
                                        abf[ab][:, hl * 128:hl * 128 + 128],
                                        vtok[:, c, bo:bo + 128][:, hl * 64:hl * 64 + 64],
                                        start=True, stop=False)
                        last = e.matmul(cx.pb[obank][:, ocol + hl * 64:ocol + hl * 64 + 64],
                                        qp[hl][:, c, bo:bo + 128],
                                        wbf[:, (h // 2) * 64:(h // 2) * 64 + 64],
                                        start=False, stop=True)
                    return last
                p.op("pe", oim, r=[abfk[ab], vtokk[c], qpAk[c], qpBk[c], wbfk],
                     w=[cx.pbk[obank]])

                if kph < 5:
                    continue

                def smm(e, c=c, bo=bo):
                    last = None
                    for hl in range(2):
                        last = e.matmul(cx.pb[3 + hl][:, c * 64:c * 64 + 64],
                                        ktok[:, c, bo:bo + 128],
                                        vtok[:, c, bo:bo + 128][:, hl * 64:hl * 64 + 64],
                                        start=True, stop=True)
                    return last
                p.op("pe", smm, r=[ktokk[c], vtokk[c]], w=[cx.pbk[3], cx.pbk[4]])
            if kph < 6:
                continue
            for bi in range(2):
                p.op("ac", lambda e, bi=bi: e.activation(
                    cx.f32a[:, bi * 512:bi * 512 + 512], cx.pb[1 + bi][:], AF.Copy),
                    r=[cx.pbk[1 + bi]], w=[cx.fa_t[bi]])
                p.op("ac", lambda e, bi=bi: e.activation(
                    cx.f32b[:, bi * 512:bi * 512 + 512], cx.pb[1 + bi][:], AF.Square),
                    r=[cx.pbk[1 + bi]], w=[cx.fb_t[bi]])
                p.op("dv", lambda e, bi=bi: e.tensor_reduce(
                    sums_o[:, bi * 8:bi * 8 + 8],
                    cx.f32a[:, bi * 512:bi * 512 + 512].rearrange("p (h d) -> p h d", d=64),
                    AX.X, ALU.add), r=[cx.fa_t[bi]], w=[sok])
                p.op("dv", lambda e, bi=bi: e.tensor_reduce(
                    sums_q[:, bi * 8:bi * 8 + 8],
                    cx.f32b[:, bi * 512:bi * 512 + 512].rearrange("p (h d) -> p h d", d=64),
                    AX.X, ALU.add), r=[cx.fb_t[bi]], w=[sqk_])
            p.op("dv", lambda e: e.tensor_scalar(mean, sums_o, 1.0 / 64, None, ALU.mult),
                 r=[sok], w=[mk])
            p.op("dv", lambda e: e.tensor_scalar(var, sums_q, 1.0 / 64, None, ALU.mult),
                 r=[sqk_], w=[vk])
            p.op("dv", lambda e: e.tensor_tensor(m2, mean, mean, ALU.mult),
                 r=[mk], w=[m2k])
            p.op("dv", lambda e: e.tensor_tensor(var, var, m2, ALU.subtract),
                 r=[vk, m2k], w=[vk])
            p.op("ac", lambda e: e.activation(var, var, AF.Sqrt, bias=eps[:, 0:1]),
                 r=[vk, cx.ck["eps"]], w=[vk])
            p.op("dv", lambda e: e.reciprocal(rstd, var), r=[vk], w=[rk_])
            for bi in range(2):
                p.op("dv", lambda e, bi=bi: e.tensor_tensor(
                    cx.f32a[:, bi * 512:bi * 512 + 512].rearrange("p (h d) -> p h d", d=64),
                    cx.f32a[:, bi * 512:bi * 512 + 512].rearrange("p (h d) -> p h d", d=64),
                    mean[:, bi * 8:bi * 8 + 8].unsqueeze(2).broadcast_to((128, 8, 64)),
                    ALU.subtract), r=[cx.fa_t[bi], mk], w=[cx.fa_t[bi]])
                p.op("dv", lambda e, bi=bi: e.tensor_tensor(
                    cx.f32b[:, bi * 512:bi * 512 + 512].rearrange("p (h d) -> p h d", d=64),
                    lng_sb[:].rearrange("p (h d) -> p h d", d=64),
                    rstd[:, bi * 8:bi * 8 + 8].unsqueeze(2).broadcast_to((128, 8, 64)),
                    ALU.mult), r=[lngk, rk_, sqk_, cx.fb_t[bi]], w=[cx.fb_t[bi]])
                p.op("dv", lambda e, bi=bi: e.tensor_tensor(
                    cx.f32a[:, bi * 512:bi * 512 + 512], cx.f32a[:, bi * 512:bi * 512 + 512],
                    cx.f32b[:, bi * 512:bi * 512 + 512], ALU.mult),
                    r=[cx.fa_t[bi], cx.fb_t[bi]], w=[cx.fa_t[bi]])
                p.op("dv", lambda e, bi=bi: e.tensor_tensor(
                    cx.f32a[:, bi * 512:bi * 512 + 512], cx.f32a[:, bi * 512:bi * 512 + 512],
                    lnb_sb[:, 0:512], ALU.add),
                    r=[cx.fa_t[bi], lnbk], w=[cx.fa_t[bi]])
                p.op("dv", lambda e, bi=bi, blk=blk: e.tensor_tensor(
                    ogbf[:, blk, bi * 512:bi * 512 + 512], cx.f32a[:, bi * 512:bi * 512 + 512],
                    gats[:, bi * 512:bi * 512 + 512], ALU.mult),
                    r=[cx.fa_t[bi], gatsk[bi]], w=ogbfk[blk])
            p.op("dv", lambda e: e.tensor_tensor(wacc[0:64, :], wacc[0:64, :],
                                                 cx.pb[3][0:64, :], ALU.add),
                 r=[cx.pbk[3]] + list(wacck) + [wbfk], w=wacck)
            p.op("dv", lambda e: e.tensor_tensor(wacc[64:128, :], wacc[64:128, :],
                                                 cx.pb[4][64:128, :], ALU.add),
                 r=[cx.pbk[4]] + list(wacck) + [wbfk], w=wacck)
        if kph < 7:
            continue
        for blk in range(4):
            bo = blk * 128
            for cp in range(4):
                for j in range(2):
                    c = cp * 2 + j
                    tps = cx.pbf[j][:, cp * 128:cp * 128 + 128]
                    p.op("pe", lambda e, c=c, blk=blk, tps=tps: e.transpose(
                        tps, ogbf[:, blk, c * 128:c * 128 + 128], cx.c_id_bf[:]),
                        r=list(ogbfk[blk]) + [cx.ck_id_bf], w=[cx.pbfk[j]])
                for j in range(2):
                    c = cp * 2 + j
                    tps = cx.pbf[j][:, cp * 128:cp * 128 + 128]
                    p.op("dv", lambda e, c=c, bo=bo, tps=tps: e.tensor_copy(
                        ogT[:, c, bo:bo + 128], tps), r=[cx.pbfk[j]], w=[ogTk[c]])
        if kph < 8:
            continue
        for c in range(CT):
            b = c % 2

            def womm(e, c=c, b=b):
                last = None
                for k in range(CT):
                    last = e.matmul(cx.pb[b][:], wo_view[:, k, c, :], ogT[:, k, :],
                                    start=(k == 0), stop=(k == CT - 1))
                return last
            p.op("pe", womm, r=list(ogTk) + [wok], w=[cx.pbk[b]])
            freg = 4 + c % 2
            p.op("ac", lambda e, b=b, freg=freg: e.activation(
                cx.f32a[:, freg * 512:freg * 512 + 512], cx.pb[b][:], AF.Copy),
                r=[cx.pbk[b]], w=[cx.fa_t[freg]])
            p.op("po", lambda e, c=c, t0=t0, freg=freg: e.dma_start(
                out=sT[c, :, t0:t0 + 512], in_=cx.f32a[:, freg * 512:freg * 512 + 512]),
                r=[cx.fa_t[freg]], w=[sTk[c % len(sTk)]], dma=True)


def mix_pass(cx, xn32, xn32k, mu_sb, muk, dstbf, dstbfk):
    p = cx.p
    for c in range(CT):
        p.op("sy", lambda e, c=c: e.dma_start(out=cx.f32a[:], in_=xn32[c]),
             r=list(xn32k), w=cx.fa_t, dma=True)
        p.op("dv", lambda e: e.tensor_tensor(cx.f32b[:, 1:4096], cx.f32a[:, 1:4096],
                                             cx.f32a[:, 0:4095], ALU.subtract),
             r=cx.fa_t, w=cx.fb_t)
        p.op("dv", lambda e: e.tensor_copy(cx.f32b[:, 0:1], cx.f32a[:, 0:1]),
             r=[cx.fa_t[0]], w=[cx.fb_t[0]])
        p.op("dv", lambda e, c=c: e.scalar_tensor_tensor(
            cx.acc[:, 1:4096], cx.f32b[:, 1:4096], mu_sb[:, c:c + 1],
            cx.f32a[:, 0:4095], ALU.mult, ALU.add),
            r=cx.fb_t + [muk] + cx.fa_t, w=cx.acc_t[0:8])
        p.op("dv", lambda e, c=c: e.tensor_scalar(
            cx.acc[:, 0:1], cx.f32b[:, 0:1], mu_sb[:, c:c + 1], None, ALU.mult),
            r=[cx.fb_t[0], muk], w=[cx.acc_t[0]])
        p.op("po", lambda e, c=c: e.dma_start(out=dstbf[c], in_=cx.acc[:, 0:4096]),
             r=cx.acc_t[0:8], w=[dstbfk[0]], dma=True)


def combine_pass(cx, xT, xTk, sT, sTk, gT, gTk, out, outk, token_major):
    p = cx.p
    res = cx.acc[:].rearrange("p (c f) -> p c f", f=512)
    xa = cx.f32a[:].rearrange("p (c f) -> p c f", f=512)
    ga = cx.f32b[:].rearrange("p (c f) -> p c f", f=512)
    for tt in range(NTT):
        t0 = tt * 512
        p.op("sy", lambda e, t0=t0, xa=xa: e.dma_start(
            out=xa[:, :, :], in_=xT[:, :, t0:t0 + 512].rearrange("c p t -> p c t")),
            r=list(xTk), w=cx.fa_t, dma=True)
        p.op("sy", lambda e, t0=t0, ga=ga: e.dma_start(
            out=ga[:, :, :], in_=gT[:, :, t0:t0 + 512].rearrange("c p t -> p c t")),
            r=list(gTk), w=cx.fb_t, dma=True)
        for c in range(CT):
            p.op("ac", lambda e, c=c, ga=ga: e.activation(ga[:, c, :], ga[:, c, :], AF.Silu),
                 r=[cx.fb_t[c]], w=[cx.fb_t[c]])
        p.op("sy", lambda e, t0=t0, res=res: e.dma_start(
            out=res[:, 0:8, :], in_=sT[:, :, t0:t0 + 512].rearrange("c p t -> p c t")),
            r=list(sTk), w=cx.acc_t[0:8], dma=True)
        for c in range(CT):
            p.op("dv", lambda e, c=c, ga=ga, res=res: e.tensor_tensor(
                ga[:, c, :], res[:, c, :], ga[:, c, :], ALU.mult),
                r=[cx.acc_t[c], cx.fb_t[c]], w=[cx.fb_t[c]])
            p.op("dv", lambda e, c=c, xa=xa, ga=ga, res=res: e.tensor_tensor(
                res[:, c, :], xa[:, c, :], ga[:, c, :], ALU.add),
                r=[cx.fa_t[c], cx.fb_t[c]], w=[cx.acc_t[c]])
        if not token_major:
            p.op("po", lambda e, t0=t0, res=res: e.dma_start(
                out=out[:, :, t0:t0 + 512].rearrange("c p t -> p c t"), in_=res[:, 0:8, :]),
                r=cx.acc_t[0:8], w=[outk[0]], dma=True)
        else:
            for blk in range(4):
                bo = blk * 128
                for c in range(CT):
                    b = 4 + c % 2
                    p.op("pe", lambda e, c=c, bo=bo, b=b, res=res: e.transpose(
                        cx.pb[b][:, 0:128], res[:, c, bo:bo + 128], cx.c["ident"][:]),
                        r=[cx.acc_t[c], cx.ck["ident"]], w=[cx.pbk[b]])
                    p.op("dv", lambda e, c=c, b=b: e.tensor_copy(
                        cx.f32b[:, c * 128:c * 128 + 128], cx.pb[b][:, 0:128]),
                        r=[cx.pbk[b]], w=[cx.fb_t[c // 4]])
                p.op("po", lambda e, t0=t0, bo=bo: e.dma_start(
                    out=out[t0 + bo:t0 + bo + 128, :], in_=cx.f32b[:, 0:1024]),
                    r=cx.fb_t[0:2], w=[outk[0]], dma=True)





def copy_out(cx, srcarr, srck, xout, xoutk):
    p = cx.p
    for c in range(CT):
        k1 = Trk()
        p.op("sy", lambda e, c=c: e.dma_start(out=cx.f32a[:], in_=srcarr[c]),
             r=list(srck), w=cx.fa_t, dma=True)
        p.op("po", lambda e, c=c: e.dma_start(out=xout[c], in_=cx.f32a[:]),
             r=cx.fa_t, w=[xoutk[0]], dma=True)

def build_layer0(cx, xT, xTk, xout, xoutk, token_major=False):
    import os
    kpass = int(os.environ.get("KPASS", "99"))
    p = cx.p

    wq, wqk = win(cx, 0, "wq", (128, 8192), lambda i: tile_lhs(np.asarray(i["L0_attn_Wq"]).T))
    wk_, wkk = win(cx, 0, "wk", (128, 8192), lambda i: tile_lhs(np.asarray(i["L0_attn_Wk"]).T))
    wv, wvk = win(cx, 0, "wv", (128, 8192), lambda i: tile_lhs(np.asarray(i["L0_attn_Wv"]).T))
    wg, wgk = win(cx, 0, "wg", (128, 8, 1024), lambda i: tile_rhs(np.asarray(i["L0_attn_Wg"]).T))
    wo, wok = win(cx, 0, "wo", (128, 8192), lambda i: tile_lhs(np.asarray(i["L0_attn_Wo"]).T))
    conv, convk = win(cx, 0, "conv", (128, 24, 4), lambda i: np.concatenate([
        np.asarray(i["L0_attn_conv_q"]).reshape(8, 128, 4).transpose(1, 0, 2),
        np.asarray(i["L0_attn_conv_k"]).reshape(8, 128, 4).transpose(1, 0, 2),
        np.asarray(i["L0_attn_conv_v"]).reshape(8, 128, 4).transpose(1, 0, 2)], axis=1))
    winit, winitk = win(cx, 0, "winit", (128, 512), lambda i: np.ascontiguousarray(
        np.asarray(i["L0_attn_W_init"])[0].reshape(8, 2, 64, 64).transpose(1, 2, 0, 3).reshape(128, 512)))
    lng, lngk0 = win(cx, 0, "lng", (128, 512), lambda i: np.tile(np.asarray(i["L0_attn_ln_gamma"]), (128, 8)))
    lnb, lnbk0 = win(cx, 0, "lnb", (128, 512), lambda i: np.tile(np.asarray(i["L0_attn_ln_beta"]), (128, 8)))
    wr, wrk = win(cx, 0, "wr", (128, 8192), lambda i: tile_lhs(np.asarray(i["L0_ffn_W_r"]).T))
    wkf, wkfk = win(cx, 0, "wkf", (128, 8192), lambda i: tile_lhs(np.asarray(i["L0_ffn_W_k"]).T))
    wvf, wvfk = win(cx, 0, "wvf", (128, 8192), lambda i: tile_lhs(np.asarray(i["L0_ffn_W_v"]).T))
    mu, muk = win(cx, 0, "mu", (128, 16), lambda i: np.concatenate([
        col_tile(i["L0_ffn_mu_r"]), col_tile(i["L0_ffn_mu_k"])], axis=1))

    xn32, xn32k = dram(cx, "l0_xn32", (CT, 128, T), n_trk=NTT)
    xnbf, xnbfk = dram(cx, "l0_xnbf", (CT, 128, T), BF16, n_trk=NTT)
    qraw, qrawk = dram(cx, "l0_qraw", (CT, 128, T), n_trk=CT)
    kraw, krawk = dram(cx, "l0_kraw", (CT, 128, T), n_trk=CT)
    vraw, vrawk = dram(cx, "l0_vraw", (CT, 128, T), n_trk=CT)
    qpad, qpadk = dram(cx, "l0_qpad", (CT, 2, 128, T), BF16, n_trk=CT)
    kbf, kbfk = dram(cx, "l0_kbf", (CT, 128, T), BF16, n_trk=CT)
    vbf, vbfk = dram(cx, "l0_vbf", (CT, 128, T), BF16, n_trk=CT)
    sT, sTk = dram(cx, "l0_sT", (CT, 128, T), n_trk=CT)
    xrbf, xrbfk = dram(cx, "l0_xrbf", (CT, 128, T), BF16, n_trk=1)
    xkbf, xkbfk = dram(cx, "l0_xkbf", (CT, 128, T), BF16, n_trk=1)
    rT, rTk = dram(cx, "l0_rT", (CT, 128, T), n_trk=CT)
    kabf, kabfk = dram(cx, "l0_kabf", (CT, 128, T), BF16, n_trk=CT)
    gT, gTk = dram(cx, "l0_gT", (CT, 128, T), n_trk=CT)

    conv_sb = p.sbuf("l0_conv_sb", [128, 24, 4])
    convsk = Trk()
    p.op("sy", lambda e: e.dma_start(out=conv_sb[:], in_=conv), r=[convk], w=[convsk], dma=True)
    mu_sb = p.sbuf("l0_mu_sb", [128, 16])
    musk = Trk()
    p.op("sy", lambda e: e.dma_start(out=mu_sb[:], in_=mu), r=[muk], w=[musk], dma=True)
    lng_sb = p.sbuf("l0_lng_sb", [128, 512])
    lngk = Trk()
    p.op("sy", lambda e: e.dma_start(out=lng_sb[:], in_=lng), r=[lngk0], w=[lngk], dma=True)
    lnb_sb = p.sbuf("l0_lnb_sb", [128, 512])
    lnbk = Trk()
    p.op("sy", lambda e: e.dma_start(out=lnb_sb[:], in_=lnb), r=[lnbk0], w=[lnbk], dma=True)

    ln_pass(cx, xT, xTk, xn32, xn32k, xnbf, xnbfk)
    if kpass <= 1:
        copy_out(cx, xn32, xn32k, xout, xoutk)
        return

    wav = cx.wa[:].rearrange("p (k c f) -> p k c f", k=8, f=128)
    wbv = cx.wb[:].rearrange("p (k c f) -> p k c f", k=8, f=128)

    def raw_epil(dst, dstk):
        def epil(c, tt, b):
            freg = 6 + (c + tt) % 2
            p.op("ac", lambda e, b=b, freg=freg: e.activation(
                cx.f32a[:, freg * 512:freg * 512 + 512], cx.pb[b][:], AF.Copy),
                r=[cx.pbk[b]], w=[cx.fa_t[freg]])
            p.op("po", lambda e, c=c, tt=tt, freg=freg: e.dma_start(
                out=dst[c, :, tt * 512:tt * 512 + 512],
                in_=cx.f32a[:, freg * 512:freg * 512 + 512]),
                r=[cx.fa_t[freg]], w=[dstk[c]], dma=True)
        return epil

    load_w(cx, cx.wa[:], wq, wqk, cx.wak)
    mm_cm(cx, wav, cx.wak, xnbf, xnbfk, raw_epil(qraw, qrawk))
    if kpass <= 2:
        copy_out(cx, qraw, qrawk, xout, xoutk)
        return
    load_w(cx, cx.wb[:], wk_, wkk, cx.wbk)
    mm_cm(cx, wbv, cx.wbk, xnbf, xnbfk, raw_epil(kraw, krawk), banks=(2, 3))
    conv_silu_pass(cx, qraw, qrawk, conv_sb[:, 0:8, :], convsk, qpad, qpadk, l2norm=True, pad_heads=True)
    if kpass <= 3:
        copy_out(cx, xn32, xn32k, xout, xoutk)
        return
    load_w(cx, cx.wa[:], wv, wvk, cx.wak)
    mm_cm(cx, wav, cx.wak, xnbf, xnbfk, raw_epil(vraw, vrawk))
    conv_silu_pass(cx, kraw, krawk, conv_sb[:, 8:16, :], convsk, kbf, kbfk, l2norm=True)
    conv_silu_pass(cx, vraw, vrawk, conv_sb[:, 16:24, :], convsk, vbf, vbfk, l2norm=False)
    if kpass <= 4:
        copy_out(cx, xn32, xn32k, xout, xoutk)
        return

    wgv = cx.wa[:].rearrange("p (k f) -> p k f", f=1024)
    load_w(cx, cx.wa[:], wg, wgk, cx.wak)
    load_w(cx, cx.wb[:], wo, wok, cx.wbk)
    l0_attn_core(cx, qpad, qpadk, kbf, kbfk, vbf, vbfk, xnbf, xnbfk,
                 winit, winitk, lng_sb, lngk, lnb_sb, lnbk,
                 wgv, cx.wak, wbv, cx.wbk, sT, sTk)
    if kpass <= 5:
        copy_out(cx, sT, sTk, xout, xoutk)
        return

    mix_pass(cx, xn32, xn32k, mu_sb[:, 0:8], musk, xrbf, xrbfk)
    mix_pass(cx, xn32, xn32k, mu_sb[:, 8:16], musk, xkbf, xkbfk)

    def sig_epil(c, tt, b):
        freg = 6 + (c + tt) % 2
        p.op("ac", lambda e, b=b, freg=freg: e.activation(
            cx.f32a[:, freg * 512:freg * 512 + 512], cx.pb[b][:], AF.Sigmoid),
            r=[cx.pbk[b]], w=[cx.fa_t[freg]])
        p.op("po", lambda e, c=c, tt=tt, freg=freg: e.dma_start(
            out=rT[c, :, tt * 512:tt * 512 + 512],
            in_=cx.f32a[:, freg * 512:freg * 512 + 512]),
            r=[cx.fa_t[freg]], w=[rTk[c]], dma=True)

    load_w(cx, cx.wa[:], wr, wrk, cx.wak)
    mm_cm(cx, wav, cx.wak, xrbf, [xrbfk[0]], sig_epil)
    if kpass <= 6:
        copy_out(cx, rT, rTk, xout, xoutk)
        return

    def relsq_epil(c, tt, b):
        freg = 4 + (c + tt) % 2
        p.op("dv", lambda e, b=b, freg=freg: e.tensor_scalar(
            cx.f32b[:, freg * 512:freg * 512 + 512], cx.pb[b][:], 0.0, None, ALU.max),
            r=[cx.pbk[b]], w=[cx.fb_t[freg]])
        p.op("ac", lambda e, freg=freg: e.activation(
            cx.f32a[:, freg * 512:freg * 512 + 512],
            cx.f32b[:, freg * 512:freg * 512 + 512], AF.Square),
            r=[cx.fb_t[freg]], w=[cx.fa_t[freg]])
        p.op("po", lambda e, c=c, tt=tt, freg=freg: e.dma_start(
            out=kabf[c, :, tt * 512:tt * 512 + 512],
            in_=cx.f32a[:, freg * 512:freg * 512 + 512]),
            r=[cx.fa_t[freg]], w=[kabfk[c]], dma=True)

    load_w(cx, cx.wb[:], wkf, wkfk, cx.wbk)
    mm_cm(cx, wbv, cx.wbk, xkbf, [xkbfk[0]], relsq_epil, banks=(2, 3))

    def g0_epil(c, tt, b):
        freg = 4 + (c + tt) % 2
        p.op("sy", lambda e, c=c, tt=tt, freg=freg: e.dma_start(
            out=cx.f32b[:, freg * 512:freg * 512 + 512],
            in_=rT[c, :, tt * 512:tt * 512 + 512]),
            r=[rTk[c]], w=[cx.fb_t[freg]], dma=True)
        p.op("dv", lambda e, b=b, freg=freg: e.tensor_tensor(
            cx.f32a[:, freg * 512:freg * 512 + 512], cx.pb[b][:],
            cx.f32b[:, freg * 512:freg * 512 + 512], ALU.mult),
            r=[cx.pbk[b], cx.fb_t[freg]], w=[cx.fa_t[freg]])
        p.op("po", lambda e, c=c, tt=tt, freg=freg: e.dma_start(
            out=gT[c, :, tt * 512:tt * 512 + 512],
            in_=cx.f32a[:, freg * 512:freg * 512 + 512]),
            r=[cx.fa_t[freg]], w=[gTk[c]], dma=True)

    load_w(cx, cx.wa[:], wvf, wvfk, cx.wak)
    mm_cm(cx, wav, cx.wak, kabf, kabfk, g0_epil)
    if kpass <= 7:
        copy_out(cx, gT, gTk, xout, xoutk)
        return

    combine_pass(cx, xT, xTk, sT, sTk, gT, gTk, xout, xoutk, token_major)






def moe_pass(cx, srcbf, srcbfk, gR, gRk, w1bf, w1k, w2bf, w2k, hE, hEk,
             outT, outTk, act_fn, n_exp):
    p = cx.p
    gbc = cx.f32b[:, 0:512]
    gbck = cx.fb_t[0]
    grow = cx.r1[0]
    growk = cx.r1k[0]

    def load_gbc(e_, tt):
        t0 = tt * 512
        p.op("sy", lambda e, e_=e_, t0=t0: e.dma_start(
            out=grow[:], in_=gR[e_, t0:t0 + 512]), r=[gRk[0]], w=[growk], dma=True)
        p.op("pe", lambda e: e.matmul(cx.pb[4][:], cx.c["ones_row"][:], grow[:],
                                      start=True, stop=True),
             r=[growk, cx.ck["ones_row"]], w=[cx.pbk[4]])
        p.op("ac", lambda e: e.activation(gbc, cx.pb[4][:], AF.Copy),
             r=[cx.pbk[4]], w=[gbck])

    w1buf = [cx.big[:, 0:8192].rearrange("p (k c f) -> p k c f", k=8, f=128),
             cx.big[:, 8192:16384].rearrange("p (k c f) -> p k c f", k=8, f=128)]
    w1bufk = [regt(cx.big_t, 0, 8192), regt(cx.big_t, 8192, 16384)]
    w2buf = [cx.big[:, 16384:20480].rearrange("p (k c f) -> p k c f", k=8, f=128),
             cx.big[:, 20480:24576].rearrange("p (k c f) -> p k c f", k=8, f=128)]
    w2bufk = [regt(cx.big_t, 16384, 20480), regt(cx.big_t, 20480, 24576)]
    for e_ in range(n_exp):
        for cg in range(4):
            wsel = (e_ * 4 + cg) % 2
            wav = w1buf[wsel]
            wavk = w1bufk[wsel]
            p.op("sy", lambda e, e_=e_, cg=cg, wsel=wsel: e.dma_start(
                out=cx.big[:, wsel * 8192:wsel * 8192 + 8192],
                in_=w1bf[e_, cg].rearrange("p k c f -> p (k c f)")),
                r=[w1k], w=w1bufk[wsel], dma=True)
            for tt in range(NTT):
                t0 = tt * 512
                rhs = cx.rhs[tt % 2]
                rk = cx.rhsk[tt % 2]
                p.op("sy", lambda e, t0=t0, rhs=rhs: e.dma_start(
                    out=rhs[:, :, :], in_=srcbf[:, :, t0:t0 + 512].rearrange("k p t -> p k t")),
                    r=[srcbfk[tt]], w=rk, dma=True)
                for c in range(CT):
                    b = (tt * CT + c) % 2

                    def hmm(e, c=c, rhs=rhs, b=b, wav=wav):
                        last = None
                        for k in range(CT):
                            last = e.matmul(cx.pb[b][:], wav[:, k, c, :], rhs[:, k, :],
                                            start=(k == 0), stop=(k == CT - 1))
                        return last
                    p.op("pe", hmm, r=rk[:CT] + wavk, w=[cx.pbk[b]])
                    wreg = (tt * CT + c) % 4
                    p.op("ac", lambda e, b=b, wreg=wreg: e.activation(
                        cx.wc[:, wreg * 512:wreg * 512 + 512], cx.pb[b][:], act_fn),
                        r=[cx.pbk[b]], w=[cx.wck[wreg]])
                    p.op("po", lambda e, cg=cg, c=c, t0=t0, wreg=wreg: e.dma_start(
                        out=hE[cg * 8 + c, :, t0:t0 + 512],
                        in_=cx.wc[:, wreg * 512:wreg * 512 + 512]),
                        r=[cx.wck[wreg]], w=[hEk[(cg * 8 + c) % 8]], dma=True)
        for tt in range(NTT):
            t0 = tt * 512
            load_gbc(e_, tt)
            for chalf in range(2):
                for kg in range(4):
                    wsel = kg % 2
                    p.op("sy", lambda e, e_=e_, chalf=chalf, kg=kg, wsel=wsel: e.dma_start(
                        out=cx.big[:, 16384 + wsel * 4096:16384 + wsel * 4096 + 4096],
                        in_=w2bf[e_, chalf, kg].rearrange("p k c f -> p (k c f)")),
                        r=[w2k], w=w2bufk[wsel], dma=True)
                    rhs = cx.rhs[kg % 2]
                    rk = cx.rhsk[kg % 2]
                    p.op("sy", lambda e, kg=kg, t0=t0, rhs=rhs: e.dma_start(
                        out=rhs[:, :, :],
                        in_=hE[kg * 8:kg * 8 + 8, :, t0:t0 + 512].rearrange("k p t -> p k t")),
                        r=[hEk[k % 8] for k in range(kg * 8, kg * 8 + 8)], w=rk, dma=True)
                    for c in range(4):
                        def ymm(e, c=c, kg=kg, rhs=rhs, wsel=wsel):
                            last = None
                            for k in range(CT):
                                last = e.matmul(cx.pb[c][:], w2buf[wsel][:, k, c, :], rhs[:, k, :],
                                                start=(kg == 0 and k == 0),
                                                stop=(kg == 3 and k == CT - 1))
                            return last
                        p.op("pe", ymm, r=rk[:CT] + w2bufk[wsel], w=[cx.pbk[c]])
                for c in range(4):
                    d = chalf * 4 + c
                    freg = 2 + c
                    p.op("dv", lambda e, c=c, freg=freg: e.tensor_tensor(
                        cx.f32a[:, freg * 512:freg * 512 + 512], cx.pb[c][:], gbc, ALU.mult),
                        r=[cx.pbk[c], gbck], w=[cx.fa_t[freg]])
                    p.op("po", lambda e, d=d, t0=t0, freg=freg: e.dma_start(
                        out=outT[d, :, t0:t0 + 512],
                        in_=cx.f32a[:, freg * 512:freg * 512 + 512],
                        accum_op=ALU.add),
                        r=[cx.fa_t[freg]], w=[outTk[d % len(outTk)]], dma=True)



def build_layer1(cx, xT, xTk, xout, xoutk, token_major=False):
    p = cx.p

    qkva, qkvak = win(cx, 1, "qkva", (3, 128, 8192),
                      lambda i: np.stack([tile_lhs(np.asarray(i["L1_attn_w_qkva"]).T[:, part * 1024:(part + 1) * 1024])
                                          for part in range(3)]))
    gam, gamk = win(cx, 1, "gamma", (128, 8), lambda i: col_tile(i["L1_attn_gamma"]))
    rw, rwk = win(cx, 1, "rw", (128, 8, 9), lambda i: tile_rhs(np.asarray(i["L1_ffn_router_w"])))
    rb, rbk = win(cx, 1, "rb", (1, 9), lambda i: np.asarray(i["L1_ffn_router_b"]).reshape(1, 9))
    w1bf, w1k = precast(cx, 1, "w1", (8, 4, 128, 8, 8, 128), lambda i: np.ascontiguousarray(
        np.asarray(i["L1_ffn_w1"]).reshape(8, 8, 128, 4, 8, 128).transpose(0, 3, 2, 1, 4, 5)), n_parts=8)
    w2bf, w2k = precast(cx, 1, "w2", (8, 2, 4, 128, 8, 4, 128), lambda i: np.ascontiguousarray(
        np.asarray(i["L1_ffn_w2"]).reshape(8, 4, 8, 128, 2, 4, 128).transpose(0, 4, 1, 3, 2, 5, 6)), n_parts=8)
    hE, hEk = dram(cx, "l1_hE", (32, 128, T), BF16, n_trk=8)

    xn32, xn32k = dram(cx, "l1_xn32", (CT, 128, T), n_trk=NTT)
    xnbf, xnbfk = dram(cx, "l1_xnbf", (CT, 128, T), BF16, n_trk=NTT)
    hbf, hbfk = dram(cx, "l1_hbf", (CT, 128, T), BF16, n_trk=NTT)
    qT, qTk = dram(cx, "l1_qT", (CT, 128, T), n_trk=CT)
    kvT, kvTk = dram(cx, "l1_kvT", (CT, 128, T), n_trk=CT)
    aT, aTk = dram(cx, "l1_aT", (CT, 128, T), n_trk=CT)
    sT, sTk = dram(cx, "l1_sT", (CT, 128, T), n_trk=CT)
    gR, gRk = dram(cx, "l1_gR", (16, T), n_trk=1)
    g0T, g0Tk = dram(cx, "l1_g0T", (CT, 128, T), n_trk=CT)

    gam_sb = p.sbuf("l1_gam_sb", [128, 8])
    gamsk = Trk()
    p.op("sy", lambda e: e.dma_start(out=gam_sb[:], in_=gam), r=[gamk], w=[gamsk], dma=True)
    rw_sb = p.sbuf("l1_rw_sb", [128, 8, 9], F32)
    rwsk = Trk()
    p.op("sy", lambda e: e.dma_start(out=rw_sb[:], in_=rw), r=[rwk], w=[rwsk], dma=True)
    rb_sb = p.sbuf("l1_rb_sb", [1, 9])
    rbsk = Trk()
    p.op("sy", lambda e: e.dma_start(out=rb_sb[:], in_=rb), r=[rbk], w=[rbsk], dma=True)

    ln_pass(cx, xT, xTk, xn32, xn32k, xnbf, xnbfk)

    eps = cx.c["eps"]
    for tt in range(NTT):
        t0 = tt * 512
        xa = cx.f32a[:].rearrange("p (c f) -> p c f", f=512)
        sq = cx.f32b[:].rearrange("p (c f) -> p c f", f=512)
        for c in range(CT):
            p.op("sy", lambda e, c=c, t0=t0, xa=xa: e.dma_start(
                out=xa[:, c, :], in_=xn32[c, :, t0:t0 + 512]),
                r=[xn32k[tt]], w=[cx.fa_t[c]], dma=True)
            p.op("ac", lambda e, c=c, xa=xa, sq=sq: e.activation(sq[:, c, :], xa[:, c, :], AF.Square),
                 r=[cx.fa_t[c]], w=[cx.fb_t[c]])

        def mmsum(e, sq=sq):
            last = None
            for c in range(CT):
                last = e.matmul(cx.pb[0][0:1, :], cx.c["ones_col"][:], sq[:, c, :],
                                start=(c == 0), stop=(c == CT - 1))
            return last
        p.op("pe", mmsum, r=cx.fb_t + [cx.ck["ones_col"]], w=[cx.pbk[0]])
        r1, r1k = cx.r1, cx.r1k
        p.op("ac", lambda e: e.activation(r1[0][:], cx.pb[0][0:1, :], AF.Sqrt),
             r=[cx.pbk[0]], w=[r1k[0]])
        p.op("dv", lambda e: e.tensor_scalar(r1[0][:], r1[0][:], 1e-12, None, ALU.max),
             r=[r1k[0]], w=[r1k[0]])
        p.op("dv", lambda e: e.reciprocal(r1[1][:], r1[0][:]), r=[r1k[0]], w=[r1k[1]])
        p.op("dv", lambda e: e.tensor_scalar(r1[1][:], r1[1][:], 32.0, None, ALU.mult),
             r=[r1k[1]], w=[r1k[1]])
        p.op("pe", lambda e: e.matmul(cx.pb[1][:], cx.c["ones_row"][:], r1[1][:],
                                      start=True, stop=True),
             r=[r1k[1], cx.ck["ones_row"]], w=[cx.pbk[1]])
        for c in range(CT):
            p.op("dv", lambda e, c=c, xa=xa, sq=sq: e.scalar_tensor_tensor(
                sq[:, c, :], xa[:, c, :], gam_sb[:, c:c + 1], cx.pb[1][:],
                ALU.mult, ALU.mult),
                r=[cx.fa_t[c], gamsk, cx.pbk[1], cx.fb_t[c]], w=[cx.fb_t[c]])
            p.op("po", lambda e, c=c, t0=t0, sq=sq: e.dma_start(
                out=hbf[c, :, t0:t0 + 512], in_=sq[:, c, :]),
                r=[cx.fb_t[c]], w=[hbfk[tt]], dma=True)

    wav = cx.wa[:].rearrange("p (k c f) -> p k c f", k=8, f=128)

    def part_epil(dst, dstk, func):
        def epil(c, tt, b):
            freg = 6 + (c + tt) % 2
            p.op("ac", lambda e, b=b, freg=freg: e.activation(
                cx.f32a[:, freg * 512:freg * 512 + 512], cx.pb[b][:], func),
                r=[cx.pbk[b]], w=[cx.fa_t[freg]])
            p.op("po", lambda e, c=c, tt=tt, freg=freg: e.dma_start(
                out=dst[c, :, tt * 512:tt * 512 + 512],
                in_=cx.f32a[:, freg * 512:freg * 512 + 512]),
                r=[cx.fa_t[freg]], w=[dstk[c]], dma=True)
        return epil

    for part, (dst, dstk, func) in enumerate((
            (qT, qTk, AF.Copy), (kvT, kvTk, AF.Copy), (aT, aTk, AF.Sigmoid))):
        p.op("po", lambda e, part=part: e.dma_start(
            out=cx.wa[:], in_=qkva[part]),
            r=[qkvak], w=[cx.wak], dma=True)
        mm_cm(cx, wav, cx.wak, hbf, hbfk, part_epil(dst, dstk, func),
              banks=((0, 1) if part % 2 == 0 else (2, 3)))

    for c in range(CT):
        p.op("sy", lambda e, c=c: e.dma_start(out=cx.f32a[:], in_=aT[c]),
             r=[aTk[c]], w=cx.fa_t, dma=True)
        p.op("sy", lambda e, c=c: e.dma_start(out=cx.f32b[:], in_=kvT[c]),
             r=[kvTk[c]], w=cx.fb_t, dma=True)
        p.op("dv", lambda e: e.tensor_tensor_scan(
            cx.acc[:, 0:4096], cx.f32a[:], cx.f32b[:], 0.0, ALU.mult, ALU.add),
            r=cx.fa_t + cx.fb_t, w=cx.acc_t[0:8])
        p.op("sy", lambda e, c=c: e.dma_start(out=cx.f32b[:], in_=qT[c]),
             r=[qTk[c]], w=cx.fb_t, dma=True)
        p.op("dv", lambda e: e.tensor_tensor(cx.acc[:, 0:4096], cx.acc[:, 0:4096],
                                             cx.f32b[:], ALU.mult),
             r=cx.fb_t + cx.acc_t[0:8], w=cx.acc_t[0:8])
        p.op("po", lambda e, c=c: e.dma_start(out=sT[c], in_=cx.acc[:, 0:4096]),
             r=cx.acc_t[0:8], w=[sTk[0]], dma=True)

    for ch in range(NTT):
        t0 = ch * 512
        xnc = cx.f32a[:].rearrange("p (c f) -> p c f", f=512)
        xnck = cx.fa_t
        p.op("sy", lambda e, t0=t0, xnc=xnc: e.dma_start(
            out=xnc[:, :, :], in_=xn32[:, :, t0:t0 + 512].rearrange("c p t -> p c t")),
            r=[xn32k[ch]], w=cx.fa_t, dma=True)
        for blk in range(4):
            bo = blk * 128

            def rmm(e, bo=bo, xnc=xnc):
                last = None
                for k in range(CT):
                    last = e.matmul(cx.pb[0][:, 0:9], xnc[:, k, bo:bo + 128],
                                    rw_sb[:, k, :], start=(k == 0), stop=False)
                last = e.matmul(cx.pb[0][:, 0:9], cx.c["ones_row"][:], rb_sb[:],
                                start=False, stop=True)
                return last
            p.op("pe", rmm, r=list(xnck) + [rwsk, rbsk, cx.ck["ones_row"]], w=[cx.pbk[0]])
            lg = cx.sm[:, 896:912]
            lgk = cx.sm_t[14]
            mx = cx.sm[:, 960:961]
            mxk = cx.sm_t[15]
            p.op("ac", lambda e: e.activation(lg[:, 0:9], cx.pb[0][:, 0:9], AF.Copy),
                 r=[cx.pbk[0]], w=[lgk])
            p.op("dv", lambda e: e.reduce_max(mx, lg[:, 0:9], axis=AX.X),
                 r=[lgk], w=[mxk])
            p.op("dv", lambda e: e.tensor_scalar(mx, mx, -1.0, None, ALU.mult),
                 r=[mxk], w=[mxk])
            p.op("ac", lambda e: e.activation(lg[:, 0:9], lg[:, 0:9], AF.Exp,
                                              bias=mx, accum_out=cx.sm[:, 961:962]),
                 r=[lgk, mxk], w=[lgk, mxk])
            p.op("dv", lambda e: e.reciprocal(mx, cx.sm[:, 961:962]),
                 r=[mxk], w=[mxk])
            p.op("dv", lambda e: e.tensor_scalar(lg[:, 0:9], lg[:, 0:9], mx, None, ALU.mult),
                 r=[lgk, mxk], w=[lgk])
            p.op("dv", lambda e: e.reduce_max(mx, lg[:, 0:9], axis=AX.X),
                 r=[lgk], w=[mxk])
            p.op("dv", lambda e: e.tensor_scalar(cx.sm[:, 912:921], lg[:, 0:9],
                                                 mx, None, ALU.is_lt),
                 r=[lgk, mxk], w=[cx.sm_t[14]])
            p.op("dv", lambda e: e.tensor_tensor(cx.sm[:, 928:937], lg[:, 0:9],
                                                 cx.sm[:, 912:921], ALU.mult),
                 r=[cx.sm_t[14]], w=[cx.sm_t[14]])
            p.op("dv", lambda e: e.reduce_max(mx, cx.sm[:, 928:937], axis=AX.X),
                 r=[cx.sm_t[14]], w=[mxk])
            p.op("dv", lambda e: e.tensor_scalar(cx.sm[:, 912:921], lg[:, 0:9],
                                                 mx, None, ALU.is_ge),
                 r=[lgk, mxk], w=[cx.sm_t[14]])
            p.op("dv", lambda e: e.tensor_tensor(lg[:, 0:9], lg[:, 0:9],
                                                 cx.sm[:, 912:921], ALU.mult),
                 r=[cx.sm_t[14], lgk], w=[lgk])
            p.op("dv", lambda e: e.memset(lg[:, 9:16], 0.0), r=[lgk], w=[lgk])
            p.op("pe", lambda e: e.transpose(cx.pb[1][0:16, 0:128],
                                             lg[:, 0:16], cx.c["ident"][:]),
                 r=[lgk, cx.ck["ident"]], w=[cx.pbk[1]])
            p.op("ac", lambda e: e.activation(cx.f32b[0:16, 3584:3712], cx.pb[1][0:16, 0:128], AF.Copy),
                 r=[cx.pbk[1]], w=[cx.fb_t[7]])
            p.op("po", lambda e, t0=t0, bo=bo: e.dma_start(
                out=gR[:, t0 + bo:t0 + bo + 128], in_=cx.f32b[0:16, 3584:3712]),
                r=[cx.fb_t[7]], w=[gRk[0]], dma=True)

    for tt in range(NTT):
        t0 = tt * 512
        gbc = cx.f32b[:, 0:512]
        gbck = cx.fb_t[0]
        grow = cx.r1[0]
        growk = cx.r1k[0]
        p.op("sy", lambda e, t0=t0: e.dma_start(
            out=grow[:], in_=gR[8, t0:t0 + 512]), r=[gRk[0]], w=[growk], dma=True)
        p.op("pe", lambda e: e.matmul(cx.pb[4][:], cx.c["ones_row"][:], grow[:],
                                      start=True, stop=True),
             r=[growk, cx.ck["ones_row"]], w=[cx.pbk[4]])
        p.op("ac", lambda e: e.activation(gbc, cx.pb[4][:], AF.Copy),
             r=[cx.pbk[4]], w=[gbck])
        xnc = cx.rhs[0][:]
        xnck = cx.rhsk[0]
        p.op("sy", lambda e, t0=t0, xnc=xnc: e.dma_start(
            out=xnc[:, :, :], in_=xnbf[:, :, t0:t0 + 512].rearrange("c p t -> p c t")),
            r=[xnbfk[tt]], w=xnck, dma=True)
        for d in range(CT):
            freg = 2 + d % 4
            p.op("dv", lambda e, d=d, freg=freg, xnc=xnc, gbc=gbc: e.tensor_tensor(
                cx.f32a[:, freg * 512:freg * 512 + 512], xnc[:, d, :], gbc, ALU.mult),
                r=[xnck[d], gbck], w=[cx.fa_t[freg]])
            p.op("po", lambda e, d=d, t0=t0, freg=freg: e.dma_start(
                out=g0T[d, :, t0:t0 + 512], in_=cx.f32a[:, freg * 512:freg * 512 + 512]),
                r=[cx.fa_t[freg]], w=[g0Tk[d % CT]], dma=True)

    moe_pass(cx, xnbf, xnbfk, gR, gRk, w1bf, w1k, w2bf, w2k, hE, hEk,
             g0T, g0Tk, AF.Gelu, 8)

    combine_pass(cx, xT, xTk, sT, sTk, g0T, g0Tk, xout, xoutk, token_major)





def u_bound_pass(cx, qpad, qpadk, kbf, kbfk, negU_col, negUk):
    p = cx.p
    r1, r1k = cx.r1, cx.r1k
    p.op("dv", lambda e: e.memset(r1[2][:, 0:2], 0.0), w=[r1k[2]])
    for which, (arr, arrk, pad) in enumerate(((qpad, qpadk, True), (kbf, kbfk, False))):
        for c in range(CT):
            for tt in range(NTT):
                t0 = tt * 512
                wreg = tt % 2
                nsub = 2 if pad else 1
                for j in range(nsub):
                    src_ap = arr[c, j, :, t0:t0 + 512] if pad else arr[c, :, t0:t0 + 512]
                    p.op("sy", lambda e, src_ap=src_ap, wreg=wreg, j=j: e.dma_start(
                        out=cx.wc[:, (wreg * 2 + j) * 512:(wreg * 2 + j) * 512 + 512],
                        in_=src_ap),
                        r=[arrk[c]], w=[cx.wck[wreg * 2 + j]], dma=True)
                    freg = 4 + wreg
                    p.op("ac", lambda e, wreg=wreg, j=j, freg=freg: e.activation(
                        cx.f32a[:, freg * 512:freg * 512 + 512],
                        cx.wc[:, (wreg * 2 + j) * 512:(wreg * 2 + j) * 512 + 512], AF.Square),
                        r=[cx.wck[wreg * 2 + j]], w=[cx.fa_t[freg]])
                    p.op("pe", lambda e, freg=freg, j=j, nsub=nsub: e.matmul(
                        cx.pb[5][0:1, :], cx.c["ones_col"][:],
                        cx.f32a[:, freg * 512:freg * 512 + 512],
                        start=(j == 0), stop=(j == nsub - 1)),
                        r=[cx.fa_t[freg], cx.ck["ones_col"]], w=[cx.pbk[5]])
                p.op("ac", lambda e: e.activation(r1[0][:], cx.pb[5][0:1, :], AF.Copy),
                     r=[cx.pbk[5]], w=[r1k[0]])
                p.op("dv", lambda e: e.reduce_max(r1[1][:, 0:1], r1[0][:], axis=AX.X),
                     r=[r1k[0]], w=[r1k[1]])
                p.op("dv", lambda e, which=which: e.tensor_tensor(
                    r1[2][:, which:which + 1], r1[2][:, which:which + 1],
                    r1[1][:, 0:1], ALU.max),
                    r=[r1k[1]], w=[r1k[2]])
    p.op("ac", lambda e: e.activation(r1[2][:, 0:2], r1[2][:, 0:2], AF.Sqrt),
         r=[r1k[2]], w=[r1k[2]])
    p.op("dv", lambda e: e.tensor_tensor(r1[2][:, 0:1], r1[2][:, 0:1], r1[2][:, 1:2], ALU.mult),
         r=[r1k[2]], w=[r1k[2]])
    p.op("dv", lambda e: e.tensor_scalar(r1[2][:, 0:1], r1[2][:, 0:1], -0.125, None, ALU.mult),
         r=[r1k[2]], w=[r1k[2]])
    p.op("pe", lambda e: e.matmul(cx.pb[5][:, 0:1], cx.c["ones_row"][:], r1[2][:, 0:1],
                                  start=True, stop=True),
         r=[r1k[2], cx.ck["ones_row"]], w=[cx.pbk[5]])
    p.op("ac", lambda e: e.activation(negU_col, cx.pb[5][:, 0:1], AF.Copy),
         r=[cx.pbk[5]], w=[negUk])




def msa_pass(cx, qpad, qpadk, kbf, kbfk, vbf, vbfk, negU_col, negUk, yT, yTk):
    p = cx.p
    big = cx.big
    vtok = big[:, 0:16640].rearrange("p (a b) -> p a b", b=520)
    vtokk = regt(cx.big_t, 0, 16640)
    for hh in range(2):
        for tkb in range(32):
            p.op("dv", lambda e, tkb=tkb: e.memset(
                vtok[:, tkb, :].rearrange("p (h x) -> p h x", x=65)[:, :, 64], 1.0),
                r=[], w=regt(cx.big_t, tkb * 520, tkb * 520 + 520))
        for cl in range(4):
            c = hh * 4 + cl
            for tt in range(NTT):
                t0 = tt * 512
                wreg = tt % 2
                p.op("sy", lambda e, c=c, t0=t0, wreg=wreg: e.dma_start(
                    out=cx.wc[:, wreg * 512:wreg * 512 + 512], in_=vbf[c, :, t0:t0 + 512]),
                    r=[vbfk[c]], w=[cx.wck[wreg]], dma=True)
                for blk in range(4):
                    tkb = tt * 4 + blk
                    bank = blk % 2
                    tps = cx.pbf[bank][:, 0:128]
                    p.op("pe", lambda e, wreg=wreg, blk=blk, tps=tps: e.transpose(
                        tps, cx.wc[:, wreg * 512 + blk * 128:wreg * 512 + blk * 128 + 128],
                        cx.c_id_bf[:]),
                        r=[cx.wck[wreg], cx.ck_id_bf], w=[cx.pbfk[bank]])
                    for j in range(2):
                        p.op("dv", lambda e, tkb=tkb, cl=cl, j=j, tps=tps: e.tensor_copy(
                            vtok[:, tkb, (2 * cl + j) * 65:(2 * cl + j) * 65 + 64],
                            tps[:, j * 64:j * 64 + 64]),
                            r=[cx.pbfk[bank]],
                            w=regt(cx.big_t, tkb * 520, tkb * 520 + 520))
        for hl in range(8):
            h = hh * 8 + hl
            c = h // 2
            j = h % 2
            if j == 0:
                p.op("sy", lambda e, c=c: e.dma_start(
                    out=cx.rhs[0][:].rearrange("p k t -> p (k t)"), in_=kbf[c]),
                    r=[kbfk[c]], w=cx.rhsk[0], dma=True)
            for tq in range(NTT):
                t0 = tq * 512
                p.op("sy", lambda e, c=c, j=j, t0=t0: e.dma_start(
                    out=cx.rhs[1][:, 0, :], in_=qpad[c, j, :, t0:t0 + 512]),
                    r=[qpadk[c]], w=[cx.rhsk[1][0]], dma=True)
                pvb = 2 + tq % 2
                for tk in range(32):
                    ab = tk % 2
                    p.op("pe", lambda e, tk=tk, ab=ab: e.matmul(
                        cx.pb[ab][:], cx.rhs[0][:, tk // 4, (tk % 4) * 128:(tk % 4) * 128 + 128],
                        cx.rhs[1][:, 0, :], start=True, stop=True),
                        r=[cx.rhsk[0][tk // 4], cx.rhsk[1][0]], w=[cx.pbk[ab]])
                    preg = 2 + tk % 2
                    p.op("ac", lambda e, ab=ab, preg=preg: e.activation(
                        cx.wc[:, preg * 512:preg * 512 + 512], cx.pb[ab][:], AF.Exp,
                        bias=negU_col, scale=cx.c["eps"][:, 3:4]),
                        r=[cx.pbk[ab], negUk, cx.ck["eps"]], w=[cx.wck[preg]])
                    p.op("pe", lambda e, tk=tk, hl=hl, preg=preg, pvb=pvb: e.matmul(
                        cx.pb[pvb][0:65, :], vtok[:, tk, hl * 65:hl * 65 + 65],
                        cx.wc[:, preg * 512:preg * 512 + 512],
                        start=(tk == 0), stop=(tk == 31)),
                        r=[cx.wck[preg]] + regt(cx.big_t, tk * 520, tk * 520 + 520),
                        w=[cx.pbk[pvb]])
                r1, r1k = cx.r1, cx.r1k
                p.op("ac", lambda e, pvb=pvb: e.activation(r1[0][:], cx.pb[pvb][64:65, :], AF.Copy),
                     r=[cx.pbk[pvb]], w=[r1k[0]])
                p.op("dv", lambda e: e.reciprocal(r1[1][:], r1[0][:]),
                     r=[r1k[0]], w=[r1k[1]])
                p.op("pe", lambda e: e.matmul(cx.pb[4][0:64, :], cx.c["ones_row"][0:1, 0:64],
                                              r1[1][:], start=True, stop=True),
                     r=[r1k[1], cx.ck["ones_row"]], w=[cx.pbk[4]])
                p.op("ac", lambda e: e.activation(cx.f32a[0:64, 3072:3584], cx.pb[4][0:64, :], AF.Copy),
                     r=[cx.pbk[4]], w=[cx.fa_t[6]])
                p.op("dv", lambda e, pvb=pvb: e.tensor_tensor(
                    cx.wc[0:64, 0:512], cx.pb[pvb][0:64, :], cx.f32a[0:64, 3072:3584],
                    ALU.mult), r=[cx.pbk[pvb], cx.fa_t[6]], w=[cx.wck[0]])
                p.op("po", lambda e, c=c, j=j, t0=t0: e.dma_start(
                    out=yT[c, j * 64:j * 64 + 64, t0:t0 + 512], in_=cx.wc[0:64, 0:512]),
                    r=[cx.wck[0]], w=[yTk[c]], dma=True)



def build_layer2(cx, xT, xTk, xout, xoutk, token_major=False):
    p = cx.p
    eps = cx.c["eps"]

    mix, mixk = win(cx, 2, "mix", (128, 24), lambda i: np.concatenate([
        col_tile(np.asarray(i["L2_attn_mix_k"]).reshape(-1)),
        col_tile(np.asarray(i["L2_attn_mix_v"]).reshape(-1)),
        col_tile(np.asarray(i["L2_attn_mix_r"]).reshape(-1))], axis=1))
    wk_, wkk = win(cx, 2, "wk", (128, 8192), lambda i: tile_lhs(np.asarray(i["L2_attn_Wk"]).T))
    wv, wvk = win(cx, 2, "wv", (128, 8192), lambda i: tile_lhs(np.asarray(i["L2_attn_Wv"]).T))
    wr, wrk = win(cx, 2, "wr", (128, 8192), lambda i: tile_lhs(np.asarray(i["L2_attn_Wr"]).T))
    wo, wok = win(cx, 2, "wo", (128, 8192), lambda i: tile_lhs(np.asarray(i["L2_attn_Wo"]).T))
    tdf, tdfk = win(cx, 2, "tdf", (1, 32), lambda i: np.concatenate([
        np.asarray(i["L2_attn_time_decay"]), np.asarray(i["L2_attn_time_first"])]).reshape(1, 32))
    lncols, lncolsk = win(cx, 2, "lncols", (128, 32), lambda i: np.concatenate([
        col_tile(i["L2_ffn_ln1_w"]), col_tile(i["L2_ffn_ln1_b"]),
        col_tile(i["L2_ffn_ln2_w"]), col_tile(i["L2_ffn_ln2_b"])], axis=1))
    wqkv, wqkvk = win(cx, 2, "wqkv", (3, 128, 8192),
                      lambda i: np.stack([tile_lhs(np.asarray(i["L2_ffn_w_qkv"])[:, part * 1024:(part + 1) * 1024])
                                          for part in range(3)]))
    wproj, wprojk = win(cx, 2, "wproj", (128, 8192), lambda i: tile_lhs(np.asarray(i["L2_ffn_w_proj"])))
    gw, gwk = win(cx, 2, "gw", (128, 8, 8), lambda i: tile_rhs(np.asarray(i["L2_ffn_gate_w"])))
    gb, gbk = win(cx, 2, "gb", (1, 8), lambda i: np.asarray(i["L2_ffn_gate_b"]).reshape(1, 8))
    w1bf, w1k = precast(cx, 2, "w1", (8, 4, 128, 8, 8, 128), lambda i: np.ascontiguousarray(
        np.asarray(i["L2_ffn_w1"]).reshape(8, 8, 128, 4, 8, 128).transpose(0, 3, 2, 1, 4, 5)), n_parts=8)
    w2bf, w2k = precast(cx, 2, "w2", (8, 2, 4, 128, 8, 4, 128), lambda i: np.ascontiguousarray(
        np.asarray(i["L2_ffn_w2"]).reshape(8, 4, 8, 128, 2, 4, 128).transpose(0, 4, 1, 3, 2, 5, 6)), n_parts=8)

    xn32, xn32k = dram(cx, "l2_xn32", (CT, 128, T), n_trk=NTT)
    xnbf, xnbfk = dram(cx, "l2_xnbf", (CT, 128, T), BF16, n_trk=NTT)
    xkin, xkink = dram(cx, "l2_xkin", (CT, 128, T), BF16, n_trk=1)
    xvin, xvink = dram(cx, "l2_xvin", (CT, 128, T), BF16, n_trk=1)
    xrin, xrink = dram(cx, "l2_xrin", (CT, 128, T), BF16, n_trk=1)
    kT32, kT32k = dram(cx, "l2_kT32", (CT, 128, T), n_trk=CT)
    vT32, vT32k = dram(cx, "l2_vT32", (CT, 128, T), n_trk=CT)
    rT32, rT32k = dram(cx, "l2_rT32", (CT, 128, T), n_trk=CT)
    rwkvbf, rwkvbfk = dram(cx, "l2_rwkvbf", (CT, 128, T), BF16, n_trk=CT)
    sT, sTk = dram(cx, "l2_sT", (CT, 128, T), n_trk=CT)
    x132, x132k = dram(cx, "l2_x132", (CT, 128, T), n_trk=NTT)
    x1bf, x1bfk = dram(cx, "l2_x1bf", (CT, 128, T), BF16, n_trk=NTT)
    qpad2, qpad2k = dram(cx, "l2_qpad", (CT, 2, 128, T), BF16, n_trk=CT)
    kbf2, kbf2k = dram(cx, "l2_kbf", (CT, 128, T), BF16, n_trk=CT)
    vbf2, vbf2k = dram(cx, "l2_vbf", (CT, 128, T), BF16, n_trk=CT)
    yT2, yT2k = dram(cx, "l2_yT", (CT, 128, T), BF16, n_trk=CT)
    z1, z1k = dram(cx, "l2_z1", (CT, 128, T), n_trk=NTT)
    x232, x232k = dram(cx, "l2_x232", (CT, 128, T), n_trk=NTT)
    x2bf, x2bfk = dram(cx, "l2_x2bf", (CT, 128, T), BF16, n_trk=NTT)
    gR2, gR2k = dram(cx, "l2_gR", (16, T), n_trk=1)
    z2T, z2Tk = dram(cx, "l2_z2T", (CT, 128, T), n_trk=CT)
    hE2, hE2k = dram(cx, "l2_hE", (32, 128, T), BF16, n_trk=8)

    mix_sb = p.sbuf("l2_mix_sb", [128, 24])
    mixsk = Trk()
    p.op("sy", lambda e: e.dma_start(out=mix_sb[:], in_=mix), r=[mixk], w=[mixsk], dma=True)
    ln_sb = p.sbuf("l2_ln_sb", [128, 32])
    lnsk = Trk()
    p.op("sy", lambda e: e.dma_start(out=ln_sb[:], in_=lncols), r=[lncolsk], w=[lnsk], dma=True)
    gw_sb = p.sbuf("l2_gw_sb", [128, 8, 8], F32)
    gwsk = Trk()
    p.op("sy", lambda e: e.dma_start(out=gw_sb[:], in_=gw), r=[gwk], w=[gwsk], dma=True)
    gb_sb = p.sbuf("l2_gb_sb", [1, 8])
    gbsk = Trk()
    p.op("sy", lambda e: e.dma_start(out=gb_sb[:], in_=gb), r=[gbk], w=[gbsk], dma=True)
    tdf_sb = p.sbuf("l2_tdf_sb", [1, 32])
    tdfsk = Trk()
    p.op("sy", lambda e: e.dma_start(out=tdf_sb[:], in_=tdf), r=[tdfk], w=[tdfsk], dma=True)

    ln_pass(cx, xT, xTk, xn32, xn32k, xnbf, xnbfk)
    mix_pass(cx, xn32, xn32k, mix_sb[:, 0:8], mixsk, xkin, xkink)
    mix_pass(cx, xn32, xn32k, mix_sb[:, 8:16], mixsk, xvin, xvink)
    mix_pass(cx, xn32, xn32k, mix_sb[:, 16:24], mixsk, xrin, xrink)
    wav = cx.wa[:].rearrange("p (k c f) -> p k c f", k=8, f=128)
    wbv = cx.wb[:].rearrange("p (k c f) -> p k c f", k=8, f=128)

    def f32_epil(dst, dstk, func):
        def epil(c, tt, b):
            freg = 6 + (c + tt) % 2
            p.op("ac", lambda e, b=b, freg=freg: e.activation(
                cx.f32a[:, freg * 512:freg * 512 + 512], cx.pb[b][:], func),
                r=[cx.pbk[b]], w=[cx.fa_t[freg]])
            p.op("po", lambda e, c=c, tt=tt, freg=freg: e.dma_start(
                out=dst[c, :, tt * 512:tt * 512 + 512],
                in_=cx.f32a[:, freg * 512:freg * 512 + 512]),
                r=[cx.fa_t[freg]], w=[dstk[c % len(dstk)]], dma=True)
        return epil

    load_w(cx, cx.wa[:], wk_, wkk, cx.wak)
    mm_cm(cx, wav, cx.wak, xkin, [xkink[0]], f32_epil(kT32, kT32k, AF.Copy))
    load_w(cx, cx.wb[:], wv, wvk, cx.wbk)
    mm_cm(cx, wbv, cx.wbk, xvin, [xvink[0]], f32_epil(vT32, vT32k, AF.Copy), banks=(2, 3))
    load_w(cx, cx.wa[:], wr, wrk, cx.wak)
    mm_cm(cx, wav, cx.wak, xrin, [xrink[0]], f32_epil(rT32, rT32k, AF.Sigmoid))

    big = cx.big
    Xtok = big[:, 0:8320].bitcast(F32).rearrange("p (a b) -> p a b", b=1040)
    Xtokk = regt(cx.big_t, 0, 8320)
    rwkv_sb = big[:, 8704:12800].rearrange("p (a b) -> p a b", b=512)
    rwkk = regt(cx.big_t, 8704, 12800)
    Dv = big[:, 12800:16896].bitcast(F32).rearrange("p (h f) -> p h f", f=128)
    Dvk = regt(cx.big_t, 12800, 16896)
    srow = big[0:1, 16896:18976].bitcast(F32)
    srowk = regt(cx.big_t, 16896, 18976)
    crow = big[0:1, 18976:21056].bitcast(F32)
    crowk = regt(cx.big_t, 18976, 21056)
    ewCrow = big[0:1, 21056:23136].bitcast(F32)
    ewCk = regt(cx.big_t, 21056, 23136)
    lnew_bc = cx.sm[:, 128:144]
    lnewk = cx.sm_t[2]
    euh_bc = cx.sm[:, 192:208]
    euhk = cx.sm_t[3]
    ewvC = cx.sm[:, 256:272]
    ewvCk = cx.sm_t[4]
    r1, r1k = cx.r1, cx.r1k

    p.op("ac", lambda e: e.activation(r1[0][:, 0:16], tdf_sb[:, 0:16], AF.Exp),
         r=[tdfsk], w=[r1k[0]])
    p.op("dv", lambda e: e.tensor_scalar(r1[0][:, 0:16], r1[0][:, 0:16], -1.0, None, ALU.mult),
         r=[r1k[0]], w=[r1k[0]])
    p.op("ac", lambda e: e.activation(r1[1][:, 0:16], tdf_sb[:, 16:32], AF.Exp),
         r=[tdfsk], w=[r1k[1]])
    p.op("pe", lambda e: e.matmul(cx.pb[5][:, 0:16], cx.c["ones_row"][:], r1[0][:, 0:16],
                                  start=True, stop=True),
         r=[r1k[0], cx.ck["ones_row"]], w=[cx.pbk[5]])
    p.op("ac", lambda e: e.activation(lnew_bc, cx.pb[5][:, 0:16], AF.Copy),
         r=[cx.pbk[5]], w=[lnewk])
    p.op("pe", lambda e: e.matmul(cx.pb[5][:, 0:16], cx.c["ones_row"][:], r1[1][:, 0:16],
                                  start=True, stop=True),
         r=[r1k[1], cx.ck["ones_row"]], w=[cx.pbk[5]])
    p.op("ac", lambda e: e.activation(euh_bc, cx.pb[5][:, 0:16], AF.Copy),
         r=[cx.pbk[5]], w=[euhk])
    for h in range(16):
        p.op("ac", lambda e, h=h: e.activation(
            r1[h // 4][:, (h % 4) * 128:(h % 4) * 128 + 128], cx.c["iota_row"][:],
            AF.Exp, scale=lnew_bc[0:1, h:h + 1]),
            r=[lnewk, cx.ck["iota_row"]], w=[r1k[h // 4]])
        p.op("ac", lambda e, h=h: e.activation(
            ewvC[:, h:h + 1], cx.c["r127mj_col"][:], AF.Exp, scale=lnew_bc[:, h:h + 1]),
            r=[lnewk, cx.ck["r127mj_col"]], w=[ewvCk])
    p.op("ac", lambda e: e.activation(cx.sm[0:1, 320:336], lnew_bc[0:1, 0:16], AF.Exp,
                                      scale=eps[0:1, 4:5]),
         r=[lnewk, cx.ck["eps"]], w=[cx.sm_t[5]])
    p.op("dv", lambda e: e.tensor_copy(
        ewCrow.rearrange("p (h x) -> p h x", x=65),
        cx.sm[0:1, 320:336].rearrange("p h -> p h 1" if False else "p (h o) -> p h o", o=1).broadcast_to((1, 16, 65))),
        r=[cx.sm_t[5]], w=ewCk)
    for h in range(16):
        p.op("dv", lambda e, h=h: e.tensor_scalar(
            cx.f32a[:, 0:128], cx.c["pmatT"][:], lnew_bc[:, h:h + 1], None, ALU.mult),
            r=[lnewk, cx.ck["pmatT"]], w=[cx.fa_t[0]])
        p.op("dv", lambda e: e.tensor_scalar(cx.f32a[:, 0:128], cx.f32a[:, 0:128],
                                             80.0, None, ALU.min),
             r=[cx.fa_t[0]], w=[cx.fa_t[0]])
        p.op("ac", lambda e: e.activation(cx.f32a[:, 128:256], cx.f32a[:, 0:128], AF.Exp),
             r=[cx.fa_t[0]], w=[cx.fa_t[0]])
        p.op("dv", lambda e, h=h: e.tensor_tensor(Dv[:, h, :], cx.f32a[:, 128:256],
                                                  cx.c["strictT"][:], ALU.mult),
             r=[cx.fa_t[0], cx.ck["strictT"]], w=Dvk)
    p.op("dv", lambda e: e.memset(srow[:], 0.0), w=srowk)

    for g in range(NTT):
        t0 = g * 512
        ek = cx.f32a[:].rearrange("p (c f) -> p c f", f=512)
        vv = cx.f32b[:].rearrange("p (c f) -> p c f", f=512)
        Xc = cx.acc[:, 0:4096].rearrange("p (c f) -> p c f", f=512)
        wkvt = cx.acc[:, 4096:8192].rearrange("p (a b) -> p a b", b=1024)
        p.op("sy", lambda e, t0=t0, ek=ek: e.dma_start(
            out=ek[:, :, :], in_=kT32[:, :, t0:t0 + 512].rearrange("c p t -> p c t")),
            r=list(kT32k), w=cx.fa_t, dma=True)
        p.op("sy", lambda e, t0=t0, vv=vv: e.dma_start(
            out=vv[:, :, :], in_=vT32[:, :, t0:t0 + 512].rearrange("c p t -> p c t")),
            r=list(vT32k), w=cx.fb_t, dma=True)
        for c in range(CT):
            p.op("ac", lambda e, c=c, ek=ek: e.activation(ek[:, c, :], ek[:, c, :], AF.Exp),
                 r=[cx.fa_t[c]], w=[cx.fa_t[c]])
            p.op("dv", lambda e, c=c, ek=ek, vv=vv, Xc=Xc: e.tensor_tensor(
                Xc[:, c, :], ek[:, c, :], vv[:, c, :], ALU.mult),
                r=[cx.fa_t[c], cx.fb_t[c]], w=[cx.acc_t[c]])
        for blk in range(4):
            for c in range(CT):
                p.op("pe", lambda e, blk=blk, c=c, ek=ek: e.matmul(
                    cx.pb[1][:, blk * 16 + 2 * c:blk * 16 + 2 * c + 2],
                    ek[:, c, blk * 128:blk * 128 + 128], cx.c["bo2"][:],
                    start=True, stop=True),
                    r=[cx.fa_t[c], cx.ck["bo2"]], w=[cx.pbk[1]])
        for blk in range(4):
            for c in range(CT):
                bank = 4 + c % 2
                p.op("pe", lambda e, blk=blk, c=c, bank=bank, Xc=Xc: e.transpose(
                    cx.pb[bank][:, 0:128], Xc[:, c, blk * 128:blk * 128 + 128],
                    cx.c["ident"][:]),
                    r=[cx.acc_t[c], cx.ck["ident"]], w=[cx.pbk[bank]])
                for j in range(2):
                    p.op("dv", lambda e, blk=blk, c=c, j=j, bank=bank: e.tensor_copy(
                        Xtok[:, blk, (2 * c + j) * 65:(2 * c + j) * 65 + 64],
                        cx.pb[bank][:, j * 64:j * 64 + 64]),
                        r=[cx.pbk[bank]], w=Xtokk)
            p.op("ac", lambda e, blk=blk: e.activation(
                Xtok[:, blk, :].rearrange("p (h x) -> p h x", x=65)[:, :, 64],
                cx.pb[1][:, blk * 16:blk * 16 + 16], AF.Copy),
                r=[cx.pbk[1]], w=Xtokk)
        for blk in range(4):
            for h in range(16):
                nb = 2 + h // 8
                ncol = (h % 8) * 64
                p.op("pe", lambda e, blk=blk, h=h, nb=nb, ncol=ncol: e.matmul(
                    cx.pb[nb][:, ncol:ncol + 64], Dv[:, h, :],
                    Xtok[:, blk, h * 65:h * 65 + 64], start=True, stop=False),
                    r=Dvk + Xtokk, w=[cx.pbk[nb]])
                p.op("pe", lambda e, blk=blk, h=h, nb=nb, ncol=ncol: e.matmul(
                    cx.pb[nb][:, ncol:ncol + 64],
                    r1[h // 4][:, (h % 4) * 128:(h % 4) * 128 + 128],
                    srow[:, h * 65:h * 65 + 64], start=False, stop=True),
                    r=[r1k[h // 4]] + srowk, w=[cx.pbk[nb]])
                p.op("pe", lambda e, blk=blk, h=h: e.matmul(
                    cx.pb[0][:, h:h + 1], Dv[:, h, :],
                    Xtok[:, blk, h * 65 + 64:h * 65 + 65], start=True, stop=False),
                    r=Dvk + Xtokk, w=[cx.pbk[0]])
                p.op("pe", lambda e, blk=blk, h=h: e.matmul(
                    cx.pb[0][:, h:h + 1],
                    r1[h // 4][:, (h % 4) * 128:(h % 4) * 128 + 128],
                    srow[:, h * 65 + 64:h * 65 + 65], start=False, stop=True),
                    r=[r1k[h // 4]] + srowk, w=[cx.pbk[0]])
            dn = cx.sm[:, 384:400]
            dnk = cx.sm_t[6]
            p.op("dv", lambda e, blk=blk: e.tensor_tensor(
                dn, Xtok[:, blk, :].rearrange("p (h x) -> p h x", x=65)[:, :, 64],
                euh_bc, ALU.mult), r=Xtokk + [euhk], w=[dnk])
            p.op("dv", lambda e: e.tensor_scalar(dn, dn, 1e-6, None, ALU.max),
                 r=[dnk], w=[dnk])
            p.op("dv", lambda e: e.tensor_tensor(dn, dn, cx.pb[0][:, 0:16], ALU.add),
                 r=[dnk, cx.pbk[0]], w=[dnk])
            p.op("dv", lambda e: e.reciprocal(cx.sm[:, 448:464], dn),
                 r=[dnk], w=[cx.sm_t[7]])
            for bi in range(2):
                p.op("dv", lambda e, blk=blk, bi=bi: e.tensor_tensor(
                    wkvt[:, blk, bi * 512:bi * 512 + 512].rearrange("p (h x) -> p h x", x=64),
                    Xtok[:, blk, bi * 520:bi * 520 + 520].rearrange("p (h x) -> p h x", x=65)[:, :, 0:64],
                    euh_bc[:, bi * 8:bi * 8 + 8].unsqueeze(2).broadcast_to((128, 8, 64)),
                    ALU.mult), r=Xtokk + [euhk], w=[cx.acc_t[8 + blk * 2 + bi]])
                p.op("dv", lambda e, blk=blk, bi=bi: e.tensor_tensor(
                    wkvt[:, blk, bi * 512:bi * 512 + 512],
                    wkvt[:, blk, bi * 512:bi * 512 + 512], cx.pb[2 + bi][:], ALU.add),
                    r=[cx.pbk[2 + bi], cx.acc_t[8 + blk * 2 + bi]],
                    w=[cx.acc_t[8 + blk * 2 + bi]])
                p.op("dv", lambda e, blk=blk, bi=bi: e.tensor_tensor(
                    wkvt[:, blk, bi * 512:bi * 512 + 512].rearrange("p (h x) -> p h x", x=64),
                    wkvt[:, blk, bi * 512:bi * 512 + 512].rearrange("p (h x) -> p h x", x=64),
                    cx.sm[:, 448 + bi * 8:448 + bi * 8 + 8].unsqueeze(2).broadcast_to((128, 8, 64)),
                    ALU.mult), r=[cx.sm_t[7], cx.acc_t[8 + blk * 2 + bi]],
                    w=[cx.acc_t[8 + blk * 2 + bi]])
            for h in range(16):
                p.op("pe", lambda e, blk=blk, h=h: e.matmul(
                    cx.pb[0][0:1, h * 32:h * 32 + 32] if False else cx.pb[0][0:1, 448 + h:448 + h + 1],
                    ewvC[:, h:h + 1], Xtok[:, blk, h * 65 + 64:h * 65 + 65],
                    start=True, stop=True),
                    r=[ewvCk] + Xtokk, w=[cx.pbk[0]])
                cb = 1 + h // 8
                p.op("pe", lambda e, blk=blk, h=h, cb=cb: e.matmul(
                    cx.pb[cb][0:1, (h % 8) * 64:(h % 8) * 64 + 64],
                    ewvC[:, h:h + 1], Xtok[:, blk, h * 65:h * 65 + 64],
                    start=True, stop=True),
                    r=[ewvCk] + Xtokk, w=[cx.pbk[cb]])
            for cb in (1, 2):
                p.op("ac", lambda e, cb=cb: e.activation(
                    crow.rearrange("p (h x) -> p h x", x=65)[:, (cb - 1) * 8:(cb - 1) * 8 + 8, 0:64],
                    cx.pb[cb][0:1, :].rearrange("p (h x) -> p h x", x=64),
                    AF.Copy), r=[cx.pbk[cb]], w=crowk)
            p.op("ac", lambda e: e.activation(
                crow.rearrange("p (h x) -> p h x", x=65)[:, :, 64],
                cx.pb[0][0:1, 448:464], AF.Copy), r=[cx.pbk[0]], w=crowk)
            p.op("dv", lambda e: e.tensor_tensor(srow[:], srow[:], ewCrow[:], ALU.mult),
                 r=ewCk + srowk, w=srowk)
            p.op("dv", lambda e: e.tensor_tensor(srow[:], srow[:], crow[:], ALU.add),
                 r=crowk + srowk, w=srowk)
        p.op("sy", lambda e, t0=t0, vv=vv: e.dma_start(
            out=vv[:, :, :], in_=rT32[:, :, t0:t0 + 512].rearrange("c p t -> p c t")),
            r=list(rT32k), w=cx.fb_t, dma=True)
        for blk in range(4):
            for c in range(CT):
                bank = 4 + c % 2
                p.op("pe", lambda e, blk=blk, c=c, bank=bank, wkvt=wkvt: e.transpose(
                    cx.pb[bank][:, 0:128], wkvt[:, blk, c * 128:c * 128 + 128],
                    cx.c["ident"][:]),
                    r=[cx.acc_t[8 + blk * 2 + c // 4], cx.ck["ident"]], w=[cx.pbk[bank]])
                p.op("dv", lambda e, blk=blk, c=c, bank=bank, vv=vv: e.tensor_tensor(
                    rwkv_sb[:, c, blk * 128:blk * 128 + 128], cx.pb[bank][:, 0:128],
                    vv[:, c, blk * 128:blk * 128 + 128], ALU.mult),
                    r=[cx.pbk[bank], cx.fb_t[c]], w=[rwkk[c]])
        for c in range(CT):
            p.op("po", lambda e, c=c, t0=t0: e.dma_start(
                out=rwkvbf[c, :, t0:t0 + 512], in_=rwkv_sb[:, c, :]),
                r=[rwkk[c]], w=[rwkvbfk[c]], dma=True)

    load_w(cx, cx.wb[:], wo, wok, cx.wbk)
    mm_cm(cx, wbv, cx.wbk, rwkvbf, rwkvbfk, f32_epil(sT, sTk, AF.Copy), banks=(2, 3))

    ln_pass(cx, xn32, xn32k, x132, x132k, x1bf, x1bfk,
            gamma_col=ln_sb[:, 0:8], beta_col=ln_sb[:, 8:16], gbk=[lnsk])

    def qpad_epil(c, tt, b):
        wreg = (c + tt) % 2
        p.op("ac", lambda e, b=b, wreg=wreg: e.activation(
            cx.wc[:, wreg * 512:wreg * 512 + 512], cx.pb[b][:], AF.Copy),
            r=[cx.pbk[b]], w=[cx.wck[wreg]])
        t0 = tt * 512
        p.op("po", lambda e, c=c, t0=t0, wreg=wreg: e.dma_start(
            out=qpad2[c, 0, 0:64, t0:t0 + 512], in_=cx.wc[0:64, wreg * 512:wreg * 512 + 512]),
            r=[cx.wck[wreg]], w=[qpad2k[c]], dma=True)
        p.op("po", lambda e, c=c, t0=t0, wreg=wreg: e.dma_start(
            out=qpad2[c, 1, 64:128, t0:t0 + 512], in_=cx.wc[64:128, wreg * 512:wreg * 512 + 512]),
            r=[cx.wck[wreg]], w=[qpad2k[c]], dma=True)
        p.op("po", lambda e, c=c, t0=t0: e.dma_start(
            out=qpad2[c, 0, 64:128, t0:t0 + 512], in_=cx.zero_bf[64:128, :]),
            r=[cx.ck_zero], w=[qpad2k[c]], dma=True)
        p.op("po", lambda e, c=c, t0=t0: e.dma_start(
            out=qpad2[c, 1, 0:64, t0:t0 + 512], in_=cx.zero_bf[0:64, :]),
            r=[cx.ck_zero], w=[qpad2k[c]], dma=True)

    def bf_epil(dst, dstk):
        def epil(c, tt, b):
            wreg = (c + tt) % 2
            p.op("ac", lambda e, b=b, wreg=wreg: e.activation(
                cx.wc[:, wreg * 512:wreg * 512 + 512], cx.pb[b][:], AF.Copy),
                r=[cx.pbk[b]], w=[cx.wck[wreg]])
            p.op("po", lambda e, c=c, tt=tt, wreg=wreg: e.dma_start(
                out=dst[c, :, tt * 512:tt * 512 + 512],
                in_=cx.wc[:, wreg * 512:wreg * 512 + 512]),
                r=[cx.wck[wreg]], w=[dstk[c]], dma=True)
        return epil

    for part, epil in ((0, qpad_epil), (1, bf_epil(kbf2, kbf2k)), (2, bf_epil(vbf2, vbf2k))):
        p.op("po", lambda e, part=part: e.dma_start(
            out=cx.wa[:], in_=wqkv[part]), r=[wqkvk], w=[cx.wak], dma=True)
        mm_cm(cx, wav, cx.wak, x1bf, x1bfk, epil,
              banks=((0, 1) if part % 2 == 0 else (2, 3)))

    negU = cx.sm[:, 512:513]
    negUk = cx.sm_t[8]
    u_bound_pass(cx, qpad2, qpad2k, kbf2, kbf2k, negU, negUk)
    msa_pass(cx, qpad2, qpad2k, kbf2, kbf2k, vbf2, vbf2k, negU, negUk, yT2, yT2k)

    load_w(cx, cx.wa[:], wproj, wprojk, cx.wak)

    def z1_epil(c, tt, b):
        freg = 4 + (c + tt) % 2
        t0 = tt * 512
        p.op("sy", lambda e, c=c, t0=t0, freg=freg: e.dma_start(
            out=cx.f32b[:, freg * 512:freg * 512 + 512], in_=xn32[c, :, t0:t0 + 512]),
            r=[xn32k[tt]], w=[cx.fb_t[freg]], dma=True)
        p.op("dv", lambda e, b=b, freg=freg: e.tensor_tensor(
            cx.f32a[:, freg * 512:freg * 512 + 512], cx.pb[b][:],
            cx.f32b[:, freg * 512:freg * 512 + 512], ALU.add),
            r=[cx.pbk[b], cx.fb_t[freg]], w=[cx.fa_t[freg]])
        p.op("po", lambda e, c=c, t0=t0, freg=freg: e.dma_start(
            out=z1[c, :, t0:t0 + 512], in_=cx.f32a[:, freg * 512:freg * 512 + 512]),
            r=[cx.fa_t[freg]], w=[z1k[tt]], dma=True)
    mm_cm(cx, wav, cx.wak, yT2, yT2k, z1_epil)

    ln_pass(cx, z1, z1k, x232, x232k, x2bf, x2bfk,
            gamma_col=ln_sb[:, 16:24], beta_col=ln_sb[:, 24:32], gbk=[lnsk])

    for ch in range(NTT):
        t0 = ch * 512
        xnc = cx.f32a[:].rearrange("p (c f) -> p c f", f=512)
        p.op("sy", lambda e, t0=t0, xnc=xnc: e.dma_start(
            out=xnc[:, :, :], in_=x232[:, :, t0:t0 + 512].rearrange("c p t -> p c t")),
            r=[x232k[ch]], w=cx.fa_t, dma=True)
        for blk in range(4):
            bo = blk * 128

            def rmm(e, bo=bo, xnc=xnc):
                last = None
                for k in range(CT):
                    last = e.matmul(cx.pb[0][:, 0:8], xnc[:, k, bo:bo + 128],
                                    gw_sb[:, k, :], start=(k == 0), stop=False)
                last = e.matmul(cx.pb[0][:, 0:8], cx.c["ones_row"][:], gb_sb[:],
                                start=False, stop=True)
                return last
            p.op("pe", rmm, r=cx.fa_t + [gwsk, gbsk, cx.ck["ones_row"]], w=[cx.pbk[0]])
            lg = cx.sm[:, 896:912]
            lgk = cx.sm_t[14]
            mx = cx.sm[:, 960:961]
            mxk = cx.sm_t[15]
            p.op("ac", lambda e: e.activation(lg[:, 0:8], cx.pb[0][:, 0:8], AF.Copy),
                 r=[cx.pbk[0]], w=[lgk])
            p.op("dv", lambda e: e.reduce_max(mx, lg[:, 0:8], axis=AX.X),
                 r=[lgk], w=[mxk])
            p.op("dv", lambda e: e.tensor_scalar(lg[:, 0:8], lg[:, 0:8], mx, None, ALU.is_ge),
                 r=[lgk, mxk], w=[lgk])
            p.op("dv", lambda e: e.memset(lg[:, 8:16], 0.0), r=[lgk], w=[lgk])
            p.op("pe", lambda e: e.transpose(cx.pb[1][0:16, 0:128], lg[:, 0:16],
                                             cx.c["ident"][:]),
                 r=[lgk, cx.ck["ident"]], w=[cx.pbk[1]])
            p.op("ac", lambda e: e.activation(cx.f32b[0:16, 3584:3712], cx.pb[1][0:16, 0:128], AF.Copy),
                 r=[cx.pbk[1]], w=[cx.fb_t[7]])
            p.op("po", lambda e, t0=t0, bo=bo: e.dma_start(
                out=gR2[:, t0 + bo:t0 + bo + 128], in_=cx.f32b[0:16, 3584:3712]),
                r=[cx.fb_t[7]], w=[gR2k[0]], dma=True)

    for c in range(CT):
        p.op("po", lambda e, c=c: e.dma_start(out=z2T[c], in_=z1[c]),
             r=list(z1k), w=[z2Tk[c]], dma=True)
    moe_pass(cx, x2bf, x2bfk, gR2, gR2k, w1bf, w1k, w2bf, w2k, hE2, hE2k,
             z2T, z2Tk, AF.Gelu_apprx_tanh, 8)

    combine_pass(cx, xT, xTk, sT, sTk, z2T, z2Tk, xout, xoutk, token_major)





def build_layer3(cx, xT, xTk, xout, xoutk, token_major=False):
    p = cx.p
    eps = cx.c["eps"]

    def qkv_perm(which):
        idx = np.arange(16)[:, None] * 192 + which * 64 + np.arange(64)[None, :]
        return idx.reshape(-1)

    wqkv, wqkvk = win(cx, 3, "wqkv", (3, 128, 8192), lambda i: np.stack([
        tile_lhs(np.asarray(i["L3_attn_W_qkv"]).T[:, qkv_perm(w)]) for w in range(3)]))
    bqkv, bqkvk = win(cx, 3, "bqkv", (128, 24), lambda i: np.concatenate([
        col_tile(np.asarray(i["L3_attn_b_qkv"])[qkv_perm(w)]) for w in range(3)], axis=1))
    wout, woutk = win(cx, 3, "wout", (128, 8192), lambda i: tile_lhs(np.asarray(i["L3_attn_W_out"]).T))
    bout, boutk = win(cx, 3, "bout", (128, 8), lambda i: col_tile(i["L3_attn_b_out"]))
    wg3, wg3k = win(cx, 3, "wg3", (128, 8, 16), lambda i: tile_rhs(np.asarray(i["L3_ffn_Wg"]).T))
    w1bf, w1k = precast(cx, 3, "w1", (16, 128, 8192), lambda i: np.stack([
        tile_lhs(np.asarray(i["L3_ffn_W1"])[e].T) for e in range(16)]), n_parts=8)
    w2bf, w2k = precast(cx, 3, "w2", (16, 128, 8192), lambda i: np.stack([
        tile_lhs(np.asarray(i["L3_ffn_W2"])[e].T) for e in range(16)]), n_parts=8)
    w3bf, w3k = precast(cx, 3, "w3", (16, 128, 8192), lambda i: np.stack([
        tile_lhs(np.asarray(i["L3_ffn_W3"])[e].T) for e in range(16)]), n_parts=8)
    b13, b13k = win(cx, 3, "b13", (16, 128, 16), lambda i: np.stack([
        np.concatenate([col_tile(np.asarray(i["L3_ffn_b1"])[e]),
                        col_tile(np.asarray(i["L3_ffn_b3"])[e])], axis=1) for e in range(16)]))
    b2c, b2ck = win(cx, 3, "b2c", (16, 128, 8), lambda i: np.stack([
        col_tile(np.asarray(i["L3_ffn_b2"])[e]) for e in range(16)]))

    xn32, xn32k = dram(cx, "l3_xn32", (CT, 128, T), n_trk=NTT)
    xnbf, xnbfk = dram(cx, "l3_xnbf", (CT, 128, T), BF16, n_trk=NTT)
    qpad3, qpad3k = dram(cx, "l3_qpad", (CT, 2, 128, T), BF16, n_trk=CT)
    kbf3, kbf3k = dram(cx, "l3_kbf", (CT, 128, T), BF16, n_trk=CT)
    vbf3, vbf3k = dram(cx, "l3_vbf", (CT, 128, T), BF16, n_trk=CT)
    yT3, yT3k = dram(cx, "l3_yT", (CT, 128, T), BF16, n_trk=CT)
    sT, sTk = dram(cx, "l3_sT", (CT, 128, T), n_trk=CT)
    gR3, gR3k = dram(cx, "l3_gR", (16, T), n_trk=1)
    hbf3, hbf3k = dram(cx, "l3_hbf", (CT, 128, T), BF16, n_trk=CT)
    g0T, g0Tk = dram(cx, "l3_g0T", (CT, 128, T), n_trk=CT)

    bq_sb = p.sbuf("l3_bq_sb", [128, 24])
    bqsk = Trk()
    p.op("sy", lambda e: e.dma_start(out=bq_sb[:], in_=bqkv), r=[bqkvk], w=[bqsk], dma=True)
    bo_sb = p.sbuf("l3_bo_sb", [128, 8])
    bosk = Trk()
    p.op("sy", lambda e: e.dma_start(out=bo_sb[:], in_=bout), r=[boutk], w=[bosk], dma=True)
    wg_sb = p.sbuf("l3_wg_sb", [128, 8, 16], F32)
    wgsk = Trk()
    p.op("sy", lambda e: e.dma_start(out=wg_sb[:], in_=wg3), r=[wg3k], w=[wgsk], dma=True)
    b13_sb = p.sbuf("l3_b13_sb", [128, 16])
    b13sk = Trk()
    b2_sb = p.sbuf("l3_b2_sb", [128, 8])
    b2sk = Trk()

    ln_pass(cx, xT, xTk, xn32, xn32k, xnbf, xnbfk)

    wav = cx.wa[:].rearrange("p (k c f) -> p k c f", k=8, f=128)
    wbv = cx.wb[:].rearrange("p (k c f) -> p k c f", k=8, f=128)

    def qpad_epil3(c, tt, b):
        wreg = (c + tt) % 2
        p.op("ac", lambda e, b=b, c=c, wreg=wreg: e.activation(
            cx.wc[:, wreg * 512:wreg * 512 + 512], cx.pb[b][:], AF.Identity,
            bias=bq_sb[:, c:c + 1]),
            r=[cx.pbk[b], bqsk], w=[cx.wck[wreg]])
        t0 = tt * 512
        p.op("po", lambda e, c=c, t0=t0, wreg=wreg: e.dma_start(
            out=qpad3[c, 0, 0:64, t0:t0 + 512], in_=cx.wc[0:64, wreg * 512:wreg * 512 + 512]),
            r=[cx.wck[wreg]], w=[qpad3k[c]], dma=True)
        p.op("po", lambda e, c=c, t0=t0, wreg=wreg: e.dma_start(
            out=qpad3[c, 1, 64:128, t0:t0 + 512], in_=cx.wc[64:128, wreg * 512:wreg * 512 + 512]),
            r=[cx.wck[wreg]], w=[qpad3k[c]], dma=True)
        p.op("po", lambda e, c=c, t0=t0: e.dma_start(
            out=qpad3[c, 0, 64:128, t0:t0 + 512], in_=cx.zero_bf[64:128, :]),
            r=[cx.ck_zero], w=[qpad3k[c]], dma=True)
        p.op("po", lambda e, c=c, t0=t0: e.dma_start(
            out=qpad3[c, 1, 0:64, t0:t0 + 512], in_=cx.zero_bf[0:64, :]),
            r=[cx.ck_zero], w=[qpad3k[c]], dma=True)

    def bias_bf_epil(dst, dstk, which):
        def epil(c, tt, b):
            wreg = (c + tt) % 2
            p.op("ac", lambda e, b=b, c=c, wreg=wreg: e.activation(
                cx.wc[:, wreg * 512:wreg * 512 + 512], cx.pb[b][:], AF.Identity,
                bias=bq_sb[:, which * 8 + c:which * 8 + c + 1]),
                r=[cx.pbk[b], bqsk], w=[cx.wck[wreg]])
            p.op("po", lambda e, c=c, tt=tt, wreg=wreg: e.dma_start(
                out=dst[c, :, tt * 512:tt * 512 + 512],
                in_=cx.wc[:, wreg * 512:wreg * 512 + 512]),
                r=[cx.wck[wreg]], w=[dstk[c]], dma=True)
        return epil

    for part, epil in ((0, qpad_epil3), (1, bias_bf_epil(kbf3, kbf3k, 1)),
                       (2, bias_bf_epil(vbf3, vbf3k, 2))):
        p.op("po", lambda e, part=part: e.dma_start(
            out=cx.wa[:], in_=wqkv[part]), r=[wqkvk], w=[cx.wak], dma=True)
        mm_cm(cx, wav, cx.wak, xnbf, xnbfk, epil,
              banks=((0, 1) if part % 2 == 0 else (2, 3)))

    negU = cx.sm[:, 512:513]
    negUk = cx.sm_t[8]
    u_bound_pass(cx, qpad3, qpad3k, kbf3, kbf3k, negU, negUk)
    msa_pass(cx, qpad3, qpad3k, kbf3, kbf3k, vbf3, vbf3k, negU, negUk, yT3, yT3k)

    load_w(cx, cx.wa[:], wout, woutk, cx.wak)

    def sT_epil(c, tt, b):
        freg = 6 + (c + tt) % 2
        p.op("ac", lambda e, b=b, c=c, freg=freg: e.activation(
            cx.f32a[:, freg * 512:freg * 512 + 512], cx.pb[b][:], AF.Identity,
            bias=bo_sb[:, c:c + 1]),
            r=[cx.pbk[b], bosk], w=[cx.fa_t[freg]])
        p.op("po", lambda e, c=c, tt=tt, freg=freg: e.dma_start(
            out=sT[c, :, tt * 512:tt * 512 + 512],
            in_=cx.f32a[:, freg * 512:freg * 512 + 512]),
            r=[cx.fa_t[freg]], w=[sTk[c % len(sTk)]], dma=True)
    mm_cm(cx, wav, cx.wak, yT3, yT3k, sT_epil)

    for ch in range(NTT):
        t0 = ch * 512
        xnc = cx.f32a[:].rearrange("p (c f) -> p c f", f=512)
        for c in range(CT):
            p.op("sy", lambda e, c=c, t0=t0, xnc=xnc: e.dma_start(
                out=xnc[:, c, :], in_=xn32[c, :, t0:t0 + 512]),
                r=[xn32k[ch]], w=[cx.fa_t[c]], dma=True)
        for blk in range(4):
            bo = blk * 128

            def gmm(e, bo=bo, xnc=xnc):
                last = None
                for k in range(CT):
                    last = e.matmul(cx.pb[0][:, 0:16], xnc[:, k, bo:bo + 128],
                                    wg_sb[:, k, :], start=(k == 0), stop=(k == CT - 1))
                return last
            p.op("pe", gmm, r=cx.fa_t + [wgsk], w=[cx.pbk[0]])
            sg = cx.sm[:, 896:912]
            sgk = cx.sm_t[14]
            tmp = cx.sm[:, 912:928]
            g4 = cx.sm[:, 928:944]
            mx = cx.sm[:, 960:961]
            mxk = cx.sm_t[15]
            p.op("ac", lambda e: e.activation(sg, cx.pb[0][:, 0:16], AF.Sigmoid),
                 r=[cx.pbk[0]], w=[sgk])
            p.op("dv", lambda e: e.tensor_reduce(
                g4[:, 0:4], sg.rearrange("p (g x) -> p g x", x=4), AX.X, ALU.max),
                r=[sgk], w=[sgk])
            p.op("dv", lambda e: e.reduce_max(mx, g4[:, 0:4], axis=AX.X),
                 r=[sgk], w=[mxk])
            p.op("dv", lambda e: e.tensor_scalar(tmp[:, 0:4], g4[:, 0:4], mx, None, ALU.is_lt),
                 r=[sgk, mxk], w=[sgk])
            p.op("dv", lambda e: e.tensor_tensor(tmp[:, 4:8], g4[:, 0:4], tmp[:, 0:4], ALU.mult),
                 r=[sgk], w=[sgk])
            p.op("dv", lambda e: e.reduce_max(mx, tmp[:, 4:8], axis=AX.X),
                 r=[sgk], w=[mxk])
            p.op("dv", lambda e: e.tensor_scalar(g4[:, 0:4], g4[:, 0:4], mx, None, ALU.is_ge),
                 r=[sgk, mxk], w=[sgk])
            p.op("dv", lambda e: e.tensor_scalar(g4[:, 4:8], g4[:, 0:4], -1.0, 1e30,
                                                 ALU.add, ALU.mult),
                 r=[sgk], w=[sgk])
            p.op("dv", lambda e: e.tensor_tensor(
                tmp.rearrange("p (g x) -> p g x", x=4),
                sg.rearrange("p (g x) -> p g x", x=4),
                g4[:, 0:4].unsqueeze(2).broadcast_to((128, 4, 4)), ALU.mult),
                r=[sgk], w=[sgk])
            p.op("dv", lambda e: e.tensor_tensor(
                tmp.rearrange("p (g x) -> p g x", x=4),
                tmp.rearrange("p (g x) -> p g x", x=4),
                g4[:, 4:8].unsqueeze(2).broadcast_to((128, 4, 4)), ALU.add),
                r=[sgk], w=[sgk])
            p.op("dv", lambda e: e.tensor_copy(cx.sm[:, 944:960], tmp),
                 r=[sgk], w=[sgk])
            for it in range(3):
                p.op("dv", lambda e: e.reduce_max(mx, cx.sm[:, 944:960], axis=AX.X),
                     r=[sgk], w=[mxk])
                p.op("dv", lambda e: e.tensor_scalar(cx.sm[:, 880:896], cx.sm[:, 944:960], mx, None, ALU.is_ge),
                     r=[sgk, mxk], w=[cx.sm_t[13]])
                p.op("dv", lambda e: e.scalar_tensor_tensor(
                    cx.sm[:, 944:960], cx.sm[:, 880:896], -2e30, cx.sm[:, 944:960],
                    ALU.mult, ALU.add), r=[sgk, cx.sm_t[13]], w=[sgk])
            p.op("dv", lambda e: e.reduce_max(mx, cx.sm[:, 944:960], axis=AX.X),
                 r=[sgk], w=[mxk])
            p.op("dv", lambda e: e.tensor_scalar(cx.sm[:, 912:928], tmp, mx, None, ALU.is_ge),
                 r=[sgk, mxk], w=[sgk])
            p.op("dv", lambda e: e.tensor_tensor(cx.sm[:, 912:928], cx.sm[:, 912:928], sg, ALU.mult),
                 r=[sgk], w=[sgk])
            p.op("dv", lambda e: e.tensor_reduce(mx, cx.sm[:, 912:928].rearrange("p (o x) -> p o x", o=1),
                                                 AX.X, ALU.add),
                 r=[sgk], w=[mxk])
            p.op("dv", lambda e: e.reciprocal(mx, mx), r=[mxk], w=[mxk])
            p.op("dv", lambda e: e.tensor_scalar(cx.sm[:, 912:928], cx.sm[:, 912:928],
                                                 mx, 2.5, ALU.mult, ALU.mult),
                 r=[sgk, mxk], w=[sgk])
            p.op("pe", lambda e: e.transpose(cx.pb[1][0:16, 0:128], cx.sm[:, 912:928],
                                             cx.c["ident"][:]),
                 r=[sgk, cx.ck["ident"]], w=[cx.pbk[1]])
            p.op("ac", lambda e: e.activation(cx.f32b[0:16, 3584:3712], cx.pb[1][0:16, 0:128], AF.Copy),
                 r=[cx.pbk[1]], w=[cx.fb_t[7]])
            p.op("po", lambda e, t0=t0, bo=bo: e.dma_start(
                out=gR3[:, t0 + bo:t0 + bo + 128], in_=cx.f32b[0:16, 3584:3712]),
                r=[cx.fb_t[7]], w=[gR3k[0]], dma=True)

    gbc = cx.f32b[:, 0:512]
    gbck = cx.fb_t[0]
    grow = cx.r1[0]
    growk = cx.r1k[0]
    for e_ in range(16):
        p.op("sy", lambda e, e_=e_: e.dma_start(out=b13_sb[:], in_=b13[e_]),
             r=[b13k], w=[b13sk], dma=True)
        p.op("sy", lambda e, e_=e_: e.dma_start(out=b2_sb[:], in_=b2c[e_]),
             r=[b2ck], w=[b2sk], dma=True)
        p.op("sy", lambda e, e_=e_: e.dma_start(out=cx.wa[:], in_=w1bf[e_]),
             r=[w1k], w=[cx.wak], dma=True)
        p.op("sy", lambda e, e_=e_: e.dma_start(out=cx.wb[:], in_=w3bf[e_]),
             r=[w3k], w=[cx.wbk], dma=True)
        for tt in range(NTT):
            t0 = tt * 512
            rhs = cx.rhs[tt % 2]
            rk = cx.rhsk[tt % 2]
            p.op("sy", lambda e, t0=t0, rhs=rhs: e.dma_start(
                out=rhs[:, :, :], in_=xnbf[:, :, t0:t0 + 512].rearrange("k p t -> p k t")),
                r=[xnbfk[tt]], w=rk, dma=True)
            for c in range(CT):
                b1b = (tt * CT + c) % 2
                b3b = 2 + (tt * CT + c) % 2

                def m1(e, c=c, rhs=rhs, b1b=b1b):
                    last = None
                    for k in range(CT):
                        last = e.matmul(cx.pb[b1b][:], wav[:, k, c, :], rhs[:, k, :],
                                        start=(k == 0), stop=(k == CT - 1))
                    return last
                p.op("pe", m1, r=rk[:CT] + [cx.wak], w=[cx.pbk[b1b]])

                def m3(e, c=c, rhs=rhs, b3b=b3b):
                    last = None
                    for k in range(CT):
                        last = e.matmul(cx.pb[b3b][:], wbv[:, k, c, :], rhs[:, k, :],
                                        start=(k == 0), stop=(k == CT - 1))
                    return last
                p.op("pe", m3, r=rk[:CT] + [cx.wbk], w=[cx.pbk[b3b]])
                freg = 4 + (tt * CT + c) % 2
                p.op("ac", lambda e, c=c, b1b=b1b, freg=freg: e.activation(
                    cx.f32a[:, freg * 512:freg * 512 + 512], cx.pb[b1b][:], AF.Silu,
                    bias=b13_sb[:, c:c + 1]),
                    r=[cx.pbk[b1b], b13sk], w=[cx.fa_t[freg]])
                wreg = (tt * CT + c) % 4
                p.op("dv", lambda e, c=c, b3b=b3b, freg=freg, wreg=wreg: e.scalar_tensor_tensor(
                    cx.wc[:, wreg * 512:wreg * 512 + 512], cx.pb[b3b][:],
                    b13_sb[:, 8 + c:8 + c + 1], cx.f32a[:, freg * 512:freg * 512 + 512],
                    ALU.add, ALU.mult),
                    r=[cx.pbk[b3b], b13sk, cx.fa_t[freg]], w=[cx.wck[wreg]])
                p.op("po", lambda e, c=c, t0=t0, wreg=wreg: e.dma_start(
                    out=hbf3[c, :, t0:t0 + 512], in_=cx.wc[:, wreg * 512:wreg * 512 + 512]),
                    r=[cx.wck[wreg]], w=[hbf3k[c]], dma=True)
        p.op("sy", lambda e, e_=e_: e.dma_start(out=cx.wa[:], in_=w2bf[e_]),
             r=[w2k], w=[cx.wak], dma=True)
        for tt in range(NTT):
            t0 = tt * 512
            p.op("sy", lambda e, e_=e_, t0=t0: e.dma_start(
                out=grow[:], in_=gR3[e_, t0:t0 + 512]), r=[gR3k[0]], w=[growk], dma=True)
            p.op("pe", lambda e: e.matmul(cx.pb[4][:], cx.c["ones_row"][:], grow[:],
                                          start=True, stop=True),
                 r=[growk, cx.ck["ones_row"]], w=[cx.pbk[4]])
            p.op("ac", lambda e: e.activation(gbc, cx.pb[4][:], AF.Copy),
                 r=[cx.pbk[4]], w=[gbck])
            rhs = cx.rhs[tt % 2]
            rk = cx.rhsk[tt % 2]
            p.op("sy", lambda e, t0=t0, rhs=rhs: e.dma_start(
                out=rhs[:, :, :], in_=hbf3[:, :, t0:t0 + 512].rearrange("k p t -> p k t")),
                r=list(hbf3k), w=rk, dma=True)
            for c in range(CT):
                b = (tt * CT + c) % 2

                def m2(e, c=c, rhs=rhs, b=b):
                    last = None
                    for k in range(CT):
                        last = e.matmul(cx.pb[b][:], wav[:, k, c, :], rhs[:, k, :],
                                        start=(k == 0), stop=(k == CT - 1))
                    return last
                p.op("pe", m2, r=rk[:CT] + [cx.wak], w=[cx.pbk[b]])
                freg = 4 + (tt * CT + c) % 2
                p.op("dv", lambda e, b=b, c=c, freg=freg: e.scalar_tensor_tensor(
                    cx.f32a[:, freg * 512:freg * 512 + 512], cx.pb[b][:],
                    b2_sb[:, c:c + 1], gbc, ALU.add, ALU.mult),
                    r=[cx.pbk[b], b2sk, gbck], w=[cx.fa_t[freg]])
                p.op("po", lambda e, c=c, t0=t0, freg=freg, e_=e_: e.dma_start(
                    out=g0T[c, :, t0:t0 + 512],
                    in_=cx.f32a[:, freg * 512:freg * 512 + 512],
                    accum_op=(ALU.add if e_ > 0 else ALU.bypass)),
                    r=[cx.fa_t[freg]], w=[g0Tk[c]], dma=True)

    combine_pass(cx, xT, xTk, sT, sTk, g0T, g0Tk, xout, xoutk, token_major)



LAYER_BUILDERS = {}


def build_nc(layers, final_tm):
    nc = bass.Bass("TRN2", target_bir_lowering=False)
    p = Prog(nc)
    consts = build_consts()
    cx = make_ctx(p, nc, consts)
    cx.host_inputs = {}

    xin = nc.dram_tensor("xin", [CT, 128, T], F32, kind="ExternalInput").ap()
    cur, curk = xin, [Trk()]
    for idx, li in enumerate(layers):
        last = idx == len(layers) - 1
        if last:
            if final_tm:
                nxt = nc.dram_tensor("out", [T, D], F32, kind="ExternalOutput").ap()
            else:
                nxt = nc.dram_tensor("out", [CT, 128, T], F32, kind="ExternalOutput").ap()
            nxtk = [Trk()]
        else:
            nxt, nxtk = dram(cx, f"x_l{li}", (CT, 128, T))
        builder = {0: build_layer0, 1: build_layer1, 2: build_layer2, 3: build_layer3}.get(li) or LAYER_BUILDERS[li]
        builder(cx, cur, curk, nxt, nxtk, token_major=(last and final_tm))
        cur, curk = nxt, nxtk

    p.op("sy", lambda e: e.nop(), r=curk)
    p.emit()
    return nc, cx.host_inputs


def host_prep(inputs, host_fns, xin_list):
    shared = {name: np.ascontiguousarray(fn(inputs), dtype=np.float32)
              for name, fn in host_fns.items()}
    consts = build_consts()
    in_maps = []
    for ci in range(N_CORES):
        m = {"xin": xin_list[ci]}
        m.update(shared)
        for name, arr in consts.items():
            m["c_" + name] = arr
        in_maps.append(m)
    return in_maps


def kernel(**inputs):
    import os
    fused = os.environ.get("KFUSE", "1") == "1"
    x = np.asarray(inputs["L0_attn_hidden_states"], np.float32)
    xin_list = [np.ascontiguousarray(x[ci].T.reshape(CT, 128, T)) for ci in range(N_CORES)]
    if fused:
        nc, host_fns = build_nc([0, 1, 2, 3], final_tm=True)
        in_maps = host_prep(inputs, host_fns, xin_list)
        res = run_bass_kernel_spmd(nc, in_maps, core_ids=list(range(N_CORES)))
        return np.stack([res.results[i]["out"] for i in range(N_CORES)], axis=0)
    cur = xin_list
    for li in range(4):
        nc, host_fns = build_nc([li], final_tm=(li == 3))
        in_maps = host_prep(inputs, host_fns, cur)
        res = run_bass_kernel_spmd(nc, in_maps, core_ids=list(range(N_CORES)))
        cur = [np.asarray(res.results[i]["out"]) for i in range(N_CORES)]
    return np.stack(cur, axis=0)


# version 04
# speedup vs baseline: 1.0721x; 1.0721x over previous
import contextlib
import math
import numpy as np
import ml_dtypes
import concourse.bass as bass
import concourse.mybir as mybir
from concourse.bass_utils import run_bass_kernel_spmd

F32 = mybir.dt.float32
BF16 = mybir.dt.bfloat16
AF = mybir.ActivationFunctionType
ALU = mybir.AluOpType
AX = mybir.AxisListType

T = 4096
D = 1024
CT = D // 128
NTT = T // 512
N_CORES = 8

ENGINES = ("sy", "ac", "dv", "pe", "po")
SAME_ENGINE_SYNC = {"sy", "ac", "dv", "po"}


class Trk:
    __slots__ = ("w", "readers")

    def __init__(self):
        self.w = None
        self.readers = []


def trks(n):
    return [Trk() for _ in range(n)]


def regt(lst, lo, hi, width=512):
    return lst[lo // width:(hi + width - 1) // width]


N_DMA_SEMS = 64


class Prog:

    def __init__(self, nc):
        self.nc = nc
        self.es = contextlib.ExitStack()
        self.ops = {e: [] for e in ENGINES}
        self.gseq = 0

    def sbuf(self, name, shape, dtype=F32):
        return self.es.enter_context(self.nc.sbuf_tensor(name, shape, dtype))

    def psum(self, name, shape, dtype=F32):
        return self.es.enter_context(self.nc.psum_tensor(name, shape, dtype))

    def op(self, eng, fn, r=(), w=(), dma=False):
        i = len(self.ops[eng])
        deps = set()
        for t in r:
            if t.w is not None:
                deps.add(t.w)
            t.readers.append((eng, i))
        for t in w:
            if t.w is not None:
                deps.add(t.w)
            for rd in t.readers:
                deps.add(rd)
            t.w = (eng, i)
            t.readers = []
        deps.discard((eng, i))
        deps = {d for d in deps if d[0] != eng or eng in SAME_ENGINE_SYNC}
        self.ops[eng].append([fn, deps, dma, False, self.gseq])
        self.gseq += 1
        return i

    def emit(self):
        nc = self.nc
        for eng in ENGINES:
            for rec in self.ops[eng]:
                for (de, dj) in rec[1]:
                    self.ops[de][dj][3] = True
        compval = {}
        for eng in ENGINES:
            c = 0
            for j, rec in enumerate(self.ops[eng]):
                if not rec[2] and rec[3]:
                    c += 1
                    compval[(eng, j)] = c
        dmaslot = {}
        pool_of = {}
        npools = 0
        for eng in ENGINES:
            if any(rec[2] for rec in self.ops[eng]):
                pool_of[eng] = npools
                npools += 1
        per_pool = N_DMA_SEMS // max(npools, 1)
        for eng, pi in pool_of.items():
            dma_ops = [(rec[4], j) for j, rec in enumerate(self.ops[eng])
                       if rec[2] and rec[3]]
            dma_ops.sort()
            gens = [0] * per_pool
            for n, (_, j) in enumerate(dma_ops):
                si = pi * per_pool + n % per_pool
                gens[n % per_pool] += 1
                dmaslot[(eng, j)] = (si, gens[n % per_pool])
        csem = {e: self.es.enter_context(nc.semaphore(f"sem_{e}"))
                for e in ENGINES}
        dsem = [self.es.enter_context(nc.semaphore(f"dsem{i}"))
                for i in range(npools * per_pool)]
        junk = {eng: self.es.enter_context(nc.semaphore(f"sem_junk_{eng}"))
                for eng in pool_of}

        final_gen = {}
        for (eng, j), (si, gen) in dmaslot.items():
            final_gen[si] = max(final_gen.get(si, 0), gen)
        junk_cnt = {eng: sum(1 for rec in self.ops[eng] if rec[2] and not rec[3])
                    for eng in pool_of}

        def section(eng_name):
            def body(engobj):
                waited = {}

                def wait(key, v):
                    if v > waited.get(key, 0):
                        sem = dsem[key[1]] if key[0] == "d" else csem[key[1]]
                        engobj.wait_ge(sem, v)
                        waited[key] = v
                for j, rec in enumerate(self.ops[eng_name]):
                    fn, deps, dma, inc, _ = rec
                    wants = {}
                    for (de, dj) in deps:
                        drec = self.ops[de][dj]
                        if drec[2]:
                            si, gen = dmaslot[(de, dj)]
                            key = ("d", si)
                            v = 16 * gen
                        else:
                            key = ("c", de)
                            v = compval[(de, dj)]
                        if v > wants.get(key, 0):
                            wants[key] = v
                    for key in sorted(wants):
                        wait(key, wants[key])
                    if dma and inc:
                        si, gen = dmaslot[(eng_name, j)]
                        if gen > 1:
                            wait(("d", si), 16 * (gen - 1))
                    inst = fn(engobj)
                    if dma:
                        if inc:
                            si, gen = dmaslot[(eng_name, j)]
                            inst.then_inc(dsem[si], 16)
                        else:
                            inst.then_inc(junk[eng_name], 16)
                    elif inc:
                        inst.then_inc(csem[eng_name], 1)
                if eng_name in pool_of:
                    pi = pool_of[eng_name]
                    for si in range(pi * per_pool, (pi + 1) * per_pool):
                        if final_gen.get(si):
                            wait(("d", si), 16 * final_gen[si])
                    if junk_cnt[eng_name]:
                        engobj.wait_ge(junk[eng_name], 16 * junk_cnt[eng_name])
            return body

        with nc.Block() as block:
            block.sync(section("sy"))
            block.scalar(section("ac"))
            block.vector(section("dv"))
            block.tensor(section("pe"))
            block.gpsimd(section("po"))
        return csem



def tile_lhs(w_eff):
    din, dout = w_eff.shape
    kt, ct = din // 128, dout // 128
    return np.ascontiguousarray(
        w_eff.reshape(kt, 128, ct, 128).transpose(1, 0, 2, 3).reshape(128, -1)
    ).astype(np.float32)


def tile_rhs(w_eff):
    din, dout = w_eff.shape
    kt = din // 128
    return np.ascontiguousarray(
        w_eff.reshape(kt, 128, dout).transpose(1, 0, 2)
    ).astype(np.float32)


def col_tile(vec):
    return np.ascontiguousarray(np.asarray(vec).reshape(CT, 128).T).astype(np.float32)


def build_consts():
    c = {}
    c["ones_col"] = np.ones((128, 1), np.float32)
    c["ones_row"] = np.ones((1, 128), np.float32)
    bo2 = np.zeros((128, 2), np.float32)
    bo2[:64, 0] = 1.0
    bo2[64:, 1] = 1.0
    c["bo2"] = bo2
    c["bo2T"] = np.ascontiguousarray(bo2.T)
    c["ident"] = np.eye(128, dtype=np.float32)
    tk = np.arange(128)[:, None]
    tq = np.arange(128)[None, :]
    mt = ((tk // 16 < tq // 16) | ((tk // 16 == tq // 16) & (tk <= tq))).astype(np.float32)
    c["maskT2"] = np.ascontiguousarray(np.concatenate([mt, mt], axis=1))
    eps = np.zeros((128, 8), np.float32)
    eps[:, 0] = 1e-5
    eps[:, 1] = 1e-12
    eps[:, 2] = 1e-6
    eps[:, 3] = 0.125
    eps[:, 4] = 128.0
    c["eps"] = eps
    jj = np.arange(128)[:, None]
    ii = np.arange(128)[None, :]
    c["pmatT"] = (ii - 1.0 - jj).astype(np.float32)
    c["strictT"] = (jj < ii).astype(np.float32)
    c["iota_row"] = np.arange(128, dtype=np.float32).reshape(1, 128)
    c["r127mj_col"] = (127.0 - np.arange(128, dtype=np.float32)).reshape(128, 1)
    return c



class Ctx:
    pass


def make_ctx(p, nc, const_host):
    cx = Ctx()
    cx.p = p
    cx.nc = nc
    cx.pbpair = [p.psum(f"pbp{i}", [128, 1024]) for i in range(3)]
    cx.pb = [cx.pbpair[i // 2][:, (i % 2) * 512:(i % 2) * 512 + 512] for i in range(6)]
    cx.pbk = trks(6)
    cx.pbf = [p.psum("pbfA", [128, 1024], BF16), p.psum("pbfB", [128, 1024], BF16)]
    cx.pbfk = trks(2)
    cx.rhs = [p.sbuf("ws_rhsA", [128, 8, 512], BF16),
              p.sbuf("ws_rhsB", [128, 8, 512], BF16)]
    cx.rhsk = [trks(8), trks(8)]
    cx.wa = p.sbuf("ws_wa", [128, 8192], BF16)
    cx.wak = Trk()
    cx.wb = p.sbuf("ws_wb", [128, 8192], BF16)
    cx.wbk = Trk()
    cx.wc = p.sbuf("ws_wc", [128, 2048], BF16)
    cx.wck = trks(4)
    cx.f32a = p.sbuf("ws_f32a", [128, 4096], F32)
    cx.fa_t = trks(8)
    cx.f32b = p.sbuf("ws_f32b", [128, 4096], F32)
    cx.fb_t = trks(8)
    cx.acc = p.sbuf("ws_acc", [128, 8192], F32)
    cx.acc_t = trks(16)
    cx.big = p.sbuf("ws_big", [128, 25600], BF16)
    cx.big_t = trks(50)
    cx.sm = p.sbuf("ws_sm", [128, 1024], F32)
    cx.sm_t = trks(16)
    cx.r2a = p.sbuf("ws_r2a", [2, 512], F32)
    cx.r2ak = Trk()
    cx.r2b = p.sbuf("ws_r2b", [2, 512], F32)
    cx.r2bk = Trk()
    cx.r1 = [p.sbuf(f"ws_r1{i}", [1, 512], F32) for i in range(4)]
    cx.r1k = trks(4)
    cx.c = {}
    cx.ck = {}
    for name, arr in const_host.items():
        dh = nc.dram_tensor("c_" + name, list(arr.shape), F32, kind="ExternalInput")
        sb = p.sbuf("cs_" + name, list(arr.shape), F32)
        k = Trk()
        ap = dh.ap()
        p.op("sy", lambda e, sb=sb, ap=ap: e.dma_start(out=sb[:], in_=ap),
             w=[k], dma=True)
        cx.c[name] = sb
        cx.ck[name] = k
    cx.c_id_bf = p.sbuf("cs_id_bf", [128, 128], BF16)
    cx.ck_id_bf = Trk()
    p.op("dv", lambda e: e.tensor_copy(cx.c_id_bf[:], cx.c["ident"][:]),
         r=[cx.ck["ident"]], w=[cx.ck_id_bf])
    cx.zero_bf = p.sbuf("cs_zero_bf", [128, 512], BF16)
    cx.ck_zero = Trk()
    p.op("dv", lambda e: e.memset(cx.zero_bf[:], 0.0), w=[cx.ck_zero])
    return cx


def win(cx, layer, name, shape, host_fn):
    nm = f"L{layer}_{name}"
    dh = cx.nc.dram_tensor(nm, list(shape), F32, kind="ExternalInput")
    cx.host_inputs[nm] = host_fn
    return dh.ap(), Trk()


def dram(cx, name, shape, dtype=F32, kind="Internal", n_trk=1):
    h = cx.nc.dram_tensor(name, list(shape), dtype, kind=kind)
    return h.ap(), trks(n_trk)


def load_w(cx, sb_ap, dram_ap, rtrk, wtrk):
    cx.p.op("po", lambda e: e.dma_start(out=sb_ap, in_=dram_ap),
            r=[rtrk], w=[wtrk], dma=True)


def precast(cx, layer, name, shape, host_fn, n_parts=4):
    ap32, k32 = win(cx, layer, name, shape, host_fn)
    flat = int(np.prod(shape))
    h = cx.nc.dram_tensor(f"L{layer}_{name}_bf", list(shape), BF16, kind="Internal")
    bf_ap = h.ap()
    kbf = Trk()
    step = flat // n_parts
    f32_flat = ap32.rearrange(" ".join(f"a{i}" for i in range(len(shape))) + " -> (" +
                              " ".join(f"a{i}" for i in range(len(shape))) + ")")         if False else None
    for i in range(n_parts):
        cx.p.op("po", lambda e, i=i: e.dma_start(
            out=bf_ap.flatten()[i * step:(i + 1) * step],
            in_=ap32.flatten()[i * step:(i + 1) * step]),
            r=[k32], w=[kbf], dma=True)
    return bf_ap, kbf



def ln_pass(cx, src, srck, dst32, dst32k, dstbf, dstbfk, gamma_col=None,
            beta_col=None, gbk=()):
    p = cx.p
    eps = cx.c["eps"]
    for tt in range(NTT):
        t0 = tt * 512
        xa = cx.f32a[:].rearrange("p (c f) -> p c f", f=512)
        sq = cx.f32b[:].rearrange("p (c f) -> p c f", f=512)
        p.op("sy", lambda e, t0=t0, xa=xa: e.dma_start(
            out=xa[:, :, :], in_=src[:, :, t0:t0 + 512].rearrange("c p t -> p c t")),
            r=list(srck), w=cx.fa_t, dma=True)
        for c in range(CT):
            p.op("ac", lambda e, c=c: e.activation(sq[:, c, :], xa[:, c, :], AF.Square),
                 r=[cx.fa_t[c]], w=[cx.fb_t[c]])

        def mmsum(srcv, srck_, bank):
            def f(e, srcv=srcv, bank=bank):
                last = None
                for c in range(CT):
                    last = e.matmul(cx.pb[bank][0:1, :], cx.c["ones_col"][:],
                                    srcv[:, c, :], start=(c == 0), stop=(c == CT - 1))
                return last
            p.op("pe", f, r=list(srck_) + [cx.ck["ones_col"]], w=[cx.pbk[bank]])
        mmsum(xa, cx.fa_t, 0)
        mmsum(sq, cx.fb_t, 1)
        r1, r1k = cx.r1, cx.r1k
        p.op("dv", lambda e: e.tensor_scalar(r1[0][:], cx.pb[0][0:1, :], 1.0 / D, None, ALU.mult),
             r=[cx.pbk[0]], w=[r1k[0]])
        p.op("dv", lambda e: e.tensor_scalar(r1[1][:], cx.pb[1][0:1, :], 1.0 / D, None, ALU.mult),
             r=[cx.pbk[1]], w=[r1k[1]])
        p.op("dv", lambda e: e.tensor_tensor(r1[2][:], r1[0][:], r1[0][:], ALU.mult),
             r=[r1k[0]], w=[r1k[2]])
        p.op("dv", lambda e: e.tensor_tensor(r1[2][:], r1[1][:], r1[2][:], ALU.subtract),
             r=[r1k[1], r1k[2]], w=[r1k[2]])
        p.op("ac", lambda e: e.activation(r1[2][:], r1[2][:], AF.Sqrt, bias=eps[0:1, 0:1]),
             r=[r1k[2], cx.ck["eps"]], w=[r1k[2]])
        p.op("dv", lambda e: e.reciprocal(r1[3][:], r1[2][:]),
             r=[r1k[2]], w=[r1k[3]])
        p.op("dv", lambda e: e.tensor_tensor(r1[1][:], r1[0][:], r1[3][:], ALU.mult),
             r=[r1k[0], r1k[3]], w=[r1k[1]])
        p.op("dv", lambda e: e.tensor_scalar(r1[1][:], r1[1][:], -1.0, None, ALU.mult),
             r=[r1k[1]], w=[r1k[1]])
        p.op("pe", lambda e: e.matmul(cx.pb[2][:], cx.c["ones_row"][:], r1[3][:],
                                      start=True, stop=True),
             r=[r1k[3], cx.ck["ones_row"]], w=[cx.pbk[2]])
        p.op("pe", lambda e: e.matmul(cx.pb[3][:], cx.c["ones_row"][:], r1[1][:],
                                      start=True, stop=True),
             r=[r1k[1], cx.ck["ones_row"]], w=[cx.pbk[3]])
        for c in range(CT):
            p.op("dv", lambda e, c=c: e.tensor_tensor(sq[:, c, :], xa[:, c, :], cx.pb[2][:], ALU.mult),
                 r=[cx.fa_t[c], cx.pbk[2], cx.fb_t[c]], w=[cx.fb_t[c]])
            p.op("dv", lambda e, c=c: e.tensor_tensor(sq[:, c, :], sq[:, c, :], cx.pb[3][:], ALU.add),
                 r=[cx.fb_t[c], cx.pbk[3]], w=[cx.fb_t[c]])
            if gamma_col is not None:
                p.op("dv", lambda e, c=c: e.tensor_scalar(
                    sq[:, c, :], sq[:, c, :], gamma_col[:, c:c + 1], beta_col[:, c:c + 1],
                    ALU.mult, ALU.add), r=[cx.fb_t[c]] + list(gbk), w=[cx.fb_t[c]])
        p.op("sy", lambda e, t0=t0, sq=sq: e.dma_start(
            out=dst32[:, :, t0:t0 + 512].rearrange("c p t -> p c t"), in_=sq[:, :, :]),
            r=cx.fb_t, w=[dst32k[tt]], dma=True)
        p.op("po", lambda e, t0=t0, sq=sq: e.dma_start(
            out=dstbf[:, :, t0:t0 + 512].rearrange("c p t -> p c t"), in_=sq[:, :, :]),
            r=cx.fb_t, w=[dstbfk[tt]], dma=True)


def mm_cm(cx, w_view, wk, src, srck_tt, epil, n_out=CT, kt=CT, banks=(0, 1)):
    p = cx.p
    nb = len(banks)
    for tt in range(NTT):
        t0 = tt * 512
        rhs = cx.rhs[tt % 2]
        rk = cx.rhsk[tt % 2]
        p.op("sy", lambda e, t0=t0, rhs=rhs: e.dma_start(
            out=rhs[:, 0:kt, :], in_=src[0:kt, :, t0:t0 + 512].rearrange("k p t -> p k t")),
            r=[srck_tt[min(tt, len(srck_tt) - 1)]], w=rk[:kt], dma=True)
        for c in range(n_out):
            b = banks[(tt * n_out + c) % nb]

            def f(e, c=c, rhs=rhs, b=b):
                last = None
                for k in range(kt):
                    last = e.matmul(cx.pb[b][:], w_view[:, k, c, :], rhs[:, k, :],
                                    start=(k == 0), stop=(k == kt - 1))
                return last
            p.op("pe", f, r=rk[:kt] + [wk], w=[cx.pbk[b]])
            epil(c, tt, b)


def conv_silu_pass(cx, src, srck_c, conv_sb, convk, dstbf, dstbfk_c, l2norm,
                   pad_heads=False):
    p = cx.p
    eps = cx.c["eps"]
    for c in range(CT):
        p.op("sy", lambda e, c=c: e.dma_start(out=cx.f32a[:], in_=src[c]),
             r=[srck_c[c]], w=cx.fa_t, dma=True)
        p.op("dv", lambda e, c=c: e.tensor_scalar(cx.sm[:, 960:961], conv_sb[:, c, 3:4],
                                                  1.0, None, ALU.add),
             r=[convk], w=[cx.sm_t[15]])
        p.op("dv", lambda e: e.tensor_scalar(cx.acc[:, 0:4096], cx.f32a[:], cx.sm[:, 960:961],
                                             None, ALU.mult),
             r=cx.fa_t + [cx.sm_t[15]], w=cx.acc_t[0:8])
        for j, sh in ((2, 1), (1, 2), (0, 3)):
            p.op("dv", lambda e, j=j, sh=sh, c=c: e.scalar_tensor_tensor(
                cx.acc[:, sh:4096], cx.f32a[:, 0:4096 - sh], conv_sb[:, c, j:j + 1],
                cx.acc[:, sh:4096], ALU.mult, ALU.add),
                r=cx.fa_t + [convk] + cx.acc_t[0:8], w=cx.acc_t[0:8])
        p.op("ac", lambda e: e.activation(cx.f32b[:], cx.acc[:, 0:4096], AF.Silu),
             r=cx.acc_t[0:8], w=cx.fb_t)
        if not l2norm:
            p.op("po", lambda e, c=c: e.dma_start(out=dstbf[c], in_=cx.f32b[:]),
                 r=cx.fb_t, w=[dstbfk_c[c]], dma=True)
            continue
        p.op("ac", lambda e: e.activation(cx.acc[:, 0:4096], cx.f32b[:], AF.Square),
             r=cx.fb_t, w=cx.acc_t[0:8])
        for ts_ in range(NTT):
            o = ts_ * 512
            p.op("pe", lambda e, o=o: e.matmul(cx.pb[4][0:2, :], cx.c["bo2"][:],
                                               cx.acc[:, o:o + 512], start=True, stop=True),
                 r=[cx.acc_t[ts_], cx.ck["bo2"]], w=[cx.pbk[4]])
            p.op("ac", lambda e: e.activation(cx.r2a[:], cx.pb[4][0:2, :], AF.Sqrt,
                                              bias=eps[0:2, 1:2]),
                 r=[cx.pbk[4], cx.ck["eps"]], w=[cx.r2ak])
            p.op("dv", lambda e: e.reciprocal(cx.r2b[:], cx.r2a[:]),
                 r=[cx.r2ak], w=[cx.r2bk])
            p.op("pe", lambda e: e.matmul(cx.pb[5][:], cx.c["bo2T"][:], cx.r2b[:],
                                          start=True, stop=True),
                 r=[cx.r2bk, cx.ck["bo2T"]], w=[cx.pbk[5]])
            reg = ts_ % 2
            p.op("dv", lambda e, o=o, reg=reg: e.tensor_tensor(
                cx.wc[:, reg * 512:reg * 512 + 512], cx.f32b[:, o:o + 512],
                cx.pb[5][:], ALU.mult),
                r=[cx.fb_t[ts_], cx.pbk[5]], w=[cx.wck[reg]])
            if not pad_heads:
                p.op("po", lambda e, c=c, o=o, reg=reg: e.dma_start(
                    out=dstbf[c, :, o:o + 512], in_=cx.wc[:, reg * 512:reg * 512 + 512]),
                    r=[cx.wck[reg]], w=[dstbfk_c[c]], dma=True)
            else:
                p.op("po", lambda e, c=c, o=o, reg=reg: e.dma_start(
                    out=dstbf[c, 0, 0:64, o:o + 512],
                    in_=cx.wc[0:64, reg * 512:reg * 512 + 512]),
                    r=[cx.wck[reg]], w=[dstbfk_c[c]], dma=True)
                p.op("po", lambda e, c=c, o=o, reg=reg: e.dma_start(
                    out=dstbf[c, 1, 64:128, o:o + 512],
                    in_=cx.wc[64:128, reg * 512:reg * 512 + 512]),
                    r=[cx.wck[reg]], w=[dstbfk_c[c]], dma=True)
                p.op("po", lambda e, c=c, o=o: e.dma_start(
                    out=dstbf[c, 0, 64:128, o:o + 512], in_=cx.zero_bf[64:128, :]),
                    r=[cx.ck_zero], w=[dstbfk_c[c]], dma=True)
                p.op("po", lambda e, c=c, o=o: e.dma_start(
                    out=dstbf[c, 1, 0:64, o:o + 512], in_=cx.zero_bf[0:64, :]),
                    r=[cx.ck_zero], w=[dstbfk_c[c]], dma=True)


def l0_attn_core(cx, qpad, qpadk, kbf, kbfk, vbf, vbfk, xnbf, xnbfk,
                 winit_ap, winitk, lng_sb, lngk, lnb_sb, lnbk,
                 wg_sb, wgk, wo_view, wok, sT, sTk):
    import os
    kph = int(os.environ.get("KPH", "99"))
    p = cx.p
    big = cx.big

    def bview(off, w3):
        v = big[:, off:off + w3[1] * w3[2]]
        return v.rearrange("p (a b) -> p a b", b=w3[2])
    xnc = cx.rhs[0][:]
    xnck = cx.rhsk[0]
    qpA = cx.rhs[1][:]
    qpAk = cx.rhsk[1]
    kc = bview(0, (128, 8, 512))
    vc = bview(4096, (128, 8, 512))
    ogbf = bview(4096, (128, 4, 1024))
    ktok = bview(8192, (128, 8, 512))
    vtok = bview(12288, (128, 8, 512))
    ogT = bview(16384, (128, 8, 512))
    qpB = bview(20480, (128, 8, 512))
    abf = [big[:, 24576:24832], big[:, 25088:25344]]
    kB = lambda lo, hi: regt(cx.big_t, lo, hi)
    kck = [kB(c * 512, c * 512 + 512)[0] for c in range(8)]
    vck = [kB(4096 + c * 512, 4096 + c * 512 + 512)[0] for c in range(8)]
    ogbfk = [kB(4096 + b * 1024, 4096 + b * 1024 + 1024) for b in range(4)]
    ktokk = [kB(8192 + c * 512, 8192 + c * 512 + 512)[0] for c in range(8)]
    vtokk = [kB(12288 + c * 512, 12288 + c * 512 + 512)[0] for c in range(8)]
    ogTk = kB(16384, 20480)
    qpBk = [kB(20480 + c * 512, 20480 + c * 512 + 512)[0] for c in range(8)]
    abfk = [cx.big_t[48], cx.big_t[49]]
    wacc = cx.sm[:, 0:512]
    wacck = cx.sm_t[0:8]
    wbf = cx.wc[:, 1024:1536]
    wbfk = cx.wck[2]
    gats = cx.acc[:, 4096:5120]
    gatsk = cx.acc_t[8:10]
    p.op("sy", lambda e: e.dma_start(out=wacc, in_=winit_ap),
         r=[winitk], w=wacck, dma=True)
    eps = cx.c["eps"]
    sums_o = cx.sm[:, 512:528]
    sums_q = cx.sm[:, 576:592]
    mean = cx.sm[:, 640:656]
    var = cx.sm[:, 704:720]
    rstd = cx.sm[:, 768:784]
    m2 = cx.sm[:, 832:848]
    sok, sqk_, mk, vk, rk_, m2k = (cx.sm_t[8], cx.sm_t[9], cx.sm_t[10],
                                   cx.sm_t[11], cx.sm_t[12], cx.sm_t[13])

    for ch in range(NTT):
        t0 = ch * 512
        p.op("sy", lambda e, t0=t0: e.dma_start(
            out=qpA[:, :, :], in_=qpad[:, 0, :, t0:t0 + 512].rearrange("c p t -> p c t")),
            r=list(qpadk), w=qpAk, dma=True)
        p.op("sy", lambda e, t0=t0: e.dma_start(
            out=qpB[:, :, :], in_=qpad[:, 1, :, t0:t0 + 512].rearrange("c p t -> p c t")),
            r=list(qpadk), w=qpBk, dma=True)
        p.op("sy", lambda e, t0=t0: e.dma_start(
            out=kc[:, :, :], in_=kbf[:, :, t0:t0 + 512].rearrange("c p t -> p c t")),
            r=list(kbfk), w=kck, dma=True)
        p.op("sy", lambda e, t0=t0: e.dma_start(
            out=vc[:, :, :], in_=vbf[:, :, t0:t0 + 512].rearrange("c p t -> p c t")),
            r=list(vbfk), w=vck, dma=True)
        p.op("sy", lambda e, t0=t0: e.dma_start(
            out=xnc[:, :, :], in_=xnbf[:, :, t0:t0 + 512].rearrange("c p t -> p c t")),
            r=[xnbfk[min(ch, len(xnbfk) - 1)]], w=xnck, dma=True)
        for blk in range(4):
            bo = blk * 128
            for c in range(CT):
                kps = cx.pbf[0][:, c * 128:c * 128 + 128]
                vps = cx.pbf[1][:, c * 128:c * 128 + 128]
                p.op("pe", lambda e, c=c, bo=bo, kps=kps: e.transpose(
                    kps, kc[:, c, bo:bo + 128], cx.c_id_bf[:]),
                    r=[kck[c], cx.ck_id_bf], w=[cx.pbfk[0]])
                p.op("pe", lambda e, c=c, bo=bo, vps=vps: e.transpose(
                    vps, vc[:, c, bo:bo + 128], cx.c_id_bf[:]),
                    r=[vck[c], cx.ck_id_bf], w=[cx.pbfk[1]])
                p.op("dv", lambda e, c=c, bo=bo, kps=kps: e.tensor_copy(
                    ktok[:, c, bo:bo + 128], kps), r=[cx.pbfk[0]], w=[ktokk[c]])
                p.op("dv", lambda e, c=c, bo=bo, vps=vps: e.tensor_copy(
                    vtok[:, c, bo:bo + 128], vps), r=[cx.pbfk[1]], w=[vtokk[c]])
        for blk in range(4):
            bo = blk * 128
            if kph < 2:
                continue
            p.op("dv", lambda e: e.tensor_copy(wbf, wacc), r=list(wacck), w=[wbfk])
            for half in range(2):
                def gmm(e, half=half, bo=bo):
                    last = None
                    for k in range(CT):
                        last = e.matmul(cx.pb[5][:], xnc[:, k, bo:bo + 128],
                                        wg_sb[:, k, half * 512:half * 512 + 512],
                                        start=(k == 0), stop=(k == CT - 1))
                    return last
                p.op("pe", gmm, r=xnck + [wgk], w=[cx.pbk[5]])
                p.op("dv", lambda e, half=half: e.tensor_copy(
                    gats[:, half * 512:half * 512 + 512], cx.pb[5][:]),
                    r=[cx.pbk[5]], w=[gatsk[half]])
            for c in range(CT):
                if kph < 3:
                    continue
                qp = [qpA, qpB]
                qpk = [qpAk, qpBk]

                def amm(e, c=c, bo=bo):
                    last = None
                    for j in range(2):
                        last = e.matmul(cx.pb[0][:, j * 128:j * 128 + 128],
                                        kc[:, c, bo:bo + 128],
                                        qp[j][:, c, bo:bo + 128],
                                        start=True, stop=True)
                    return last
                p.op("pe", amm, r=[kck[c], qpAk[c], qpBk[c]], w=[cx.pbk[0]])
                ab = c % 2
                p.op("dv", lambda e, c=c, ab=ab: e.tensor_tensor(
                    abf[ab], cx.pb[0][:, 0:256], cx.c["maskT2"][:], ALU.mult),
                    r=[cx.pbk[0], cx.ck["maskT2"]], w=[abfk[ab]])
                if kph < 4:
                    continue
                obank = 1 + c // 4
                ocol = (c % 4) * 128

                def oim(e, c=c, bo=bo, ab=ab, obank=obank, ocol=ocol):
                    last = None
                    for hl in range(2):
                        h = 2 * c + hl
                        last = e.matmul(cx.pb[obank][:, ocol + hl * 64:ocol + hl * 64 + 64],
                                        abf[ab][:, hl * 128:hl * 128 + 128],
                                        vtok[:, c, bo:bo + 128][:, hl * 64:hl * 64 + 64],
                                        start=True, stop=False)
                        last = e.matmul(cx.pb[obank][:, ocol + hl * 64:ocol + hl * 64 + 64],
                                        qp[hl][:, c, bo:bo + 128],
                                        wbf[:, (h // 2) * 64:(h // 2) * 64 + 64],
                                        start=False, stop=True)
                    return last
                p.op("pe", oim, r=[abfk[ab], vtokk[c], qpAk[c], qpBk[c], wbfk],
                     w=[cx.pbk[obank]])

                if kph < 5:
                    continue

                def smm(e, c=c, bo=bo):
                    last = None
                    for hl in range(2):
                        last = e.matmul(cx.pb[3 + hl][:, c * 64:c * 64 + 64],
                                        ktok[:, c, bo:bo + 128],
                                        vtok[:, c, bo:bo + 128][:, hl * 64:hl * 64 + 64],
                                        start=True, stop=True)
                    return last
                p.op("pe", smm, r=[ktokk[c], vtokk[c]], w=[cx.pbk[3], cx.pbk[4]])
            if kph < 6:
                continue
            for bi in range(2):
                p.op("ac", lambda e, bi=bi: e.activation(
                    cx.f32a[:, bi * 512:bi * 512 + 512], cx.pb[1 + bi][:], AF.Copy),
                    r=[cx.pbk[1 + bi]], w=[cx.fa_t[bi]])
                p.op("ac", lambda e, bi=bi: e.activation(
                    cx.f32b[:, bi * 512:bi * 512 + 512], cx.pb[1 + bi][:], AF.Square),
                    r=[cx.pbk[1 + bi]], w=[cx.fb_t[bi]])
                p.op("dv", lambda e, bi=bi: e.tensor_reduce(
                    sums_o[:, bi * 8:bi * 8 + 8],
                    cx.f32a[:, bi * 512:bi * 512 + 512].rearrange("p (h d) -> p h d", d=64),
                    AX.X, ALU.add), r=[cx.fa_t[bi]], w=[sok])
                p.op("dv", lambda e, bi=bi: e.tensor_reduce(
                    sums_q[:, bi * 8:bi * 8 + 8],
                    cx.f32b[:, bi * 512:bi * 512 + 512].rearrange("p (h d) -> p h d", d=64),
                    AX.X, ALU.add), r=[cx.fb_t[bi]], w=[sqk_])
            p.op("dv", lambda e: e.tensor_scalar(mean, sums_o, 1.0 / 64, None, ALU.mult),
                 r=[sok], w=[mk])
            p.op("dv", lambda e: e.tensor_scalar(var, sums_q, 1.0 / 64, None, ALU.mult),
                 r=[sqk_], w=[vk])
            p.op("dv", lambda e: e.tensor_tensor(m2, mean, mean, ALU.mult),
                 r=[mk], w=[m2k])
            p.op("dv", lambda e: e.tensor_tensor(var, var, m2, ALU.subtract),
                 r=[vk, m2k], w=[vk])
            p.op("ac", lambda e: e.activation(var, var, AF.Sqrt, bias=eps[:, 0:1]),
                 r=[vk, cx.ck["eps"]], w=[vk])
            p.op("dv", lambda e: e.reciprocal(rstd, var), r=[vk], w=[rk_])
            for bi in range(2):
                p.op("dv", lambda e, bi=bi: e.tensor_tensor(
                    cx.f32a[:, bi * 512:bi * 512 + 512].rearrange("p (h d) -> p h d", d=64),
                    cx.f32a[:, bi * 512:bi * 512 + 512].rearrange("p (h d) -> p h d", d=64),
                    mean[:, bi * 8:bi * 8 + 8].unsqueeze(2).broadcast_to((128, 8, 64)),
                    ALU.subtract), r=[cx.fa_t[bi], mk], w=[cx.fa_t[bi]])
                p.op("dv", lambda e, bi=bi: e.tensor_tensor(
                    cx.f32b[:, bi * 512:bi * 512 + 512].rearrange("p (h d) -> p h d", d=64),
                    lng_sb[:].rearrange("p (h d) -> p h d", d=64),
                    rstd[:, bi * 8:bi * 8 + 8].unsqueeze(2).broadcast_to((128, 8, 64)),
                    ALU.mult), r=[lngk, rk_, sqk_, cx.fb_t[bi]], w=[cx.fb_t[bi]])
                p.op("dv", lambda e, bi=bi: e.tensor_tensor(
                    cx.f32a[:, bi * 512:bi * 512 + 512], cx.f32a[:, bi * 512:bi * 512 + 512],
                    cx.f32b[:, bi * 512:bi * 512 + 512], ALU.mult),
                    r=[cx.fa_t[bi], cx.fb_t[bi]], w=[cx.fa_t[bi]])
                p.op("dv", lambda e, bi=bi: e.tensor_tensor(
                    cx.f32a[:, bi * 512:bi * 512 + 512], cx.f32a[:, bi * 512:bi * 512 + 512],
                    lnb_sb[:, 0:512], ALU.add),
                    r=[cx.fa_t[bi], lnbk], w=[cx.fa_t[bi]])
                p.op("dv", lambda e, bi=bi, blk=blk: e.tensor_tensor(
                    ogbf[:, blk, bi * 512:bi * 512 + 512], cx.f32a[:, bi * 512:bi * 512 + 512],
                    gats[:, bi * 512:bi * 512 + 512], ALU.mult),
                    r=[cx.fa_t[bi], gatsk[bi]], w=ogbfk[blk])
            p.op("dv", lambda e: e.tensor_tensor(wacc[0:64, :], wacc[0:64, :],
                                                 cx.pb[3][0:64, :], ALU.add),
                 r=[cx.pbk[3]] + list(wacck) + [wbfk], w=wacck)
            p.op("dv", lambda e: e.tensor_tensor(wacc[64:128, :], wacc[64:128, :],
                                                 cx.pb[4][64:128, :], ALU.add),
                 r=[cx.pbk[4]] + list(wacck) + [wbfk], w=wacck)
        if kph < 7:
            continue
        for blk in range(4):
            bo = blk * 128
            for cp in range(4):
                for j in range(2):
                    c = cp * 2 + j
                    tps = cx.pbf[j][:, cp * 128:cp * 128 + 128]
                    p.op("pe", lambda e, c=c, blk=blk, tps=tps: e.transpose(
                        tps, ogbf[:, blk, c * 128:c * 128 + 128], cx.c_id_bf[:]),
                        r=list(ogbfk[blk]) + [cx.ck_id_bf], w=[cx.pbfk[j]])
                for j in range(2):
                    c = cp * 2 + j
                    tps = cx.pbf[j][:, cp * 128:cp * 128 + 128]
                    p.op("dv", lambda e, c=c, bo=bo, tps=tps: e.tensor_copy(
                        ogT[:, c, bo:bo + 128], tps), r=[cx.pbfk[j]], w=[ogTk[c]])
        if kph < 8:
            continue
        for c in range(CT):
            b = c % 2

            def womm(e, c=c, b=b):
                last = None
                for k in range(CT):
                    last = e.matmul(cx.pb[b][:], wo_view[:, k, c, :], ogT[:, k, :],
                                    start=(k == 0), stop=(k == CT - 1))
                return last
            p.op("pe", womm, r=list(ogTk) + [wok], w=[cx.pbk[b]])
            freg = 4 + c % 4
            p.op("ac", lambda e, b=b, freg=freg: e.activation(
                cx.f32a[:, freg * 512:freg * 512 + 512], cx.pb[b][:], AF.Copy),
                r=[cx.pbk[b]], w=[cx.fa_t[freg]])
            p.op("po", lambda e, c=c, t0=t0, freg=freg: e.dma_start(
                out=sT[c, :, t0:t0 + 512], in_=cx.f32a[:, freg * 512:freg * 512 + 512]),
                r=[cx.fa_t[freg]], w=[sTk[c % len(sTk)]], dma=True)


def mix_pass(cx, xn32, xn32k, mu_sb, muk, dstbf, dstbfk):
    p = cx.p
    for c in range(CT):
        p.op("sy", lambda e, c=c: e.dma_start(out=cx.f32a[:], in_=xn32[c]),
             r=list(xn32k), w=cx.fa_t, dma=True)
        p.op("dv", lambda e: e.tensor_tensor(cx.f32b[:, 1:4096], cx.f32a[:, 1:4096],
                                             cx.f32a[:, 0:4095], ALU.subtract),
             r=cx.fa_t, w=cx.fb_t)
        p.op("dv", lambda e: e.tensor_copy(cx.f32b[:, 0:1], cx.f32a[:, 0:1]),
             r=[cx.fa_t[0]], w=[cx.fb_t[0]])
        p.op("dv", lambda e, c=c: e.scalar_tensor_tensor(
            cx.acc[:, 1:4096], cx.f32b[:, 1:4096], mu_sb[:, c:c + 1],
            cx.f32a[:, 0:4095], ALU.mult, ALU.add),
            r=cx.fb_t + [muk] + cx.fa_t, w=cx.acc_t[0:8])
        p.op("dv", lambda e, c=c: e.tensor_scalar(
            cx.acc[:, 0:1], cx.f32b[:, 0:1], mu_sb[:, c:c + 1], None, ALU.mult),
            r=[cx.fb_t[0], muk], w=[cx.acc_t[0]])
        p.op("po", lambda e, c=c: e.dma_start(out=dstbf[c], in_=cx.acc[:, 0:4096]),
             r=cx.acc_t[0:8], w=[dstbfk[0]], dma=True)


def combine_pass(cx, xT, xTk, sT, sTk, gT, gTk, out, outk, token_major):
    p = cx.p
    res = cx.acc[:].rearrange("p (c f) -> p c f", f=512)
    xa = cx.f32a[:].rearrange("p (c f) -> p c f", f=512)
    ga = cx.f32b[:].rearrange("p (c f) -> p c f", f=512)
    for tt in range(NTT):
        t0 = tt * 512
        p.op("sy", lambda e, t0=t0, xa=xa: e.dma_start(
            out=xa[:, :, :], in_=xT[:, :, t0:t0 + 512].rearrange("c p t -> p c t")),
            r=list(xTk), w=cx.fa_t, dma=True)
        p.op("sy", lambda e, t0=t0, ga=ga: e.dma_start(
            out=ga[:, :, :], in_=gT[:, :, t0:t0 + 512].rearrange("c p t -> p c t")),
            r=list(gTk), w=cx.fb_t, dma=True)
        for c in range(CT):
            p.op("ac", lambda e, c=c, ga=ga: e.activation(ga[:, c, :], ga[:, c, :], AF.Silu),
                 r=[cx.fb_t[c]], w=[cx.fb_t[c]])
        p.op("sy", lambda e, t0=t0, res=res: e.dma_start(
            out=res[:, 0:8, :], in_=sT[:, :, t0:t0 + 512].rearrange("c p t -> p c t")),
            r=list(sTk), w=cx.acc_t[0:8], dma=True)
        for c in range(CT):
            p.op("dv", lambda e, c=c, ga=ga, res=res: e.tensor_tensor(
                ga[:, c, :], res[:, c, :], ga[:, c, :], ALU.mult),
                r=[cx.acc_t[c], cx.fb_t[c]], w=[cx.fb_t[c]])
            p.op("dv", lambda e, c=c, xa=xa, ga=ga, res=res: e.tensor_tensor(
                res[:, c, :], xa[:, c, :], ga[:, c, :], ALU.add),
                r=[cx.fa_t[c], cx.fb_t[c]], w=[cx.acc_t[c]])
        if not token_major:
            p.op("po", lambda e, t0=t0, res=res: e.dma_start(
                out=out[:, :, t0:t0 + 512].rearrange("c p t -> p c t"), in_=res[:, 0:8, :]),
                r=cx.acc_t[0:8], w=[outk[0]], dma=True)
        else:
            for blk in range(4):
                bo = blk * 128
                for c in range(CT):
                    b = 4 + c % 2
                    p.op("pe", lambda e, c=c, bo=bo, b=b, res=res: e.transpose(
                        cx.pb[b][:, 0:128], res[:, c, bo:bo + 128], cx.c["ident"][:]),
                        r=[cx.acc_t[c], cx.ck["ident"]], w=[cx.pbk[b]])
                    p.op("dv", lambda e, c=c, b=b: e.tensor_copy(
                        cx.f32b[:, c * 128:c * 128 + 128], cx.pb[b][:, 0:128]),
                        r=[cx.pbk[b]], w=[cx.fb_t[c // 4]])
                p.op("po", lambda e, t0=t0, bo=bo: e.dma_start(
                    out=out[t0 + bo:t0 + bo + 128, :], in_=cx.f32b[:, 0:1024]),
                    r=cx.fb_t[0:2], w=[outk[0]], dma=True)





def copy_out(cx, srcarr, srck, xout, xoutk):
    p = cx.p
    for c in range(CT):
        k1 = Trk()
        p.op("sy", lambda e, c=c: e.dma_start(out=cx.f32a[:], in_=srcarr[c]),
             r=list(srck), w=cx.fa_t, dma=True)
        p.op("po", lambda e, c=c: e.dma_start(out=xout[c], in_=cx.f32a[:]),
             r=cx.fa_t, w=[xoutk[0]], dma=True)

def build_layer0(cx, xT, xTk, xout, xoutk, token_major=False):
    import os
    kpass = int(os.environ.get("KPASS", "99"))
    p = cx.p

    wq, wqk = win(cx, 0, "wq", (128, 8192), lambda i: tile_lhs(np.asarray(i["L0_attn_Wq"]).T))
    wk_, wkk = win(cx, 0, "wk", (128, 8192), lambda i: tile_lhs(np.asarray(i["L0_attn_Wk"]).T))
    wv, wvk = win(cx, 0, "wv", (128, 8192), lambda i: tile_lhs(np.asarray(i["L0_attn_Wv"]).T))
    wg, wgk = win(cx, 0, "wg", (128, 8, 1024), lambda i: tile_rhs(np.asarray(i["L0_attn_Wg"]).T))
    wo, wok = win(cx, 0, "wo", (128, 8192), lambda i: tile_lhs(np.asarray(i["L0_attn_Wo"]).T))
    conv, convk = win(cx, 0, "conv", (128, 24, 4), lambda i: np.concatenate([
        np.asarray(i["L0_attn_conv_q"]).reshape(8, 128, 4).transpose(1, 0, 2),
        np.asarray(i["L0_attn_conv_k"]).reshape(8, 128, 4).transpose(1, 0, 2),
        np.asarray(i["L0_attn_conv_v"]).reshape(8, 128, 4).transpose(1, 0, 2)], axis=1))
    winit, winitk = win(cx, 0, "winit", (128, 512), lambda i: np.ascontiguousarray(
        np.asarray(i["L0_attn_W_init"])[0].reshape(8, 2, 64, 64).transpose(1, 2, 0, 3).reshape(128, 512)))
    lng, lngk0 = win(cx, 0, "lng", (128, 512), lambda i: np.tile(np.asarray(i["L0_attn_ln_gamma"]), (128, 8)))
    lnb, lnbk0 = win(cx, 0, "lnb", (128, 512), lambda i: np.tile(np.asarray(i["L0_attn_ln_beta"]), (128, 8)))
    wr, wrk = win(cx, 0, "wr", (128, 8192), lambda i: tile_lhs(np.asarray(i["L0_ffn_W_r"]).T))
    wkf, wkfk = win(cx, 0, "wkf", (128, 8192), lambda i: tile_lhs(np.asarray(i["L0_ffn_W_k"]).T))
    wvf, wvfk = win(cx, 0, "wvf", (128, 8192), lambda i: tile_lhs(np.asarray(i["L0_ffn_W_v"]).T))
    mu, muk = win(cx, 0, "mu", (128, 16), lambda i: np.concatenate([
        col_tile(i["L0_ffn_mu_r"]), col_tile(i["L0_ffn_mu_k"])], axis=1))

    xn32, xn32k = dram(cx, "l0_xn32", (CT, 128, T), n_trk=NTT)
    xnbf, xnbfk = dram(cx, "l0_xnbf", (CT, 128, T), BF16, n_trk=NTT)
    qraw, qrawk = dram(cx, "l0_qraw", (CT, 128, T), n_trk=CT)
    kraw, krawk = dram(cx, "l0_kraw", (CT, 128, T), n_trk=CT)
    vraw, vrawk = dram(cx, "l0_vraw", (CT, 128, T), n_trk=CT)
    qpad, qpadk = dram(cx, "l0_qpad", (CT, 2, 128, T), BF16, n_trk=CT)
    kbf, kbfk = dram(cx, "l0_kbf", (CT, 128, T), BF16, n_trk=CT)
    vbf, vbfk = dram(cx, "l0_vbf", (CT, 128, T), BF16, n_trk=CT)
    sT, sTk = dram(cx, "l0_sT", (CT, 128, T), n_trk=CT)
    xrbf, xrbfk = dram(cx, "l0_xrbf", (CT, 128, T), BF16, n_trk=1)
    xkbf, xkbfk = dram(cx, "l0_xkbf", (CT, 128, T), BF16, n_trk=1)
    rT, rTk = dram(cx, "l0_rT", (CT, 128, T), n_trk=CT)
    kabf, kabfk = dram(cx, "l0_kabf", (CT, 128, T), BF16, n_trk=CT)
    gT, gTk = dram(cx, "l0_gT", (CT, 128, T), n_trk=CT)

    conv_sb = p.sbuf("l0_conv_sb", [128, 24, 4])
    convsk = Trk()
    p.op("sy", lambda e: e.dma_start(out=conv_sb[:], in_=conv), r=[convk], w=[convsk], dma=True)
    mu_sb = p.sbuf("l0_mu_sb", [128, 16])
    musk = Trk()
    p.op("sy", lambda e: e.dma_start(out=mu_sb[:], in_=mu), r=[muk], w=[musk], dma=True)
    lng_sb = p.sbuf("l0_lng_sb", [128, 512])
    lngk = Trk()
    p.op("sy", lambda e: e.dma_start(out=lng_sb[:], in_=lng), r=[lngk0], w=[lngk], dma=True)
    lnb_sb = p.sbuf("l0_lnb_sb", [128, 512])
    lnbk = Trk()
    p.op("sy", lambda e: e.dma_start(out=lnb_sb[:], in_=lnb), r=[lnbk0], w=[lnbk], dma=True)

    ln_pass(cx, xT, xTk, xn32, xn32k, xnbf, xnbfk)
    if kpass <= 1:
        copy_out(cx, xn32, xn32k, xout, xoutk)
        return

    wav = cx.wa[:].rearrange("p (k c f) -> p k c f", k=8, f=128)
    wbv = cx.wb[:].rearrange("p (k c f) -> p k c f", k=8, f=128)

    def raw_epil(dst, dstk):
        def epil(c, tt, b):
            freg = 4 + (c + CT * tt) % 4
            p.op("ac", lambda e, b=b, freg=freg: e.activation(
                cx.f32a[:, freg * 512:freg * 512 + 512], cx.pb[b][:], AF.Copy),
                r=[cx.pbk[b]], w=[cx.fa_t[freg]])
            p.op("po" if freg % 2 == 0 else "ac", lambda e, c=c, tt=tt, freg=freg: e.dma_start(
                out=dst[c, :, tt * 512:tt * 512 + 512],
                in_=cx.f32a[:, freg * 512:freg * 512 + 512]),
                r=[cx.fa_t[freg]], w=[dstk[c]], dma=True)
        return epil

    load_w(cx, cx.wa[:], wq, wqk, cx.wak)
    mm_cm(cx, wav, cx.wak, xnbf, xnbfk, raw_epil(qraw, qrawk))
    if kpass <= 2:
        copy_out(cx, qraw, qrawk, xout, xoutk)
        return
    load_w(cx, cx.wb[:], wk_, wkk, cx.wbk)
    mm_cm(cx, wbv, cx.wbk, xnbf, xnbfk, raw_epil(kraw, krawk), banks=(2, 3))
    conv_silu_pass(cx, qraw, qrawk, conv_sb[:, 0:8, :], convsk, qpad, qpadk, l2norm=True, pad_heads=True)
    if kpass <= 3:
        copy_out(cx, xn32, xn32k, xout, xoutk)
        return
    load_w(cx, cx.wa[:], wv, wvk, cx.wak)
    mm_cm(cx, wav, cx.wak, xnbf, xnbfk, raw_epil(vraw, vrawk))
    conv_silu_pass(cx, kraw, krawk, conv_sb[:, 8:16, :], convsk, kbf, kbfk, l2norm=True)
    conv_silu_pass(cx, vraw, vrawk, conv_sb[:, 16:24, :], convsk, vbf, vbfk, l2norm=False)
    if kpass <= 4:
        copy_out(cx, xn32, xn32k, xout, xoutk)
        return

    wgv = cx.wa[:].rearrange("p (k f) -> p k f", f=1024)
    load_w(cx, cx.wa[:], wg, wgk, cx.wak)
    load_w(cx, cx.wb[:], wo, wok, cx.wbk)
    l0_attn_core(cx, qpad, qpadk, kbf, kbfk, vbf, vbfk, xnbf, xnbfk,
                 winit, winitk, lng_sb, lngk, lnb_sb, lnbk,
                 wgv, cx.wak, wbv, cx.wbk, sT, sTk)
    if kpass <= 5:
        copy_out(cx, sT, sTk, xout, xoutk)
        return

    mix_pass(cx, xn32, xn32k, mu_sb[:, 0:8], musk, xrbf, xrbfk)
    mix_pass(cx, xn32, xn32k, mu_sb[:, 8:16], musk, xkbf, xkbfk)

    def sig_epil(c, tt, b):
        freg = 4 + (c + CT * tt) % 4
        p.op("ac", lambda e, b=b, freg=freg: e.activation(
            cx.f32a[:, freg * 512:freg * 512 + 512], cx.pb[b][:], AF.Sigmoid),
            r=[cx.pbk[b]], w=[cx.fa_t[freg]])
        p.op("po", lambda e, c=c, tt=tt, freg=freg: e.dma_start(
            out=rT[c, :, tt * 512:tt * 512 + 512],
            in_=cx.f32a[:, freg * 512:freg * 512 + 512]),
            r=[cx.fa_t[freg]], w=[rTk[c]], dma=True)

    load_w(cx, cx.wa[:], wr, wrk, cx.wak)
    mm_cm(cx, wav, cx.wak, xrbf, [xrbfk[0]], sig_epil)
    if kpass <= 6:
        copy_out(cx, rT, rTk, xout, xoutk)
        return

    def relsq_epil(c, tt, b):
        freg = 4 + (c + CT * tt) % 4
        p.op("dv", lambda e, b=b, freg=freg: e.tensor_scalar(
            cx.f32b[:, freg * 512:freg * 512 + 512], cx.pb[b][:], 0.0, None, ALU.max),
            r=[cx.pbk[b]], w=[cx.fb_t[freg]])
        p.op("ac", lambda e, freg=freg: e.activation(
            cx.f32a[:, freg * 512:freg * 512 + 512],
            cx.f32b[:, freg * 512:freg * 512 + 512], AF.Square),
            r=[cx.fb_t[freg]], w=[cx.fa_t[freg]])
        p.op("po", lambda e, c=c, tt=tt, freg=freg: e.dma_start(
            out=kabf[c, :, tt * 512:tt * 512 + 512],
            in_=cx.f32a[:, freg * 512:freg * 512 + 512]),
            r=[cx.fa_t[freg]], w=[kabfk[c]], dma=True)

    load_w(cx, cx.wb[:], wkf, wkfk, cx.wbk)
    mm_cm(cx, wbv, cx.wbk, xkbf, [xkbfk[0]], relsq_epil, banks=(2, 3))

    def g0_epil(c, tt, b):
        freg = 4 + (c + CT * tt) % 4
        p.op("sy", lambda e, c=c, tt=tt, freg=freg: e.dma_start(
            out=cx.f32b[:, freg * 512:freg * 512 + 512],
            in_=rT[c, :, tt * 512:tt * 512 + 512]),
            r=[rTk[c]], w=[cx.fb_t[freg]], dma=True)
        p.op("dv", lambda e, b=b, freg=freg: e.tensor_tensor(
            cx.f32a[:, freg * 512:freg * 512 + 512], cx.pb[b][:],
            cx.f32b[:, freg * 512:freg * 512 + 512], ALU.mult),
            r=[cx.pbk[b], cx.fb_t[freg]], w=[cx.fa_t[freg]])
        p.op("po", lambda e, c=c, tt=tt, freg=freg: e.dma_start(
            out=gT[c, :, tt * 512:tt * 512 + 512],
            in_=cx.f32a[:, freg * 512:freg * 512 + 512]),
            r=[cx.fa_t[freg]], w=[gTk[c]], dma=True)

    load_w(cx, cx.wa[:], wvf, wvfk, cx.wak)
    mm_cm(cx, wav, cx.wak, kabf, kabfk, g0_epil)
    if kpass <= 7:
        copy_out(cx, gT, gTk, xout, xoutk)
        return

    combine_pass(cx, xT, xTk, sT, sTk, gT, gTk, xout, xoutk, token_major)






def moe_pass(cx, srcbf, srcbfk, gR, gRk, w1bf, w1k, w2bf, w2k, hE, hEk,
             outT, outTk, act_fn, n_exp):
    p = cx.p
    gbc = cx.f32b[:, 0:512]
    gbck = cx.fb_t[0]
    grow = cx.r1[0]
    growk = cx.r1k[0]

    def load_gbc(e_, tt):
        t0 = tt * 512
        p.op("sy", lambda e, e_=e_, t0=t0: e.dma_start(
            out=grow[:], in_=gR[e_, t0:t0 + 512]), r=[gRk[0]], w=[growk], dma=True)
        p.op("pe", lambda e: e.matmul(cx.pb[4][:], cx.c["ones_row"][:], grow[:],
                                      start=True, stop=True),
             r=[growk, cx.ck["ones_row"]], w=[cx.pbk[4]])
        p.op("ac", lambda e: e.activation(gbc, cx.pb[4][:], AF.Copy),
             r=[cx.pbk[4]], w=[gbck])

    w1buf = [cx.big[:, 0:8192].rearrange("p (k c f) -> p k c f", k=8, f=128),
             cx.big[:, 8192:16384].rearrange("p (k c f) -> p k c f", k=8, f=128)]
    w1bufk = [regt(cx.big_t, 0, 8192), regt(cx.big_t, 8192, 16384)]
    w2buf = [cx.big[:, 16384:20480].rearrange("p (k c f) -> p k c f", k=8, f=128),
             cx.big[:, 20480:24576].rearrange("p (k c f) -> p k c f", k=8, f=128)]
    w2bufk = [regt(cx.big_t, 16384, 20480), regt(cx.big_t, 20480, 24576)]
    for e_ in range(n_exp):
        for cg in range(4):
            wsel = (e_ * 4 + cg) % 2
            wav = w1buf[wsel]
            wavk = w1bufk[wsel]
            p.op("sy", lambda e, e_=e_, cg=cg, wsel=wsel: e.dma_start(
                out=cx.big[:, wsel * 8192:wsel * 8192 + 8192],
                in_=w1bf[e_, cg].rearrange("p k c f -> p (k c f)")),
                r=[w1k], w=w1bufk[wsel], dma=True)
            for tt in range(NTT):
                t0 = tt * 512
                rhs = cx.rhs[tt % 2]
                rk = cx.rhsk[tt % 2]
                p.op("sy", lambda e, t0=t0, rhs=rhs: e.dma_start(
                    out=rhs[:, :, :], in_=srcbf[:, :, t0:t0 + 512].rearrange("k p t -> p k t")),
                    r=[srcbfk[tt]], w=rk, dma=True)
                for c in range(CT):
                    b = (tt * CT + c) % 2

                    def hmm(e, c=c, rhs=rhs, b=b, wav=wav):
                        last = None
                        for k in range(CT):
                            last = e.matmul(cx.pb[b][:], wav[:, k, c, :], rhs[:, k, :],
                                            start=(k == 0), stop=(k == CT - 1))
                        return last
                    p.op("pe", hmm, r=rk[:CT] + wavk, w=[cx.pbk[b]])
                    wreg = (tt * CT + c) % 4
                    p.op("ac", lambda e, b=b, wreg=wreg: e.activation(
                        cx.wc[:, wreg * 512:wreg * 512 + 512], cx.pb[b][:], act_fn),
                        r=[cx.pbk[b]], w=[cx.wck[wreg]])
                    p.op("po" if wreg % 2 == 0 else "ac", lambda e, cg=cg, c=c, t0=t0, wreg=wreg: e.dma_start(
                        out=hE[cg * 8 + c, :, t0:t0 + 512],
                        in_=cx.wc[:, wreg * 512:wreg * 512 + 512]),
                        r=[cx.wck[wreg]], w=[hEk[(cg * 8 + c) % 8]], dma=True)
        for tt in range(NTT):
            t0 = tt * 512
            load_gbc(e_, tt)
            for chalf in range(2):
                for kg in range(4):
                    wsel = kg % 2
                    p.op("sy", lambda e, e_=e_, chalf=chalf, kg=kg, wsel=wsel: e.dma_start(
                        out=cx.big[:, 16384 + wsel * 4096:16384 + wsel * 4096 + 4096],
                        in_=w2bf[e_, chalf, kg].rearrange("p k c f -> p (k c f)")),
                        r=[w2k], w=w2bufk[wsel], dma=True)
                    rhs = cx.rhs[kg % 2]
                    rk = cx.rhsk[kg % 2]
                    p.op("sy", lambda e, kg=kg, t0=t0, rhs=rhs: e.dma_start(
                        out=rhs[:, :, :],
                        in_=hE[kg * 8:kg * 8 + 8, :, t0:t0 + 512].rearrange("k p t -> p k t")),
                        r=[hEk[k % 8] for k in range(kg * 8, kg * 8 + 8)], w=rk, dma=True)
                    for c in range(4):
                        def ymm(e, c=c, kg=kg, rhs=rhs, wsel=wsel):
                            last = None
                            for k in range(CT):
                                last = e.matmul(cx.pb[c][:], w2buf[wsel][:, k, c, :], rhs[:, k, :],
                                                start=(kg == 0 and k == 0),
                                                stop=(kg == 3 and k == CT - 1))
                            return last
                        p.op("pe", ymm, r=rk[:CT] + w2bufk[wsel], w=[cx.pbk[c]])
                for c in range(4):
                    d = chalf * 4 + c
                    freg = 2 + c
                    p.op("dv", lambda e, c=c, freg=freg: e.tensor_tensor(
                        cx.f32a[:, freg * 512:freg * 512 + 512], cx.pb[c][:], gbc, ALU.mult),
                        r=[cx.pbk[c], gbck], w=[cx.fa_t[freg]])
                    p.op("po", lambda e, d=d, t0=t0, freg=freg: e.dma_start(
                        out=outT[d, :, t0:t0 + 512],
                        in_=cx.f32a[:, freg * 512:freg * 512 + 512],
                        accum_op=ALU.add),
                        r=[cx.fa_t[freg]], w=[outTk[d % len(outTk)]], dma=True)



def build_layer1(cx, xT, xTk, xout, xoutk, token_major=False):
    p = cx.p

    qkva, qkvak = win(cx, 1, "qkva", (3, 128, 8192),
                      lambda i: np.stack([tile_lhs(np.asarray(i["L1_attn_w_qkva"]).T[:, part * 1024:(part + 1) * 1024])
                                          for part in range(3)]))
    gam, gamk = win(cx, 1, "gamma", (128, 8), lambda i: col_tile(i["L1_attn_gamma"]))
    rw, rwk = win(cx, 1, "rw", (128, 8, 9), lambda i: tile_rhs(np.asarray(i["L1_ffn_router_w"])))
    rb, rbk = win(cx, 1, "rb", (1, 9), lambda i: np.asarray(i["L1_ffn_router_b"]).reshape(1, 9))
    w1bf, w1k = precast(cx, 1, "w1", (8, 4, 128, 8, 8, 128), lambda i: np.ascontiguousarray(
        np.asarray(i["L1_ffn_w1"]).reshape(8, 8, 128, 4, 8, 128).transpose(0, 3, 2, 1, 4, 5)), n_parts=8)
    w2bf, w2k = precast(cx, 1, "w2", (8, 2, 4, 128, 8, 4, 128), lambda i: np.ascontiguousarray(
        np.asarray(i["L1_ffn_w2"]).reshape(8, 4, 8, 128, 2, 4, 128).transpose(0, 4, 1, 3, 2, 5, 6)), n_parts=8)
    hE, hEk = dram(cx, "l1_hE", (32, 128, T), BF16, n_trk=8)

    xn32, xn32k = dram(cx, "l1_xn32", (CT, 128, T), n_trk=NTT)
    xnbf, xnbfk = dram(cx, "l1_xnbf", (CT, 128, T), BF16, n_trk=NTT)
    hbf, hbfk = dram(cx, "l1_hbf", (CT, 128, T), BF16, n_trk=NTT)
    qT, qTk = dram(cx, "l1_qT", (CT, 128, T), n_trk=CT)
    kvT, kvTk = dram(cx, "l1_kvT", (CT, 128, T), n_trk=CT)
    aT, aTk = dram(cx, "l1_aT", (CT, 128, T), n_trk=CT)
    sT, sTk = dram(cx, "l1_sT", (CT, 128, T), n_trk=CT)
    gR, gRk = dram(cx, "l1_gR", (16, T), n_trk=1)
    g0T, g0Tk = dram(cx, "l1_g0T", (CT, 128, T), n_trk=CT)

    gam_sb = p.sbuf("l1_gam_sb", [128, 8])
    gamsk = Trk()
    p.op("sy", lambda e: e.dma_start(out=gam_sb[:], in_=gam), r=[gamk], w=[gamsk], dma=True)
    rw_sb = p.sbuf("l1_rw_sb", [128, 8, 9], F32)
    rwsk = Trk()
    p.op("sy", lambda e: e.dma_start(out=rw_sb[:], in_=rw), r=[rwk], w=[rwsk], dma=True)
    rb_sb = p.sbuf("l1_rb_sb", [1, 9])
    rbsk = Trk()
    p.op("sy", lambda e: e.dma_start(out=rb_sb[:], in_=rb), r=[rbk], w=[rbsk], dma=True)

    ln_pass(cx, xT, xTk, xn32, xn32k, xnbf, xnbfk)

    eps = cx.c["eps"]
    for tt in range(NTT):
        t0 = tt * 512
        xa = cx.f32a[:].rearrange("p (c f) -> p c f", f=512)
        sq = cx.f32b[:].rearrange("p (c f) -> p c f", f=512)
        for c in range(CT):
            p.op("sy", lambda e, c=c, t0=t0, xa=xa: e.dma_start(
                out=xa[:, c, :], in_=xn32[c, :, t0:t0 + 512]),
                r=[xn32k[tt]], w=[cx.fa_t[c]], dma=True)
            p.op("ac", lambda e, c=c, xa=xa, sq=sq: e.activation(sq[:, c, :], xa[:, c, :], AF.Square),
                 r=[cx.fa_t[c]], w=[cx.fb_t[c]])

        def mmsum(e, sq=sq):
            last = None
            for c in range(CT):
                last = e.matmul(cx.pb[0][0:1, :], cx.c["ones_col"][:], sq[:, c, :],
                                start=(c == 0), stop=(c == CT - 1))
            return last
        p.op("pe", mmsum, r=cx.fb_t + [cx.ck["ones_col"]], w=[cx.pbk[0]])
        r1, r1k = cx.r1, cx.r1k
        p.op("ac", lambda e: e.activation(r1[0][:], cx.pb[0][0:1, :], AF.Sqrt),
             r=[cx.pbk[0]], w=[r1k[0]])
        p.op("dv", lambda e: e.tensor_scalar(r1[0][:], r1[0][:], 1e-12, None, ALU.max),
             r=[r1k[0]], w=[r1k[0]])
        p.op("dv", lambda e: e.reciprocal(r1[1][:], r1[0][:]), r=[r1k[0]], w=[r1k[1]])
        p.op("dv", lambda e: e.tensor_scalar(r1[1][:], r1[1][:], 32.0, None, ALU.mult),
             r=[r1k[1]], w=[r1k[1]])
        p.op("pe", lambda e: e.matmul(cx.pb[1][:], cx.c["ones_row"][:], r1[1][:],
                                      start=True, stop=True),
             r=[r1k[1], cx.ck["ones_row"]], w=[cx.pbk[1]])
        for c in range(CT):
            p.op("dv", lambda e, c=c, xa=xa, sq=sq: e.scalar_tensor_tensor(
                sq[:, c, :], xa[:, c, :], gam_sb[:, c:c + 1], cx.pb[1][:],
                ALU.mult, ALU.mult),
                r=[cx.fa_t[c], gamsk, cx.pbk[1], cx.fb_t[c]], w=[cx.fb_t[c]])
            p.op("po", lambda e, c=c, t0=t0, sq=sq: e.dma_start(
                out=hbf[c, :, t0:t0 + 512], in_=sq[:, c, :]),
                r=[cx.fb_t[c]], w=[hbfk[tt]], dma=True)

    wav = cx.wa[:].rearrange("p (k c f) -> p k c f", k=8, f=128)

    def part_epil(dst, dstk, func):
        def epil(c, tt, b):
            freg = 4 + (c + CT * tt) % 4
            p.op("ac", lambda e, b=b, freg=freg: e.activation(
                cx.f32a[:, freg * 512:freg * 512 + 512], cx.pb[b][:], func),
                r=[cx.pbk[b]], w=[cx.fa_t[freg]])
            p.op("po" if freg % 2 == 0 else "ac", lambda e, c=c, tt=tt, freg=freg: e.dma_start(
                out=dst[c, :, tt * 512:tt * 512 + 512],
                in_=cx.f32a[:, freg * 512:freg * 512 + 512]),
                r=[cx.fa_t[freg]], w=[dstk[c]], dma=True)
        return epil

    for part, (dst, dstk, func) in enumerate((
            (qT, qTk, AF.Copy), (kvT, kvTk, AF.Copy), (aT, aTk, AF.Sigmoid))):
        p.op("po", lambda e, part=part: e.dma_start(
            out=cx.wa[:], in_=qkva[part]),
            r=[qkvak], w=[cx.wak], dma=True)
        mm_cm(cx, wav, cx.wak, hbf, hbfk, part_epil(dst, dstk, func),
              banks=((0, 1) if part % 2 == 0 else (2, 3)))

    for ch in range(NTT):
        t0 = ch * 512
        xnc = cx.f32a[:].rearrange("p (c f) -> p c f", f=512)
        xnck = cx.fa_t
        p.op("sy", lambda e, t0=t0, xnc=xnc: e.dma_start(
            out=xnc[:, :, :], in_=xn32[:, :, t0:t0 + 512].rearrange("c p t -> p c t")),
            r=[xn32k[ch]], w=cx.fa_t, dma=True)
        for blk in range(4):
            bo = blk * 128

            def rmm(e, bo=bo, xnc=xnc):
                last = None
                for k in range(CT):
                    last = e.matmul(cx.pb[0][:, 0:9], xnc[:, k, bo:bo + 128],
                                    rw_sb[:, k, :], start=(k == 0), stop=False)
                last = e.matmul(cx.pb[0][:, 0:9], cx.c["ones_row"][:], rb_sb[:],
                                start=False, stop=True)
                return last
            p.op("pe", rmm, r=list(xnck) + [rwsk, rbsk, cx.ck["ones_row"]], w=[cx.pbk[0]])
            lg = cx.sm[:, 896:912]
            lgk = cx.sm_t[14]
            mx = cx.sm[:, 960:961]
            mxk = cx.sm_t[15]
            p.op("ac", lambda e: e.activation(lg[:, 0:9], cx.pb[0][:, 0:9], AF.Copy),
                 r=[cx.pbk[0]], w=[lgk])
            p.op("dv", lambda e: e.reduce_max(mx, lg[:, 0:9], axis=AX.X),
                 r=[lgk], w=[mxk])
            p.op("dv", lambda e: e.tensor_scalar(mx, mx, -1.0, None, ALU.mult),
                 r=[mxk], w=[mxk])
            p.op("ac", lambda e: e.activation(lg[:, 0:9], lg[:, 0:9], AF.Exp,
                                              bias=mx, accum_out=cx.sm[:, 961:962]),
                 r=[lgk, mxk], w=[lgk, mxk])
            p.op("dv", lambda e: e.reciprocal(mx, cx.sm[:, 961:962]),
                 r=[mxk], w=[mxk])
            p.op("dv", lambda e: e.tensor_scalar(lg[:, 0:9], lg[:, 0:9], mx, None, ALU.mult),
                 r=[lgk, mxk], w=[lgk])
            p.op("dv", lambda e: e.reduce_max(mx, lg[:, 0:9], axis=AX.X),
                 r=[lgk], w=[mxk])
            p.op("dv", lambda e: e.tensor_scalar(cx.sm[:, 912:921], lg[:, 0:9],
                                                 mx, None, ALU.is_lt),
                 r=[lgk, mxk], w=[cx.sm_t[14]])
            p.op("dv", lambda e: e.tensor_tensor(cx.sm[:, 928:937], lg[:, 0:9],
                                                 cx.sm[:, 912:921], ALU.mult),
                 r=[cx.sm_t[14]], w=[cx.sm_t[14]])
            p.op("dv", lambda e: e.reduce_max(mx, cx.sm[:, 928:937], axis=AX.X),
                 r=[cx.sm_t[14]], w=[mxk])
            p.op("dv", lambda e: e.tensor_scalar(cx.sm[:, 912:921], lg[:, 0:9],
                                                 mx, None, ALU.is_ge),
                 r=[lgk, mxk], w=[cx.sm_t[14]])
            p.op("dv", lambda e: e.tensor_tensor(lg[:, 0:9], lg[:, 0:9],
                                                 cx.sm[:, 912:921], ALU.mult),
                 r=[cx.sm_t[14], lgk], w=[lgk])
            p.op("dv", lambda e: e.memset(lg[:, 9:16], 0.0), r=[lgk], w=[lgk])
            p.op("pe", lambda e: e.transpose(cx.pb[1][0:16, 0:128],
                                             lg[:, 0:16], cx.c["ident"][:]),
                 r=[lgk, cx.ck["ident"]], w=[cx.pbk[1]])
            p.op("ac", lambda e: e.activation(cx.f32b[0:16, 3584:3712], cx.pb[1][0:16, 0:128], AF.Copy),
                 r=[cx.pbk[1]], w=[cx.fb_t[7]])
            p.op("po", lambda e, t0=t0, bo=bo: e.dma_start(
                out=gR[:, t0 + bo:t0 + bo + 128], in_=cx.f32b[0:16, 3584:3712]),
                r=[cx.fb_t[7]], w=[gRk[0]], dma=True)

    for tt in range(NTT):
        t0 = tt * 512
        gbc = cx.f32b[:, 0:512]
        gbck = cx.fb_t[0]
        grow = cx.r1[0]
        growk = cx.r1k[0]
        p.op("sy", lambda e, t0=t0: e.dma_start(
            out=grow[:], in_=gR[8, t0:t0 + 512]), r=[gRk[0]], w=[growk], dma=True)
        p.op("pe", lambda e: e.matmul(cx.pb[4][:], cx.c["ones_row"][:], grow[:],
                                      start=True, stop=True),
             r=[growk, cx.ck["ones_row"]], w=[cx.pbk[4]])
        p.op("ac", lambda e: e.activation(gbc, cx.pb[4][:], AF.Copy),
             r=[cx.pbk[4]], w=[gbck])
        xnc = cx.rhs[0][:]
        xnck = cx.rhsk[0]
        p.op("sy", lambda e, t0=t0, xnc=xnc: e.dma_start(
            out=xnc[:, :, :], in_=xnbf[:, :, t0:t0 + 512].rearrange("c p t -> p c t")),
            r=[xnbfk[tt]], w=xnck, dma=True)
        for d in range(CT):
            freg = 2 + d % 4
            p.op("dv", lambda e, d=d, freg=freg, xnc=xnc, gbc=gbc: e.tensor_tensor(
                cx.f32a[:, freg * 512:freg * 512 + 512], xnc[:, d, :], gbc, ALU.mult),
                r=[xnck[d], gbck], w=[cx.fa_t[freg]])
            p.op("po", lambda e, d=d, t0=t0, freg=freg: e.dma_start(
                out=g0T[d, :, t0:t0 + 512], in_=cx.f32a[:, freg * 512:freg * 512 + 512]),
                r=[cx.fa_t[freg]], w=[g0Tk[d % CT]], dma=True)

    moe_pass(cx, xnbf, xnbfk, gR, gRk, w1bf, w1k, w2bf, w2k, hE, hEk,
             g0T, g0Tk, AF.Gelu, 8)

    for c in range(CT):
        p.op("sy", lambda e, c=c: e.dma_start(out=cx.f32a[:], in_=aT[c]),
             r=[aTk[c]], w=cx.fa_t, dma=True)
        p.op("sy", lambda e, c=c: e.dma_start(out=cx.f32b[:], in_=kvT[c]),
             r=[kvTk[c]], w=cx.fb_t, dma=True)
        p.op("dv", lambda e: e.tensor_tensor_scan(
            cx.acc[:, 0:4096], cx.f32a[:], cx.f32b[:], 0.0, ALU.mult, ALU.add),
            r=cx.fa_t + cx.fb_t, w=cx.acc_t[0:8])
        p.op("sy", lambda e, c=c: e.dma_start(out=cx.f32b[:], in_=qT[c]),
             r=[qTk[c]], w=cx.fb_t, dma=True)
        p.op("dv", lambda e: e.tensor_tensor(cx.acc[:, 0:4096], cx.acc[:, 0:4096],
                                             cx.f32b[:], ALU.mult),
             r=cx.fb_t + cx.acc_t[0:8], w=cx.acc_t[0:8])
        p.op("po", lambda e, c=c: e.dma_start(out=sT[c], in_=cx.acc[:, 0:4096]),
             r=cx.acc_t[0:8], w=[sTk[0]], dma=True)


    combine_pass(cx, xT, xTk, sT, sTk, g0T, g0Tk, xout, xoutk, token_major)





def u_bound_pass(cx, qpad, qpadk, kbf, kbfk, negU_col, negUk):
    p = cx.p
    r1, r1k = cx.r1, cx.r1k
    p.op("dv", lambda e: e.memset(r1[2][:, 0:2], 0.0), w=[r1k[2]])
    for which, (arr, arrk, pad) in enumerate(((qpad, qpadk, True), (kbf, kbfk, False))):
        for c in range(CT):
            for tt in range(NTT):
                t0 = tt * 512
                wreg = tt % 2
                nsub = 2 if pad else 1
                for j in range(nsub):
                    src_ap = arr[c, j, :, t0:t0 + 512] if pad else arr[c, :, t0:t0 + 512]
                    p.op("sy", lambda e, src_ap=src_ap, wreg=wreg, j=j: e.dma_start(
                        out=cx.wc[:, (wreg * 2 + j) * 512:(wreg * 2 + j) * 512 + 512],
                        in_=src_ap),
                        r=[arrk[c]], w=[cx.wck[wreg * 2 + j]], dma=True)
                    freg = 4 + wreg
                    p.op("ac", lambda e, wreg=wreg, j=j, freg=freg: e.activation(
                        cx.f32a[:, freg * 512:freg * 512 + 512],
                        cx.wc[:, (wreg * 2 + j) * 512:(wreg * 2 + j) * 512 + 512], AF.Square),
                        r=[cx.wck[wreg * 2 + j]], w=[cx.fa_t[freg]])
                    p.op("pe", lambda e, freg=freg, j=j, nsub=nsub: e.matmul(
                        cx.pb[5][0:1, :], cx.c["ones_col"][:],
                        cx.f32a[:, freg * 512:freg * 512 + 512],
                        start=(j == 0), stop=(j == nsub - 1)),
                        r=[cx.fa_t[freg], cx.ck["ones_col"]], w=[cx.pbk[5]])
                p.op("ac", lambda e: e.activation(r1[0][:], cx.pb[5][0:1, :], AF.Copy),
                     r=[cx.pbk[5]], w=[r1k[0]])
                p.op("dv", lambda e: e.reduce_max(r1[1][:, 0:1], r1[0][:], axis=AX.X),
                     r=[r1k[0]], w=[r1k[1]])
                p.op("dv", lambda e, which=which: e.tensor_tensor(
                    r1[2][:, which:which + 1], r1[2][:, which:which + 1],
                    r1[1][:, 0:1], ALU.max),
                    r=[r1k[1]], w=[r1k[2]])
    p.op("ac", lambda e: e.activation(r1[2][:, 0:2], r1[2][:, 0:2], AF.Sqrt),
         r=[r1k[2]], w=[r1k[2]])
    p.op("dv", lambda e: e.tensor_tensor(r1[2][:, 0:1], r1[2][:, 0:1], r1[2][:, 1:2], ALU.mult),
         r=[r1k[2]], w=[r1k[2]])
    p.op("dv", lambda e: e.tensor_scalar(r1[2][:, 0:1], r1[2][:, 0:1], -0.125, None, ALU.mult),
         r=[r1k[2]], w=[r1k[2]])
    p.op("pe", lambda e: e.matmul(cx.pb[5][:, 0:1], cx.c["ones_row"][:], r1[2][:, 0:1],
                                  start=True, stop=True),
         r=[r1k[2], cx.ck["ones_row"]], w=[cx.pbk[5]])
    p.op("ac", lambda e: e.activation(negU_col, cx.pb[5][:, 0:1], AF.Copy),
         r=[cx.pbk[5]], w=[negUk])




def msa_pass(cx, qpad, qpadk, kbf, kbfk, vbf, vbfk, negU_col, negUk, yT, yTk):
    p = cx.p
    big = cx.big
    vtok = big[:, 0:16640].rearrange("p (a b) -> p a b", b=520)
    vtokk = regt(cx.big_t, 0, 16640)
    for hh in range(2):
        for tkb in range(32):
            p.op("dv", lambda e, tkb=tkb: e.memset(
                vtok[:, tkb, :].rearrange("p (h x) -> p h x", x=65)[:, :, 64], 1.0),
                r=[], w=regt(cx.big_t, tkb * 520, tkb * 520 + 520))
        for cl in range(4):
            c = hh * 4 + cl
            for tt in range(NTT):
                t0 = tt * 512
                wreg = tt % 2
                p.op("sy", lambda e, c=c, t0=t0, wreg=wreg: e.dma_start(
                    out=cx.wc[:, wreg * 512:wreg * 512 + 512], in_=vbf[c, :, t0:t0 + 512]),
                    r=[vbfk[c]], w=[cx.wck[wreg]], dma=True)
                for blk in range(4):
                    tkb = tt * 4 + blk
                    bank = blk % 2
                    tps = cx.pbf[bank][:, 0:128]
                    p.op("pe", lambda e, wreg=wreg, blk=blk, tps=tps: e.transpose(
                        tps, cx.wc[:, wreg * 512 + blk * 128:wreg * 512 + blk * 128 + 128],
                        cx.c_id_bf[:]),
                        r=[cx.wck[wreg], cx.ck_id_bf], w=[cx.pbfk[bank]])
                    for j in range(2):
                        p.op("dv", lambda e, tkb=tkb, cl=cl, j=j, tps=tps: e.tensor_copy(
                            vtok[:, tkb, (2 * cl + j) * 65:(2 * cl + j) * 65 + 64],
                            tps[:, j * 64:j * 64 + 64]),
                            r=[cx.pbfk[bank]],
                            w=regt(cx.big_t, tkb * 520, tkb * 520 + 520))
        for hl in range(8):
            h = hh * 8 + hl
            c = h // 2
            j = h % 2
            if j == 0:
                p.op("sy", lambda e, c=c: e.dma_start(
                    out=cx.rhs[0][:].rearrange("p k t -> p (k t)"), in_=kbf[c]),
                    r=[kbfk[c]], w=cx.rhsk[0], dma=True)
            for tq in range(NTT):
                t0 = tq * 512
                p.op("sy", lambda e, c=c, j=j, t0=t0: e.dma_start(
                    out=cx.rhs[1][:, 0, :], in_=qpad[c, j, :, t0:t0 + 512]),
                    r=[qpadk[c]], w=[cx.rhsk[1][0]], dma=True)
                pvb = 2 + tq % 2
                for tkp in range(16):
                    for half in range(2):
                        tk = tkp * 2 + half
                        p.op("pe", lambda e, tk=tk, half=half: e.matmul(
                            cx.pb[half][:], cx.rhs[0][:, tk // 4, (tk % 4) * 128:(tk % 4) * 128 + 128],
                            cx.rhs[1][:, 0, :], start=True, stop=True),
                            r=[cx.rhsk[0][tk // 4], cx.rhsk[1][0]], w=[cx.pbk[half]])
                    p.op("ac", lambda e: e.activation(
                        cx.wc[:, 1024:2048], cx.pbpair[0][:], AF.Exp,
                        bias=negU_col, scale=cx.c["eps"][:, 3:4]),
                        r=[cx.pbk[0], cx.pbk[1], negUk, cx.ck["eps"]],
                        w=[cx.wck[2], cx.wck[3]])
                    for half in range(2):
                        tk = tkp * 2 + half
                        p.op("pe", lambda e, tk=tk, hl=hl, half=half, pvb=pvb: e.matmul(
                            cx.pb[pvb][0:65, :], vtok[:, tk, hl * 65:hl * 65 + 65],
                            cx.wc[:, 1024 + half * 512:1024 + half * 512 + 512],
                            start=(tk == 0), stop=(tk == 31)),
                            r=[cx.wck[2 + half]] + regt(cx.big_t, tk * 520, tk * 520 + 520),
                            w=[cx.pbk[pvb]])
                r1, r1k = cx.r1, cx.r1k
                p.op("ac", lambda e, pvb=pvb: e.activation(r1[0][:], cx.pb[pvb][64:65, :], AF.Copy),
                     r=[cx.pbk[pvb]], w=[r1k[0]])
                p.op("dv", lambda e: e.reciprocal(r1[1][:], r1[0][:]),
                     r=[r1k[0]], w=[r1k[1]])
                p.op("pe", lambda e: e.matmul(cx.pb[4][0:64, :], cx.c["ones_row"][0:1, 0:64],
                                              r1[1][:], start=True, stop=True),
                     r=[r1k[1], cx.ck["ones_row"]], w=[cx.pbk[4]])
                p.op("ac", lambda e: e.activation(cx.f32a[0:64, 3072:3584], cx.pb[4][0:64, :], AF.Copy),
                     r=[cx.pbk[4]], w=[cx.fa_t[6]])
                p.op("dv", lambda e, pvb=pvb: e.tensor_tensor(
                    cx.wc[0:64, 0:512], cx.pb[pvb][0:64, :], cx.f32a[0:64, 3072:3584],
                    ALU.mult), r=[cx.pbk[pvb], cx.fa_t[6]], w=[cx.wck[0]])
                p.op("po", lambda e, c=c, j=j, t0=t0: e.dma_start(
                    out=yT[c, j * 64:j * 64 + 64, t0:t0 + 512], in_=cx.wc[0:64, 0:512]),
                    r=[cx.wck[0]], w=[yTk[c]], dma=True)



def build_layer2(cx, xT, xTk, xout, xoutk, token_major=False):
    p = cx.p
    eps = cx.c["eps"]

    mix, mixk = win(cx, 2, "mix", (128, 24), lambda i: np.concatenate([
        col_tile(np.asarray(i["L2_attn_mix_k"]).reshape(-1)),
        col_tile(np.asarray(i["L2_attn_mix_v"]).reshape(-1)),
        col_tile(np.asarray(i["L2_attn_mix_r"]).reshape(-1))], axis=1))
    wk_, wkk = win(cx, 2, "wk", (128, 8192), lambda i: tile_lhs(np.asarray(i["L2_attn_Wk"]).T))
    wv, wvk = win(cx, 2, "wv", (128, 8192), lambda i: tile_lhs(np.asarray(i["L2_attn_Wv"]).T))
    wr, wrk = win(cx, 2, "wr", (128, 8192), lambda i: tile_lhs(np.asarray(i["L2_attn_Wr"]).T))
    wo, wok = win(cx, 2, "wo", (128, 8192), lambda i: tile_lhs(np.asarray(i["L2_attn_Wo"]).T))
    tdf, tdfk = win(cx, 2, "tdf", (1, 32), lambda i: np.concatenate([
        np.asarray(i["L2_attn_time_decay"]), np.asarray(i["L2_attn_time_first"])]).reshape(1, 32))
    lncols, lncolsk = win(cx, 2, "lncols", (128, 32), lambda i: np.concatenate([
        col_tile(i["L2_ffn_ln1_w"]), col_tile(i["L2_ffn_ln1_b"]),
        col_tile(i["L2_ffn_ln2_w"]), col_tile(i["L2_ffn_ln2_b"])], axis=1))
    wqkv, wqkvk = win(cx, 2, "wqkv", (3, 128, 8192),
                      lambda i: np.stack([tile_lhs(np.asarray(i["L2_ffn_w_qkv"])[:, part * 1024:(part + 1) * 1024])
                                          for part in range(3)]))
    wproj, wprojk = win(cx, 2, "wproj", (128, 8192), lambda i: tile_lhs(np.asarray(i["L2_ffn_w_proj"])))
    gw, gwk = win(cx, 2, "gw", (128, 8, 8), lambda i: tile_rhs(np.asarray(i["L2_ffn_gate_w"])))
    gb, gbk = win(cx, 2, "gb", (1, 8), lambda i: np.asarray(i["L2_ffn_gate_b"]).reshape(1, 8))
    w1bf, w1k = precast(cx, 2, "w1", (8, 4, 128, 8, 8, 128), lambda i: np.ascontiguousarray(
        np.asarray(i["L2_ffn_w1"]).reshape(8, 8, 128, 4, 8, 128).transpose(0, 3, 2, 1, 4, 5)), n_parts=8)
    w2bf, w2k = precast(cx, 2, "w2", (8, 2, 4, 128, 8, 4, 128), lambda i: np.ascontiguousarray(
        np.asarray(i["L2_ffn_w2"]).reshape(8, 4, 8, 128, 2, 4, 128).transpose(0, 4, 1, 3, 2, 5, 6)), n_parts=8)

    xn32, xn32k = dram(cx, "l2_xn32", (CT, 128, T), n_trk=NTT)
    xnbf, xnbfk = dram(cx, "l2_xnbf", (CT, 128, T), BF16, n_trk=NTT)
    xkin, xkink = dram(cx, "l2_xkin", (CT, 128, T), BF16, n_trk=1)
    xvin, xvink = dram(cx, "l2_xvin", (CT, 128, T), BF16, n_trk=1)
    xrin, xrink = dram(cx, "l2_xrin", (CT, 128, T), BF16, n_trk=1)
    kT32, kT32k = dram(cx, "l2_kT32", (CT, 128, T), n_trk=CT)
    vT32, vT32k = dram(cx, "l2_vT32", (CT, 128, T), n_trk=CT)
    rT32, rT32k = dram(cx, "l2_rT32", (CT, 128, T), n_trk=CT)
    rwkvbf, rwkvbfk = dram(cx, "l2_rwkvbf", (CT, 128, T), BF16, n_trk=CT)
    sT, sTk = dram(cx, "l2_sT", (CT, 128, T), n_trk=CT)
    x132, x132k = dram(cx, "l2_x132", (CT, 128, T), n_trk=NTT)
    x1bf, x1bfk = dram(cx, "l2_x1bf", (CT, 128, T), BF16, n_trk=NTT)
    qpad2, qpad2k = dram(cx, "l2_qpad", (CT, 2, 128, T), BF16, n_trk=CT)
    kbf2, kbf2k = dram(cx, "l2_kbf", (CT, 128, T), BF16, n_trk=CT)
    vbf2, vbf2k = dram(cx, "l2_vbf", (CT, 128, T), BF16, n_trk=CT)
    yT2, yT2k = dram(cx, "l2_yT", (CT, 128, T), BF16, n_trk=CT)
    z1, z1k = dram(cx, "l2_z1", (CT, 128, T), n_trk=NTT)
    x232, x232k = dram(cx, "l2_x232", (CT, 128, T), n_trk=NTT)
    x2bf, x2bfk = dram(cx, "l2_x2bf", (CT, 128, T), BF16, n_trk=NTT)
    gR2, gR2k = dram(cx, "l2_gR", (16, T), n_trk=1)
    z2T, z2Tk = dram(cx, "l2_z2T", (CT, 128, T), n_trk=CT)
    hE2, hE2k = dram(cx, "l2_hE", (32, 128, T), BF16, n_trk=8)

    mix_sb = p.sbuf("l2_mix_sb", [128, 24])
    mixsk = Trk()
    p.op("sy", lambda e: e.dma_start(out=mix_sb[:], in_=mix), r=[mixk], w=[mixsk], dma=True)
    ln_sb = p.sbuf("l2_ln_sb", [128, 32])
    lnsk = Trk()
    p.op("sy", lambda e: e.dma_start(out=ln_sb[:], in_=lncols), r=[lncolsk], w=[lnsk], dma=True)
    gw_sb = p.sbuf("l2_gw_sb", [128, 8, 8], F32)
    gwsk = Trk()
    p.op("sy", lambda e: e.dma_start(out=gw_sb[:], in_=gw), r=[gwk], w=[gwsk], dma=True)
    gb_sb = p.sbuf("l2_gb_sb", [1, 8])
    gbsk = Trk()
    p.op("sy", lambda e: e.dma_start(out=gb_sb[:], in_=gb), r=[gbk], w=[gbsk], dma=True)
    tdf_sb = p.sbuf("l2_tdf_sb", [1, 32])
    tdfsk = Trk()
    p.op("sy", lambda e: e.dma_start(out=tdf_sb[:], in_=tdf), r=[tdfk], w=[tdfsk], dma=True)

    ln_pass(cx, xT, xTk, xn32, xn32k, xnbf, xnbfk)
    mix_pass(cx, xn32, xn32k, mix_sb[:, 0:8], mixsk, xkin, xkink)
    mix_pass(cx, xn32, xn32k, mix_sb[:, 8:16], mixsk, xvin, xvink)
    mix_pass(cx, xn32, xn32k, mix_sb[:, 16:24], mixsk, xrin, xrink)
    wav = cx.wa[:].rearrange("p (k c f) -> p k c f", k=8, f=128)
    wbv = cx.wb[:].rearrange("p (k c f) -> p k c f", k=8, f=128)

    def f32_epil(dst, dstk, func):
        def epil(c, tt, b):
            freg = 4 + (c + CT * tt) % 4
            p.op("ac", lambda e, b=b, freg=freg: e.activation(
                cx.f32a[:, freg * 512:freg * 512 + 512], cx.pb[b][:], func),
                r=[cx.pbk[b]], w=[cx.fa_t[freg]])
            p.op("po" if freg % 2 == 0 else "ac", lambda e, c=c, tt=tt, freg=freg: e.dma_start(
                out=dst[c, :, tt * 512:tt * 512 + 512],
                in_=cx.f32a[:, freg * 512:freg * 512 + 512]),
                r=[cx.fa_t[freg]], w=[dstk[c % len(dstk)]], dma=True)
        return epil

    load_w(cx, cx.wa[:], wk_, wkk, cx.wak)
    mm_cm(cx, wav, cx.wak, xkin, [xkink[0]], f32_epil(kT32, kT32k, AF.Copy))
    load_w(cx, cx.wb[:], wv, wvk, cx.wbk)
    mm_cm(cx, wbv, cx.wbk, xvin, [xvink[0]], f32_epil(vT32, vT32k, AF.Copy), banks=(2, 3))
    load_w(cx, cx.wa[:], wr, wrk, cx.wak)
    mm_cm(cx, wav, cx.wak, xrin, [xrink[0]], f32_epil(rT32, rT32k, AF.Sigmoid))

    big = cx.big
    Xtok = big[:, 0:8320].bitcast(F32).rearrange("p (a b) -> p a b", b=1040)
    Xtokk = regt(cx.big_t, 0, 8320)
    rwkv_sb = big[:, 8704:12800].rearrange("p (a b) -> p a b", b=512)
    rwkk = regt(cx.big_t, 8704, 12800)
    Dv = big[:, 12800:16896].bitcast(F32).rearrange("p (h f) -> p h f", f=128)
    Dvk = regt(cx.big_t, 12800, 16896)
    srow = big[0:1, 16896:18976].bitcast(F32)
    srowk = regt(cx.big_t, 16896, 18976)
    crow = big[0:1, 18976:21056].bitcast(F32)
    crowk = regt(cx.big_t, 18976, 21056)
    ewCrow = big[0:1, 21056:23136].bitcast(F32)
    ewCk = regt(cx.big_t, 21056, 23136)
    lnew_bc = cx.sm[:, 128:144]
    lnewk = cx.sm_t[2]
    euh_bc = cx.sm[:, 192:208]
    euhk = cx.sm_t[3]
    ewvC = cx.sm[:, 256:272]
    ewvCk = cx.sm_t[4]
    r1, r1k = cx.r1, cx.r1k

    p.op("ac", lambda e: e.activation(r1[0][:, 0:16], tdf_sb[:, 0:16], AF.Exp),
         r=[tdfsk], w=[r1k[0]])
    p.op("dv", lambda e: e.tensor_scalar(r1[0][:, 0:16], r1[0][:, 0:16], -1.0, None, ALU.mult),
         r=[r1k[0]], w=[r1k[0]])
    p.op("ac", lambda e: e.activation(r1[1][:, 0:16], tdf_sb[:, 16:32], AF.Exp),
         r=[tdfsk], w=[r1k[1]])
    p.op("pe", lambda e: e.matmul(cx.pb[5][:, 0:16], cx.c["ones_row"][:], r1[0][:, 0:16],
                                  start=True, stop=True),
         r=[r1k[0], cx.ck["ones_row"]], w=[cx.pbk[5]])
    p.op("ac", lambda e: e.activation(lnew_bc, cx.pb[5][:, 0:16], AF.Copy),
         r=[cx.pbk[5]], w=[lnewk])
    p.op("pe", lambda e: e.matmul(cx.pb[5][:, 0:16], cx.c["ones_row"][:], r1[1][:, 0:16],
                                  start=True, stop=True),
         r=[r1k[1], cx.ck["ones_row"]], w=[cx.pbk[5]])
    p.op("ac", lambda e: e.activation(euh_bc, cx.pb[5][:, 0:16], AF.Copy),
         r=[cx.pbk[5]], w=[euhk])
    for h in range(16):
        p.op("ac", lambda e, h=h: e.activation(
            r1[h // 4][:, (h % 4) * 128:(h % 4) * 128 + 128], cx.c["iota_row"][:],
            AF.Exp, scale=lnew_bc[0:1, h:h + 1]),
            r=[lnewk, cx.ck["iota_row"]], w=[r1k[h // 4]])
        p.op("ac", lambda e, h=h: e.activation(
            ewvC[:, h:h + 1], cx.c["r127mj_col"][:], AF.Exp, scale=lnew_bc[:, h:h + 1]),
            r=[lnewk, cx.ck["r127mj_col"]], w=[ewvCk])
    p.op("ac", lambda e: e.activation(cx.sm[0:1, 320:336], lnew_bc[0:1, 0:16], AF.Exp,
                                      scale=eps[0:1, 4:5]),
         r=[lnewk, cx.ck["eps"]], w=[cx.sm_t[5]])
    p.op("dv", lambda e: e.tensor_copy(
        ewCrow.rearrange("p (h x) -> p h x", x=65),
        cx.sm[0:1, 320:336].rearrange("p h -> p h 1" if False else "p (h o) -> p h o", o=1).broadcast_to((1, 16, 65))),
        r=[cx.sm_t[5]], w=ewCk)
    for h in range(16):
        p.op("dv", lambda e, h=h: e.tensor_scalar(
            cx.f32a[:, 0:128], cx.c["pmatT"][:], lnew_bc[:, h:h + 1], None, ALU.mult),
            r=[lnewk, cx.ck["pmatT"]], w=[cx.fa_t[0]])
        p.op("dv", lambda e: e.tensor_scalar(cx.f32a[:, 0:128], cx.f32a[:, 0:128],
                                             80.0, None, ALU.min),
             r=[cx.fa_t[0]], w=[cx.fa_t[0]])
        p.op("ac", lambda e: e.activation(cx.f32a[:, 128:256], cx.f32a[:, 0:128], AF.Exp),
             r=[cx.fa_t[0]], w=[cx.fa_t[0]])
        p.op("dv", lambda e, h=h: e.tensor_tensor(Dv[:, h, :], cx.f32a[:, 128:256],
                                                  cx.c["strictT"][:], ALU.mult),
             r=[cx.fa_t[0], cx.ck["strictT"]], w=Dvk)
    p.op("dv", lambda e: e.memset(srow[:], 0.0), w=srowk)

    for g in range(NTT):
        t0 = g * 512
        ek = cx.f32a[:].rearrange("p (c f) -> p c f", f=512)
        vv = cx.f32b[:].rearrange("p (c f) -> p c f", f=512)
        Xc = cx.acc[:, 0:4096].rearrange("p (c f) -> p c f", f=512)
        wkvt = cx.acc[:, 4096:8192].rearrange("p (a b) -> p a b", b=1024)
        p.op("sy", lambda e, t0=t0, ek=ek: e.dma_start(
            out=ek[:, :, :], in_=kT32[:, :, t0:t0 + 512].rearrange("c p t -> p c t")),
            r=list(kT32k), w=cx.fa_t, dma=True)
        p.op("sy", lambda e, t0=t0, vv=vv: e.dma_start(
            out=vv[:, :, :], in_=vT32[:, :, t0:t0 + 512].rearrange("c p t -> p c t")),
            r=list(vT32k), w=cx.fb_t, dma=True)
        for c in range(CT):
            p.op("ac", lambda e, c=c, ek=ek: e.activation(ek[:, c, :], ek[:, c, :], AF.Exp),
                 r=[cx.fa_t[c]], w=[cx.fa_t[c]])
            p.op("dv", lambda e, c=c, ek=ek, vv=vv, Xc=Xc: e.tensor_tensor(
                Xc[:, c, :], ek[:, c, :], vv[:, c, :], ALU.mult),
                r=[cx.fa_t[c], cx.fb_t[c]], w=[cx.acc_t[c]])
        for blk in range(4):
            for c in range(CT):
                p.op("pe", lambda e, blk=blk, c=c, ek=ek: e.matmul(
                    cx.pb[1][:, blk * 16 + 2 * c:blk * 16 + 2 * c + 2],
                    ek[:, c, blk * 128:blk * 128 + 128], cx.c["bo2"][:],
                    start=True, stop=True),
                    r=[cx.fa_t[c], cx.ck["bo2"]], w=[cx.pbk[1]])
        for blk in range(4):
            for c in range(CT):
                bank = 4 + c % 2
                p.op("pe", lambda e, blk=blk, c=c, bank=bank, Xc=Xc: e.transpose(
                    cx.pb[bank][:, 0:128], Xc[:, c, blk * 128:blk * 128 + 128],
                    cx.c["ident"][:]),
                    r=[cx.acc_t[c], cx.ck["ident"]], w=[cx.pbk[bank]])
                for j in range(2):
                    p.op("dv", lambda e, blk=blk, c=c, j=j, bank=bank: e.tensor_copy(
                        Xtok[:, blk, (2 * c + j) * 65:(2 * c + j) * 65 + 64],
                        cx.pb[bank][:, j * 64:j * 64 + 64]),
                        r=[cx.pbk[bank]], w=Xtokk)
            p.op("ac", lambda e, blk=blk: e.activation(
                Xtok[:, blk, :].rearrange("p (h x) -> p h x", x=65)[:, :, 64],
                cx.pb[1][:, blk * 16:blk * 16 + 16], AF.Copy),
                r=[cx.pbk[1]], w=Xtokk)
        for blk in range(4):
            for h in range(16):
                nb = 2 + h // 8
                ncol = (h % 8) * 64
                p.op("pe", lambda e, blk=blk, h=h, nb=nb, ncol=ncol: e.matmul(
                    cx.pb[nb][:, ncol:ncol + 64], Dv[:, h, :],
                    Xtok[:, blk, h * 65:h * 65 + 64], start=True, stop=False),
                    r=Dvk + Xtokk, w=[cx.pbk[nb]])
                p.op("pe", lambda e, blk=blk, h=h, nb=nb, ncol=ncol: e.matmul(
                    cx.pb[nb][:, ncol:ncol + 64],
                    r1[h // 4][:, (h % 4) * 128:(h % 4) * 128 + 128],
                    srow[:, h * 65:h * 65 + 64], start=False, stop=True),
                    r=[r1k[h // 4]] + srowk, w=[cx.pbk[nb]])
                p.op("pe", lambda e, blk=blk, h=h: e.matmul(
                    cx.pb[0][:, h:h + 1], Dv[:, h, :],
                    Xtok[:, blk, h * 65 + 64:h * 65 + 65], start=True, stop=False),
                    r=Dvk + Xtokk, w=[cx.pbk[0]])
                p.op("pe", lambda e, blk=blk, h=h: e.matmul(
                    cx.pb[0][:, h:h + 1],
                    r1[h // 4][:, (h % 4) * 128:(h % 4) * 128 + 128],
                    srow[:, h * 65 + 64:h * 65 + 65], start=False, stop=True),
                    r=[r1k[h // 4]] + srowk, w=[cx.pbk[0]])
            dn = cx.sm[:, 384:400]
            dnk = cx.sm_t[6]
            p.op("dv", lambda e, blk=blk: e.tensor_tensor(
                dn, Xtok[:, blk, :].rearrange("p (h x) -> p h x", x=65)[:, :, 64],
                euh_bc, ALU.mult), r=Xtokk + [euhk], w=[dnk])
            p.op("dv", lambda e: e.tensor_scalar(dn, dn, 1e-6, None, ALU.max),
                 r=[dnk], w=[dnk])
            p.op("dv", lambda e: e.tensor_tensor(dn, dn, cx.pb[0][:, 0:16], ALU.add),
                 r=[dnk, cx.pbk[0]], w=[dnk])
            p.op("dv", lambda e: e.reciprocal(cx.sm[:, 448:464], dn),
                 r=[dnk], w=[cx.sm_t[7]])
            for bi in range(2):
                p.op("dv", lambda e, blk=blk, bi=bi: e.tensor_tensor(
                    wkvt[:, blk, bi * 512:bi * 512 + 512].rearrange("p (h x) -> p h x", x=64),
                    Xtok[:, blk, bi * 520:bi * 520 + 520].rearrange("p (h x) -> p h x", x=65)[:, :, 0:64],
                    euh_bc[:, bi * 8:bi * 8 + 8].unsqueeze(2).broadcast_to((128, 8, 64)),
                    ALU.mult), r=Xtokk + [euhk], w=[cx.acc_t[8 + blk * 2 + bi]])
                p.op("dv", lambda e, blk=blk, bi=bi: e.tensor_tensor(
                    wkvt[:, blk, bi * 512:bi * 512 + 512],
                    wkvt[:, blk, bi * 512:bi * 512 + 512], cx.pb[2 + bi][:], ALU.add),
                    r=[cx.pbk[2 + bi], cx.acc_t[8 + blk * 2 + bi]],
                    w=[cx.acc_t[8 + blk * 2 + bi]])
                p.op("dv", lambda e, blk=blk, bi=bi: e.tensor_tensor(
                    wkvt[:, blk, bi * 512:bi * 512 + 512].rearrange("p (h x) -> p h x", x=64),
                    wkvt[:, blk, bi * 512:bi * 512 + 512].rearrange("p (h x) -> p h x", x=64),
                    cx.sm[:, 448 + bi * 8:448 + bi * 8 + 8].unsqueeze(2).broadcast_to((128, 8, 64)),
                    ALU.mult), r=[cx.sm_t[7], cx.acc_t[8 + blk * 2 + bi]],
                    w=[cx.acc_t[8 + blk * 2 + bi]])
            for h in range(16):
                p.op("pe", lambda e, blk=blk, h=h: e.matmul(
                    cx.pb[0][0:1, h * 32:h * 32 + 32] if False else cx.pb[0][0:1, 448 + h:448 + h + 1],
                    ewvC[:, h:h + 1], Xtok[:, blk, h * 65 + 64:h * 65 + 65],
                    start=True, stop=True),
                    r=[ewvCk] + Xtokk, w=[cx.pbk[0]])
                cb = 1 + h // 8
                p.op("pe", lambda e, blk=blk, h=h, cb=cb: e.matmul(
                    cx.pb[cb][0:1, (h % 8) * 64:(h % 8) * 64 + 64],
                    ewvC[:, h:h + 1], Xtok[:, blk, h * 65:h * 65 + 64],
                    start=True, stop=True),
                    r=[ewvCk] + Xtokk, w=[cx.pbk[cb]])
            for cb in (1, 2):
                p.op("ac", lambda e, cb=cb: e.activation(
                    crow.rearrange("p (h x) -> p h x", x=65)[:, (cb - 1) * 8:(cb - 1) * 8 + 8, 0:64],
                    cx.pb[cb][0:1, :].rearrange("p (h x) -> p h x", x=64),
                    AF.Copy), r=[cx.pbk[cb]], w=crowk)
            p.op("ac", lambda e: e.activation(
                crow.rearrange("p (h x) -> p h x", x=65)[:, :, 64],
                cx.pb[0][0:1, 448:464], AF.Copy), r=[cx.pbk[0]], w=crowk)
            p.op("dv", lambda e: e.tensor_tensor(srow[:], srow[:], ewCrow[:], ALU.mult),
                 r=ewCk + srowk, w=srowk)
            p.op("dv", lambda e: e.tensor_tensor(srow[:], srow[:], crow[:], ALU.add),
                 r=crowk + srowk, w=srowk)
        p.op("sy", lambda e, t0=t0, vv=vv: e.dma_start(
            out=vv[:, :, :], in_=rT32[:, :, t0:t0 + 512].rearrange("c p t -> p c t")),
            r=list(rT32k), w=cx.fb_t, dma=True)
        for blk in range(4):
            for c in range(CT):
                bank = 4 + c % 2
                p.op("pe", lambda e, blk=blk, c=c, bank=bank, wkvt=wkvt: e.transpose(
                    cx.pb[bank][:, 0:128], wkvt[:, blk, c * 128:c * 128 + 128],
                    cx.c["ident"][:]),
                    r=[cx.acc_t[8 + blk * 2 + c // 4], cx.ck["ident"]], w=[cx.pbk[bank]])
                p.op("dv", lambda e, blk=blk, c=c, bank=bank, vv=vv: e.tensor_tensor(
                    rwkv_sb[:, c, blk * 128:blk * 128 + 128], cx.pb[bank][:, 0:128],
                    vv[:, c, blk * 128:blk * 128 + 128], ALU.mult),
                    r=[cx.pbk[bank], cx.fb_t[c]], w=[rwkk[c]])
        for c in range(CT):
            p.op("po", lambda e, c=c, t0=t0: e.dma_start(
                out=rwkvbf[c, :, t0:t0 + 512], in_=rwkv_sb[:, c, :]),
                r=[rwkk[c]], w=[rwkvbfk[c]], dma=True)

    load_w(cx, cx.wb[:], wo, wok, cx.wbk)
    mm_cm(cx, wbv, cx.wbk, rwkvbf, rwkvbfk, f32_epil(sT, sTk, AF.Copy), banks=(2, 3))

    ln_pass(cx, xn32, xn32k, x132, x132k, x1bf, x1bfk,
            gamma_col=ln_sb[:, 0:8], beta_col=ln_sb[:, 8:16], gbk=[lnsk])

    def qpad_epil(c, tt, b):
        wreg = (c + CT * tt) % 4
        p.op("ac", lambda e, b=b, wreg=wreg: e.activation(
            cx.wc[:, wreg * 512:wreg * 512 + 512], cx.pb[b][:], AF.Copy),
            r=[cx.pbk[b]], w=[cx.wck[wreg]])
        t0 = tt * 512
        p.op("po", lambda e, c=c, t0=t0, wreg=wreg: e.dma_start(
            out=qpad2[c, 0, 0:64, t0:t0 + 512], in_=cx.wc[0:64, wreg * 512:wreg * 512 + 512]),
            r=[cx.wck[wreg]], w=[qpad2k[c]], dma=True)
        p.op("po", lambda e, c=c, t0=t0, wreg=wreg: e.dma_start(
            out=qpad2[c, 1, 64:128, t0:t0 + 512], in_=cx.wc[64:128, wreg * 512:wreg * 512 + 512]),
            r=[cx.wck[wreg]], w=[qpad2k[c]], dma=True)
        p.op("po", lambda e, c=c, t0=t0: e.dma_start(
            out=qpad2[c, 0, 64:128, t0:t0 + 512], in_=cx.zero_bf[64:128, :]),
            r=[cx.ck_zero], w=[qpad2k[c]], dma=True)
        p.op("po", lambda e, c=c, t0=t0: e.dma_start(
            out=qpad2[c, 1, 0:64, t0:t0 + 512], in_=cx.zero_bf[0:64, :]),
            r=[cx.ck_zero], w=[qpad2k[c]], dma=True)

    def bf_epil(dst, dstk):
        def epil(c, tt, b):
            wreg = (c + CT * tt) % 4
            p.op("ac", lambda e, b=b, wreg=wreg: e.activation(
                cx.wc[:, wreg * 512:wreg * 512 + 512], cx.pb[b][:], AF.Copy),
                r=[cx.pbk[b]], w=[cx.wck[wreg]])
            p.op("po" if wreg % 2 == 0 else "ac", lambda e, c=c, tt=tt, wreg=wreg: e.dma_start(
                out=dst[c, :, tt * 512:tt * 512 + 512],
                in_=cx.wc[:, wreg * 512:wreg * 512 + 512]),
                r=[cx.wck[wreg]], w=[dstk[c]], dma=True)
        return epil

    for part, epil in ((0, qpad_epil), (1, bf_epil(kbf2, kbf2k)), (2, bf_epil(vbf2, vbf2k))):
        p.op("po", lambda e, part=part: e.dma_start(
            out=cx.wa[:], in_=wqkv[part]), r=[wqkvk], w=[cx.wak], dma=True)
        mm_cm(cx, wav, cx.wak, x1bf, x1bfk, epil,
              banks=((0, 1) if part % 2 == 0 else (2, 3)))

    negU = cx.sm[:, 512:513]
    negUk = cx.sm_t[8]
    u_bound_pass(cx, qpad2, qpad2k, kbf2, kbf2k, negU, negUk)
    msa_pass(cx, qpad2, qpad2k, kbf2, kbf2k, vbf2, vbf2k, negU, negUk, yT2, yT2k)

    load_w(cx, cx.wa[:], wproj, wprojk, cx.wak)

    def z1_epil(c, tt, b):
        freg = 4 + (c + CT * tt) % 4
        t0 = tt * 512
        p.op("sy", lambda e, c=c, t0=t0, freg=freg: e.dma_start(
            out=cx.f32b[:, freg * 512:freg * 512 + 512], in_=xn32[c, :, t0:t0 + 512]),
            r=[xn32k[tt]], w=[cx.fb_t[freg]], dma=True)
        p.op("dv", lambda e, b=b, freg=freg: e.tensor_tensor(
            cx.f32a[:, freg * 512:freg * 512 + 512], cx.pb[b][:],
            cx.f32b[:, freg * 512:freg * 512 + 512], ALU.add),
            r=[cx.pbk[b], cx.fb_t[freg]], w=[cx.fa_t[freg]])
        p.op("po", lambda e, c=c, t0=t0, freg=freg: e.dma_start(
            out=z1[c, :, t0:t0 + 512], in_=cx.f32a[:, freg * 512:freg * 512 + 512]),
            r=[cx.fa_t[freg]], w=[z1k[tt]], dma=True)
    mm_cm(cx, wav, cx.wak, yT2, yT2k, z1_epil)

    ln_pass(cx, z1, z1k, x232, x232k, x2bf, x2bfk,
            gamma_col=ln_sb[:, 16:24], beta_col=ln_sb[:, 24:32], gbk=[lnsk])

    for ch in range(NTT):
        t0 = ch * 512
        xnc = cx.f32a[:].rearrange("p (c f) -> p c f", f=512)
        p.op("sy", lambda e, t0=t0, xnc=xnc: e.dma_start(
            out=xnc[:, :, :], in_=x232[:, :, t0:t0 + 512].rearrange("c p t -> p c t")),
            r=[x232k[ch]], w=cx.fa_t, dma=True)
        for blk in range(4):
            bo = blk * 128

            def rmm(e, bo=bo, xnc=xnc):
                last = None
                for k in range(CT):
                    last = e.matmul(cx.pb[0][:, 0:8], xnc[:, k, bo:bo + 128],
                                    gw_sb[:, k, :], start=(k == 0), stop=False)
                last = e.matmul(cx.pb[0][:, 0:8], cx.c["ones_row"][:], gb_sb[:],
                                start=False, stop=True)
                return last
            p.op("pe", rmm, r=cx.fa_t + [gwsk, gbsk, cx.ck["ones_row"]], w=[cx.pbk[0]])
            lg = cx.sm[:, 896:912]
            lgk = cx.sm_t[14]
            mx = cx.sm[:, 960:961]
            mxk = cx.sm_t[15]
            p.op("ac", lambda e: e.activation(lg[:, 0:8], cx.pb[0][:, 0:8], AF.Copy),
                 r=[cx.pbk[0]], w=[lgk])
            p.op("dv", lambda e: e.reduce_max(mx, lg[:, 0:8], axis=AX.X),
                 r=[lgk], w=[mxk])
            p.op("dv", lambda e: e.tensor_scalar(lg[:, 0:8], lg[:, 0:8], mx, None, ALU.is_ge),
                 r=[lgk, mxk], w=[lgk])
            p.op("dv", lambda e: e.memset(lg[:, 8:16], 0.0), r=[lgk], w=[lgk])
            p.op("pe", lambda e: e.transpose(cx.pb[1][0:16, 0:128], lg[:, 0:16],
                                             cx.c["ident"][:]),
                 r=[lgk, cx.ck["ident"]], w=[cx.pbk[1]])
            p.op("ac", lambda e: e.activation(cx.f32b[0:16, 3584:3712], cx.pb[1][0:16, 0:128], AF.Copy),
                 r=[cx.pbk[1]], w=[cx.fb_t[7]])
            p.op("po", lambda e, t0=t0, bo=bo: e.dma_start(
                out=gR2[:, t0 + bo:t0 + bo + 128], in_=cx.f32b[0:16, 3584:3712]),
                r=[cx.fb_t[7]], w=[gR2k[0]], dma=True)

    for c in range(CT):
        p.op("po", lambda e, c=c: e.dma_start(out=z2T[c], in_=z1[c]),
             r=list(z1k), w=[z2Tk[c]], dma=True)
    moe_pass(cx, x2bf, x2bfk, gR2, gR2k, w1bf, w1k, w2bf, w2k, hE2, hE2k,
             z2T, z2Tk, AF.Gelu_apprx_tanh, 8)

    combine_pass(cx, xT, xTk, sT, sTk, z2T, z2Tk, xout, xoutk, token_major)





def build_layer3(cx, xT, xTk, xout, xoutk, token_major=False):
    p = cx.p
    eps = cx.c["eps"]

    def qkv_perm(which):
        idx = np.arange(16)[:, None] * 192 + which * 64 + np.arange(64)[None, :]
        return idx.reshape(-1)

    wqkv, wqkvk = win(cx, 3, "wqkv", (3, 128, 8192), lambda i: np.stack([
        tile_lhs(np.asarray(i["L3_attn_W_qkv"]).T[:, qkv_perm(w)]) for w in range(3)]))
    bqkv, bqkvk = win(cx, 3, "bqkv", (128, 24), lambda i: np.concatenate([
        col_tile(np.asarray(i["L3_attn_b_qkv"])[qkv_perm(w)]) for w in range(3)], axis=1))
    wout, woutk = win(cx, 3, "wout", (128, 8192), lambda i: tile_lhs(np.asarray(i["L3_attn_W_out"]).T))
    bout, boutk = win(cx, 3, "bout", (128, 8), lambda i: col_tile(i["L3_attn_b_out"]))
    wg3, wg3k = win(cx, 3, "wg3", (128, 8, 16), lambda i: tile_rhs(np.asarray(i["L3_ffn_Wg"]).T))
    w1bf, w1k = precast(cx, 3, "w1", (16, 128, 8192), lambda i: np.stack([
        tile_lhs(np.asarray(i["L3_ffn_W1"])[e].T) for e in range(16)]), n_parts=8)
    w2bf, w2k = precast(cx, 3, "w2", (16, 128, 8192), lambda i: np.stack([
        tile_lhs(np.asarray(i["L3_ffn_W2"])[e].T) for e in range(16)]), n_parts=8)
    w3bf, w3k = precast(cx, 3, "w3", (16, 128, 8192), lambda i: np.stack([
        tile_lhs(np.asarray(i["L3_ffn_W3"])[e].T) for e in range(16)]), n_parts=8)
    b13, b13k = win(cx, 3, "b13", (16, 128, 16), lambda i: np.stack([
        np.concatenate([col_tile(np.asarray(i["L3_ffn_b1"])[e]),
                        col_tile(np.asarray(i["L3_ffn_b3"])[e])], axis=1) for e in range(16)]))
    b2c, b2ck = win(cx, 3, "b2c", (16, 128, 8), lambda i: np.stack([
        col_tile(np.asarray(i["L3_ffn_b2"])[e]) for e in range(16)]))

    xn32, xn32k = dram(cx, "l3_xn32", (CT, 128, T), n_trk=NTT)
    xnbf, xnbfk = dram(cx, "l3_xnbf", (CT, 128, T), BF16, n_trk=NTT)
    qpad3, qpad3k = dram(cx, "l3_qpad", (CT, 2, 128, T), BF16, n_trk=CT)
    kbf3, kbf3k = dram(cx, "l3_kbf", (CT, 128, T), BF16, n_trk=CT)
    vbf3, vbf3k = dram(cx, "l3_vbf", (CT, 128, T), BF16, n_trk=CT)
    yT3, yT3k = dram(cx, "l3_yT", (CT, 128, T), BF16, n_trk=CT)
    sT, sTk = dram(cx, "l3_sT", (CT, 128, T), n_trk=CT)
    gR3, gR3k = dram(cx, "l3_gR", (16, T), n_trk=1)
    hbf3, hbf3k = dram(cx, "l3_hbf", (CT, 128, T), BF16, n_trk=CT)
    g0T, g0Tk = dram(cx, "l3_g0T", (CT, 128, T), n_trk=CT)

    bq_sb = p.sbuf("l3_bq_sb", [128, 24])
    bqsk = Trk()
    p.op("sy", lambda e: e.dma_start(out=bq_sb[:], in_=bqkv), r=[bqkvk], w=[bqsk], dma=True)
    bo_sb = p.sbuf("l3_bo_sb", [128, 8])
    bosk = Trk()
    p.op("sy", lambda e: e.dma_start(out=bo_sb[:], in_=bout), r=[boutk], w=[bosk], dma=True)
    wg_sb = p.sbuf("l3_wg_sb", [128, 8, 16], F32)
    wgsk = Trk()
    p.op("sy", lambda e: e.dma_start(out=wg_sb[:], in_=wg3), r=[wg3k], w=[wgsk], dma=True)
    b13_sb = p.sbuf("l3_b13_sb", [128, 16])
    b13sk = Trk()
    b2_sb = p.sbuf("l3_b2_sb", [128, 8])
    b2sk = Trk()

    ln_pass(cx, xT, xTk, xn32, xn32k, xnbf, xnbfk)

    wav = cx.wa[:].rearrange("p (k c f) -> p k c f", k=8, f=128)
    wbv = cx.wb[:].rearrange("p (k c f) -> p k c f", k=8, f=128)

    def qpad_epil3(c, tt, b):
        wreg = (c + CT * tt) % 4
        p.op("ac", lambda e, b=b, c=c, wreg=wreg: e.activation(
            cx.wc[:, wreg * 512:wreg * 512 + 512], cx.pb[b][:], AF.Identity,
            bias=bq_sb[:, c:c + 1]),
            r=[cx.pbk[b], bqsk], w=[cx.wck[wreg]])
        t0 = tt * 512
        p.op("po", lambda e, c=c, t0=t0, wreg=wreg: e.dma_start(
            out=qpad3[c, 0, 0:64, t0:t0 + 512], in_=cx.wc[0:64, wreg * 512:wreg * 512 + 512]),
            r=[cx.wck[wreg]], w=[qpad3k[c]], dma=True)
        p.op("po", lambda e, c=c, t0=t0, wreg=wreg: e.dma_start(
            out=qpad3[c, 1, 64:128, t0:t0 + 512], in_=cx.wc[64:128, wreg * 512:wreg * 512 + 512]),
            r=[cx.wck[wreg]], w=[qpad3k[c]], dma=True)
        p.op("po", lambda e, c=c, t0=t0: e.dma_start(
            out=qpad3[c, 0, 64:128, t0:t0 + 512], in_=cx.zero_bf[64:128, :]),
            r=[cx.ck_zero], w=[qpad3k[c]], dma=True)
        p.op("po", lambda e, c=c, t0=t0: e.dma_start(
            out=qpad3[c, 1, 0:64, t0:t0 + 512], in_=cx.zero_bf[0:64, :]),
            r=[cx.ck_zero], w=[qpad3k[c]], dma=True)

    def bias_bf_epil(dst, dstk, which):
        def epil(c, tt, b):
            wreg = (c + CT * tt) % 4
            p.op("ac", lambda e, b=b, c=c, wreg=wreg: e.activation(
                cx.wc[:, wreg * 512:wreg * 512 + 512], cx.pb[b][:], AF.Identity,
                bias=bq_sb[:, which * 8 + c:which * 8 + c + 1]),
                r=[cx.pbk[b], bqsk], w=[cx.wck[wreg]])
            p.op("po" if wreg % 2 == 0 else "ac", lambda e, c=c, tt=tt, wreg=wreg: e.dma_start(
                out=dst[c, :, tt * 512:tt * 512 + 512],
                in_=cx.wc[:, wreg * 512:wreg * 512 + 512]),
                r=[cx.wck[wreg]], w=[dstk[c]], dma=True)
        return epil

    for part, epil in ((0, qpad_epil3), (1, bias_bf_epil(kbf3, kbf3k, 1)),
                       (2, bias_bf_epil(vbf3, vbf3k, 2))):
        p.op("po", lambda e, part=part: e.dma_start(
            out=cx.wa[:], in_=wqkv[part]), r=[wqkvk], w=[cx.wak], dma=True)
        mm_cm(cx, wav, cx.wak, xnbf, xnbfk, epil,
              banks=((0, 1) if part % 2 == 0 else (2, 3)))

    negU = cx.sm[:, 512:513]
    negUk = cx.sm_t[8]
    u_bound_pass(cx, qpad3, qpad3k, kbf3, kbf3k, negU, negUk)
    msa_pass(cx, qpad3, qpad3k, kbf3, kbf3k, vbf3, vbf3k, negU, negUk, yT3, yT3k)

    load_w(cx, cx.wa[:], wout, woutk, cx.wak)

    def sT_epil(c, tt, b):
        freg = 4 + (c + CT * tt) % 4
        p.op("ac", lambda e, b=b, c=c, freg=freg: e.activation(
            cx.f32a[:, freg * 512:freg * 512 + 512], cx.pb[b][:], AF.Identity,
            bias=bo_sb[:, c:c + 1]),
            r=[cx.pbk[b], bosk], w=[cx.fa_t[freg]])
        p.op("po", lambda e, c=c, tt=tt, freg=freg: e.dma_start(
            out=sT[c, :, tt * 512:tt * 512 + 512],
            in_=cx.f32a[:, freg * 512:freg * 512 + 512]),
            r=[cx.fa_t[freg]], w=[sTk[c % len(sTk)]], dma=True)
    mm_cm(cx, wav, cx.wak, yT3, yT3k, sT_epil)

    for ch in range(NTT):
        t0 = ch * 512
        xnc = cx.f32a[:].rearrange("p (c f) -> p c f", f=512)
        for c in range(CT):
            p.op("sy", lambda e, c=c, t0=t0, xnc=xnc: e.dma_start(
                out=xnc[:, c, :], in_=xn32[c, :, t0:t0 + 512]),
                r=[xn32k[ch]], w=[cx.fa_t[c]], dma=True)
        for blk in range(4):
            bo = blk * 128

            def gmm(e, bo=bo, xnc=xnc):
                last = None
                for k in range(CT):
                    last = e.matmul(cx.pb[0][:, 0:16], xnc[:, k, bo:bo + 128],
                                    wg_sb[:, k, :], start=(k == 0), stop=(k == CT - 1))
                return last
            p.op("pe", gmm, r=cx.fa_t + [wgsk], w=[cx.pbk[0]])
            sg = cx.sm[:, 896:912]
            sgk = cx.sm_t[14]
            tmp = cx.sm[:, 912:928]
            g4 = cx.sm[:, 928:944]
            mx = cx.sm[:, 960:961]
            mxk = cx.sm_t[15]
            p.op("ac", lambda e: e.activation(sg, cx.pb[0][:, 0:16], AF.Sigmoid),
                 r=[cx.pbk[0]], w=[sgk])
            p.op("dv", lambda e: e.tensor_reduce(
                g4[:, 0:4], sg.rearrange("p (g x) -> p g x", x=4), AX.X, ALU.max),
                r=[sgk], w=[sgk])
            p.op("dv", lambda e: e.reduce_max(mx, g4[:, 0:4], axis=AX.X),
                 r=[sgk], w=[mxk])
            p.op("dv", lambda e: e.tensor_scalar(tmp[:, 0:4], g4[:, 0:4], mx, None, ALU.is_lt),
                 r=[sgk, mxk], w=[sgk])
            p.op("dv", lambda e: e.tensor_tensor(tmp[:, 4:8], g4[:, 0:4], tmp[:, 0:4], ALU.mult),
                 r=[sgk], w=[sgk])
            p.op("dv", lambda e: e.reduce_max(mx, tmp[:, 4:8], axis=AX.X),
                 r=[sgk], w=[mxk])
            p.op("dv", lambda e: e.tensor_scalar(g4[:, 0:4], g4[:, 0:4], mx, None, ALU.is_ge),
                 r=[sgk, mxk], w=[sgk])
            p.op("dv", lambda e: e.tensor_scalar(g4[:, 4:8], g4[:, 0:4], -1.0, 1e30,
                                                 ALU.add, ALU.mult),
                 r=[sgk], w=[sgk])
            p.op("dv", lambda e: e.tensor_tensor(
                tmp.rearrange("p (g x) -> p g x", x=4),
                sg.rearrange("p (g x) -> p g x", x=4),
                g4[:, 0:4].unsqueeze(2).broadcast_to((128, 4, 4)), ALU.mult),
                r=[sgk], w=[sgk])
            p.op("dv", lambda e: e.tensor_tensor(
                tmp.rearrange("p (g x) -> p g x", x=4),
                tmp.rearrange("p (g x) -> p g x", x=4),
                g4[:, 4:8].unsqueeze(2).broadcast_to((128, 4, 4)), ALU.add),
                r=[sgk], w=[sgk])
            p.op("dv", lambda e: e.tensor_copy(cx.sm[:, 944:960], tmp),
                 r=[sgk], w=[sgk])
            for it in range(3):
                p.op("dv", lambda e: e.reduce_max(mx, cx.sm[:, 944:960], axis=AX.X),
                     r=[sgk], w=[mxk])
                p.op("dv", lambda e: e.tensor_scalar(cx.sm[:, 880:896], cx.sm[:, 944:960], mx, None, ALU.is_ge),
                     r=[sgk, mxk], w=[cx.sm_t[13]])
                p.op("dv", lambda e: e.scalar_tensor_tensor(
                    cx.sm[:, 944:960], cx.sm[:, 880:896], -2e30, cx.sm[:, 944:960],
                    ALU.mult, ALU.add), r=[sgk, cx.sm_t[13]], w=[sgk])
            p.op("dv", lambda e: e.reduce_max(mx, cx.sm[:, 944:960], axis=AX.X),
                 r=[sgk], w=[mxk])
            p.op("dv", lambda e: e.tensor_scalar(cx.sm[:, 912:928], tmp, mx, None, ALU.is_ge),
                 r=[sgk, mxk], w=[sgk])
            p.op("dv", lambda e: e.tensor_tensor(cx.sm[:, 912:928], cx.sm[:, 912:928], sg, ALU.mult),
                 r=[sgk], w=[sgk])
            p.op("dv", lambda e: e.tensor_reduce(mx, cx.sm[:, 912:928].rearrange("p (o x) -> p o x", o=1),
                                                 AX.X, ALU.add),
                 r=[sgk], w=[mxk])
            p.op("dv", lambda e: e.reciprocal(mx, mx), r=[mxk], w=[mxk])
            p.op("dv", lambda e: e.tensor_scalar(cx.sm[:, 912:928], cx.sm[:, 912:928],
                                                 mx, 2.5, ALU.mult, ALU.mult),
                 r=[sgk, mxk], w=[sgk])
            p.op("pe", lambda e: e.transpose(cx.pb[1][0:16, 0:128], cx.sm[:, 912:928],
                                             cx.c["ident"][:]),
                 r=[sgk, cx.ck["ident"]], w=[cx.pbk[1]])
            p.op("ac", lambda e: e.activation(cx.f32b[0:16, 3584:3712], cx.pb[1][0:16, 0:128], AF.Copy),
                 r=[cx.pbk[1]], w=[cx.fb_t[7]])
            p.op("po", lambda e, t0=t0, bo=bo: e.dma_start(
                out=gR3[:, t0 + bo:t0 + bo + 128], in_=cx.f32b[0:16, 3584:3712]),
                r=[cx.fb_t[7]], w=[gR3k[0]], dma=True)

    gbc = cx.f32b[:, 0:512]
    gbck = cx.fb_t[0]
    grow = cx.r1[0]
    growk = cx.r1k[0]
    for e_ in range(16):
        p.op("sy", lambda e, e_=e_: e.dma_start(out=b13_sb[:], in_=b13[e_]),
             r=[b13k], w=[b13sk], dma=True)
        p.op("sy", lambda e, e_=e_: e.dma_start(out=b2_sb[:], in_=b2c[e_]),
             r=[b2ck], w=[b2sk], dma=True)
        p.op("sy", lambda e, e_=e_: e.dma_start(out=cx.wa[:], in_=w1bf[e_]),
             r=[w1k], w=[cx.wak], dma=True)
        p.op("sy", lambda e, e_=e_: e.dma_start(out=cx.wb[:], in_=w3bf[e_]),
             r=[w3k], w=[cx.wbk], dma=True)
        for tt in range(NTT):
            t0 = tt * 512
            rhs = cx.rhs[tt % 2]
            rk = cx.rhsk[tt % 2]
            p.op("sy", lambda e, t0=t0, rhs=rhs: e.dma_start(
                out=rhs[:, :, :], in_=xnbf[:, :, t0:t0 + 512].rearrange("k p t -> p k t")),
                r=[xnbfk[tt]], w=rk, dma=True)
            for c in range(CT):
                b1b = (tt * CT + c) % 2
                b3b = 2 + (tt * CT + c) % 2

                def m1(e, c=c, rhs=rhs, b1b=b1b):
                    last = None
                    for k in range(CT):
                        last = e.matmul(cx.pb[b1b][:], wav[:, k, c, :], rhs[:, k, :],
                                        start=(k == 0), stop=(k == CT - 1))
                    return last
                p.op("pe", m1, r=rk[:CT] + [cx.wak], w=[cx.pbk[b1b]])

                def m3(e, c=c, rhs=rhs, b3b=b3b):
                    last = None
                    for k in range(CT):
                        last = e.matmul(cx.pb[b3b][:], wbv[:, k, c, :], rhs[:, k, :],
                                        start=(k == 0), stop=(k == CT - 1))
                    return last
                p.op("pe", m3, r=rk[:CT] + [cx.wbk], w=[cx.pbk[b3b]])
                freg = 4 + (tt * CT + c) % 4
                p.op("ac", lambda e, c=c, b1b=b1b, freg=freg: e.activation(
                    cx.f32a[:, freg * 512:freg * 512 + 512], cx.pb[b1b][:], AF.Silu,
                    bias=b13_sb[:, c:c + 1]),
                    r=[cx.pbk[b1b], b13sk], w=[cx.fa_t[freg]])
                wreg = (tt * CT + c) % 4
                p.op("dv", lambda e, c=c, b3b=b3b, freg=freg, wreg=wreg: e.scalar_tensor_tensor(
                    cx.wc[:, wreg * 512:wreg * 512 + 512], cx.pb[b3b][:],
                    b13_sb[:, 8 + c:8 + c + 1], cx.f32a[:, freg * 512:freg * 512 + 512],
                    ALU.add, ALU.mult),
                    r=[cx.pbk[b3b], b13sk, cx.fa_t[freg]], w=[cx.wck[wreg]])
                p.op("po" if wreg % 2 == 0 else "ac", lambda e, c=c, t0=t0, wreg=wreg: e.dma_start(
                    out=hbf3[c, :, t0:t0 + 512], in_=cx.wc[:, wreg * 512:wreg * 512 + 512]),
                    r=[cx.wck[wreg]], w=[hbf3k[c]], dma=True)
        p.op("sy", lambda e, e_=e_: e.dma_start(out=cx.wa[:], in_=w2bf[e_]),
             r=[w2k], w=[cx.wak], dma=True)
        for tt in range(NTT):
            t0 = tt * 512
            p.op("sy", lambda e, e_=e_, t0=t0: e.dma_start(
                out=grow[:], in_=gR3[e_, t0:t0 + 512]), r=[gR3k[0]], w=[growk], dma=True)
            p.op("pe", lambda e: e.matmul(cx.pb[4][:], cx.c["ones_row"][:], grow[:],
                                          start=True, stop=True),
                 r=[growk, cx.ck["ones_row"]], w=[cx.pbk[4]])
            p.op("ac", lambda e: e.activation(gbc, cx.pb[4][:], AF.Copy),
                 r=[cx.pbk[4]], w=[gbck])
            rhs = cx.rhs[tt % 2]
            rk = cx.rhsk[tt % 2]
            p.op("sy", lambda e, t0=t0, rhs=rhs: e.dma_start(
                out=rhs[:, :, :], in_=hbf3[:, :, t0:t0 + 512].rearrange("k p t -> p k t")),
                r=list(hbf3k), w=rk, dma=True)
            for c in range(CT):
                b = (tt * CT + c) % 2

                def m2(e, c=c, rhs=rhs, b=b):
                    last = None
                    for k in range(CT):
                        last = e.matmul(cx.pb[b][:], wav[:, k, c, :], rhs[:, k, :],
                                        start=(k == 0), stop=(k == CT - 1))
                    return last
                p.op("pe", m2, r=rk[:CT] + [cx.wak], w=[cx.pbk[b]])
                freg = 4 + (tt * CT + c) % 4
                p.op("dv", lambda e, b=b, c=c, freg=freg: e.scalar_tensor_tensor(
                    cx.f32a[:, freg * 512:freg * 512 + 512], cx.pb[b][:],
                    b2_sb[:, c:c + 1], gbc, ALU.add, ALU.mult),
                    r=[cx.pbk[b], b2sk, gbck], w=[cx.fa_t[freg]])
                p.op("po", lambda e, c=c, t0=t0, freg=freg, e_=e_: e.dma_start(
                    out=g0T[c, :, t0:t0 + 512],
                    in_=cx.f32a[:, freg * 512:freg * 512 + 512],
                    accum_op=(ALU.add if e_ > 0 else ALU.bypass)),
                    r=[cx.fa_t[freg]], w=[g0Tk[c]], dma=True)

    combine_pass(cx, xT, xTk, sT, sTk, g0T, g0Tk, xout, xoutk, token_major)



LAYER_BUILDERS = {}


def build_nc(layers, final_tm):
    nc = bass.Bass("TRN2", target_bir_lowering=False)
    p = Prog(nc)
    consts = build_consts()
    cx = make_ctx(p, nc, consts)
    cx.host_inputs = {}

    xin = nc.dram_tensor("xin", [CT, 128, T], F32, kind="ExternalInput").ap()
    cur, curk = xin, [Trk()]
    for idx, li in enumerate(layers):
        last = idx == len(layers) - 1
        if last:
            if final_tm:
                nxt = nc.dram_tensor("out", [T, D], F32, kind="ExternalOutput").ap()
            else:
                nxt = nc.dram_tensor("out", [CT, 128, T], F32, kind="ExternalOutput").ap()
            nxtk = [Trk()]
        else:
            nxt, nxtk = dram(cx, f"x_l{li}", (CT, 128, T))
        builder = {0: build_layer0, 1: build_layer1, 2: build_layer2, 3: build_layer3}.get(li) or LAYER_BUILDERS[li]
        builder(cx, cur, curk, nxt, nxtk, token_major=(last and final_tm))
        cur, curk = nxt, nxtk

    p.op("sy", lambda e: e.nop(), r=curk)
    p.emit()
    return nc, cx.host_inputs


def host_prep(inputs, host_fns, xin_list):
    shared = {name: np.ascontiguousarray(fn(inputs), dtype=np.float32)
              for name, fn in host_fns.items()}
    consts = build_consts()
    in_maps = []
    for ci in range(N_CORES):
        m = {"xin": xin_list[ci]}
        m.update(shared)
        for name, arr in consts.items():
            m["c_" + name] = arr
        in_maps.append(m)
    return in_maps


def kernel(**inputs):
    import os
    fused = os.environ.get("KFUSE", "1") == "1"
    x = np.asarray(inputs["L0_attn_hidden_states"], np.float32)
    xin_list = [np.ascontiguousarray(x[ci].T.reshape(CT, 128, T)) for ci in range(N_CORES)]
    if fused:
        nc, host_fns = build_nc([0, 1, 2, 3], final_tm=True)
        in_maps = host_prep(inputs, host_fns, xin_list)
        res = run_bass_kernel_spmd(nc, in_maps, core_ids=list(range(N_CORES)))
        return np.stack([res.results[i]["out"] for i in range(N_CORES)], axis=0)
    cur = xin_list
    for li in range(4):
        nc, host_fns = build_nc([li], final_tm=(li == 3))
        in_maps = host_prep(inputs, host_fns, cur)
        res = run_bass_kernel_spmd(nc, in_maps, core_ids=list(range(N_CORES)))
        cur = [np.asarray(res.results[i]["out"]) for i in range(N_CORES)]
    return np.stack(cur, axis=0)


# version 08
# speedup vs baseline: 1.3008x; 1.2133x over previous
import contextlib
import math
import numpy as np
import ml_dtypes
import concourse.bass as bass
import concourse.mybir as mybir
from concourse.bass_utils import run_bass_kernel_spmd

F32 = mybir.dt.float32
BF16 = mybir.dt.bfloat16
AF = mybir.ActivationFunctionType
ALU = mybir.AluOpType
AX = mybir.AxisListType

T = 4096
D = 1024
CT = D // 128
NTT = T // 512
N_CORES = 8

ENGINES = ("sy", "ac", "dv", "pe", "po")
SAME_ENGINE_SYNC = {"sy", "ac", "dv", "po"}


class Trk:
    __slots__ = ("w", "readers")

    def __init__(self):
        self.w = None
        self.readers = []


def trks(n):
    return [Trk() for _ in range(n)]


def regt(lst, lo, hi, width=512):
    return lst[lo // width:(hi + width - 1) // width]


N_DMA_SEMS = 64


class Prog:

    def __init__(self, nc):
        self.nc = nc
        self.es = contextlib.ExitStack()
        self.ops = {e: [] for e in ENGINES}
        self.gseq = 0

    def sbuf(self, name, shape, dtype=F32):
        return self.es.enter_context(self.nc.sbuf_tensor(name, shape, dtype))

    def psum(self, name, shape, dtype=F32):
        return self.es.enter_context(self.nc.psum_tensor(name, shape, dtype))

    def op(self, eng, fn, r=(), w=(), dma=False):
        i = len(self.ops[eng])
        deps = set()
        for t in r:
            if t.w is not None:
                deps.add(t.w)
            t.readers.append((eng, i))
        for t in w:
            if t.w is not None:
                deps.add(t.w)
            for rd in t.readers:
                deps.add(rd)
            t.w = (eng, i)
            t.readers = []
        deps.discard((eng, i))
        deps = {d for d in deps if d[0] != eng or eng in SAME_ENGINE_SYNC}
        self.ops[eng].append([fn, deps, dma, False, self.gseq])
        self.gseq += 1
        return i

    def emit(self):
        nc = self.nc
        for eng in ENGINES:
            for rec in self.ops[eng]:
                for (de, dj) in rec[1]:
                    self.ops[de][dj][3] = True
        compval = {}
        for eng in ENGINES:
            c = 0
            for j, rec in enumerate(self.ops[eng]):
                if not rec[2] and rec[3]:
                    c += 1
                    compval[(eng, j)] = c
        dmaslot = {}
        pool_of = {}
        npools = 0
        for eng in ENGINES:
            if any(rec[2] for rec in self.ops[eng]):
                pool_of[eng] = npools
                npools += 1
        per_pool = N_DMA_SEMS // max(npools, 1)
        for eng, pi in pool_of.items():
            dma_ops = [(rec[4], j) for j, rec in enumerate(self.ops[eng])
                       if rec[2] and rec[3]]
            dma_ops.sort()
            gens = [0] * per_pool
            for n, (_, j) in enumerate(dma_ops):
                si = pi * per_pool + n % per_pool
                gens[n % per_pool] += 1
                dmaslot[(eng, j)] = (si, gens[n % per_pool])
        csem = {e: self.es.enter_context(nc.semaphore(f"sem_{e}"))
                for e in ENGINES}
        dsem = [self.es.enter_context(nc.semaphore(f"dsem{i}"))
                for i in range(npools * per_pool)]
        junk = {eng: self.es.enter_context(nc.semaphore(f"sem_junk_{eng}"))
                for eng in pool_of}

        final_gen = {}
        for (eng, j), (si, gen) in dmaslot.items():
            final_gen[si] = max(final_gen.get(si, 0), gen)
        junk_cnt = {eng: sum(1 for rec in self.ops[eng] if rec[2] and not rec[3])
                    for eng in pool_of}

        def section(eng_name):
            def body(engobj):
                waited = {}

                def wait(key, v):
                    if v > waited.get(key, 0):
                        sem = dsem[key[1]] if key[0] == "d" else csem[key[1]]
                        engobj.wait_ge(sem, v)
                        waited[key] = v
                for j, rec in enumerate(self.ops[eng_name]):
                    fn, deps, dma, inc, _ = rec
                    wants = {}
                    for (de, dj) in deps:
                        drec = self.ops[de][dj]
                        if drec[2]:
                            si, gen = dmaslot[(de, dj)]
                            key = ("d", si)
                            v = 16 * gen
                        else:
                            key = ("c", de)
                            v = compval[(de, dj)]
                        if v > wants.get(key, 0):
                            wants[key] = v
                    for key in sorted(wants):
                        wait(key, wants[key])
                    if dma and inc:
                        si, gen = dmaslot[(eng_name, j)]
                        if gen > 1:
                            wait(("d", si), 16 * (gen - 1))
                    inst = fn(engobj)
                    if dma:
                        if inc:
                            si, gen = dmaslot[(eng_name, j)]
                            inst.then_inc(dsem[si], 16)
                        else:
                            inst.then_inc(junk[eng_name], 16)
                    elif inc:
                        inst.then_inc(csem[eng_name], 1)
                if eng_name in pool_of:
                    pi = pool_of[eng_name]
                    for si in range(pi * per_pool, (pi + 1) * per_pool):
                        if final_gen.get(si):
                            wait(("d", si), 16 * final_gen[si])
                    if junk_cnt[eng_name]:
                        engobj.wait_ge(junk[eng_name], 16 * junk_cnt[eng_name])
            return body

        with nc.Block() as block:
            block.sync(section("sy"))
            block.scalar(section("ac"))
            block.vector(section("dv"))
            block.tensor(section("pe"))
            block.gpsimd(section("po"))
        return csem



def tile_lhs(w_eff):
    din, dout = w_eff.shape
    kt, ct = din // 128, dout // 128
    return np.ascontiguousarray(
        w_eff.reshape(kt, 128, ct, 128).transpose(1, 0, 2, 3).reshape(128, -1)
    ).astype(np.float32)


def tile_rhs(w_eff):
    din, dout = w_eff.shape
    kt = din // 128
    return np.ascontiguousarray(
        w_eff.reshape(kt, 128, dout).transpose(1, 0, 2)
    ).astype(np.float32)


def col_tile(vec):
    return np.ascontiguousarray(np.asarray(vec).reshape(CT, 128).T).astype(np.float32)


def build_consts():
    c = {}
    c["ones_col"] = np.ones((128, 1), np.float32)
    c["ones_row"] = np.ones((1, 128), np.float32)
    bo2 = np.zeros((128, 2), np.float32)
    bo2[:64, 0] = 1.0
    bo2[64:, 1] = 1.0
    c["bo2"] = bo2
    c["bo2T"] = np.ascontiguousarray(bo2.T)
    c["ident"] = np.eye(128, dtype=np.float32)
    tk = np.arange(128)[:, None]
    tq = np.arange(128)[None, :]
    mt = ((tk // 16 < tq // 16) | ((tk // 16 == tq // 16) & (tk <= tq))).astype(np.float32)
    c["maskT2"] = np.ascontiguousarray(np.concatenate([mt, mt], axis=1))
    eps = np.zeros((128, 8), np.float32)
    eps[:, 0] = 1e-5
    eps[:, 1] = 1e-12
    eps[:, 2] = 1e-6
    eps[:, 3] = 0.125
    eps[:, 4] = 128.0
    c["eps"] = eps
    jj = np.arange(128)[:, None]
    ii = np.arange(128)[None, :]
    c["pmatT"] = (ii - 1.0 - jj).astype(np.float32)
    c["strictT"] = (jj < ii).astype(np.float32)
    c["iota_row"] = np.arange(128, dtype=np.float32).reshape(1, 128)
    c["r127mj_col"] = (127.0 - np.arange(128, dtype=np.float32)).reshape(128, 1)
    return c



class Ctx:
    pass


def make_ctx(p, nc, const_host):
    cx = Ctx()
    cx.p = p
    cx.nc = nc
    cx.pbpair = [p.psum(f"pbp{i}", [128, 1024]) for i in range(3)]
    cx.pb = [cx.pbpair[i // 2][:, (i % 2) * 512:(i % 2) * 512 + 512] for i in range(6)]
    cx.pbk = trks(6)
    cx.pbf = [p.psum("pbfA", [128, 1024], BF16), p.psum("pbfB", [128, 1024], BF16)]
    cx.pbfk = trks(2)
    cx.rhs = [p.sbuf("ws_rhsA", [128, 8, 512], BF16),
              p.sbuf("ws_rhsB", [128, 8, 512], BF16)]
    cx.rhsk = [trks(8), trks(8)]
    cx.wa = p.sbuf("ws_wa", [128, 8192], BF16)
    cx.wak = Trk()
    cx.wb = p.sbuf("ws_wb", [128, 8192], BF16)
    cx.wbk = Trk()
    cx.wc = p.sbuf("ws_wc", [128, 2048], BF16)
    cx.wck = trks(4)
    cx.f32a = p.sbuf("ws_f32a", [128, 4096], F32)
    cx.fa_t = trks(8)
    cx.f32b = p.sbuf("ws_f32b", [128, 4096], F32)
    cx.fb_t = trks(8)
    cx.acc = p.sbuf("ws_acc", [128, 8192], F32)
    cx.acc_t = trks(16)
    cx.big = p.sbuf("ws_big", [128, 25600], BF16)
    cx.big_t = trks(50)
    cx.sm = p.sbuf("ws_sm", [128, 1024], F32)
    cx.sm_t = trks(16)
    cx.r2a = p.sbuf("ws_r2a", [2, 512], F32)
    cx.r2ak = Trk()
    cx.r2b = p.sbuf("ws_r2b", [2, 512], F32)
    cx.r2bk = Trk()
    cx.r1 = [p.sbuf(f"ws_r1{i}", [1, 512], F32) for i in range(4)]
    cx.r1k = trks(4)
    cx.c = {}
    cx.ck = {}
    for name, arr in const_host.items():
        dh = nc.dram_tensor("c_" + name, list(arr.shape), F32, kind="ExternalInput")
        sb = p.sbuf("cs_" + name, list(arr.shape), F32)
        k = Trk()
        ap = dh.ap()
        p.op("sy", lambda e, sb=sb, ap=ap: e.dma_start(out=sb[:], in_=ap),
             w=[k], dma=True)
        cx.c[name] = sb
        cx.ck[name] = k
    cx.c_id_bf = p.sbuf("cs_id_bf", [128, 128], BF16)
    cx.ck_id_bf = Trk()
    p.op("dv", lambda e: e.tensor_copy(cx.c_id_bf[:], cx.c["ident"][:]),
         r=[cx.ck["ident"]], w=[cx.ck_id_bf])
    cx.zero_bf = p.sbuf("cs_zero_bf", [128, 512], BF16)
    cx.ck_zero = Trk()
    p.op("dv", lambda e: e.memset(cx.zero_bf[:], 0.0), w=[cx.ck_zero])
    return cx


def win(cx, layer, name, shape, host_fn):
    nm = f"L{layer}_{name}"
    dh = cx.nc.dram_tensor(nm, list(shape), F32, kind="ExternalInput")
    cx.host_inputs[nm] = host_fn
    return dh.ap(), Trk()


def dram(cx, name, shape, dtype=F32, kind="Internal", n_trk=1):
    h = cx.nc.dram_tensor(name, list(shape), dtype, kind=kind)
    return h.ap(), trks(n_trk)


def load_w(cx, sb_ap, dram_ap, rtrk, wtrk):
    cx.p.op("po", lambda e: e.dma_start(out=sb_ap, in_=dram_ap),
            r=[rtrk], w=[wtrk], dma=True)


def precast(cx, layer, name, shape, host_fn, n_parts=4):
    ap32, k32 = win(cx, layer, name, shape, host_fn)
    flat = int(np.prod(shape))
    h = cx.nc.dram_tensor(f"L{layer}_{name}_bf", list(shape), BF16, kind="Internal")
    bf_ap = h.ap()
    kbf = Trk()
    step = flat // n_parts
    f32_flat = ap32.rearrange(" ".join(f"a{i}" for i in range(len(shape))) + " -> (" +
                              " ".join(f"a{i}" for i in range(len(shape))) + ")")         if False else None
    for i in range(n_parts):
        cx.p.op("po", lambda e, i=i: e.dma_start(
            out=bf_ap.flatten()[i * step:(i + 1) * step],
            in_=ap32.flatten()[i * step:(i + 1) * step]),
            r=[k32], w=[kbf], dma=True)
    return bf_ap, kbf



def ln_pass(cx, src, srck, dst32, dst32k, dstbf, dstbfk, gamma_col=None,
            beta_col=None, gbk=()):
    p = cx.p
    eps = cx.c["eps"]
    for tt in range(NTT):
        t0 = tt * 512
        xa = cx.f32a[:].rearrange("p (c f) -> p c f", f=512)
        sq = cx.f32b[:].rearrange("p (c f) -> p c f", f=512)
        p.op("sy", lambda e, t0=t0, xa=xa: e.dma_start(
            out=xa[:, :, :], in_=src[:, :, t0:t0 + 512].rearrange("c p t -> p c t")),
            r=list(srck), w=cx.fa_t, dma=True)
        for c in range(CT):
            p.op("ac", lambda e, c=c: e.activation(sq[:, c, :], xa[:, c, :], AF.Square),
                 r=[cx.fa_t[c]], w=[cx.fb_t[c]])

        def mmsum(srcv, srck_, bank):
            def f(e, srcv=srcv, bank=bank):
                last = None
                for c in range(CT):
                    last = e.matmul(cx.pb[bank][0:1, :], cx.c["ones_col"][:],
                                    srcv[:, c, :], start=(c == 0), stop=(c == CT - 1))
                return last
            p.op("pe", f, r=list(srck_) + [cx.ck["ones_col"]], w=[cx.pbk[bank]])
        mmsum(xa, cx.fa_t, 0)
        mmsum(sq, cx.fb_t, 1)
        r1, r1k = cx.r1, cx.r1k
        p.op("dv", lambda e: e.tensor_scalar(r1[0][:], cx.pb[0][0:1, :], 1.0 / D, None, ALU.mult),
             r=[cx.pbk[0]], w=[r1k[0]])
        p.op("dv", lambda e: e.tensor_scalar(r1[1][:], cx.pb[1][0:1, :], 1.0 / D, None, ALU.mult),
             r=[cx.pbk[1]], w=[r1k[1]])
        p.op("dv", lambda e: e.tensor_tensor(r1[2][:], r1[0][:], r1[0][:], ALU.mult),
             r=[r1k[0]], w=[r1k[2]])
        p.op("dv", lambda e: e.tensor_tensor(r1[2][:], r1[1][:], r1[2][:], ALU.subtract),
             r=[r1k[1], r1k[2]], w=[r1k[2]])
        p.op("ac", lambda e: e.activation(r1[2][:], r1[2][:], AF.Sqrt, bias=eps[0:1, 0:1]),
             r=[r1k[2], cx.ck["eps"]], w=[r1k[2]])
        p.op("dv", lambda e: e.reciprocal(r1[3][:], r1[2][:]),
             r=[r1k[2]], w=[r1k[3]])
        p.op("dv", lambda e: e.tensor_tensor(r1[1][:], r1[0][:], r1[3][:], ALU.mult),
             r=[r1k[0], r1k[3]], w=[r1k[1]])
        p.op("dv", lambda e: e.tensor_scalar(r1[1][:], r1[1][:], -1.0, None, ALU.mult),
             r=[r1k[1]], w=[r1k[1]])
        p.op("pe", lambda e: e.matmul(cx.pb[2][:], cx.c["ones_row"][:], r1[3][:],
                                      start=True, stop=True),
             r=[r1k[3], cx.ck["ones_row"]], w=[cx.pbk[2]])
        p.op("pe", lambda e: e.matmul(cx.pb[3][:], cx.c["ones_row"][:], r1[1][:],
                                      start=True, stop=True),
             r=[r1k[1], cx.ck["ones_row"]], w=[cx.pbk[3]])
        for c in range(CT):
            p.op("dv", lambda e, c=c: e.tensor_tensor(sq[:, c, :], xa[:, c, :], cx.pb[2][:], ALU.mult),
                 r=[cx.fa_t[c], cx.pbk[2], cx.fb_t[c]], w=[cx.fb_t[c]])
            p.op("dv", lambda e, c=c: e.tensor_tensor(sq[:, c, :], sq[:, c, :], cx.pb[3][:], ALU.add),
                 r=[cx.fb_t[c], cx.pbk[3]], w=[cx.fb_t[c]])
            if gamma_col is not None:
                p.op("dv", lambda e, c=c: e.tensor_scalar(
                    sq[:, c, :], sq[:, c, :], gamma_col[:, c:c + 1], beta_col[:, c:c + 1],
                    ALU.mult, ALU.add), r=[cx.fb_t[c]] + list(gbk), w=[cx.fb_t[c]])
        p.op("sy", lambda e, t0=t0, sq=sq: e.dma_start(
            out=dst32[:, :, t0:t0 + 512].rearrange("c p t -> p c t"), in_=sq[:, :, :]),
            r=cx.fb_t, w=[dst32k[tt]], dma=True)
        p.op("po", lambda e, t0=t0, sq=sq: e.dma_start(
            out=dstbf[:, :, t0:t0 + 512].rearrange("c p t -> p c t"), in_=sq[:, :, :]),
            r=cx.fb_t, w=[dstbfk[tt]], dma=True)


def mm_cm(cx, w_view, wk, src, srck_tt, epil, n_out=CT, kt=CT, banks=(0, 1),
          src_by_c=False):
    p = cx.p
    nb = len(banks)
    for tt in range(NTT):
        t0 = tt * 512
        rhs = cx.rhs[tt % 2]
        rk = cx.rhsk[tt % 2]
        sdep = list(srck_tt) if src_by_c else [srck_tt[min(tt, len(srck_tt) - 1)]]
        p.op("sy", lambda e, t0=t0, rhs=rhs: e.dma_start(
            out=rhs[:, 0:kt, :], in_=src[0:kt, :, t0:t0 + 512].rearrange("k p t -> p k t")),
            r=sdep, w=rk[:kt], dma=True)
        for c in range(n_out):
            b = banks[(tt * n_out + c) % nb]

            def f(e, c=c, rhs=rhs, b=b):
                last = None
                for k in range(kt):
                    last = e.matmul(cx.pb[b][:], w_view[:, k, c, :], rhs[:, k, :],
                                    start=(k == 0), stop=(k == kt - 1))
                return last
            p.op("pe", f, r=rk[:kt] + [wk], w=[cx.pbk[b]])
            epil(c, tt, b)


def conv_silu_pass(cx, src, srck_c, conv_sb, convk, dstbf, dstbfk_c, l2norm,
                   pad_heads=False):
    p = cx.p
    eps = cx.c["eps"]
    for c in range(CT):
        p.op("sy", lambda e, c=c: e.dma_start(out=cx.f32a[:], in_=src[c]),
             r=[srck_c[c]], w=cx.fa_t, dma=True)
        p.op("dv", lambda e, c=c: e.tensor_scalar(cx.sm[:, 960:961], conv_sb[:, c, 3:4],
                                                  1.0, None, ALU.add),
             r=[convk], w=[cx.sm_t[15]])
        p.op("dv", lambda e: e.tensor_scalar(cx.acc[:, 0:4096], cx.f32a[:], cx.sm[:, 960:961],
                                             None, ALU.mult),
             r=cx.fa_t + [cx.sm_t[15]], w=cx.acc_t[0:8])
        for j, sh in ((2, 1), (1, 2), (0, 3)):
            p.op("dv", lambda e, j=j, sh=sh, c=c: e.scalar_tensor_tensor(
                cx.acc[:, sh:4096], cx.f32a[:, 0:4096 - sh], conv_sb[:, c, j:j + 1],
                cx.acc[:, sh:4096], ALU.mult, ALU.add),
                r=cx.fa_t + [convk] + cx.acc_t[0:8], w=cx.acc_t[0:8])
        p.op("ac", lambda e: e.activation(cx.f32b[:], cx.acc[:, 0:4096], AF.Silu),
             r=cx.acc_t[0:8], w=cx.fb_t)
        if not l2norm:
            p.op("po", lambda e, c=c: e.dma_start(out=dstbf[c], in_=cx.f32b[:]),
                 r=cx.fb_t, w=[dstbfk_c[c]], dma=True)
            continue
        p.op("ac", lambda e: e.activation(cx.acc[:, 0:4096], cx.f32b[:], AF.Square),
             r=cx.fb_t, w=cx.acc_t[0:8])
        for ts_ in range(NTT):
            o = ts_ * 512
            p.op("pe", lambda e, o=o: e.matmul(cx.pb[4][0:2, :], cx.c["bo2"][:],
                                               cx.acc[:, o:o + 512], start=True, stop=True),
                 r=[cx.acc_t[ts_], cx.ck["bo2"]], w=[cx.pbk[4]])
            p.op("ac", lambda e: e.activation(cx.r2a[:], cx.pb[4][0:2, :], AF.Sqrt,
                                              bias=eps[0:2, 1:2]),
                 r=[cx.pbk[4], cx.ck["eps"]], w=[cx.r2ak])
            p.op("dv", lambda e: e.reciprocal(cx.r2b[:], cx.r2a[:]),
                 r=[cx.r2ak], w=[cx.r2bk])
            p.op("pe", lambda e: e.matmul(cx.pb[5][:], cx.c["bo2T"][:], cx.r2b[:],
                                          start=True, stop=True),
                 r=[cx.r2bk, cx.ck["bo2T"]], w=[cx.pbk[5]])
            reg = ts_ % 2
            p.op("dv", lambda e, o=o, reg=reg: e.tensor_tensor(
                cx.wc[:, reg * 512:reg * 512 + 512], cx.f32b[:, o:o + 512],
                cx.pb[5][:], ALU.mult),
                r=[cx.fb_t[ts_], cx.pbk[5]], w=[cx.wck[reg]])
            if not pad_heads:
                p.op("po", lambda e, c=c, o=o, reg=reg: e.dma_start(
                    out=dstbf[c, :, o:o + 512], in_=cx.wc[:, reg * 512:reg * 512 + 512]),
                    r=[cx.wck[reg]], w=[dstbfk_c[c]], dma=True)
            else:
                p.op("po", lambda e, c=c, o=o, reg=reg: e.dma_start(
                    out=dstbf[c, 0, 0:64, o:o + 512],
                    in_=cx.wc[0:64, reg * 512:reg * 512 + 512]),
                    r=[cx.wck[reg]], w=[dstbfk_c[c]], dma=True)
                p.op("po", lambda e, c=c, o=o, reg=reg: e.dma_start(
                    out=dstbf[c, 1, 64:128, o:o + 512],
                    in_=cx.wc[64:128, reg * 512:reg * 512 + 512]),
                    r=[cx.wck[reg]], w=[dstbfk_c[c]], dma=True)
                p.op("po", lambda e, c=c, o=o: e.dma_start(
                    out=dstbf[c, 0, 64:128, o:o + 512], in_=cx.zero_bf[64:128, :]),
                    r=[cx.ck_zero], w=[dstbfk_c[c]], dma=True)
                p.op("po", lambda e, c=c, o=o: e.dma_start(
                    out=dstbf[c, 1, 0:64, o:o + 512], in_=cx.zero_bf[0:64, :]),
                    r=[cx.ck_zero], w=[dstbfk_c[c]], dma=True)


def l0_attn_core(cx, qpad, qpadk, kbf, kbfk, vbf, vbfk, xnbf, xnbfk,
                 winit_ap, winitk, lng_sb, lngk, lnb_sb, lnbk,
                 wg_sb, wgk, wo_view, wok, sT, sTk):
    import os
    kph = int(os.environ.get("KPH", "99"))
    p = cx.p
    big = cx.big

    def bview(off, w3):
        v = big[:, off:off + w3[1] * w3[2]]
        return v.rearrange("p (a b) -> p a b", b=w3[2])
    xnc = cx.rhs[0][:]
    xnck = cx.rhsk[0]
    qpA = cx.rhs[1][:]
    qpAk = cx.rhsk[1]
    kc = bview(0, (128, 8, 512))
    vc = bview(4096, (128, 8, 512))
    ogbf = bview(4096, (128, 4, 1024))
    ktok = bview(8192, (128, 8, 512))
    vtok = bview(12288, (128, 8, 512))
    ogT = bview(16384, (128, 8, 512))
    qpB = bview(20480, (128, 8, 512))
    abf = [big[:, 24576:24832], big[:, 25088:25344]]
    kB = lambda lo, hi: regt(cx.big_t, lo, hi)
    kck = [kB(c * 512, c * 512 + 512)[0] for c in range(8)]
    vck = [kB(4096 + c * 512, 4096 + c * 512 + 512)[0] for c in range(8)]
    ogbfk = [kB(4096 + b * 1024, 4096 + b * 1024 + 1024) for b in range(4)]
    ktokk = [kB(8192 + c * 512, 8192 + c * 512 + 512)[0] for c in range(8)]
    vtokk = [kB(12288 + c * 512, 12288 + c * 512 + 512)[0] for c in range(8)]
    ogTk = kB(16384, 20480)
    qpBk = [kB(20480 + c * 512, 20480 + c * 512 + 512)[0] for c in range(8)]
    abfk = [cx.big_t[48], cx.big_t[49]]
    wacc = cx.sm[:, 0:512]
    wacck = cx.sm_t[0:8]
    wbf = cx.wc[:, 1024:1536]
    wbfk = cx.wck[2]
    gats = cx.acc[:, 4096:5120]
    gatsk = cx.acc_t[8:10]
    p.op("sy", lambda e: e.dma_start(out=wacc, in_=winit_ap),
         r=[winitk], w=wacck, dma=True)
    eps = cx.c["eps"]
    sums_o = cx.sm[:, 512:528]
    sums_q = cx.sm[:, 576:592]
    mean = cx.sm[:, 640:656]
    var = cx.sm[:, 704:720]
    rstd = cx.sm[:, 768:784]
    m2 = cx.sm[:, 832:848]
    sok, sqk_, mk, vk, rk_, m2k = (cx.sm_t[8], cx.sm_t[9], cx.sm_t[10],
                                   cx.sm_t[11], cx.sm_t[12], cx.sm_t[13])

    for ch in range(NTT):
        t0 = ch * 512
        p.op("sy", lambda e, t0=t0: e.dma_start(
            out=qpA[:, :, :], in_=qpad[:, 0, :, t0:t0 + 512].rearrange("c p t -> p c t")),
            r=list(qpadk), w=qpAk, dma=True)
        p.op("sy", lambda e, t0=t0: e.dma_start(
            out=qpB[:, :, :], in_=qpad[:, 1, :, t0:t0 + 512].rearrange("c p t -> p c t")),
            r=list(qpadk), w=qpBk, dma=True)
        p.op("sy", lambda e, t0=t0: e.dma_start(
            out=kc[:, :, :], in_=kbf[:, :, t0:t0 + 512].rearrange("c p t -> p c t")),
            r=list(kbfk), w=kck, dma=True)
        p.op("sy", lambda e, t0=t0: e.dma_start(
            out=vc[:, :, :], in_=vbf[:, :, t0:t0 + 512].rearrange("c p t -> p c t")),
            r=list(vbfk), w=vck, dma=True)
        p.op("sy", lambda e, t0=t0: e.dma_start(
            out=xnc[:, :, :], in_=xnbf[:, :, t0:t0 + 512].rearrange("c p t -> p c t")),
            r=[xnbfk[min(ch, len(xnbfk) - 1)]], w=xnck, dma=True)
        for blk in range(4):
            bo = blk * 128
            for c in range(CT):
                kps = cx.pbf[0][:, c * 128:c * 128 + 128]
                vps = cx.pbf[1][:, c * 128:c * 128 + 128]
                p.op("pe", lambda e, c=c, bo=bo, kps=kps: e.transpose(
                    kps, kc[:, c, bo:bo + 128], cx.c_id_bf[:]),
                    r=[kck[c], cx.ck_id_bf], w=[cx.pbfk[0]])
                p.op("pe", lambda e, c=c, bo=bo, vps=vps: e.transpose(
                    vps, vc[:, c, bo:bo + 128], cx.c_id_bf[:]),
                    r=[vck[c], cx.ck_id_bf], w=[cx.pbfk[1]])
                p.op("dv", lambda e, c=c, bo=bo, kps=kps: e.tensor_copy(
                    ktok[:, c, bo:bo + 128], kps), r=[cx.pbfk[0]], w=[ktokk[c]])
                p.op("dv", lambda e, c=c, bo=bo, vps=vps: e.tensor_copy(
                    vtok[:, c, bo:bo + 128], vps), r=[cx.pbfk[1]], w=[vtokk[c]])
        for blk in range(4):
            bo = blk * 128
            if kph < 2:
                continue
            p.op("dv", lambda e: e.tensor_copy(wbf, wacc), r=list(wacck), w=[wbfk])
            for half in range(2):
                def gmm(e, half=half, bo=bo):
                    last = None
                    for k in range(CT):
                        last = e.matmul(cx.pb[5][:], xnc[:, k, bo:bo + 128],
                                        wg_sb[:, k, half * 512:half * 512 + 512],
                                        start=(k == 0), stop=(k == CT - 1))
                    return last
                p.op("pe", gmm, r=xnck + [wgk], w=[cx.pbk[5]])
                p.op("dv", lambda e, half=half: e.tensor_copy(
                    gats[:, half * 512:half * 512 + 512], cx.pb[5][:]),
                    r=[cx.pbk[5]], w=[gatsk[half]])
            for c in range(CT):
                if kph < 3:
                    continue
                qp = [qpA, qpB]
                qpk = [qpAk, qpBk]

                def amm(e, c=c, bo=bo):
                    last = None
                    for j in range(2):
                        last = e.matmul(cx.pb[0][:, j * 128:j * 128 + 128],
                                        kc[:, c, bo:bo + 128],
                                        qp[j][:, c, bo:bo + 128],
                                        start=True, stop=True)
                    return last
                p.op("pe", amm, r=[kck[c], qpAk[c], qpBk[c]], w=[cx.pbk[0]])
                ab = c % 2
                p.op("dv", lambda e, c=c, ab=ab: e.tensor_tensor(
                    abf[ab], cx.pb[0][:, 0:256], cx.c["maskT2"][:], ALU.mult),
                    r=[cx.pbk[0], cx.ck["maskT2"]], w=[abfk[ab]])
                if kph < 4:
                    continue
                obank = 1 + c // 4
                ocol = (c % 4) * 128

                def oim(e, c=c, bo=bo, ab=ab, obank=obank, ocol=ocol):
                    last = None
                    for hl in range(2):
                        h = 2 * c + hl
                        last = e.matmul(cx.pb[obank][:, ocol + hl * 64:ocol + hl * 64 + 64],
                                        abf[ab][:, hl * 128:hl * 128 + 128],
                                        vtok[:, c, bo:bo + 128][:, hl * 64:hl * 64 + 64],
                                        start=True, stop=False)
                        last = e.matmul(cx.pb[obank][:, ocol + hl * 64:ocol + hl * 64 + 64],
                                        qp[hl][:, c, bo:bo + 128],
                                        wbf[:, (h // 2) * 64:(h // 2) * 64 + 64],
                                        start=False, stop=True)
                    return last
                p.op("pe", oim, r=[abfk[ab], vtokk[c], qpAk[c], qpBk[c], wbfk],
                     w=[cx.pbk[obank]])

                if kph < 5:
                    continue

                def smm(e, c=c, bo=bo):
                    last = None
                    for hl in range(2):
                        last = e.matmul(cx.pb[3 + hl][:, c * 64:c * 64 + 64],
                                        ktok[:, c, bo:bo + 128],
                                        vtok[:, c, bo:bo + 128][:, hl * 64:hl * 64 + 64],
                                        start=True, stop=True)
                    return last
                p.op("pe", smm, r=[ktokk[c], vtokk[c]], w=[cx.pbk[3], cx.pbk[4]])
            if kph < 6:
                continue
            for bi in range(2):
                p.op("ac", lambda e, bi=bi: e.activation(
                    cx.f32a[:, bi * 512:bi * 512 + 512], cx.pb[1 + bi][:], AF.Copy),
                    r=[cx.pbk[1 + bi]], w=[cx.fa_t[bi]])
                p.op("ac", lambda e, bi=bi: e.activation(
                    cx.f32b[:, bi * 512:bi * 512 + 512], cx.pb[1 + bi][:], AF.Square),
                    r=[cx.pbk[1 + bi]], w=[cx.fb_t[bi]])
                p.op("dv", lambda e, bi=bi: e.tensor_reduce(
                    sums_o[:, bi * 8:bi * 8 + 8],
                    cx.f32a[:, bi * 512:bi * 512 + 512].rearrange("p (h d) -> p h d", d=64),
                    AX.X, ALU.add), r=[cx.fa_t[bi]], w=[sok])
                p.op("dv", lambda e, bi=bi: e.tensor_reduce(
                    sums_q[:, bi * 8:bi * 8 + 8],
                    cx.f32b[:, bi * 512:bi * 512 + 512].rearrange("p (h d) -> p h d", d=64),
                    AX.X, ALU.add), r=[cx.fb_t[bi]], w=[sqk_])
            p.op("dv", lambda e: e.tensor_scalar(mean, sums_o, 1.0 / 64, None, ALU.mult),
                 r=[sok], w=[mk])
            p.op("dv", lambda e: e.tensor_scalar(var, sums_q, 1.0 / 64, None, ALU.mult),
                 r=[sqk_], w=[vk])
            p.op("dv", lambda e: e.tensor_tensor(m2, mean, mean, ALU.mult),
                 r=[mk], w=[m2k])
            p.op("dv", lambda e: e.tensor_tensor(var, var, m2, ALU.subtract),
                 r=[vk, m2k], w=[vk])
            p.op("ac", lambda e: e.activation(var, var, AF.Sqrt, bias=eps[:, 0:1]),
                 r=[vk, cx.ck["eps"]], w=[vk])
            p.op("dv", lambda e: e.reciprocal(rstd, var), r=[vk], w=[rk_])
            for bi in range(2):
                p.op("dv", lambda e, bi=bi: e.tensor_tensor(
                    cx.f32a[:, bi * 512:bi * 512 + 512].rearrange("p (h d) -> p h d", d=64),
                    cx.f32a[:, bi * 512:bi * 512 + 512].rearrange("p (h d) -> p h d", d=64),
                    mean[:, bi * 8:bi * 8 + 8].unsqueeze(2).broadcast_to((128, 8, 64)),
                    ALU.subtract), r=[cx.fa_t[bi], mk], w=[cx.fa_t[bi]])
                p.op("dv", lambda e, bi=bi: e.tensor_tensor(
                    cx.f32b[:, bi * 512:bi * 512 + 512].rearrange("p (h d) -> p h d", d=64),
                    lng_sb[:].rearrange("p (h d) -> p h d", d=64),
                    rstd[:, bi * 8:bi * 8 + 8].unsqueeze(2).broadcast_to((128, 8, 64)),
                    ALU.mult), r=[lngk, rk_, sqk_, cx.fb_t[bi]], w=[cx.fb_t[bi]])
                p.op("dv", lambda e, bi=bi: e.tensor_tensor(
                    cx.f32a[:, bi * 512:bi * 512 + 512], cx.f32a[:, bi * 512:bi * 512 + 512],
                    cx.f32b[:, bi * 512:bi * 512 + 512], ALU.mult),
                    r=[cx.fa_t[bi], cx.fb_t[bi]], w=[cx.fa_t[bi]])
                p.op("dv", lambda e, bi=bi: e.tensor_tensor(
                    cx.f32a[:, bi * 512:bi * 512 + 512], cx.f32a[:, bi * 512:bi * 512 + 512],
                    lnb_sb[:, 0:512], ALU.add),
                    r=[cx.fa_t[bi], lnbk], w=[cx.fa_t[bi]])
                p.op("dv", lambda e, bi=bi, blk=blk: e.tensor_tensor(
                    ogbf[:, blk, bi * 512:bi * 512 + 512], cx.f32a[:, bi * 512:bi * 512 + 512],
                    gats[:, bi * 512:bi * 512 + 512], ALU.mult),
                    r=[cx.fa_t[bi], gatsk[bi]], w=ogbfk[blk])
            p.op("dv", lambda e: e.tensor_tensor(wacc[0:64, :], wacc[0:64, :],
                                                 cx.pb[3][0:64, :], ALU.add),
                 r=[cx.pbk[3]] + list(wacck) + [wbfk], w=wacck)
            p.op("dv", lambda e: e.tensor_tensor(wacc[64:128, :], wacc[64:128, :],
                                                 cx.pb[4][64:128, :], ALU.add),
                 r=[cx.pbk[4]] + list(wacck) + [wbfk], w=wacck)
        if kph < 7:
            continue
        for blk in range(4):
            bo = blk * 128
            for cp in range(4):
                for j in range(2):
                    c = cp * 2 + j
                    tps = cx.pbf[j][:, cp * 128:cp * 128 + 128]
                    p.op("pe", lambda e, c=c, blk=blk, tps=tps: e.transpose(
                        tps, ogbf[:, blk, c * 128:c * 128 + 128], cx.c_id_bf[:]),
                        r=list(ogbfk[blk]) + [cx.ck_id_bf], w=[cx.pbfk[j]])
                for j in range(2):
                    c = cp * 2 + j
                    tps = cx.pbf[j][:, cp * 128:cp * 128 + 128]
                    p.op("dv", lambda e, c=c, bo=bo, tps=tps: e.tensor_copy(
                        ogT[:, c, bo:bo + 128], tps), r=[cx.pbfk[j]], w=[ogTk[c]])
        if kph < 8:
            continue
        for c in range(CT):
            b = c % 2

            def womm(e, c=c, b=b):
                last = None
                for k in range(CT):
                    last = e.matmul(cx.pb[b][:], wo_view[:, k, c, :], ogT[:, k, :],
                                    start=(k == 0), stop=(k == CT - 1))
                return last
            p.op("pe", womm, r=list(ogTk) + [wok], w=[cx.pbk[b]])
            freg = 4 + c % 4
            p.op("ac", lambda e, b=b, freg=freg: e.activation(
                cx.f32a[:, freg * 512:freg * 512 + 512], cx.pb[b][:], AF.Copy),
                r=[cx.pbk[b]], w=[cx.fa_t[freg]])
            p.op("po", lambda e, c=c, t0=t0, freg=freg: e.dma_start(
                out=sT[c, :, t0:t0 + 512], in_=cx.f32a[:, freg * 512:freg * 512 + 512]),
                r=[cx.fa_t[freg]], w=[sTk[c % len(sTk)]], dma=True)


def mix_pass(cx, xn32, xn32k, mu_sb, muk, dstbf, dstbfk):
    p = cx.p
    for c in range(CT):
        p.op("sy", lambda e, c=c: e.dma_start(out=cx.f32a[:], in_=xn32[c]),
             r=list(xn32k), w=cx.fa_t, dma=True)
        p.op("dv", lambda e: e.tensor_tensor(cx.f32b[:, 1:4096], cx.f32a[:, 1:4096],
                                             cx.f32a[:, 0:4095], ALU.subtract),
             r=cx.fa_t, w=cx.fb_t)
        p.op("dv", lambda e: e.tensor_copy(cx.f32b[:, 0:1], cx.f32a[:, 0:1]),
             r=[cx.fa_t[0]], w=[cx.fb_t[0]])
        p.op("dv", lambda e, c=c: e.scalar_tensor_tensor(
            cx.acc[:, 1:4096], cx.f32b[:, 1:4096], mu_sb[:, c:c + 1],
            cx.f32a[:, 0:4095], ALU.mult, ALU.add),
            r=cx.fb_t + [muk] + cx.fa_t, w=cx.acc_t[0:8])
        p.op("dv", lambda e, c=c: e.tensor_scalar(
            cx.acc[:, 0:1], cx.f32b[:, 0:1], mu_sb[:, c:c + 1], None, ALU.mult),
            r=[cx.fb_t[0], muk], w=[cx.acc_t[0]])
        p.op("po", lambda e, c=c: e.dma_start(out=dstbf[c], in_=cx.acc[:, 0:4096]),
             r=cx.acc_t[0:8], w=[dstbfk[0]], dma=True)


def combine_pass(cx, xT, xTk, sT, sTk, gT, gTk, out, outk, token_major):
    p = cx.p
    res = cx.acc[:].rearrange("p (c f) -> p c f", f=512)
    xa = cx.f32a[:].rearrange("p (c f) -> p c f", f=512)
    ga = cx.f32b[:].rearrange("p (c f) -> p c f", f=512)
    for tt in range(NTT):
        t0 = tt * 512
        p.op("sy", lambda e, t0=t0, xa=xa: e.dma_start(
            out=xa[:, :, :], in_=xT[:, :, t0:t0 + 512].rearrange("c p t -> p c t")),
            r=list(xTk), w=cx.fa_t, dma=True)
        p.op("sy", lambda e, t0=t0, ga=ga: e.dma_start(
            out=ga[:, :, :], in_=gT[:, :, t0:t0 + 512].rearrange("c p t -> p c t")),
            r=list(gTk), w=cx.fb_t, dma=True)
        for c in range(CT):
            p.op("ac", lambda e, c=c, ga=ga: e.activation(ga[:, c, :], ga[:, c, :], AF.Silu),
                 r=[cx.fb_t[c]], w=[cx.fb_t[c]])
        p.op("sy", lambda e, t0=t0, res=res: e.dma_start(
            out=res[:, 0:8, :], in_=sT[:, :, t0:t0 + 512].rearrange("c p t -> p c t")),
            r=list(sTk), w=cx.acc_t[0:8], dma=True)
        for c in range(CT):
            p.op("dv", lambda e, c=c, ga=ga, res=res: e.tensor_tensor(
                ga[:, c, :], res[:, c, :], ga[:, c, :], ALU.mult),
                r=[cx.acc_t[c], cx.fb_t[c]], w=[cx.fb_t[c]])
            p.op("dv", lambda e, c=c, xa=xa, ga=ga, res=res: e.tensor_tensor(
                res[:, c, :], xa[:, c, :], ga[:, c, :], ALU.add),
                r=[cx.fa_t[c], cx.fb_t[c]], w=[cx.acc_t[c]])
        if not token_major:
            p.op("po", lambda e, t0=t0, res=res: e.dma_start(
                out=out[:, :, t0:t0 + 512].rearrange("c p t -> p c t"), in_=res[:, 0:8, :]),
                r=cx.acc_t[0:8], w=[outk[0]], dma=True)
        else:
            for blk in range(4):
                bo = blk * 128
                for c in range(CT):
                    b = 4 + c % 2
                    p.op("pe", lambda e, c=c, bo=bo, b=b, res=res: e.transpose(
                        cx.pb[b][:, 0:128], res[:, c, bo:bo + 128], cx.c["ident"][:]),
                        r=[cx.acc_t[c], cx.ck["ident"]], w=[cx.pbk[b]])
                    p.op("dv", lambda e, c=c, b=b: e.tensor_copy(
                        cx.f32b[:, c * 128:c * 128 + 128], cx.pb[b][:, 0:128]),
                        r=[cx.pbk[b]], w=[cx.fb_t[c // 4]])
                p.op("po", lambda e, t0=t0, bo=bo: e.dma_start(
                    out=out[t0 + bo:t0 + bo + 128, :], in_=cx.f32b[:, 0:1024]),
                    r=cx.fb_t[0:2], w=[outk[0]], dma=True)





def copy_out(cx, srcarr, srck, xout, xoutk):
    p = cx.p
    for c in range(CT):
        k1 = Trk()
        p.op("sy", lambda e, c=c: e.dma_start(out=cx.f32a[:], in_=srcarr[c]),
             r=list(srck), w=cx.fa_t, dma=True)
        p.op("po", lambda e, c=c: e.dma_start(out=xout[c], in_=cx.f32a[:]),
             r=cx.fa_t, w=[xoutk[0]], dma=True)

def build_layer0(cx, xT, xTk, xout, xoutk, token_major=False):
    import os
    kpass = int(os.environ.get("KPASS", "99"))
    p = cx.p

    wq, wqk = win(cx, 0, "wq", (128, 8192), lambda i: tile_lhs(np.asarray(i["L0_attn_Wq"]).T))
    wk_, wkk = win(cx, 0, "wk", (128, 8192), lambda i: tile_lhs(np.asarray(i["L0_attn_Wk"]).T))
    wv, wvk = win(cx, 0, "wv", (128, 8192), lambda i: tile_lhs(np.asarray(i["L0_attn_Wv"]).T))
    wg, wgk = win(cx, 0, "wg", (128, 8, 1024), lambda i: tile_rhs(np.asarray(i["L0_attn_Wg"]).T))
    wo, wok = win(cx, 0, "wo", (128, 8192), lambda i: tile_lhs(np.asarray(i["L0_attn_Wo"]).T))
    conv, convk = win(cx, 0, "conv", (128, 24, 4), lambda i: np.concatenate([
        np.asarray(i["L0_attn_conv_q"]).reshape(8, 128, 4).transpose(1, 0, 2),
        np.asarray(i["L0_attn_conv_k"]).reshape(8, 128, 4).transpose(1, 0, 2),
        np.asarray(i["L0_attn_conv_v"]).reshape(8, 128, 4).transpose(1, 0, 2)], axis=1))
    winit, winitk = win(cx, 0, "winit", (128, 512), lambda i: np.ascontiguousarray(
        np.asarray(i["L0_attn_W_init"])[0].reshape(8, 2, 64, 64).transpose(1, 2, 0, 3).reshape(128, 512)))
    lng, lngk0 = win(cx, 0, "lng", (128, 512), lambda i: np.tile(np.asarray(i["L0_attn_ln_gamma"]), (128, 8)))
    lnb, lnbk0 = win(cx, 0, "lnb", (128, 512), lambda i: np.tile(np.asarray(i["L0_attn_ln_beta"]), (128, 8)))
    wr, wrk = win(cx, 0, "wr", (128, 8192), lambda i: tile_lhs(np.asarray(i["L0_ffn_W_r"]).T))
    wkf, wkfk = win(cx, 0, "wkf", (128, 8192), lambda i: tile_lhs(np.asarray(i["L0_ffn_W_k"]).T))
    wvf, wvfk = win(cx, 0, "wvf", (128, 8192), lambda i: tile_lhs(np.asarray(i["L0_ffn_W_v"]).T))
    mu, muk = win(cx, 0, "mu", (128, 16), lambda i: np.concatenate([
        col_tile(i["L0_ffn_mu_r"]), col_tile(i["L0_ffn_mu_k"])], axis=1))

    xn32, xn32k = dram(cx, "l0_xn32", (CT, 128, T), n_trk=NTT)
    xnbf, xnbfk = dram(cx, "l0_xnbf", (CT, 128, T), BF16, n_trk=NTT)
    qraw, qrawk = dram(cx, "l0_qraw", (CT, 128, T), n_trk=CT)
    kraw, krawk = dram(cx, "l0_kraw", (CT, 128, T), n_trk=CT)
    vraw, vrawk = dram(cx, "l0_vraw", (CT, 128, T), n_trk=CT)
    qpad, qpadk = dram(cx, "l0_qpad", (CT, 2, 128, T), BF16, n_trk=CT)
    kbf, kbfk = dram(cx, "l0_kbf", (CT, 128, T), BF16, n_trk=CT)
    vbf, vbfk = dram(cx, "l0_vbf", (CT, 128, T), BF16, n_trk=CT)
    sT, sTk = dram(cx, "l0_sT", (CT, 128, T), n_trk=CT)
    xrbf, xrbfk = dram(cx, "l0_xrbf", (CT, 128, T), BF16, n_trk=1)
    xkbf, xkbfk = dram(cx, "l0_xkbf", (CT, 128, T), BF16, n_trk=1)
    rT, rTk = dram(cx, "l0_rT", (CT, 128, T), n_trk=CT)
    kabf, kabfk = dram(cx, "l0_kabf", (CT, 128, T), BF16, n_trk=CT)
    gT, gTk = dram(cx, "l0_gT", (CT, 128, T), n_trk=CT)

    conv_sb = p.sbuf("l0_conv_sb", [128, 24, 4])
    convsk = Trk()
    p.op("sy", lambda e: e.dma_start(out=conv_sb[:], in_=conv), r=[convk], w=[convsk], dma=True)
    mu_sb = p.sbuf("l0_mu_sb", [128, 16])
    musk = Trk()
    p.op("sy", lambda e: e.dma_start(out=mu_sb[:], in_=mu), r=[muk], w=[musk], dma=True)
    lng_sb = p.sbuf("l0_lng_sb", [128, 512])
    lngk = Trk()
    p.op("sy", lambda e: e.dma_start(out=lng_sb[:], in_=lng), r=[lngk0], w=[lngk], dma=True)
    lnb_sb = p.sbuf("l0_lnb_sb", [128, 512])
    lnbk = Trk()
    p.op("sy", lambda e: e.dma_start(out=lnb_sb[:], in_=lnb), r=[lnbk0], w=[lnbk], dma=True)

    ln_pass(cx, xT, xTk, xn32, xn32k, xnbf, xnbfk)
    if kpass <= 1:
        copy_out(cx, xn32, xn32k, xout, xoutk)
        return

    wav = cx.wa[:].rearrange("p (k c f) -> p k c f", k=8, f=128)
    wbv = cx.wb[:].rearrange("p (k c f) -> p k c f", k=8, f=128)

    def raw_epil(dst, dstk):
        def epil(c, tt, b):
            freg = 4 + (c + CT * tt) % 4
            p.op("ac", lambda e, b=b, freg=freg: e.activation(
                cx.f32a[:, freg * 512:freg * 512 + 512], cx.pb[b][:], AF.Copy),
                r=[cx.pbk[b]], w=[cx.fa_t[freg]])
            p.op("po" if freg % 2 == 0 else "ac", lambda e, c=c, tt=tt, freg=freg: e.dma_start(
                out=dst[c, :, tt * 512:tt * 512 + 512],
                in_=cx.f32a[:, freg * 512:freg * 512 + 512]),
                r=[cx.fa_t[freg]], w=[dstk[c]], dma=True)
        return epil

    load_w(cx, cx.wa[:], wq, wqk, cx.wak)
    mm_cm(cx, wav, cx.wak, xnbf, xnbfk, raw_epil(qraw, qrawk))
    if kpass <= 2:
        copy_out(cx, qraw, qrawk, xout, xoutk)
        return
    load_w(cx, cx.wb[:], wk_, wkk, cx.wbk)
    mm_cm(cx, wbv, cx.wbk, xnbf, xnbfk, raw_epil(kraw, krawk), banks=(2, 3))
    conv_silu_pass(cx, qraw, qrawk, conv_sb[:, 0:8, :], convsk, qpad, qpadk, l2norm=True, pad_heads=True)
    if kpass <= 3:
        copy_out(cx, xn32, xn32k, xout, xoutk)
        return
    load_w(cx, cx.wa[:], wv, wvk, cx.wak)
    mm_cm(cx, wav, cx.wak, xnbf, xnbfk, raw_epil(vraw, vrawk))
    conv_silu_pass(cx, kraw, krawk, conv_sb[:, 8:16, :], convsk, kbf, kbfk, l2norm=True)
    conv_silu_pass(cx, vraw, vrawk, conv_sb[:, 16:24, :], convsk, vbf, vbfk, l2norm=False)
    if kpass <= 4:
        copy_out(cx, xn32, xn32k, xout, xoutk)
        return

    wgv = cx.wa[:].rearrange("p (k f) -> p k f", f=1024)
    load_w(cx, cx.wa[:], wg, wgk, cx.wak)
    load_w(cx, cx.wb[:], wo, wok, cx.wbk)
    l0_attn_core(cx, qpad, qpadk, kbf, kbfk, vbf, vbfk, xnbf, xnbfk,
                 winit, winitk, lng_sb, lngk, lnb_sb, lnbk,
                 wgv, cx.wak, wbv, cx.wbk, sT, sTk)
    if kpass <= 5:
        copy_out(cx, sT, sTk, xout, xoutk)
        return

    mix_pass(cx, xn32, xn32k, mu_sb[:, 0:8], musk, xrbf, xrbfk)
    mix_pass(cx, xn32, xn32k, mu_sb[:, 8:16], musk, xkbf, xkbfk)

    def sig_epil(c, tt, b):
        freg = 4 + (c + CT * tt) % 4
        p.op("ac", lambda e, b=b, freg=freg: e.activation(
            cx.f32a[:, freg * 512:freg * 512 + 512], cx.pb[b][:], AF.Sigmoid),
            r=[cx.pbk[b]], w=[cx.fa_t[freg]])
        p.op("po", lambda e, c=c, tt=tt, freg=freg: e.dma_start(
            out=rT[c, :, tt * 512:tt * 512 + 512],
            in_=cx.f32a[:, freg * 512:freg * 512 + 512]),
            r=[cx.fa_t[freg]], w=[rTk[c]], dma=True)

    load_w(cx, cx.wa[:], wr, wrk, cx.wak)
    mm_cm(cx, wav, cx.wak, xrbf, [xrbfk[0]], sig_epil)
    if kpass <= 6:
        copy_out(cx, rT, rTk, xout, xoutk)
        return

    def relsq_epil(c, tt, b):
        freg = 4 + (c + CT * tt) % 4
        p.op("dv", lambda e, b=b, freg=freg: e.tensor_scalar(
            cx.f32b[:, freg * 512:freg * 512 + 512], cx.pb[b][:], 0.0, None, ALU.max),
            r=[cx.pbk[b]], w=[cx.fb_t[freg]])
        p.op("ac", lambda e, freg=freg: e.activation(
            cx.f32a[:, freg * 512:freg * 512 + 512],
            cx.f32b[:, freg * 512:freg * 512 + 512], AF.Square),
            r=[cx.fb_t[freg]], w=[cx.fa_t[freg]])
        p.op("po", lambda e, c=c, tt=tt, freg=freg: e.dma_start(
            out=kabf[c, :, tt * 512:tt * 512 + 512],
            in_=cx.f32a[:, freg * 512:freg * 512 + 512]),
            r=[cx.fa_t[freg]], w=[kabfk[c]], dma=True)

    load_w(cx, cx.wb[:], wkf, wkfk, cx.wbk)
    mm_cm(cx, wbv, cx.wbk, xkbf, [xkbfk[0]], relsq_epil, banks=(2, 3))

    def g0_epil(c, tt, b):
        freg = 4 + (c + CT * tt) % 4
        p.op("sy", lambda e, c=c, tt=tt, freg=freg: e.dma_start(
            out=cx.f32b[:, freg * 512:freg * 512 + 512],
            in_=rT[c, :, tt * 512:tt * 512 + 512]),
            r=[rTk[c]], w=[cx.fb_t[freg]], dma=True)
        p.op("dv", lambda e, b=b, freg=freg: e.tensor_tensor(
            cx.f32a[:, freg * 512:freg * 512 + 512], cx.pb[b][:],
            cx.f32b[:, freg * 512:freg * 512 + 512], ALU.mult),
            r=[cx.pbk[b], cx.fb_t[freg]], w=[cx.fa_t[freg]])
        p.op("po", lambda e, c=c, tt=tt, freg=freg: e.dma_start(
            out=gT[c, :, tt * 512:tt * 512 + 512],
            in_=cx.f32a[:, freg * 512:freg * 512 + 512]),
            r=[cx.fa_t[freg]], w=[gTk[c]], dma=True)

    load_w(cx, cx.wa[:], wvf, wvfk, cx.wak)
    mm_cm(cx, wav, cx.wak, kabf, kabfk, g0_epil, src_by_c=True)
    if kpass <= 7:
        copy_out(cx, gT, gTk, xout, xoutk)
        return

    combine_pass(cx, xT, xTk, sT, sTk, gT, gTk, xout, xoutk, token_major)






def moe_pass(cx, srcbf, srcbfk, gR, gRk, w1bf, w1k, w2bf, w2k, hE, hEk,
             outT, outTk, act_fn, n_exp):
    p = cx.p
    gbc = cx.f32b[:, 0:512]
    gbck = cx.fb_t[0]
    grow = cx.r1[0]
    growk = cx.r1k[0]

    def load_gbc(e_, tt):
        t0 = tt * 512
        p.op("sy", lambda e, e_=e_, t0=t0: e.dma_start(
            out=grow[:], in_=gR[e_, t0:t0 + 512]), r=[gRk[0]], w=[growk], dma=True)
        p.op("pe", lambda e: e.matmul(cx.pb[4][:], cx.c["ones_row"][:], grow[:],
                                      start=True, stop=True),
             r=[growk, cx.ck["ones_row"]], w=[cx.pbk[4]])
        p.op("ac", lambda e: e.activation(gbc, cx.pb[4][:], AF.Copy),
             r=[cx.pbk[4]], w=[gbck])

    w1buf = [cx.big[:, 0:8192].rearrange("p (k c f) -> p k c f", k=8, f=128),
             cx.big[:, 8192:16384].rearrange("p (k c f) -> p k c f", k=8, f=128)]
    w1bufk = [regt(cx.big_t, 0, 8192), regt(cx.big_t, 8192, 16384)]
    w2buf = [cx.big[:, 16384:20480].rearrange("p (k c f) -> p k c f", k=8, f=128),
             cx.big[:, 20480:24576].rearrange("p (k c f) -> p k c f", k=8, f=128)]
    w2bufk = [regt(cx.big_t, 16384, 20480), regt(cx.big_t, 20480, 24576)]
    for e_ in range(n_exp):
        for cg in range(4):
            wsel = (e_ * 4 + cg) % 2
            wav = w1buf[wsel]
            wavk = w1bufk[wsel]
            p.op("sy", lambda e, e_=e_, cg=cg, wsel=wsel: e.dma_start(
                out=cx.big[:, wsel * 8192:wsel * 8192 + 8192],
                in_=w1bf[e_, cg].rearrange("p k c f -> p (k c f)")),
                r=[w1k], w=w1bufk[wsel], dma=True)
            for tt in range(NTT):
                t0 = tt * 512
                rhs = cx.rhs[tt % 2]
                rk = cx.rhsk[tt % 2]
                p.op("sy", lambda e, t0=t0, rhs=rhs: e.dma_start(
                    out=rhs[:, :, :], in_=srcbf[:, :, t0:t0 + 512].rearrange("k p t -> p k t")),
                    r=[srcbfk[tt]], w=rk, dma=True)
                for c in range(CT):
                    b = (tt * CT + c) % 2

                    def hmm(e, c=c, rhs=rhs, b=b, wav=wav):
                        last = None
                        for k in range(CT):
                            last = e.matmul(cx.pb[b][:], wav[:, k, c, :], rhs[:, k, :],
                                            start=(k == 0), stop=(k == CT - 1))
                        return last
                    p.op("pe", hmm, r=rk[:CT] + wavk, w=[cx.pbk[b]])
                    wreg = (tt * CT + c) % 4
                    p.op("ac", lambda e, b=b, wreg=wreg: e.activation(
                        cx.wc[:, wreg * 512:wreg * 512 + 512], cx.pb[b][:], act_fn),
                        r=[cx.pbk[b]], w=[cx.wck[wreg]])
                    p.op("po" if wreg % 2 == 0 else "ac", lambda e, cg=cg, c=c, t0=t0, wreg=wreg: e.dma_start(
                        out=hE[cg * 8 + c, :, t0:t0 + 512],
                        in_=cx.wc[:, wreg * 512:wreg * 512 + 512]),
                        r=[cx.wck[wreg]], w=[hEk[(cg * 8 + c) % 8]], dma=True)
        for tt in range(NTT):
            t0 = tt * 512
            load_gbc(e_, tt)
            for chalf in range(2):
                for kg in range(4):
                    wsel = kg % 2
                    p.op("sy", lambda e, e_=e_, chalf=chalf, kg=kg, wsel=wsel: e.dma_start(
                        out=cx.big[:, 16384 + wsel * 4096:16384 + wsel * 4096 + 4096],
                        in_=w2bf[e_, chalf, kg].rearrange("p k c f -> p (k c f)")),
                        r=[w2k], w=w2bufk[wsel], dma=True)
                    rhs = cx.rhs[kg % 2]
                    rk = cx.rhsk[kg % 2]
                    p.op("sy", lambda e, kg=kg, t0=t0, rhs=rhs: e.dma_start(
                        out=rhs[:, :, :],
                        in_=hE[kg * 8:kg * 8 + 8, :, t0:t0 + 512].rearrange("k p t -> p k t")),
                        r=[hEk[k % 8] for k in range(kg * 8, kg * 8 + 8)], w=rk, dma=True)
                    for c in range(4):
                        def ymm(e, c=c, kg=kg, rhs=rhs, wsel=wsel):
                            last = None
                            for k in range(CT):
                                last = e.matmul(cx.pb[c][:], w2buf[wsel][:, k, c, :], rhs[:, k, :],
                                                start=(kg == 0 and k == 0),
                                                stop=(kg == 3 and k == CT - 1))
                            return last
                        p.op("pe", ymm, r=rk[:CT] + w2bufk[wsel], w=[cx.pbk[c]])
                for c in range(4):
                    d = chalf * 4 + c
                    freg = 2 + c
                    p.op("dv", lambda e, c=c, freg=freg: e.tensor_tensor(
                        cx.f32a[:, freg * 512:freg * 512 + 512], cx.pb[c][:], gbc, ALU.mult),
                        r=[cx.pbk[c], gbck], w=[cx.fa_t[freg]])
                    p.op("po", lambda e, d=d, t0=t0, freg=freg: e.dma_start(
                        out=outT[d, :, t0:t0 + 512],
                        in_=cx.f32a[:, freg * 512:freg * 512 + 512],
                        accum_op=ALU.add),
                        r=[cx.fa_t[freg]], w=[outTk[d % len(outTk)]], dma=True)



def build_layer1(cx, xT, xTk, xout, xoutk, token_major=False):
    p = cx.p

    qkva, qkvak = win(cx, 1, "qkva", (3, 128, 8192),
                      lambda i: np.stack([tile_lhs(np.asarray(i["L1_attn_w_qkva"]).T[:, part * 1024:(part + 1) * 1024])
                                          for part in range(3)]))
    gam, gamk = win(cx, 1, "gamma", (128, 8), lambda i: col_tile(i["L1_attn_gamma"]))
    rw, rwk = win(cx, 1, "rw", (128, 8, 9), lambda i: tile_rhs(np.asarray(i["L1_ffn_router_w"])))
    rb, rbk = win(cx, 1, "rb", (1, 9), lambda i: np.asarray(i["L1_ffn_router_b"]).reshape(1, 9))
    w1bf, w1k = precast(cx, 1, "w1", (8, 4, 128, 8, 8, 128), lambda i: np.ascontiguousarray(
        np.asarray(i["L1_ffn_w1"]).reshape(8, 8, 128, 4, 8, 128).transpose(0, 3, 2, 1, 4, 5)), n_parts=8)
    w2bf, w2k = precast(cx, 1, "w2", (8, 2, 4, 128, 8, 4, 128), lambda i: np.ascontiguousarray(
        np.asarray(i["L1_ffn_w2"]).reshape(8, 4, 8, 128, 2, 4, 128).transpose(0, 4, 1, 3, 2, 5, 6)), n_parts=8)
    hE, hEk = dram(cx, "l1_hE", (32, 128, T), BF16, n_trk=8)

    xn32, xn32k = dram(cx, "l1_xn32", (CT, 128, T), n_trk=NTT)
    xnbf, xnbfk = dram(cx, "l1_xnbf", (CT, 128, T), BF16, n_trk=NTT)
    hbf, hbfk = dram(cx, "l1_hbf", (CT, 128, T), BF16, n_trk=NTT)
    qT, qTk = dram(cx, "l1_qT", (CT, 128, T), n_trk=CT)
    kvT, kvTk = dram(cx, "l1_kvT", (CT, 128, T), n_trk=CT)
    aT, aTk = dram(cx, "l1_aT", (CT, 128, T), n_trk=CT)
    sT, sTk = dram(cx, "l1_sT", (CT, 128, T), n_trk=CT)
    gR, gRk = dram(cx, "l1_gR", (16, T), n_trk=1)
    g0T, g0Tk = dram(cx, "l1_g0T", (CT, 128, T), n_trk=CT)

    gam_sb = p.sbuf("l1_gam_sb", [128, 8])
    gamsk = Trk()
    p.op("sy", lambda e: e.dma_start(out=gam_sb[:], in_=gam), r=[gamk], w=[gamsk], dma=True)
    rw_sb = p.sbuf("l1_rw_sb", [128, 8, 9], F32)
    rwsk = Trk()
    p.op("sy", lambda e: e.dma_start(out=rw_sb[:], in_=rw), r=[rwk], w=[rwsk], dma=True)
    rb_sb = p.sbuf("l1_rb_sb", [1, 9])
    rbsk = Trk()
    p.op("sy", lambda e: e.dma_start(out=rb_sb[:], in_=rb), r=[rbk], w=[rbsk], dma=True)

    ln_pass(cx, xT, xTk, xn32, xn32k, xnbf, xnbfk)

    eps = cx.c["eps"]
    for tt in range(NTT):
        t0 = tt * 512
        xa = cx.f32a[:].rearrange("p (c f) -> p c f", f=512)
        sq = cx.f32b[:].rearrange("p (c f) -> p c f", f=512)
        for c in range(CT):
            p.op("sy", lambda e, c=c, t0=t0, xa=xa: e.dma_start(
                out=xa[:, c, :], in_=xn32[c, :, t0:t0 + 512]),
                r=[xn32k[tt]], w=[cx.fa_t[c]], dma=True)
            p.op("ac", lambda e, c=c, xa=xa, sq=sq: e.activation(sq[:, c, :], xa[:, c, :], AF.Square),
                 r=[cx.fa_t[c]], w=[cx.fb_t[c]])

        def mmsum(e, sq=sq):
            last = None
            for c in range(CT):
                last = e.matmul(cx.pb[0][0:1, :], cx.c["ones_col"][:], sq[:, c, :],
                                start=(c == 0), stop=(c == CT - 1))
            return last
        p.op("pe", mmsum, r=cx.fb_t + [cx.ck["ones_col"]], w=[cx.pbk[0]])
        r1, r1k = cx.r1, cx.r1k
        p.op("ac", lambda e: e.activation(r1[0][:], cx.pb[0][0:1, :], AF.Sqrt),
             r=[cx.pbk[0]], w=[r1k[0]])
        p.op("dv", lambda e: e.tensor_scalar(r1[0][:], r1[0][:], 1e-12, None, ALU.max),
             r=[r1k[0]], w=[r1k[0]])
        p.op("dv", lambda e: e.reciprocal(r1[1][:], r1[0][:]), r=[r1k[0]], w=[r1k[1]])
        p.op("dv", lambda e: e.tensor_scalar(r1[1][:], r1[1][:], 32.0, None, ALU.mult),
             r=[r1k[1]], w=[r1k[1]])
        p.op("pe", lambda e: e.matmul(cx.pb[1][:], cx.c["ones_row"][:], r1[1][:],
                                      start=True, stop=True),
             r=[r1k[1], cx.ck["ones_row"]], w=[cx.pbk[1]])
        for c in range(CT):
            p.op("dv", lambda e, c=c, xa=xa, sq=sq: e.scalar_tensor_tensor(
                sq[:, c, :], xa[:, c, :], gam_sb[:, c:c + 1], cx.pb[1][:],
                ALU.mult, ALU.mult),
                r=[cx.fa_t[c], gamsk, cx.pbk[1], cx.fb_t[c]], w=[cx.fb_t[c]])
            p.op("po", lambda e, c=c, t0=t0, sq=sq: e.dma_start(
                out=hbf[c, :, t0:t0 + 512], in_=sq[:, c, :]),
                r=[cx.fb_t[c]], w=[hbfk[tt]], dma=True)

    wav = cx.wa[:].rearrange("p (k c f) -> p k c f", k=8, f=128)

    def part_epil(dst, dstk, func):
        def epil(c, tt, b):
            freg = 4 + (c + CT * tt) % 4
            p.op("ac", lambda e, b=b, freg=freg: e.activation(
                cx.f32a[:, freg * 512:freg * 512 + 512], cx.pb[b][:], func),
                r=[cx.pbk[b]], w=[cx.fa_t[freg]])
            p.op("po" if freg % 2 == 0 else "ac", lambda e, c=c, tt=tt, freg=freg: e.dma_start(
                out=dst[c, :, tt * 512:tt * 512 + 512],
                in_=cx.f32a[:, freg * 512:freg * 512 + 512]),
                r=[cx.fa_t[freg]], w=[dstk[c]], dma=True)
        return epil

    for part, (dst, dstk, func) in enumerate((
            (qT, qTk, AF.Copy), (kvT, kvTk, AF.Copy), (aT, aTk, AF.Sigmoid))):
        p.op("po", lambda e, part=part: e.dma_start(
            out=cx.wa[:], in_=qkva[part]),
            r=[qkvak], w=[cx.wak], dma=True)
        mm_cm(cx, wav, cx.wak, hbf, hbfk, part_epil(dst, dstk, func),
              banks=((0, 1) if part % 2 == 0 else (2, 3)))

    for ch in range(NTT):
        t0 = ch * 512
        xnc = cx.f32a[:].rearrange("p (c f) -> p c f", f=512)
        xnck = cx.fa_t
        p.op("sy", lambda e, t0=t0, xnc=xnc: e.dma_start(
            out=xnc[:, :, :], in_=xn32[:, :, t0:t0 + 512].rearrange("c p t -> p c t")),
            r=[xn32k[ch]], w=cx.fa_t, dma=True)
        for blk in range(4):
            bo = blk * 128

            def rmm(e, bo=bo, xnc=xnc):
                last = None
                for k in range(CT):
                    last = e.matmul(cx.pb[0][:, 0:9], xnc[:, k, bo:bo + 128],
                                    rw_sb[:, k, :], start=(k == 0), stop=False)
                last = e.matmul(cx.pb[0][:, 0:9], cx.c["ones_row"][:], rb_sb[:],
                                start=False, stop=True)
                return last
            p.op("pe", rmm, r=list(xnck) + [rwsk, rbsk, cx.ck["ones_row"]], w=[cx.pbk[0]])
            lg = cx.sm[:, 896:912]
            lgk = cx.sm_t[14]
            mx = cx.sm[:, 960:961]
            mxk = cx.sm_t[15]
            p.op("ac", lambda e: e.activation(lg[:, 0:9], cx.pb[0][:, 0:9], AF.Copy),
                 r=[cx.pbk[0]], w=[lgk])
            p.op("dv", lambda e: e.reduce_max(mx, lg[:, 0:9], axis=AX.X),
                 r=[lgk], w=[mxk])
            p.op("dv", lambda e: e.tensor_scalar(mx, mx, -1.0, None, ALU.mult),
                 r=[mxk], w=[mxk])
            p.op("ac", lambda e: e.activation(lg[:, 0:9], lg[:, 0:9], AF.Exp,
                                              bias=mx, accum_out=cx.sm[:, 961:962]),
                 r=[lgk, mxk], w=[lgk, mxk])
            p.op("dv", lambda e: e.reciprocal(mx, cx.sm[:, 961:962]),
                 r=[mxk], w=[mxk])
            p.op("dv", lambda e: e.tensor_scalar(lg[:, 0:9], lg[:, 0:9], mx, None, ALU.mult),
                 r=[lgk, mxk], w=[lgk])
            p.op("dv", lambda e: e.reduce_max(mx, lg[:, 0:9], axis=AX.X),
                 r=[lgk], w=[mxk])
            p.op("dv", lambda e: e.tensor_scalar(cx.sm[:, 912:921], lg[:, 0:9],
                                                 mx, None, ALU.is_lt),
                 r=[lgk, mxk], w=[cx.sm_t[14]])
            p.op("dv", lambda e: e.tensor_tensor(cx.sm[:, 928:937], lg[:, 0:9],
                                                 cx.sm[:, 912:921], ALU.mult),
                 r=[cx.sm_t[14]], w=[cx.sm_t[14]])
            p.op("dv", lambda e: e.reduce_max(mx, cx.sm[:, 928:937], axis=AX.X),
                 r=[cx.sm_t[14]], w=[mxk])
            p.op("dv", lambda e: e.tensor_scalar(cx.sm[:, 912:921], lg[:, 0:9],
                                                 mx, None, ALU.is_ge),
                 r=[lgk, mxk], w=[cx.sm_t[14]])
            p.op("dv", lambda e: e.tensor_tensor(lg[:, 0:9], lg[:, 0:9],
                                                 cx.sm[:, 912:921], ALU.mult),
                 r=[cx.sm_t[14], lgk], w=[lgk])
            p.op("dv", lambda e: e.memset(lg[:, 9:16], 0.0), r=[lgk], w=[lgk])
            p.op("pe", lambda e: e.transpose(cx.pb[1][0:16, 0:128],
                                             lg[:, 0:16], cx.c["ident"][:]),
                 r=[lgk, cx.ck["ident"]], w=[cx.pbk[1]])
            p.op("ac", lambda e: e.activation(cx.f32b[0:16, 3584:3712], cx.pb[1][0:16, 0:128], AF.Copy),
                 r=[cx.pbk[1]], w=[cx.fb_t[7]])
            p.op("po", lambda e, t0=t0, bo=bo: e.dma_start(
                out=gR[:, t0 + bo:t0 + bo + 128], in_=cx.f32b[0:16, 3584:3712]),
                r=[cx.fb_t[7]], w=[gRk[0]], dma=True)

    for tt in range(NTT):
        t0 = tt * 512
        gbc = cx.f32b[:, 0:512]
        gbck = cx.fb_t[0]
        grow = cx.r1[0]
        growk = cx.r1k[0]
        p.op("sy", lambda e, t0=t0: e.dma_start(
            out=grow[:], in_=gR[8, t0:t0 + 512]), r=[gRk[0]], w=[growk], dma=True)
        p.op("pe", lambda e: e.matmul(cx.pb[4][:], cx.c["ones_row"][:], grow[:],
                                      start=True, stop=True),
             r=[growk, cx.ck["ones_row"]], w=[cx.pbk[4]])
        p.op("ac", lambda e: e.activation(gbc, cx.pb[4][:], AF.Copy),
             r=[cx.pbk[4]], w=[gbck])
        xnc = cx.rhs[0][:]
        xnck = cx.rhsk[0]
        p.op("sy", lambda e, t0=t0, xnc=xnc: e.dma_start(
            out=xnc[:, :, :], in_=xnbf[:, :, t0:t0 + 512].rearrange("c p t -> p c t")),
            r=[xnbfk[tt]], w=xnck, dma=True)
        for d in range(CT):
            freg = 2 + d % 4
            p.op("dv", lambda e, d=d, freg=freg, xnc=xnc, gbc=gbc: e.tensor_tensor(
                cx.f32a[:, freg * 512:freg * 512 + 512], xnc[:, d, :], gbc, ALU.mult),
                r=[xnck[d], gbck], w=[cx.fa_t[freg]])
            p.op("po", lambda e, d=d, t0=t0, freg=freg: e.dma_start(
                out=g0T[d, :, t0:t0 + 512], in_=cx.f32a[:, freg * 512:freg * 512 + 512]),
                r=[cx.fa_t[freg]], w=[g0Tk[d % CT]], dma=True)

    moe_pass(cx, xnbf, xnbfk, gR, gRk, w1bf, w1k, w2bf, w2k, hE, hEk,
             g0T, g0Tk, AF.Gelu, 8)

    for c in range(CT):
        p.op("sy", lambda e, c=c: e.dma_start(out=cx.f32a[:], in_=aT[c]),
             r=[aTk[c]], w=cx.fa_t, dma=True)
        p.op("sy", lambda e, c=c: e.dma_start(out=cx.f32b[:], in_=kvT[c]),
             r=[kvTk[c]], w=cx.fb_t, dma=True)
        p.op("dv", lambda e: e.tensor_tensor_scan(
            cx.acc[:, 0:4096], cx.f32a[:], cx.f32b[:], 0.0, ALU.mult, ALU.add),
            r=cx.fa_t + cx.fb_t, w=cx.acc_t[0:8])
        p.op("sy", lambda e, c=c: e.dma_start(out=cx.f32b[:], in_=qT[c]),
             r=[qTk[c]], w=cx.fb_t, dma=True)
        p.op("dv", lambda e: e.tensor_tensor(cx.acc[:, 0:4096], cx.acc[:, 0:4096],
                                             cx.f32b[:], ALU.mult),
             r=cx.fb_t + cx.acc_t[0:8], w=cx.acc_t[0:8])
        p.op("po", lambda e, c=c: e.dma_start(out=sT[c], in_=cx.acc[:, 0:4096]),
             r=cx.acc_t[0:8], w=[sTk[0]], dma=True)


    combine_pass(cx, xT, xTk, sT, sTk, g0T, g0Tk, xout, xoutk, token_major)





def u_bound_pass(cx, qpad, qpadk, kbf, kbfk, negU_col, negUk):
    p = cx.p
    r1, r1k = cx.r1, cx.r1k
    p.op("dv", lambda e: e.memset(r1[2][:, 0:2], 0.0), w=[r1k[2]])
    for which, (arr, arrk, pad) in enumerate(((qpad, qpadk, True), (kbf, kbfk, False))):
        for c in range(CT):
            for tt in range(NTT):
                t0 = tt * 512
                wreg = tt % 2
                nsub = 2 if pad else 1
                for j in range(nsub):
                    src_ap = arr[c, j, :, t0:t0 + 512] if pad else arr[c, :, t0:t0 + 512]
                    p.op("sy", lambda e, src_ap=src_ap, wreg=wreg, j=j: e.dma_start(
                        out=cx.wc[:, (wreg * 2 + j) * 512:(wreg * 2 + j) * 512 + 512],
                        in_=src_ap),
                        r=[arrk[c]], w=[cx.wck[wreg * 2 + j]], dma=True)
                    freg = 4 + wreg
                    p.op("ac", lambda e, wreg=wreg, j=j, freg=freg: e.activation(
                        cx.f32a[:, freg * 512:freg * 512 + 512],
                        cx.wc[:, (wreg * 2 + j) * 512:(wreg * 2 + j) * 512 + 512], AF.Square),
                        r=[cx.wck[wreg * 2 + j]], w=[cx.fa_t[freg]])
                    p.op("pe", lambda e, freg=freg, j=j, nsub=nsub: e.matmul(
                        cx.pb[5][0:1, :], cx.c["ones_col"][:],
                        cx.f32a[:, freg * 512:freg * 512 + 512],
                        start=(j == 0), stop=(j == nsub - 1)),
                        r=[cx.fa_t[freg], cx.ck["ones_col"]], w=[cx.pbk[5]])
                p.op("ac", lambda e: e.activation(r1[0][:], cx.pb[5][0:1, :], AF.Copy),
                     r=[cx.pbk[5]], w=[r1k[0]])
                p.op("dv", lambda e: e.reduce_max(r1[1][:, 0:1], r1[0][:], axis=AX.X),
                     r=[r1k[0]], w=[r1k[1]])
                p.op("dv", lambda e, which=which: e.tensor_tensor(
                    r1[2][:, which:which + 1], r1[2][:, which:which + 1],
                    r1[1][:, 0:1], ALU.max),
                    r=[r1k[1]], w=[r1k[2]])
    p.op("ac", lambda e: e.activation(r1[2][:, 0:2], r1[2][:, 0:2], AF.Sqrt),
         r=[r1k[2]], w=[r1k[2]])
    p.op("dv", lambda e: e.tensor_tensor(r1[2][:, 0:1], r1[2][:, 0:1], r1[2][:, 1:2], ALU.mult),
         r=[r1k[2]], w=[r1k[2]])
    p.op("dv", lambda e: e.tensor_scalar(r1[2][:, 0:1], r1[2][:, 0:1], -0.125, None, ALU.mult),
         r=[r1k[2]], w=[r1k[2]])
    p.op("pe", lambda e: e.matmul(cx.pb[5][:, 0:1], cx.c["ones_row"][:], r1[2][:, 0:1],
                                  start=True, stop=True),
         r=[r1k[2], cx.ck["ones_row"]], w=[cx.pbk[5]])
    p.op("ac", lambda e: e.activation(negU_col, cx.pb[5][:, 0:1], AF.Copy),
         r=[cx.pbk[5]], w=[negUk])




def msa_pass(cx, qpad, qpadk, kbf, kbfk, vbf, vbfk, negU_col, negUk, yT, yTk):
    p = cx.p
    big = cx.big
    vtok = big[:, 0:16640].rearrange("p (a b) -> p a b", b=520)
    vtokk = regt(cx.big_t, 0, 16640)
    qb = [big[:, 17408:21504].rearrange("p (a b) -> p a b", b=512),
          big[:, 21504:25600].rearrange("p (a b) -> p a b", b=512)]
    qbflat = [big[:, 17408:21504], big[:, 21504:25600]]
    qbk = [regt(cx.big_t, 17408, 21504), regt(cx.big_t, 21504, 25600)]
    r1, r1k = cx.r1, cx.r1k
    ex = [[cx.wa[:, 0:1024], cx.wa[:, 1024:2048]],
          [cx.wa[:, 2048:3072], cx.wa[:, 3072:4096]]]
    exk = [[Trk(), Trk()], [Trk(), Trk()]]
    p.op("dv", lambda e: e.memset(cx.wa[0:1, 0:1], 0.0),
         w=[cx.wak, exk[0][0], exk[0][1], exk[1][0], exk[1][1]])
    for hh in range(2):
        for tkb in range(32):
            p.op("dv", lambda e, tkb=tkb: e.memset(
                vtok[:, tkb, :].rearrange("p (h x) -> p h x", x=65)[:, :, 64], 1.0),
                r=[], w=regt(cx.big_t, tkb * 520, tkb * 520 + 520))
        for cl in range(4):
            c = hh * 4 + cl
            for tt in range(NTT):
                t0 = tt * 512
                wreg = tt % 2
                p.op("sy", lambda e, c=c, t0=t0, wreg=wreg: e.dma_start(
                    out=cx.wc[:, wreg * 512:wreg * 512 + 512], in_=vbf[c, :, t0:t0 + 512]),
                    r=[vbfk[c]], w=[cx.wck[wreg]], dma=True)
                for blk in range(4):
                    tkb = tt * 4 + blk
                    bank = blk % 2
                    tps = cx.pbf[bank][:, 0:128]
                    p.op("pe", lambda e, wreg=wreg, blk=blk, tps=tps: e.transpose(
                        tps, cx.wc[:, wreg * 512 + blk * 128:wreg * 512 + blk * 128 + 128],
                        cx.c_id_bf[:]),
                        r=[cx.wck[wreg], cx.ck_id_bf], w=[cx.pbfk[bank]])
                    for j in range(2):
                        p.op("dv", lambda e, tkb=tkb, cl=cl, j=j, tps=tps: e.tensor_copy(
                            vtok[:, tkb, (2 * cl + j) * 65:(2 * cl + j) * 65 + 64],
                            tps[:, j * 64:j * 64 + 64]),
                            r=[cx.pbfk[bank]],
                            w=regt(cx.big_t, tkb * 520, tkb * 520 + 520))
        for hl in range(8):
            h = hh * 8 + hl
            c = h // 2
            j = h % 2
            kbuf = cx.rhs[c % 2]
            kbufk = cx.rhsk[c % 2]
            if j == 0:
                p.op("sy", lambda e, c=c, kbuf=kbuf: e.dma_start(
                    out=kbuf[:].rearrange("p k t -> p (k t)"), in_=kbf[c]),
                    r=[kbfk[c]], w=kbufk, dma=True)
            qsel = qb[h % 2]
            qselk = qbk[h % 2]
            qflat = qbflat[h % 2]
            p.op("sy", lambda e, c=c, j=j, qflat=qflat: e.dma_start(
                out=qflat, in_=qpad[c, j]),
                r=[qpadk[c]], w=qselk, dma=True)
            for tqp in range(4):
                tqA, tqB = 2 * tqp, 2 * tqp + 1

                def emit_pv(tkp):
                    tk0, tk1 = 2 * tkp, 2 * tkp + 1
                    par = tkp % 2
                    for ci, pvb in ((0, 4), (1, 5)):
                        exbuf = ex[ci][par]
                        exkk = exk[ci][par]

                        def pvm(e, tk0=tk0, tk1=tk1, hl=hl, pvb=pvb, exbuf=exbuf):
                            e.matmul(cx.pb[pvb][0:65, :], vtok[:, tk0, hl * 65:hl * 65 + 65],
                                     exbuf[:, 0:512],
                                     start=(tk0 == 0), stop=False)
                            return e.matmul(cx.pb[pvb][0:65, :], vtok[:, tk1, hl * 65:hl * 65 + 65],
                                            exbuf[:, 512:1024],
                                            start=False, stop=(tk1 == 31))
                        p.op("pe", pvm,
                             r=[exkk, cx.wak]
                             + regt(cx.big_t, tk0 * 520, tk0 * 520 + 520)
                             + regt(cx.big_t, tk1 * 520, tk1 * 520 + 520),
                             w=[cx.pbk[pvb]])

                for tkp in range(16):
                    tk0, tk1 = 2 * tkp, 2 * tkp + 1
                    par = tkp % 2
                    kt0 = kbuf[:, tk0 // 4, (tk0 % 4) * 128:(tk0 % 4) * 128 + 128]
                    kt1 = kbuf[:, tk1 // 4, (tk1 % 4) * 128:(tk1 % 4) * 128 + 128]
                    kdeps = [kbufk[tk0 // 4], kbufk[tk1 // 4]]
                    for ci, (tq, b0) in enumerate(((tqA, 0), (tqB, 2))):
                        qdep = [qselk[tq]]

                        def smm(e, tq=tq, b0=b0, kt0=kt0, kt1=kt1, qsel=qsel):
                            e.matmul(cx.pb[b0][:], kt0, qsel[:, tq, :],
                                     start=True, stop=True)
                            return e.matmul(cx.pb[b0 + 1][:], kt1, qsel[:, tq, :],
                                            start=True, stop=True)
                        p.op("pe", smm, r=kdeps + qdep,
                             w=[cx.pbk[b0], cx.pbk[b0 + 1]])
                        p.op("ac", lambda e, b0=b0, ci=ci, par=par: e.activation(
                            ex[ci][par][:], cx.pbpair[b0 // 2][:], AF.Exp,
                            bias=negU_col, scale=cx.c["eps"][:, 3:4]),
                            r=[cx.pbk[b0], cx.pbk[b0 + 1], negUk, cx.ck["eps"]],
                            w=[exk[ci][par]])
                    if tkp >= 1:
                        emit_pv(tkp - 1)
                emit_pv(15)
                for ci, (tq, sb, rb, pvb, fo, fr) in enumerate(
                        ((tqA, 0, 0, 4, 3072, 6), (tqB, 2, 2, 5, 3584, 7))):
                    t0 = tq * 512
                    p.op("ac", lambda e, pvb=pvb, rb=rb: e.activation(
                        r1[rb][:], cx.pb[pvb][64:65, :], AF.Copy),
                        r=[cx.pbk[pvb]], w=[r1k[rb]])
                    p.op("dv", lambda e, rb=rb: e.reciprocal(r1[rb + 1][:], r1[rb][:]),
                         r=[r1k[rb]], w=[r1k[rb + 1]])
                    p.op("pe", lambda e, sb=sb, rb=rb: e.matmul(
                        cx.pb[sb][0:64, :], cx.c["ones_row"][0:1, 0:64],
                        r1[rb + 1][:], start=True, stop=True),
                        r=[r1k[rb + 1], cx.ck["ones_row"]], w=[cx.pbk[sb]])
                    bcs = cx.f32a if ci == 0 else cx.f32b
                    bck = cx.fa_t[5] if ci == 0 else cx.fb_t[5]
                    p.op("ac", lambda e, sb=sb, bcs=bcs: e.activation(
                        bcs[0:64, 2560:3072], cx.pb[sb][0:64, :], AF.Copy),
                        r=[cx.pbk[sb]], w=[bck])
                    p.op("dv", lambda e, pvb=pvb, fo=fo, bcs=bcs: e.tensor_tensor(
                        cx.f32a[0:64, fo:fo + 512], cx.pb[pvb][0:64, :],
                        bcs[0:64, 2560:3072], ALU.mult),
                        r=[cx.pbk[pvb], bck], w=[cx.fa_t[fr]])
                    p.op("po", lambda e, c=c, j=j, t0=t0, fo=fo, fr=fr: e.dma_start(
                        out=yT[c, j * 64:j * 64 + 64, t0:t0 + 512],
                        in_=cx.f32a[0:64, fo:fo + 512]),
                        r=[cx.fa_t[fr]], w=[yTk[c]], dma=True)



def build_layer2(cx, xT, xTk, xout, xoutk, token_major=False):
    p = cx.p
    eps = cx.c["eps"]

    mix, mixk = win(cx, 2, "mix", (128, 24), lambda i: np.concatenate([
        col_tile(np.asarray(i["L2_attn_mix_k"]).reshape(-1)),
        col_tile(np.asarray(i["L2_attn_mix_v"]).reshape(-1)),
        col_tile(np.asarray(i["L2_attn_mix_r"]).reshape(-1))], axis=1))
    wk_, wkk = win(cx, 2, "wk", (128, 8192), lambda i: tile_lhs(np.asarray(i["L2_attn_Wk"]).T))
    wv, wvk = win(cx, 2, "wv", (128, 8192), lambda i: tile_lhs(np.asarray(i["L2_attn_Wv"]).T))
    wr, wrk = win(cx, 2, "wr", (128, 8192), lambda i: tile_lhs(np.asarray(i["L2_attn_Wr"]).T))
    wo, wok = win(cx, 2, "wo", (128, 8192), lambda i: tile_lhs(np.asarray(i["L2_attn_Wo"]).T))
    tdf, tdfk = win(cx, 2, "tdf", (1, 32), lambda i: np.concatenate([
        np.asarray(i["L2_attn_time_decay"]), np.asarray(i["L2_attn_time_first"])]).reshape(1, 32))
    lncols, lncolsk = win(cx, 2, "lncols", (128, 32), lambda i: np.concatenate([
        col_tile(i["L2_ffn_ln1_w"]), col_tile(i["L2_ffn_ln1_b"]),
        col_tile(i["L2_ffn_ln2_w"]), col_tile(i["L2_ffn_ln2_b"])], axis=1))
    wqkv, wqkvk = win(cx, 2, "wqkv", (3, 128, 8192),
                      lambda i: np.stack([tile_lhs(np.asarray(i["L2_ffn_w_qkv"])[:, part * 1024:(part + 1) * 1024])
                                          for part in range(3)]))
    wproj, wprojk = win(cx, 2, "wproj", (128, 8192), lambda i: tile_lhs(np.asarray(i["L2_ffn_w_proj"])))
    gw, gwk = win(cx, 2, "gw", (128, 8, 8), lambda i: tile_rhs(np.asarray(i["L2_ffn_gate_w"])))
    gb, gbk = win(cx, 2, "gb", (1, 8), lambda i: np.asarray(i["L2_ffn_gate_b"]).reshape(1, 8))
    w1bf, w1k = precast(cx, 2, "w1", (8, 4, 128, 8, 8, 128), lambda i: np.ascontiguousarray(
        np.asarray(i["L2_ffn_w1"]).reshape(8, 8, 128, 4, 8, 128).transpose(0, 3, 2, 1, 4, 5)), n_parts=8)
    w2bf, w2k = precast(cx, 2, "w2", (8, 2, 4, 128, 8, 4, 128), lambda i: np.ascontiguousarray(
        np.asarray(i["L2_ffn_w2"]).reshape(8, 4, 8, 128, 2, 4, 128).transpose(0, 4, 1, 3, 2, 5, 6)), n_parts=8)

    xn32, xn32k = dram(cx, "l2_xn32", (CT, 128, T), n_trk=NTT)
    xnbf, xnbfk = dram(cx, "l2_xnbf", (CT, 128, T), BF16, n_trk=NTT)
    xkin, xkink = dram(cx, "l2_xkin", (CT, 128, T), BF16, n_trk=1)
    xvin, xvink = dram(cx, "l2_xvin", (CT, 128, T), BF16, n_trk=1)
    xrin, xrink = dram(cx, "l2_xrin", (CT, 128, T), BF16, n_trk=1)
    kT32, kT32k = dram(cx, "l2_kT32", (CT, 128, T), n_trk=CT)
    vT32, vT32k = dram(cx, "l2_vT32", (CT, 128, T), n_trk=CT)
    rT32, rT32k = dram(cx, "l2_rT32", (CT, 128, T), n_trk=CT)
    rwkvbf, rwkvbfk = dram(cx, "l2_rwkvbf", (CT, 128, T), BF16, n_trk=CT)
    sT, sTk = dram(cx, "l2_sT", (CT, 128, T), n_trk=CT)
    x132, x132k = dram(cx, "l2_x132", (CT, 128, T), n_trk=NTT)
    x1bf, x1bfk = dram(cx, "l2_x1bf", (CT, 128, T), BF16, n_trk=NTT)
    qpad2, qpad2k = dram(cx, "l2_qpad", (CT, 2, 128, T), BF16, n_trk=CT)
    kbf2, kbf2k = dram(cx, "l2_kbf", (CT, 128, T), BF16, n_trk=CT)
    vbf2, vbf2k = dram(cx, "l2_vbf", (CT, 128, T), BF16, n_trk=CT)
    yT2, yT2k = dram(cx, "l2_yT", (CT, 128, T), BF16, n_trk=CT)
    z1, z1k = dram(cx, "l2_z1", (CT, 128, T), n_trk=NTT)
    x232, x232k = dram(cx, "l2_x232", (CT, 128, T), n_trk=NTT)
    x2bf, x2bfk = dram(cx, "l2_x2bf", (CT, 128, T), BF16, n_trk=NTT)
    gR2, gR2k = dram(cx, "l2_gR", (16, T), n_trk=1)
    z2T, z2Tk = dram(cx, "l2_z2T", (CT, 128, T), n_trk=CT)
    hE2, hE2k = dram(cx, "l2_hE", (32, 128, T), BF16, n_trk=8)

    mix_sb = p.sbuf("l2_mix_sb", [128, 24])
    mixsk = Trk()
    p.op("sy", lambda e: e.dma_start(out=mix_sb[:], in_=mix), r=[mixk], w=[mixsk], dma=True)
    ln_sb = p.sbuf("l2_ln_sb", [128, 32])
    lnsk = Trk()
    p.op("sy", lambda e: e.dma_start(out=ln_sb[:], in_=lncols), r=[lncolsk], w=[lnsk], dma=True)
    gw_sb = p.sbuf("l2_gw_sb", [128, 8, 8], F32)
    gwsk = Trk()
    p.op("sy", lambda e: e.dma_start(out=gw_sb[:], in_=gw), r=[gwk], w=[gwsk], dma=True)
    gb_sb = p.sbuf("l2_gb_sb", [1, 8])
    gbsk = Trk()
    p.op("sy", lambda e: e.dma_start(out=gb_sb[:], in_=gb), r=[gbk], w=[gbsk], dma=True)
    tdf_sb = p.sbuf("l2_tdf_sb", [1, 32])
    tdfsk = Trk()
    p.op("sy", lambda e: e.dma_start(out=tdf_sb[:], in_=tdf), r=[tdfk], w=[tdfsk], dma=True)

    ln_pass(cx, xT, xTk, xn32, xn32k, xnbf, xnbfk)
    mix_pass(cx, xn32, xn32k, mix_sb[:, 0:8], mixsk, xkin, xkink)
    mix_pass(cx, xn32, xn32k, mix_sb[:, 8:16], mixsk, xvin, xvink)
    mix_pass(cx, xn32, xn32k, mix_sb[:, 16:24], mixsk, xrin, xrink)
    wav = cx.wa[:].rearrange("p (k c f) -> p k c f", k=8, f=128)
    wbv = cx.wb[:].rearrange("p (k c f) -> p k c f", k=8, f=128)

    def f32_epil(dst, dstk, func):
        def epil(c, tt, b):
            freg = 4 + (c + CT * tt) % 4
            p.op("ac", lambda e, b=b, freg=freg: e.activation(
                cx.f32a[:, freg * 512:freg * 512 + 512], cx.pb[b][:], func),
                r=[cx.pbk[b]], w=[cx.fa_t[freg]])
            p.op("po" if freg % 2 == 0 else "ac", lambda e, c=c, tt=tt, freg=freg: e.dma_start(
                out=dst[c, :, tt * 512:tt * 512 + 512],
                in_=cx.f32a[:, freg * 512:freg * 512 + 512]),
                r=[cx.fa_t[freg]], w=[dstk[c % len(dstk)]], dma=True)
        return epil

    load_w(cx, cx.wa[:], wk_, wkk, cx.wak)
    mm_cm(cx, wav, cx.wak, xkin, [xkink[0]], f32_epil(kT32, kT32k, AF.Copy))
    load_w(cx, cx.wb[:], wv, wvk, cx.wbk)
    mm_cm(cx, wbv, cx.wbk, xvin, [xvink[0]], f32_epil(vT32, vT32k, AF.Copy), banks=(2, 3))
    load_w(cx, cx.wa[:], wr, wrk, cx.wak)
    mm_cm(cx, wav, cx.wak, xrin, [xrink[0]], f32_epil(rT32, rT32k, AF.Sigmoid))

    big = cx.big
    Xtok = big[:, 0:8320].bitcast(F32).rearrange("p (a b) -> p a b", b=1040)
    Xtokk = regt(cx.big_t, 0, 8320)
    rwkv_sb = big[:, 8704:12800].rearrange("p (a b) -> p a b", b=512)
    rwkk = regt(cx.big_t, 8704, 12800)
    Dv = big[:, 12800:16896].bitcast(F32).rearrange("p (h f) -> p h f", f=128)
    Dvk = regt(cx.big_t, 12800, 16896)
    srow = big[0:1, 16896:18976].bitcast(F32)
    srowk = regt(cx.big_t, 16896, 18976)
    crow = big[0:1, 18976:21056].bitcast(F32)
    crowk = regt(cx.big_t, 18976, 21056)
    ewCrow = big[0:1, 21056:23136].bitcast(F32)
    ewCk = regt(cx.big_t, 21056, 23136)
    lnew_bc = cx.sm[:, 128:144]
    lnewk = cx.sm_t[2]
    euh_bc = cx.sm[:, 192:208]
    euhk = cx.sm_t[3]
    ewvC = cx.sm[:, 256:272]
    ewvCk = cx.sm_t[4]
    r1, r1k = cx.r1, cx.r1k

    p.op("ac", lambda e: e.activation(r1[0][:, 0:16], tdf_sb[:, 0:16], AF.Exp),
         r=[tdfsk], w=[r1k[0]])
    p.op("dv", lambda e: e.tensor_scalar(r1[0][:, 0:16], r1[0][:, 0:16], -1.0, None, ALU.mult),
         r=[r1k[0]], w=[r1k[0]])
    p.op("ac", lambda e: e.activation(r1[1][:, 0:16], tdf_sb[:, 16:32], AF.Exp),
         r=[tdfsk], w=[r1k[1]])
    p.op("pe", lambda e: e.matmul(cx.pb[5][:, 0:16], cx.c["ones_row"][:], r1[0][:, 0:16],
                                  start=True, stop=True),
         r=[r1k[0], cx.ck["ones_row"]], w=[cx.pbk[5]])
    p.op("ac", lambda e: e.activation(lnew_bc, cx.pb[5][:, 0:16], AF.Copy),
         r=[cx.pbk[5]], w=[lnewk])
    p.op("pe", lambda e: e.matmul(cx.pb[5][:, 0:16], cx.c["ones_row"][:], r1[1][:, 0:16],
                                  start=True, stop=True),
         r=[r1k[1], cx.ck["ones_row"]], w=[cx.pbk[5]])
    p.op("ac", lambda e: e.activation(euh_bc, cx.pb[5][:, 0:16], AF.Copy),
         r=[cx.pbk[5]], w=[euhk])
    for h in range(16):
        p.op("ac", lambda e, h=h: e.activation(
            r1[h // 4][:, (h % 4) * 128:(h % 4) * 128 + 128], cx.c["iota_row"][:],
            AF.Exp, scale=lnew_bc[0:1, h:h + 1]),
            r=[lnewk, cx.ck["iota_row"]], w=[r1k[h // 4]])
        p.op("ac", lambda e, h=h: e.activation(
            ewvC[:, h:h + 1], cx.c["r127mj_col"][:], AF.Exp, scale=lnew_bc[:, h:h + 1]),
            r=[lnewk, cx.ck["r127mj_col"]], w=[ewvCk])
    p.op("ac", lambda e: e.activation(cx.sm[0:1, 320:336], lnew_bc[0:1, 0:16], AF.Exp,
                                      scale=eps[0:1, 4:5]),
         r=[lnewk, cx.ck["eps"]], w=[cx.sm_t[5]])
    p.op("dv", lambda e: e.tensor_copy(
        ewCrow.rearrange("p (h x) -> p h x", x=65),
        cx.sm[0:1, 320:336].rearrange("p h -> p h 1" if False else "p (h o) -> p h o", o=1).broadcast_to((1, 16, 65))),
        r=[cx.sm_t[5]], w=ewCk)
    for h in range(16):
        p.op("dv", lambda e, h=h: e.tensor_scalar(
            cx.f32a[:, 0:128], cx.c["pmatT"][:], lnew_bc[:, h:h + 1], None, ALU.mult),
            r=[lnewk, cx.ck["pmatT"]], w=[cx.fa_t[0]])
        p.op("dv", lambda e: e.tensor_scalar(cx.f32a[:, 0:128], cx.f32a[:, 0:128],
                                             80.0, None, ALU.min),
             r=[cx.fa_t[0]], w=[cx.fa_t[0]])
        p.op("ac", lambda e: e.activation(cx.f32a[:, 128:256], cx.f32a[:, 0:128], AF.Exp),
             r=[cx.fa_t[0]], w=[cx.fa_t[0]])
        p.op("dv", lambda e, h=h: e.tensor_tensor(Dv[:, h, :], cx.f32a[:, 128:256],
                                                  cx.c["strictT"][:], ALU.mult),
             r=[cx.fa_t[0], cx.ck["strictT"]], w=Dvk)
    p.op("dv", lambda e: e.memset(srow[:], 0.0), w=srowk)

    for g in range(NTT):
        t0 = g * 512
        ek = cx.f32a[:].rearrange("p (c f) -> p c f", f=512)
        vv = cx.f32b[:].rearrange("p (c f) -> p c f", f=512)
        Xc = cx.acc[:, 0:4096].rearrange("p (c f) -> p c f", f=512)
        wkvt = cx.acc[:, 4096:8192].rearrange("p (a b) -> p a b", b=1024)
        p.op("sy", lambda e, t0=t0, ek=ek: e.dma_start(
            out=ek[:, :, :], in_=kT32[:, :, t0:t0 + 512].rearrange("c p t -> p c t")),
            r=list(kT32k), w=cx.fa_t, dma=True)
        p.op("sy", lambda e, t0=t0, vv=vv: e.dma_start(
            out=vv[:, :, :], in_=vT32[:, :, t0:t0 + 512].rearrange("c p t -> p c t")),
            r=list(vT32k), w=cx.fb_t, dma=True)
        for c in range(CT):
            p.op("ac", lambda e, c=c, ek=ek: e.activation(ek[:, c, :], ek[:, c, :], AF.Exp),
                 r=[cx.fa_t[c]], w=[cx.fa_t[c]])
            p.op("dv", lambda e, c=c, ek=ek, vv=vv, Xc=Xc: e.tensor_tensor(
                Xc[:, c, :], ek[:, c, :], vv[:, c, :], ALU.mult),
                r=[cx.fa_t[c], cx.fb_t[c]], w=[cx.acc_t[c]])
        for blk in range(4):
            for c in range(CT):
                p.op("pe", lambda e, blk=blk, c=c, ek=ek: e.matmul(
                    cx.pb[1][:, blk * 16 + 2 * c:blk * 16 + 2 * c + 2],
                    ek[:, c, blk * 128:blk * 128 + 128], cx.c["bo2"][:],
                    start=True, stop=True),
                    r=[cx.fa_t[c], cx.ck["bo2"]], w=[cx.pbk[1]])
        for blk in range(4):
            for c in range(CT):
                bank = 4 + c % 2
                p.op("pe", lambda e, blk=blk, c=c, bank=bank, Xc=Xc: e.transpose(
                    cx.pb[bank][:, 0:128], Xc[:, c, blk * 128:blk * 128 + 128],
                    cx.c["ident"][:]),
                    r=[cx.acc_t[c], cx.ck["ident"]], w=[cx.pbk[bank]])
                for j in range(2):
                    p.op("dv", lambda e, blk=blk, c=c, j=j, bank=bank: e.tensor_copy(
                        Xtok[:, blk, (2 * c + j) * 65:(2 * c + j) * 65 + 64],
                        cx.pb[bank][:, j * 64:j * 64 + 64]),
                        r=[cx.pbk[bank]], w=Xtokk)
            p.op("ac", lambda e, blk=blk: e.activation(
                Xtok[:, blk, :].rearrange("p (h x) -> p h x", x=65)[:, :, 64],
                cx.pb[1][:, blk * 16:blk * 16 + 16], AF.Copy),
                r=[cx.pbk[1]], w=Xtokk)
        for blk in range(4):
            for h in range(16):
                nb = 2 + h // 8
                ncol = (h % 8) * 64
                p.op("pe", lambda e, blk=blk, h=h, nb=nb, ncol=ncol: e.matmul(
                    cx.pb[nb][:, ncol:ncol + 64], Dv[:, h, :],
                    Xtok[:, blk, h * 65:h * 65 + 64], start=True, stop=False),
                    r=Dvk + Xtokk, w=[cx.pbk[nb]])
                p.op("pe", lambda e, blk=blk, h=h, nb=nb, ncol=ncol: e.matmul(
                    cx.pb[nb][:, ncol:ncol + 64],
                    r1[h // 4][:, (h % 4) * 128:(h % 4) * 128 + 128],
                    srow[:, h * 65:h * 65 + 64], start=False, stop=True),
                    r=[r1k[h // 4]] + srowk, w=[cx.pbk[nb]])
                p.op("pe", lambda e, blk=blk, h=h: e.matmul(
                    cx.pb[0][:, h:h + 1], Dv[:, h, :],
                    Xtok[:, blk, h * 65 + 64:h * 65 + 65], start=True, stop=False),
                    r=Dvk + Xtokk, w=[cx.pbk[0]])
                p.op("pe", lambda e, blk=blk, h=h: e.matmul(
                    cx.pb[0][:, h:h + 1],
                    r1[h // 4][:, (h % 4) * 128:(h % 4) * 128 + 128],
                    srow[:, h * 65 + 64:h * 65 + 65], start=False, stop=True),
                    r=[r1k[h // 4]] + srowk, w=[cx.pbk[0]])
            dn = cx.sm[:, 384:400]
            dnk = cx.sm_t[6]
            p.op("dv", lambda e, blk=blk: e.tensor_tensor(
                dn, Xtok[:, blk, :].rearrange("p (h x) -> p h x", x=65)[:, :, 64],
                euh_bc, ALU.mult), r=Xtokk + [euhk], w=[dnk])
            p.op("dv", lambda e: e.tensor_scalar(dn, dn, 1e-6, None, ALU.max),
                 r=[dnk], w=[dnk])
            p.op("dv", lambda e: e.tensor_tensor(dn, dn, cx.pb[0][:, 0:16], ALU.add),
                 r=[dnk, cx.pbk[0]], w=[dnk])
            p.op("dv", lambda e: e.reciprocal(cx.sm[:, 448:464], dn),
                 r=[dnk], w=[cx.sm_t[7]])
            for bi in range(2):
                p.op("dv", lambda e, blk=blk, bi=bi: e.tensor_tensor(
                    wkvt[:, blk, bi * 512:bi * 512 + 512].rearrange("p (h x) -> p h x", x=64),
                    Xtok[:, blk, bi * 520:bi * 520 + 520].rearrange("p (h x) -> p h x", x=65)[:, :, 0:64],
                    euh_bc[:, bi * 8:bi * 8 + 8].unsqueeze(2).broadcast_to((128, 8, 64)),
                    ALU.mult), r=Xtokk + [euhk], w=[cx.acc_t[8 + blk * 2 + bi]])
                p.op("dv", lambda e, blk=blk, bi=bi: e.tensor_tensor(
                    wkvt[:, blk, bi * 512:bi * 512 + 512],
                    wkvt[:, blk, bi * 512:bi * 512 + 512], cx.pb[2 + bi][:], ALU.add),
                    r=[cx.pbk[2 + bi], cx.acc_t[8 + blk * 2 + bi]],
                    w=[cx.acc_t[8 + blk * 2 + bi]])
                p.op("dv", lambda e, blk=blk, bi=bi: e.tensor_tensor(
                    wkvt[:, blk, bi * 512:bi * 512 + 512].rearrange("p (h x) -> p h x", x=64),
                    wkvt[:, blk, bi * 512:bi * 512 + 512].rearrange("p (h x) -> p h x", x=64),
                    cx.sm[:, 448 + bi * 8:448 + bi * 8 + 8].unsqueeze(2).broadcast_to((128, 8, 64)),
                    ALU.mult), r=[cx.sm_t[7], cx.acc_t[8 + blk * 2 + bi]],
                    w=[cx.acc_t[8 + blk * 2 + bi]])
            for h in range(16):
                p.op("pe", lambda e, blk=blk, h=h: e.matmul(
                    cx.pb[0][0:1, h * 32:h * 32 + 32] if False else cx.pb[0][0:1, 448 + h:448 + h + 1],
                    ewvC[:, h:h + 1], Xtok[:, blk, h * 65 + 64:h * 65 + 65],
                    start=True, stop=True),
                    r=[ewvCk] + Xtokk, w=[cx.pbk[0]])
                cb = 1 + h // 8
                p.op("pe", lambda e, blk=blk, h=h, cb=cb: e.matmul(
                    cx.pb[cb][0:1, (h % 8) * 64:(h % 8) * 64 + 64],
                    ewvC[:, h:h + 1], Xtok[:, blk, h * 65:h * 65 + 64],
                    start=True, stop=True),
                    r=[ewvCk] + Xtokk, w=[cx.pbk[cb]])
            for cb in (1, 2):
                p.op("ac", lambda e, cb=cb: e.activation(
                    crow.rearrange("p (h x) -> p h x", x=65)[:, (cb - 1) * 8:(cb - 1) * 8 + 8, 0:64],
                    cx.pb[cb][0:1, :].rearrange("p (h x) -> p h x", x=64),
                    AF.Copy), r=[cx.pbk[cb]], w=crowk)
            p.op("ac", lambda e: e.activation(
                crow.rearrange("p (h x) -> p h x", x=65)[:, :, 64],
                cx.pb[0][0:1, 448:464], AF.Copy), r=[cx.pbk[0]], w=crowk)
            p.op("dv", lambda e: e.tensor_tensor(srow[:], srow[:], ewCrow[:], ALU.mult),
                 r=ewCk + srowk, w=srowk)
            p.op("dv", lambda e: e.tensor_tensor(srow[:], srow[:], crow[:], ALU.add),
                 r=crowk + srowk, w=srowk)
        p.op("sy", lambda e, t0=t0, vv=vv: e.dma_start(
            out=vv[:, :, :], in_=rT32[:, :, t0:t0 + 512].rearrange("c p t -> p c t")),
            r=list(rT32k), w=cx.fb_t, dma=True)
        for blk in range(4):
            for c in range(CT):
                bank = 4 + c % 2
                p.op("pe", lambda e, blk=blk, c=c, bank=bank, wkvt=wkvt: e.transpose(
                    cx.pb[bank][:, 0:128], wkvt[:, blk, c * 128:c * 128 + 128],
                    cx.c["ident"][:]),
                    r=[cx.acc_t[8 + blk * 2 + c // 4], cx.ck["ident"]], w=[cx.pbk[bank]])
                p.op("dv", lambda e, blk=blk, c=c, bank=bank, vv=vv: e.tensor_tensor(
                    rwkv_sb[:, c, blk * 128:blk * 128 + 128], cx.pb[bank][:, 0:128],
                    vv[:, c, blk * 128:blk * 128 + 128], ALU.mult),
                    r=[cx.pbk[bank], cx.fb_t[c]], w=[rwkk[c]])
        for c in range(CT):
            p.op("po", lambda e, c=c, t0=t0: e.dma_start(
                out=rwkvbf[c, :, t0:t0 + 512], in_=rwkv_sb[:, c, :]),
                r=[rwkk[c]], w=[rwkvbfk[c]], dma=True)

    load_w(cx, cx.wb[:], wo, wok, cx.wbk)
    mm_cm(cx, wbv, cx.wbk, rwkvbf, rwkvbfk, f32_epil(sT, sTk, AF.Copy), banks=(2, 3), src_by_c=True)

    ln_pass(cx, xn32, xn32k, x132, x132k, x1bf, x1bfk,
            gamma_col=ln_sb[:, 0:8], beta_col=ln_sb[:, 8:16], gbk=[lnsk])

    def qpad_epil(c, tt, b):
        wreg = (c + CT * tt) % 4
        p.op("ac", lambda e, b=b, wreg=wreg: e.activation(
            cx.wc[:, wreg * 512:wreg * 512 + 512], cx.pb[b][:], AF.Copy),
            r=[cx.pbk[b]], w=[cx.wck[wreg]])
        t0 = tt * 512
        p.op("po", lambda e, c=c, t0=t0, wreg=wreg: e.dma_start(
            out=qpad2[c, 0, 0:64, t0:t0 + 512], in_=cx.wc[0:64, wreg * 512:wreg * 512 + 512]),
            r=[cx.wck[wreg]], w=[qpad2k[c]], dma=True)
        p.op("po", lambda e, c=c, t0=t0, wreg=wreg: e.dma_start(
            out=qpad2[c, 1, 64:128, t0:t0 + 512], in_=cx.wc[64:128, wreg * 512:wreg * 512 + 512]),
            r=[cx.wck[wreg]], w=[qpad2k[c]], dma=True)
        p.op("po", lambda e, c=c, t0=t0: e.dma_start(
            out=qpad2[c, 0, 64:128, t0:t0 + 512], in_=cx.zero_bf[64:128, :]),
            r=[cx.ck_zero], w=[qpad2k[c]], dma=True)
        p.op("po", lambda e, c=c, t0=t0: e.dma_start(
            out=qpad2[c, 1, 0:64, t0:t0 + 512], in_=cx.zero_bf[0:64, :]),
            r=[cx.ck_zero], w=[qpad2k[c]], dma=True)

    def bf_epil(dst, dstk):
        def epil(c, tt, b):
            wreg = (c + CT * tt) % 4
            p.op("ac", lambda e, b=b, wreg=wreg: e.activation(
                cx.wc[:, wreg * 512:wreg * 512 + 512], cx.pb[b][:], AF.Copy),
                r=[cx.pbk[b]], w=[cx.wck[wreg]])
            p.op("po" if wreg % 2 == 0 else "ac", lambda e, c=c, tt=tt, wreg=wreg: e.dma_start(
                out=dst[c, :, tt * 512:tt * 512 + 512],
                in_=cx.wc[:, wreg * 512:wreg * 512 + 512]),
                r=[cx.wck[wreg]], w=[dstk[c]], dma=True)
        return epil

    for part, epil in ((0, qpad_epil), (1, bf_epil(kbf2, kbf2k)), (2, bf_epil(vbf2, vbf2k))):
        p.op("po", lambda e, part=part: e.dma_start(
            out=cx.wa[:], in_=wqkv[part]), r=[wqkvk], w=[cx.wak], dma=True)
        mm_cm(cx, wav, cx.wak, x1bf, x1bfk, epil,
              banks=((0, 1) if part % 2 == 0 else (2, 3)))

    negU = cx.sm[:, 512:513]
    negUk = cx.sm_t[8]
    u_bound_pass(cx, qpad2, qpad2k, kbf2, kbf2k, negU, negUk)
    msa_pass(cx, qpad2, qpad2k, kbf2, kbf2k, vbf2, vbf2k, negU, negUk, yT2, yT2k)

    load_w(cx, cx.wa[:], wproj, wprojk, cx.wak)

    def z1_epil(c, tt, b):
        freg = 4 + (c + CT * tt) % 4
        t0 = tt * 512
        p.op("sy", lambda e, c=c, t0=t0, freg=freg: e.dma_start(
            out=cx.f32b[:, freg * 512:freg * 512 + 512], in_=xn32[c, :, t0:t0 + 512]),
            r=[xn32k[tt]], w=[cx.fb_t[freg]], dma=True)
        p.op("dv", lambda e, b=b, freg=freg: e.tensor_tensor(
            cx.f32a[:, freg * 512:freg * 512 + 512], cx.pb[b][:],
            cx.f32b[:, freg * 512:freg * 512 + 512], ALU.add),
            r=[cx.pbk[b], cx.fb_t[freg]], w=[cx.fa_t[freg]])
        p.op("po", lambda e, c=c, t0=t0, freg=freg: e.dma_start(
            out=z1[c, :, t0:t0 + 512], in_=cx.f32a[:, freg * 512:freg * 512 + 512]),
            r=[cx.fa_t[freg]], w=[z1k[tt]], dma=True)
    mm_cm(cx, wav, cx.wak, yT2, yT2k, z1_epil, src_by_c=True)

    ln_pass(cx, z1, z1k, x232, x232k, x2bf, x2bfk,
            gamma_col=ln_sb[:, 16:24], beta_col=ln_sb[:, 24:32], gbk=[lnsk])

    for ch in range(NTT):
        t0 = ch * 512
        xnc = cx.f32a[:].rearrange("p (c f) -> p c f", f=512)
        p.op("sy", lambda e, t0=t0, xnc=xnc: e.dma_start(
            out=xnc[:, :, :], in_=x232[:, :, t0:t0 + 512].rearrange("c p t -> p c t")),
            r=[x232k[ch]], w=cx.fa_t, dma=True)
        for blk in range(4):
            bo = blk * 128

            def rmm(e, bo=bo, xnc=xnc):
                last = None
                for k in range(CT):
                    last = e.matmul(cx.pb[0][:, 0:8], xnc[:, k, bo:bo + 128],
                                    gw_sb[:, k, :], start=(k == 0), stop=False)
                last = e.matmul(cx.pb[0][:, 0:8], cx.c["ones_row"][:], gb_sb[:],
                                start=False, stop=True)
                return last
            p.op("pe", rmm, r=cx.fa_t + [gwsk, gbsk, cx.ck["ones_row"]], w=[cx.pbk[0]])
            lg = cx.sm[:, 896:912]
            lgk = cx.sm_t[14]
            mx = cx.sm[:, 960:961]
            mxk = cx.sm_t[15]
            p.op("ac", lambda e: e.activation(lg[:, 0:8], cx.pb[0][:, 0:8], AF.Copy),
                 r=[cx.pbk[0]], w=[lgk])
            p.op("dv", lambda e: e.reduce_max(mx, lg[:, 0:8], axis=AX.X),
                 r=[lgk], w=[mxk])
            p.op("dv", lambda e: e.tensor_scalar(lg[:, 0:8], lg[:, 0:8], mx, None, ALU.is_ge),
                 r=[lgk, mxk], w=[lgk])
            p.op("dv", lambda e: e.memset(lg[:, 8:16], 0.0), r=[lgk], w=[lgk])
            p.op("pe", lambda e: e.transpose(cx.pb[1][0:16, 0:128], lg[:, 0:16],
                                             cx.c["ident"][:]),
                 r=[lgk, cx.ck["ident"]], w=[cx.pbk[1]])
            p.op("ac", lambda e: e.activation(cx.f32b[0:16, 3584:3712], cx.pb[1][0:16, 0:128], AF.Copy),
                 r=[cx.pbk[1]], w=[cx.fb_t[7]])
            p.op("po", lambda e, t0=t0, bo=bo: e.dma_start(
                out=gR2[:, t0 + bo:t0 + bo + 128], in_=cx.f32b[0:16, 3584:3712]),
                r=[cx.fb_t[7]], w=[gR2k[0]], dma=True)

    for c in range(CT):
        p.op("po", lambda e, c=c: e.dma_start(out=z2T[c], in_=z1[c]),
             r=list(z1k), w=[z2Tk[c]], dma=True)
    moe_pass(cx, x2bf, x2bfk, gR2, gR2k, w1bf, w1k, w2bf, w2k, hE2, hE2k,
             z2T, z2Tk, AF.Gelu_apprx_tanh, 8)

    combine_pass(cx, xT, xTk, sT, sTk, z2T, z2Tk, xout, xoutk, token_major)





def build_layer3(cx, xT, xTk, xout, xoutk, token_major=False):
    p = cx.p
    eps = cx.c["eps"]

    def qkv_perm(which):
        idx = np.arange(16)[:, None] * 192 + which * 64 + np.arange(64)[None, :]
        return idx.reshape(-1)

    wqkv, wqkvk = win(cx, 3, "wqkv", (3, 128, 8192), lambda i: np.stack([
        tile_lhs(np.asarray(i["L3_attn_W_qkv"]).T[:, qkv_perm(w)]) for w in range(3)]))
    bqkv, bqkvk = win(cx, 3, "bqkv", (128, 24), lambda i: np.concatenate([
        col_tile(np.asarray(i["L3_attn_b_qkv"])[qkv_perm(w)]) for w in range(3)], axis=1))
    wout, woutk = win(cx, 3, "wout", (128, 8192), lambda i: tile_lhs(np.asarray(i["L3_attn_W_out"]).T))
    bout, boutk = win(cx, 3, "bout", (128, 8), lambda i: col_tile(i["L3_attn_b_out"]))
    wg3, wg3k = win(cx, 3, "wg3", (128, 8, 16), lambda i: tile_rhs(np.asarray(i["L3_ffn_Wg"]).T))
    w1bf, w1k = precast(cx, 3, "w1", (16, 128, 8192), lambda i: np.stack([
        tile_lhs(np.asarray(i["L3_ffn_W1"])[e].T) for e in range(16)]), n_parts=8)
    w2bf, w2k = precast(cx, 3, "w2", (16, 128, 8192), lambda i: np.stack([
        tile_lhs(np.asarray(i["L3_ffn_W2"])[e].T) for e in range(16)]), n_parts=8)
    w3bf, w3k = precast(cx, 3, "w3", (16, 128, 8192), lambda i: np.stack([
        tile_lhs(np.asarray(i["L3_ffn_W3"])[e].T) for e in range(16)]), n_parts=8)
    b13, b13k = win(cx, 3, "b13", (16, 128, 16), lambda i: np.stack([
        np.concatenate([col_tile(np.asarray(i["L3_ffn_b1"])[e]),
                        col_tile(np.asarray(i["L3_ffn_b3"])[e])], axis=1) for e in range(16)]))
    b2c, b2ck = win(cx, 3, "b2c", (16, 128, 8), lambda i: np.stack([
        col_tile(np.asarray(i["L3_ffn_b2"])[e]) for e in range(16)]))

    xn32, xn32k = dram(cx, "l3_xn32", (CT, 128, T), n_trk=NTT)
    xnbf, xnbfk = dram(cx, "l3_xnbf", (CT, 128, T), BF16, n_trk=NTT)
    qpad3, qpad3k = dram(cx, "l3_qpad", (CT, 2, 128, T), BF16, n_trk=CT)
    kbf3, kbf3k = dram(cx, "l3_kbf", (CT, 128, T), BF16, n_trk=CT)
    vbf3, vbf3k = dram(cx, "l3_vbf", (CT, 128, T), BF16, n_trk=CT)
    yT3, yT3k = dram(cx, "l3_yT", (CT, 128, T), BF16, n_trk=CT)
    sT, sTk = dram(cx, "l3_sT", (CT, 128, T), n_trk=CT)
    gR3, gR3k = dram(cx, "l3_gR", (16, T), n_trk=1)
    hbf3, hbf3k = dram(cx, "l3_hbf", (CT, 128, T), BF16, n_trk=CT)
    g0T, g0Tk = dram(cx, "l3_g0T", (CT, 128, T), n_trk=CT)

    bq_sb = p.sbuf("l3_bq_sb", [128, 24])
    bqsk = Trk()
    p.op("sy", lambda e: e.dma_start(out=bq_sb[:], in_=bqkv), r=[bqkvk], w=[bqsk], dma=True)
    bo_sb = p.sbuf("l3_bo_sb", [128, 8])
    bosk = Trk()
    p.op("sy", lambda e: e.dma_start(out=bo_sb[:], in_=bout), r=[boutk], w=[bosk], dma=True)
    wg_sb = p.sbuf("l3_wg_sb", [128, 8, 16], F32)
    wgsk = Trk()
    p.op("sy", lambda e: e.dma_start(out=wg_sb[:], in_=wg3), r=[wg3k], w=[wgsk], dma=True)
    b13_sb = p.sbuf("l3_b13_sb", [128, 16])
    b13sk = Trk()
    b2_sb = p.sbuf("l3_b2_sb", [128, 8])
    b2sk = Trk()

    ln_pass(cx, xT, xTk, xn32, xn32k, xnbf, xnbfk)

    wav = cx.wa[:].rearrange("p (k c f) -> p k c f", k=8, f=128)
    wbv = cx.wb[:].rearrange("p (k c f) -> p k c f", k=8, f=128)

    def qpad_epil3(c, tt, b):
        wreg = (c + CT * tt) % 4
        p.op("ac", lambda e, b=b, c=c, wreg=wreg: e.activation(
            cx.wc[:, wreg * 512:wreg * 512 + 512], cx.pb[b][:], AF.Identity,
            bias=bq_sb[:, c:c + 1]),
            r=[cx.pbk[b], bqsk], w=[cx.wck[wreg]])
        t0 = tt * 512
        p.op("po", lambda e, c=c, t0=t0, wreg=wreg: e.dma_start(
            out=qpad3[c, 0, 0:64, t0:t0 + 512], in_=cx.wc[0:64, wreg * 512:wreg * 512 + 512]),
            r=[cx.wck[wreg]], w=[qpad3k[c]], dma=True)
        p.op("po", lambda e, c=c, t0=t0, wreg=wreg: e.dma_start(
            out=qpad3[c, 1, 64:128, t0:t0 + 512], in_=cx.wc[64:128, wreg * 512:wreg * 512 + 512]),
            r=[cx.wck[wreg]], w=[qpad3k[c]], dma=True)
        p.op("po", lambda e, c=c, t0=t0: e.dma_start(
            out=qpad3[c, 0, 64:128, t0:t0 + 512], in_=cx.zero_bf[64:128, :]),
            r=[cx.ck_zero], w=[qpad3k[c]], dma=True)
        p.op("po", lambda e, c=c, t0=t0: e.dma_start(
            out=qpad3[c, 1, 0:64, t0:t0 + 512], in_=cx.zero_bf[0:64, :]),
            r=[cx.ck_zero], w=[qpad3k[c]], dma=True)

    def bias_bf_epil(dst, dstk, which):
        def epil(c, tt, b):
            wreg = (c + CT * tt) % 4
            p.op("ac", lambda e, b=b, c=c, wreg=wreg: e.activation(
                cx.wc[:, wreg * 512:wreg * 512 + 512], cx.pb[b][:], AF.Identity,
                bias=bq_sb[:, which * 8 + c:which * 8 + c + 1]),
                r=[cx.pbk[b], bqsk], w=[cx.wck[wreg]])
            p.op("po" if wreg % 2 == 0 else "ac", lambda e, c=c, tt=tt, wreg=wreg: e.dma_start(
                out=dst[c, :, tt * 512:tt * 512 + 512],
                in_=cx.wc[:, wreg * 512:wreg * 512 + 512]),
                r=[cx.wck[wreg]], w=[dstk[c]], dma=True)
        return epil

    for part, epil in ((0, qpad_epil3), (1, bias_bf_epil(kbf3, kbf3k, 1)),
                       (2, bias_bf_epil(vbf3, vbf3k, 2))):
        p.op("po", lambda e, part=part: e.dma_start(
            out=cx.wa[:], in_=wqkv[part]), r=[wqkvk], w=[cx.wak], dma=True)
        mm_cm(cx, wav, cx.wak, xnbf, xnbfk, epil,
              banks=((0, 1) if part % 2 == 0 else (2, 3)))

    negU = cx.sm[:, 512:513]
    negUk = cx.sm_t[8]
    u_bound_pass(cx, qpad3, qpad3k, kbf3, kbf3k, negU, negUk)
    msa_pass(cx, qpad3, qpad3k, kbf3, kbf3k, vbf3, vbf3k, negU, negUk, yT3, yT3k)

    load_w(cx, cx.wa[:], wout, woutk, cx.wak)

    def sT_epil(c, tt, b):
        freg = 4 + (c + CT * tt) % 4
        p.op("ac", lambda e, b=b, c=c, freg=freg: e.activation(
            cx.f32a[:, freg * 512:freg * 512 + 512], cx.pb[b][:], AF.Identity,
            bias=bo_sb[:, c:c + 1]),
            r=[cx.pbk[b], bosk], w=[cx.fa_t[freg]])
        p.op("po", lambda e, c=c, tt=tt, freg=freg: e.dma_start(
            out=sT[c, :, tt * 512:tt * 512 + 512],
            in_=cx.f32a[:, freg * 512:freg * 512 + 512]),
            r=[cx.fa_t[freg]], w=[sTk[c % len(sTk)]], dma=True)
    mm_cm(cx, wav, cx.wak, yT3, yT3k, sT_epil, src_by_c=True)

    for ch in range(NTT):
        t0 = ch * 512
        xnc = cx.f32a[:].rearrange("p (c f) -> p c f", f=512)
        for c in range(CT):
            p.op("sy", lambda e, c=c, t0=t0, xnc=xnc: e.dma_start(
                out=xnc[:, c, :], in_=xn32[c, :, t0:t0 + 512]),
                r=[xn32k[ch]], w=[cx.fa_t[c]], dma=True)
        for blk in range(4):
            bo = blk * 128

            def gmm(e, bo=bo, xnc=xnc):
                last = None
                for k in range(CT):
                    last = e.matmul(cx.pb[0][:, 0:16], xnc[:, k, bo:bo + 128],
                                    wg_sb[:, k, :], start=(k == 0), stop=(k == CT - 1))
                return last
            p.op("pe", gmm, r=cx.fa_t + [wgsk], w=[cx.pbk[0]])
            sg = cx.sm[:, 896:912]
            sgk = cx.sm_t[14]
            tmp = cx.sm[:, 912:928]
            g4 = cx.sm[:, 928:944]
            mx = cx.sm[:, 960:961]
            mxk = cx.sm_t[15]
            p.op("ac", lambda e: e.activation(sg, cx.pb[0][:, 0:16], AF.Sigmoid),
                 r=[cx.pbk[0]], w=[sgk])
            p.op("dv", lambda e: e.tensor_reduce(
                g4[:, 0:4], sg.rearrange("p (g x) -> p g x", x=4), AX.X, ALU.max),
                r=[sgk], w=[sgk])
            p.op("dv", lambda e: e.reduce_max(mx, g4[:, 0:4], axis=AX.X),
                 r=[sgk], w=[mxk])
            p.op("dv", lambda e: e.tensor_scalar(tmp[:, 0:4], g4[:, 0:4], mx, None, ALU.is_lt),
                 r=[sgk, mxk], w=[sgk])
            p.op("dv", lambda e: e.tensor_tensor(tmp[:, 4:8], g4[:, 0:4], tmp[:, 0:4], ALU.mult),
                 r=[sgk], w=[sgk])
            p.op("dv", lambda e: e.reduce_max(mx, tmp[:, 4:8], axis=AX.X),
                 r=[sgk], w=[mxk])
            p.op("dv", lambda e: e.tensor_scalar(g4[:, 0:4], g4[:, 0:4], mx, None, ALU.is_ge),
                 r=[sgk, mxk], w=[sgk])
            p.op("dv", lambda e: e.tensor_scalar(g4[:, 4:8], g4[:, 0:4], -1.0, 1e30,
                                                 ALU.add, ALU.mult),
                 r=[sgk], w=[sgk])
            p.op("dv", lambda e: e.tensor_tensor(
                tmp.rearrange("p (g x) -> p g x", x=4),
                sg.rearrange("p (g x) -> p g x", x=4),
                g4[:, 0:4].unsqueeze(2).broadcast_to((128, 4, 4)), ALU.mult),
                r=[sgk], w=[sgk])
            p.op("dv", lambda e: e.tensor_tensor(
                tmp.rearrange("p (g x) -> p g x", x=4),
                tmp.rearrange("p (g x) -> p g x", x=4),
                g4[:, 4:8].unsqueeze(2).broadcast_to((128, 4, 4)), ALU.add),
                r=[sgk], w=[sgk])
            p.op("dv", lambda e: e.tensor_copy(cx.sm[:, 944:960], tmp),
                 r=[sgk], w=[sgk])
            for it in range(3):
                p.op("dv", lambda e: e.reduce_max(mx, cx.sm[:, 944:960], axis=AX.X),
                     r=[sgk], w=[mxk])
                p.op("dv", lambda e: e.tensor_scalar(cx.sm[:, 880:896], cx.sm[:, 944:960], mx, None, ALU.is_ge),
                     r=[sgk, mxk], w=[cx.sm_t[13]])
                p.op("dv", lambda e: e.scalar_tensor_tensor(
                    cx.sm[:, 944:960], cx.sm[:, 880:896], -2e30, cx.sm[:, 944:960],
                    ALU.mult, ALU.add), r=[sgk, cx.sm_t[13]], w=[sgk])
            p.op("dv", lambda e: e.reduce_max(mx, cx.sm[:, 944:960], axis=AX.X),
                 r=[sgk], w=[mxk])
            p.op("dv", lambda e: e.tensor_scalar(cx.sm[:, 912:928], tmp, mx, None, ALU.is_ge),
                 r=[sgk, mxk], w=[sgk])
            p.op("dv", lambda e: e.tensor_tensor(cx.sm[:, 912:928], cx.sm[:, 912:928], sg, ALU.mult),
                 r=[sgk], w=[sgk])
            p.op("dv", lambda e: e.tensor_reduce(mx, cx.sm[:, 912:928].rearrange("p (o x) -> p o x", o=1),
                                                 AX.X, ALU.add),
                 r=[sgk], w=[mxk])
            p.op("dv", lambda e: e.reciprocal(mx, mx), r=[mxk], w=[mxk])
            p.op("dv", lambda e: e.tensor_scalar(cx.sm[:, 912:928], cx.sm[:, 912:928],
                                                 mx, 2.5, ALU.mult, ALU.mult),
                 r=[sgk, mxk], w=[sgk])
            p.op("pe", lambda e: e.transpose(cx.pb[1][0:16, 0:128], cx.sm[:, 912:928],
                                             cx.c["ident"][:]),
                 r=[sgk, cx.ck["ident"]], w=[cx.pbk[1]])
            p.op("ac", lambda e: e.activation(cx.f32b[0:16, 3584:3712], cx.pb[1][0:16, 0:128], AF.Copy),
                 r=[cx.pbk[1]], w=[cx.fb_t[7]])
            p.op("po", lambda e, t0=t0, bo=bo: e.dma_start(
                out=gR3[:, t0 + bo:t0 + bo + 128], in_=cx.f32b[0:16, 3584:3712]),
                r=[cx.fb_t[7]], w=[gR3k[0]], dma=True)

    gbc = cx.f32b[:, 0:512]
    gbck = cx.fb_t[0]
    grow = cx.r1[0]
    growk = cx.r1k[0]
    for e_ in range(16):
        p.op("sy", lambda e, e_=e_: e.dma_start(out=b13_sb[:], in_=b13[e_]),
             r=[b13k], w=[b13sk], dma=True)
        p.op("sy", lambda e, e_=e_: e.dma_start(out=b2_sb[:], in_=b2c[e_]),
             r=[b2ck], w=[b2sk], dma=True)
        p.op("sy", lambda e, e_=e_: e.dma_start(out=cx.wa[:], in_=w1bf[e_]),
             r=[w1k], w=[cx.wak], dma=True)
        p.op("sy", lambda e, e_=e_: e.dma_start(out=cx.wb[:], in_=w3bf[e_]),
             r=[w3k], w=[cx.wbk], dma=True)
        for tt in range(NTT):
            t0 = tt * 512
            rhs = cx.rhs[tt % 2]
            rk = cx.rhsk[tt % 2]
            p.op("sy", lambda e, t0=t0, rhs=rhs: e.dma_start(
                out=rhs[:, :, :], in_=xnbf[:, :, t0:t0 + 512].rearrange("k p t -> p k t")),
                r=[xnbfk[tt]], w=rk, dma=True)
            for c in range(CT):
                b1b = (tt * CT + c) % 2
                b3b = 2 + (tt * CT + c) % 2

                def m1(e, c=c, rhs=rhs, b1b=b1b):
                    last = None
                    for k in range(CT):
                        last = e.matmul(cx.pb[b1b][:], wav[:, k, c, :], rhs[:, k, :],
                                        start=(k == 0), stop=(k == CT - 1))
                    return last
                p.op("pe", m1, r=rk[:CT] + [cx.wak], w=[cx.pbk[b1b]])

                def m3(e, c=c, rhs=rhs, b3b=b3b):
                    last = None
                    for k in range(CT):
                        last = e.matmul(cx.pb[b3b][:], wbv[:, k, c, :], rhs[:, k, :],
                                        start=(k == 0), stop=(k == CT - 1))
                    return last
                p.op("pe", m3, r=rk[:CT] + [cx.wbk], w=[cx.pbk[b3b]])
                freg = 4 + (tt * CT + c) % 4
                p.op("ac", lambda e, c=c, b1b=b1b, freg=freg: e.activation(
                    cx.f32a[:, freg * 512:freg * 512 + 512], cx.pb[b1b][:], AF.Silu,
                    bias=b13_sb[:, c:c + 1]),
                    r=[cx.pbk[b1b], b13sk], w=[cx.fa_t[freg]])
                wreg = (tt * CT + c) % 4
                p.op("dv", lambda e, c=c, b3b=b3b, freg=freg, wreg=wreg: e.scalar_tensor_tensor(
                    cx.wc[:, wreg * 512:wreg * 512 + 512], cx.pb[b3b][:],
                    b13_sb[:, 8 + c:8 + c + 1], cx.f32a[:, freg * 512:freg * 512 + 512],
                    ALU.add, ALU.mult),
                    r=[cx.pbk[b3b], b13sk, cx.fa_t[freg]], w=[cx.wck[wreg]])
                p.op("po" if wreg % 2 == 0 else "ac", lambda e, c=c, t0=t0, wreg=wreg: e.dma_start(
                    out=hbf3[c, :, t0:t0 + 512], in_=cx.wc[:, wreg * 512:wreg * 512 + 512]),
                    r=[cx.wck[wreg]], w=[hbf3k[c]], dma=True)
        p.op("sy", lambda e, e_=e_: e.dma_start(out=cx.wa[:], in_=w2bf[e_]),
             r=[w2k], w=[cx.wak], dma=True)
        for tt in range(NTT):
            t0 = tt * 512
            p.op("sy", lambda e, e_=e_, t0=t0: e.dma_start(
                out=grow[:], in_=gR3[e_, t0:t0 + 512]), r=[gR3k[0]], w=[growk], dma=True)
            p.op("pe", lambda e: e.matmul(cx.pb[4][:], cx.c["ones_row"][:], grow[:],
                                          start=True, stop=True),
                 r=[growk, cx.ck["ones_row"]], w=[cx.pbk[4]])
            p.op("ac", lambda e: e.activation(gbc, cx.pb[4][:], AF.Copy),
                 r=[cx.pbk[4]], w=[gbck])
            rhs = cx.rhs[tt % 2]
            rk = cx.rhsk[tt % 2]
            p.op("sy", lambda e, t0=t0, rhs=rhs: e.dma_start(
                out=rhs[:, :, :], in_=hbf3[:, :, t0:t0 + 512].rearrange("k p t -> p k t")),
                r=list(hbf3k), w=rk, dma=True)
            for c in range(CT):
                b = (tt * CT + c) % 2

                def m2(e, c=c, rhs=rhs, b=b):
                    last = None
                    for k in range(CT):
                        last = e.matmul(cx.pb[b][:], wav[:, k, c, :], rhs[:, k, :],
                                        start=(k == 0), stop=(k == CT - 1))
                    return last
                p.op("pe", m2, r=rk[:CT] + [cx.wak], w=[cx.pbk[b]])
                freg = 4 + (tt * CT + c) % 4
                p.op("dv", lambda e, b=b, c=c, freg=freg: e.scalar_tensor_tensor(
                    cx.f32a[:, freg * 512:freg * 512 + 512], cx.pb[b][:],
                    b2_sb[:, c:c + 1], gbc, ALU.add, ALU.mult),
                    r=[cx.pbk[b], b2sk, gbck], w=[cx.fa_t[freg]])
                p.op("po", lambda e, c=c, t0=t0, freg=freg, e_=e_: e.dma_start(
                    out=g0T[c, :, t0:t0 + 512],
                    in_=cx.f32a[:, freg * 512:freg * 512 + 512],
                    accum_op=(ALU.add if e_ > 0 else ALU.bypass)),
                    r=[cx.fa_t[freg]], w=[g0Tk[c]], dma=True)

    combine_pass(cx, xT, xTk, sT, sTk, g0T, g0Tk, xout, xoutk, token_major)



LAYER_BUILDERS = {}


def build_nc(layers, final_tm):
    nc = bass.Bass("TRN2", target_bir_lowering=False)
    p = Prog(nc)
    consts = build_consts()
    cx = make_ctx(p, nc, consts)
    cx.host_inputs = {}

    xin = nc.dram_tensor("xin", [CT, 128, T], F32, kind="ExternalInput").ap()
    cur, curk = xin, [Trk()]
    for idx, li in enumerate(layers):
        last = idx == len(layers) - 1
        if last:
            if final_tm:
                nxt = nc.dram_tensor("out", [T, D], F32, kind="ExternalOutput").ap()
            else:
                nxt = nc.dram_tensor("out", [CT, 128, T], F32, kind="ExternalOutput").ap()
            nxtk = [Trk()]
        else:
            nxt, nxtk = dram(cx, f"x_l{li}", (CT, 128, T))
        builder = {0: build_layer0, 1: build_layer1, 2: build_layer2, 3: build_layer3}.get(li) or LAYER_BUILDERS[li]
        builder(cx, cur, curk, nxt, nxtk, token_major=(last and final_tm))
        cur, curk = nxt, nxtk

    p.op("sy", lambda e: e.nop(), r=curk)
    p.emit()
    return nc, cx.host_inputs


def host_prep(inputs, host_fns, xin_list):
    shared = {name: np.ascontiguousarray(fn(inputs), dtype=np.float32)
              for name, fn in host_fns.items()}
    consts = build_consts()
    in_maps = []
    for ci in range(N_CORES):
        m = {"xin": xin_list[ci]}
        m.update(shared)
        for name, arr in consts.items():
            m["c_" + name] = arr
        in_maps.append(m)
    return in_maps


def kernel(**inputs):
    import os
    fused = os.environ.get("KFUSE", "1") == "1"
    x = np.asarray(inputs["L0_attn_hidden_states"], np.float32)
    xin_list = [np.ascontiguousarray(x[ci].T.reshape(CT, 128, T)) for ci in range(N_CORES)]
    if fused:
        nc, host_fns = build_nc([0, 1, 2, 3], final_tm=True)
        in_maps = host_prep(inputs, host_fns, xin_list)
        res = run_bass_kernel_spmd(nc, in_maps, core_ids=list(range(N_CORES)))
        return np.stack([res.results[i]["out"] for i in range(N_CORES)], axis=0)
    cur = xin_list
    for li in range(4):
        nc, host_fns = build_nc([li], final_tm=(li == 3))
        in_maps = host_prep(inputs, host_fns, cur)
        res = run_bass_kernel_spmd(nc, in_maps, core_ids=list(range(N_CORES)))
        cur = [np.asarray(res.results[i]["out"]) for i in range(N_CORES)]
    return np.stack(cur, axis=0)
